# version 04
# baseline (speedup 1.0000x reference)
import contextlib
import numpy as np
import ml_dtypes
import concourse.bass as bass
import concourse.mybir as mybir
from concourse.bass_utils import run_bass_kernel_spmd

F32 = mybir.dt.float32
BF16 = mybir.dt.bfloat16
AF = mybir.ActivationFunctionType
ALU = mybir.AluOpType
AX = mybir.AxisListType
BFNP = ml_dtypes.bfloat16

NCORE = 8
T = 8192
TLOC = 1024
D = 2048
H = 4
DK = 256
DV = 512
KT = 16
NT = 8
NEXP = 32
INTER = 768
GAMMA = [float(1.0 - 2.0 ** (-5 - h)) for h in range(H)]
G128 = [float(np.float64(g) ** 128) for g in GAMMA]
EPS = 1e-5
TAU = 0.05

O_HT = 0
O_GP = 16384
O_QKOB = 32768
O_VB = 49152
O_F1 = 65536
O_F2 = 81920
O_M = 90112
O_BC0 = 94208
O_BC1 = 98304
O_ST = 102400
O_KTK = 104448
ACOLS = 105472


class Prog:
    ENGS = ("tensor", "vector", "scalar", "gpsimd", "sync")

    def __init__(self, sems):
        self.ops = {k: [] for k in self.ENGS}
        self.sems = sems
        self.ctr = {k: 0 for k in sems}
        self.last = {k: None for k in self.ENGS}
        self.last_kind = {k: None for k in self.ENGS}
        self.last_mark = {k: None for k in self.ENGS}

    def emit(self, eng, meth, *args, inc=None, incn=1, **kw):
        semh = self.sems[inc] if inc else None
        last = self.last
        dr = eng in ("vector", "scalar") and meth != "dma_start"

        def f(e, meth=meth, args=args, kw=kw, semh=semh, incn=incn, eng=eng, dr=dr):
            if dr:
                e.drain()
            ins = getattr(e, meth)(*args, **kw)
            last[eng] = ins
            if semh is not None:
                ins.then_inc(semh, incn)
        self.ops[eng].append(f)
        self.last_kind[eng] = "emit" if inc is None else "emit_inc"
        self.last_mark[eng] = inc
        if inc:
            self.ctr[inc] += incn
            return self.ctr[inc]
        return None

    def wait(self, eng, sem, v):
        if not v or v <= 0:
            return
        semh = self.sems[sem]

        def f(e, semh=semh, v=v):
            e.wait_ge(semh, v)
        self.ops[eng].append(f)

    def mark(self, eng, sem):
        if self.last_kind[eng] in ("mark", "emit_inc") and \
                self.last_mark[eng] == sem:
            return self.ctr[sem]
        assert self.last_kind[eng] in ("emit",), \
            f"mark({eng},{sem}) after {self.last_kind[eng]}"
        semh = self.sems[sem]
        last = self.last

        def f(e, semh=semh, eng=eng):
            last[eng].then_inc(semh, 1)
        self.ops[eng].append(f)
        self.last_kind[eng] = "mark"
        self.last_mark[eng] = sem
        self.ctr[sem] += 1
        return self.ctr[sem]

    def dma(self, queue, out, in_, sem=None):
        assert sem is not None, "dma needs an explicit slot semaphore"
        return self.emit(queue, "dma_start", inc=sem, incn=16, out=out, in_=in_)


class Psum:

    def __init__(self, P, ap):
        self.P, self.ap = P, ap
        self.free_at = None

    def twait(self):
        if self.free_at is not None:
            self.P.wait("tensor", *self.free_at)
            self.free_at = None

    def release(self, sem, val):
        self.free_at = (sem, val)


def build(dbg=()):
    nc = bass.Bass("TRN2", target_bir_lowering=False)

    def di(name, shape, dtype=F32):
        return nc.dram_tensor(name, shape, dtype, kind="ExternalInput").ap()

    x_t = di("x_t", [TLOC, D])
    wq_m = di("wq_m", [8, D, 128])
    wk_m = di("wk_m", [8, D, 128])
    wv_n = di("wv_n", [4, D, 512])
    wg_n = di("wg_n", [4, D, 512])
    wo_g = di("wo_g", [4, 512, D])
    gup = di("gup", [NEXP, 6, 2, D, 128])
    dnp = di("dnp", [NEXP, 4, INTER, 512])
    gatet = di("gatet", [D, 32])
    wmod_s = di("wmod_s", [KT, 128, 1536])
    c_cols = di("c_cols", [128, 16])
    gnorm = di("gnorm", [1, 512])
    alpha2 = di("alpha2", [1, 1])
    tabs = di("tabs", [2, H, 2, 128, TLOC])
    ident = di("ident", [128, 128], BF16)
    causalT = di("causalT", [128, 128], BF16)
    sinw = di("sinw", [128, 32])

    y = nc.dram_tensor("y", [TLOC, D], F32, kind="ExternalOutput").ap()
    dbg_out = {}
    for name, shape, dt in dbg:
        dbg_out[name] = nc.dram_tensor("dbg_" + name, list(shape), dt,
                                       kind="ExternalOutput").ap()

    modg_i = nc.dram_tensor("modg_i", [1, 1536], F32).ap()
    modg_o = nc.dram_tensor("modg_o", [NCORE, 1536], F32).ap()
    lg_i = nc.dram_tensor("lg_i", [128, 4096], BF16).ap()
    lg_o = nc.dram_tensor("lg_o", [NCORE * 128, 4096], BF16).ap()

    with contextlib.ExitStack() as ctx:
        en = ctx.enter_context
        AR = en(nc.sbuf_tensor("AR", [128, ACOLS], BF16))
        pA_t = en(nc.psum_tensor("pA", [128, 1024], F32))
        pB_t = en(nc.psum_tensor("pB", [128, 1024], F32))
        pC_t = en(nc.psum_tensor("pC", [128, 1024], F32))
        pD_t = en(nc.psum_tensor("pD", [128, 1024], F32))
        semnames = ["T", "V", "S", "C", "Lc", "Lx0", "Lx1", "Ltab",
                    "Lm0", "Lm1", "Lm2", "Lm3", "Lvg0", "Lvg1",
                    "Lwm0", "Lwm1", "Lgu0", "Lgu1",
                    "Ldn0", "Ldn1", "Ldn2", "Ldn3", "Lwo", "Lxa", "Ly",
                    "Llr0", "Llr1", "Lch0", "Lch1", "Lmi", "Lli", "Lgate"]
        semh = {s: en(nc.semaphore("sem" + s)) for s in semnames}
        block = en(nc.Block())
        P = Prog(semh)

        def B(o, n, p0=0, p1=128):
            return AR[p0:p1, o:o + n]

        def Fv(o, n, p0=0, p1=128):
            return AR[p0:p1, o:o + 2 * n].bitcast(F32)

        pAl, pAh = Psum(P, pA_t[:, 0:512]), Psum(P, pA_t[:, 512:1024])
        pBl, pBh = Psum(P, pB_t[:, 0:512]), Psum(P, pB_t[:, 512:1024])
        pCl, pCh = Psum(P, pC_t[:, 0:512]), Psum(P, pC_t[:, 512:1024])
        pDl, pDh = Psum(P, pD_t[:, 0:512]), Psum(P, pD_t[:, 512:1024])
        pT = pD_t[:, 0:512].bitcast(BF16)

        st_cur = [O_ST]

        def st_alloc(nbf):
            o = st_cur[0]
            st_cur[0] = (o + nbf + 15) // 16 * 16
            assert st_cur[0] <= O_ST + 2048, f"ST overflow {st_cur[0]}"
            return o

        ones = Fv(st_alloc(256), 128)
        csb = Fv(st_alloc(32), 16)
        silc = Fv(st_alloc(32), 16)
        id_sb = B(st_alloc(128), 128)
        cau_sb = B(st_alloc(128), 128)
        sinw_sb = Fv(st_alloc(64), 32)
        albB = Fv(st_alloc(16), 1)
        s_mean = Fv(st_alloc(16), 8)
        s_inv = Fv(st_alloc(16), 8)
        s_nmi = Fv(st_alloc(16), 8)
        s_ssq = Fv(st_alloc(16), 8)
        s_oiv = Fv(st_alloc(16), 8)
        s_t1 = Fv(st_alloc(16), 8)
        comb = Fv(st_alloc(512), 256)
        r_p = Fv(st_alloc(64), 32)
        r_cur = Fv(st_alloc(64), 32)
        r_tmp = Fv(st_alloc(64), 32)
        r_mk = Fv(st_alloc(16), 8)
        r_cnt = Fv(st_alloc(16), 8)
        r_a1 = Fv(st_alloc(64), 32)
        attn_sb = B(st_alloc(128), 128)
        attn_sbB = B(st_alloc(128), 128)

        chrow = [Fv(O_KTK, 512, 0, 1), Fv(O_M, 512, 0, 1)]
        rtmprow = Fv(O_M + 1024, 512, 0, 1)

        def hT(kt, c0, n):
            return B(O_HT + kt * 1024 + c0, n)

        def h2T(kt, c0, n):
            return B(O_QKOB + kt * 1024 + c0, n)

        def qseg(p, h, half):
            return B(O_QKOB + p * 2048 + h * 512 + half * 128, 128)

        def kseg(p, h, half):
            return B(O_QKOB + p * 2048 + h * 512 + 256 + half * 128, 128)

        def obv(p, c0, n):
            return B(O_QKOB + p * 2048 + c0, n)

        def vtile(p, c0, n):
            return B(O_VB + p * 2048 + c0, n)

        def gtile(p, c0, n):
            return B(O_GP + p * 2048 + c0, n)

        def accv(tt, c0, n):
            return Fv(4096 * tt + 2 * c0, n)

        def Mv(dt8, n=512):
            return B(O_M + dt8 * 512, n)

        ktok = B(O_KTK, 1024)
        ogT = B(O_M, 2048)

        P.emit("vector", "memset", ones, 1.0)
        v_ones = P.mark("vector", "V")

        P.dma("sync", csb, c_cols, sem="Lc")
        P.dma("sync", id_sb, ident, sem="Lc")
        P.dma("sync", cau_sb, causalT, sem="Lc")
        d_const = P.dma("sync", sinw_sb, sinw, sem="Lc")

        wslot = [Fv(O_VB, 1536), Fv(O_VB + 3072, 1536)]
        modloc = Fv(O_VB + 6144, 1536, 0, 1)

        P.wait("scalar", "Lc", d_const)
        v_silc = P.emit("scalar", "activation", silc, csb, AF.Silu, inc="S")

        mps = [pA_t[0:1, 0:512], pB_t[0:1, 0:512], pC_t[0:1, 0:512]]
        P.wait("tensor", "S", v_silc)
        t_after_k = {}
        for k in range(KT):
            if k >= 2:
                P.wait("gpsimd", "T", t_after_k[k - 2])
            d = P.dma("gpsimd", wslot[k % 2], wmod_s[k], sem="Lwm%d" % (k % 2))
            P.wait("tensor", "Lwm%d" % (k % 2), d)
            for nch in range(3):
                P.emit("tensor", "matmul", mps[nch], silc[:, k:k + 1],
                       wslot[k % 2][:, 512 * nch:512 * (nch + 1)],
                       start=(k == 0), stop=(k == KT - 1))
            t_after_k[k] = P.mark("tensor", "T")
        v_modmm = t_after_k[KT - 1]

        P.wait("scalar", "T", v_modmm)
        for nch in range(3):
            P.emit("scalar", "copy", modloc[:, 512 * nch:512 * (nch + 1)], mps[nch])
        v_modev = P.mark("scalar", "S")
        pAl.release("S", v_modev)
        pBl.release("S", v_modev)
        pCl.release("S", v_modev)

        P.wait("gpsimd", "S", v_modev)
        d_mi = P.dma("gpsimd", modg_i[:, :], modloc, sem="Lmi")
        P.wait("gpsimd", "Lmi", d_mi)
        v_cc1 = P.emit("gpsimd", "collective_compute", "AllGather", ALU.bypass,
                       replica_groups=[list(range(NCORE))],
                       ins=[modg_i[:, :]], outs=[modg_o[:, :]], inc="C")

        modflat = modg_o.rearrange("a b -> (a b)")
        bc_state = {"i": 0, "last_mm": {0: 0, 1: 0}}

        def bcast512(dst, chunk_idx, add1=False, row_math=None):
            i = bc_state["i"] % 2
            bc_state["i"] += 1
            row = chrow[i]
            P.wait("sync", "C", v_cc1)
            P.wait("sync", "T", bc_state["last_mm"][i])
            d = P.dma("sync", row,
                      modflat[512 * chunk_idx:512 * (chunk_idx + 1)].unsqueeze(0),
                      sem="Lch%d" % i)
            sem_mm, val_mm = "Lch%d" % i, d
            if row_math is not None:
                P.wait("vector", sem_mm, d)
                row_math(row)
                val_mm = P.mark("vector", "V")
                sem_mm = "V"
            P.wait("tensor", sem_mm, val_mm)
            P.wait("tensor", "V", v_ones)
            pBh.twait()
            P.emit("tensor", "matmul", pBh.ap, ones[0:1, :], row,
                   start=True, stop=True)
            mmv = P.mark("tensor", "T")
            bc_state["last_mm"][i] = mmv
            P.wait("scalar", "T", mmv)
            if add1:
                P.emit("scalar", "add", dst, pBh.ap, 1.0)
            else:
                P.emit("scalar", "copy", dst, pBh.ap)
            ev = P.mark("scalar", "S")
            pBh.release("S", ev)
            return ev

        def gamath(row):
            P.emit("vector", "tensor_scalar", rtmprow, row, 0.0, None, ALU.is_ge)
            P.emit("vector", "tensor_scalar", rtmprow, rtmprow, 2.0, -1.0,
                   ALU.mult, ALU.add)
            P.emit("vector", "tensor_tensor", row, row, rtmprow, ALU.mult)
            P.emit("vector", "tensor_scalar", row, row, 0.25, None, ALU.max)
            P.emit("vector", "tensor_tensor", row, row, rtmprow, ALU.mult)

        def rgmath(row):
            gamath(row)
            P.emit("vector", "reciprocal", row, row)

        bc0 = Fv(O_BC0, 2048)
        bc1 = Fv(O_BC1, 2048)

        d_al = P.dma("sync", chrow[0][:, 0:1], alpha2, sem="Lch0")
        P.wait("tensor", "Lch0", d_al)
        P.wait("tensor", "V", v_ones)
        pBh.twait()
        P.emit("tensor", "matmul", pB_t[:, 512:513], ones[0:1, :],
               chrow[0][:, 0:1], start=True, stop=True)
        v_mm = P.mark("tensor", "T")
        bc_state["last_mm"][0] = v_mm
        P.wait("scalar", "T", v_mm)
        P.emit("scalar", "copy", albB, pB_t[:, 512:513])
        v_alb = P.mark("scalar", "S")
        pBh.release("S", v_alb)

        for j in range(4):
            bcast512(bc0[:, 512 * j:512 * (j + 1)], 4 + j, add1=True)
        for j in range(4):
            v_casa = bcast512(bc1[:, 512 * j:512 * (j + 1)], j)

        xtile = [Fv(O_F2, 2048), Fv(O_F2 + 4096, 2048)]
        htmp = [Fv(O_F1, 2048), Fv(O_F1 + 4096, 2048)]
        hbuf = [B(O_F1 + 8192, 2048), B(O_F1 + 10240, 2048)]
        sq_scr = Fv(O_F1 + 12288, 2048)

        d_x = {0: P.dma("sync", xtile[0], x_t[0:128, :], sem="Lx0"),
               1: P.dma("sync", xtile[1], x_t[128:256, :], sem="Lx1")}
        x_free_at = {}
        v_h = {}
        tpos_res = {0: pDl, 1: pCl}
        tpos_bf = {0: pT, 1: pC_t[:, 0:512].bitcast(BF16)}

        v_h = {}
        x_free_at = {}
        vB_done = {}
        sC_done = {}

        def p1_A(t):
            xt = xtile[t % 2]
            xsem = "Lx%d" % (t % 2)
            P.wait("vector", xsem, d_x[t])
            P.emit("vector", "reduce_sum", s_mean[:, t:t + 1], xt, axis=AX.X)
            P.wait("scalar", xsem, d_x[t])
            P.emit("scalar", "activation", sq_scr, xt, AF.Square,
                   accum_out=s_ssq[:, t:t + 1])
            return P.mark("scalar", "S")

        sA_done = {0: None}
        sA_done[0] = p1_A(0)
        sA_done[1] = p1_A(1)
        for t in range(NT):
            xt, ht, hb = xtile[t % 2], htmp[t % 2], hbuf[t % 2]
            P.wait("vector", "S", sA_done[t])
            P.emit("vector", "tensor_scalar", s_mean[:, t:t + 1],
                   s_mean[:, t:t + 1], 1.0 / D, None, ALU.mult)
            P.emit("vector", "tensor_tensor", s_t1[:, t:t + 1],
                   s_mean[:, t:t + 1], s_mean[:, t:t + 1], ALU.mult)
            P.emit("vector", "tensor_scalar", s_inv[:, t:t + 1],
                   s_ssq[:, t:t + 1], 1.0 / D, None, ALU.mult)
            P.emit("vector", "tensor_tensor", s_inv[:, t:t + 1],
                   s_inv[:, t:t + 1], s_t1[:, t:t + 1], ALU.subtract)
            P.emit("vector", "tensor_scalar", s_inv[:, t:t + 1],
                   s_inv[:, t:t + 1], EPS, None, ALU.add)
            P.emit("vector", "reciprocal", s_inv[:, t:t + 1], s_inv[:, t:t + 1])
            vB = P.mark("vector", "V")
            P.wait("scalar", "V", vB)
            P.emit("scalar", "sqrt", s_inv[:, t:t + 1], s_inv[:, t:t + 1])
            sC = P.mark("scalar", "S")
            P.wait("vector", "S", sC)
            P.emit("vector", "tensor_scalar", s_nmi[:, t:t + 1],
                   s_mean[:, t:t + 1], s_inv[:, t:t + 1], -1.0,
                   ALU.mult, ALU.mult)
            vD = P.mark("vector", "V")
            P.wait("scalar", "V", vD)
            P.emit("scalar", "activation", ht, xt, AF.Identity,
                   bias=s_nmi[:, t:t + 1], scale=s_inv[:, t:t + 1])
            x_free_at[t] = P.mark("scalar", "S")
            if t + 2 < NT:
                P.wait("sync", "S", x_free_at[t])
                d_x[t + 2] = P.dma("sync", xtile[t % 2],
                                   x_t[128 * (t + 2):128 * (t + 3), :],
                                   sem="Lx%d" % (t % 2))
            if t + 2 < NT:
                sA_done[t + 2] = p1_A(t + 2)
            P.wait("vector", "S", x_free_at[t])
            P.wait("vector", "S", v_casa)
            P.emit("vector", "tensor_tensor", ht, ht, bc0, ALU.mult)
            P.emit("vector", "tensor_tensor", hb, ht, bc1, ALU.add)
            v_h[t] = P.mark("vector", "V")
            P.wait("tensor", "V", v_h[t])
            P.wait("tensor", "Lc", d_const)
            for grp in range(2):
                res = tpos_res[grp]
                pbf = tpos_bf[grp]
                res.twait()
                for j in range(8):
                    kt2 = grp * 8 + j
                    P.emit("tensor", "transpose", pbf[:, 128 * j:128 * (j + 1)],
                           hb[:, 128 * kt2:128 * (kt2 + 1)], id_sb)
                vt = P.mark("tensor", "T")
                P.wait("vector", "T", vt)
                for j in range(8):
                    kt2 = grp * 8 + j
                    P.emit("vector", "tensor_copy", hT(kt2, 128 * t, 128),
                           pbf[:, 128 * j:128 * (j + 1)])
                vv = P.mark("vector", "V")
                res.release("V", vv)
        v_hT_all = P.mark("vector", "V")

        mring = [B(O_F2 + 2048 * s, 2048) for s in range(4)]
        tabc = [Fv(O_F1, 1024), Fv(O_F1 + 2048, 1024)]
        rsc = [Fv(O_F1 + 4096, 1024), Fv(O_F1 + 6144, 1024)]
        mm_free = {}
        slot_i = 0
        v_rope_prev = 0
        d_tab_prev = 0

        for side in range(2):
            wsrc = wq_m if side == 0 else wk_m
            for h in range(H):
                P.wait("sync", "V", v_hT_all if (side == 0 and h == 0) else v_rope_prev)
                d_cos = P.dma("sync", tabc[0], tabs[side, h, 0], sem="Ltab")
                d_sin = P.dma("sync", tabc[1], tabs[side, h, 1], sem="Ltab")
                vps = []
                for half in range(2):
                    m = 2 * h + half
                    s4 = slot_i % 4
                    P.wait("gpsimd", "T", mm_free.get(s4, 0))
                    if slot_i == 0:
                        P.wait("gpsimd", "S", x_free_at[NT - 1])
                    dsl = P.dma("gpsimd",
                                mring[s4].rearrange("p (a m) -> p a m", m=128),
                                wsrc[m].rearrange("(a p) m -> p a m", p=128),
                                sem="Lm%d" % s4)
                    ptgt = [pAl, pAh] if half == 0 else [pBl, pBh]
                    P.wait("tensor", "Lm%d" % s4, dsl)
                    P.wait("tensor", "V", v_hT_all)
                    for tch in range(2):
                        ptgt[tch].twait()
                        for kt2 in range(KT):
                            P.emit("tensor", "matmul", ptgt[tch].ap,
                                   mring[s4][:, 128 * kt2:128 * (kt2 + 1)],
                                   hT(kt2, 512 * tch, 512),
                                   start=(kt2 == 0), stop=(kt2 == KT - 1))
                    vp = P.mark("tensor", "T")
                    mm_free[s4] = vp
                    vps.append(vp)
                    slot_i += 1
                P.wait("vector", "T", vps[-1])
                P.wait("vector", "Ltab", d_sin)
                q1 = pA_t[:, 0:1024]
                q2 = pB_t[:, 0:1024]
                segd = qseg if side == 0 else kseg
                P.emit("vector", "tensor_tensor", rsc[0], q1, tabc[0], ALU.mult)
                P.emit("vector", "tensor_tensor", rsc[1], q2, tabc[1], ALU.mult)
                for p in range(NT):
                    P.emit("vector", "tensor_tensor", segd(p, h, 0),
                           rsc[0][:, 128 * p:128 * (p + 1)],
                           rsc[1][:, 128 * p:128 * (p + 1)], ALU.subtract)
                P.emit("vector", "tensor_tensor", rsc[0], q2, tabc[0], ALU.mult)
                P.emit("vector", "tensor_tensor", rsc[1], q1, tabc[1], ALU.mult)
                for p in range(NT):
                    P.emit("vector", "tensor_tensor", segd(p, h, 1),
                           rsc[0][:, 128 * p:128 * (p + 1)],
                           rsc[1][:, 128 * p:128 * (p + 1)], ALU.add)
                v_rope_prev = P.mark("vector", "V")
                for pp in (pAl, pAh, pBl, pBh):
                    pp.release("V", v_rope_prev)
        v_rope_all = v_rope_prev

        vgslot = [B(O_F1 + 8192 * s, 8192) for s in range(2)]
        vg_free = {}
        vg_i = 0
        pcycle = [pAl, pAh, pBl, pBh]
        pc_i = 0
        v_g_ev = {}
        v_v_ev = {}
        P.wait("scalar", "Lmi", d_mi)

        def vg_proj(which):
            nonlocal vg_i, pc_i
            wsrc = wv_n if which == 0 else wg_n
            for nch in range(4):
                s2 = vg_i % 2
                if vg_i < 2:
                    P.wait("gpsimd", "V", v_rope_all)
                else:
                    P.wait("gpsimd", "T", vg_free[s2])
                dsl = P.dma("gpsimd",
                            vgslot[s2].rearrange("p (a m) -> p a m", m=512),
                            wsrc[nch].rearrange("(a p) m -> p a m", p=128),
                            sem="Lvg%d" % s2)
                P.wait("tensor", "Lvg%d" % s2, dsl)
                for tt in range(NT):
                    pres = pcycle[pc_i % 4]
                    pc_i += 1
                    pres.twait()
                    for kt2 in range(KT):
                        P.emit("tensor", "matmul", pres.ap,
                               hT(kt2, 128 * tt, 128),
                               vgslot[s2][:, 512 * kt2:512 * (kt2 + 1)],
                               start=(kt2 == 0), stop=(kt2 == KT - 1))
                    vmm = P.mark("tensor", "T")
                    if tt == NT - 1:
                        vg_free[s2] = vmm
                    P.wait("scalar", "T", vmm)
                    if which == 0:
                        P.emit("scalar", "copy",
                               vtile(tt, 512 * nch, 512), pres.ap)
                    else:
                        P.emit("scalar", "activation",
                               gtile(tt, 512 * nch, 512), pres.ap, AF.Silu)
                    ev = P.mark("scalar", "S")
                    pres.release("S", ev)
                    if which == 0:
                        v_v_ev[(nch, tt)] = ev
                    else:
                        v_g_ev[(nch, tt)] = ev
                vg_i += 1

        vg_proj(0)

        def Mpar(dt8, par, n=512):
            return B((O_M if par % 2 == 0 else O_F2) + dt8 * 512, n)

        P.wait("vector", "Lch0", P.ctr["Lch0"])
        P.wait("vector", "Lch1", P.ctr["Lch1"])
        P.wait("vector", "T", max(bc_state["last_mm"].values()))
        P.emit("vector", "memset", B(O_M, 4096), 0.0)
        v_m0 = P.mark("vector", "V")
        kv_cycle = [pAh, pBh, pCh, pDh]
        kv_i = 0
        v_Lupd = v_m0
        for p in range(NT):
            pDl.twait()
            P.wait("tensor", "V", v_rope_all)
            for dt8 in range(8):
                h, half = dt8 // 2, dt8 % 2
                P.emit("tensor", "transpose", pT[:, 128 * dt8:128 * (dt8 + 1)],
                       kseg(p, h, half), id_sb)
            vt = P.mark("tensor", "T")
            P.wait("vector", "T", vt)
            for dt8 in range(8):
                h = dt8 // 2
                P.emit("vector", "tensor_scalar",
                       ktok[:, 128 * dt8:128 * (dt8 + 1)],
                       pT[:, 128 * dt8:128 * (dt8 + 1)], G128[h], None, ALU.mult)
            v_kt = P.mark("vector", "V")
            pDl.release("V", v_kt)
            for dt8 in range(8):
                h, half = dt8 // 2, dt8 % 2
                pres = kv_cycle[kv_i % 4]
                kv_i += 1
                pres.twait()
                P.wait("tensor", "V", v_kt)
                P.wait("tensor", "S", v_v_ev[(h, p)])
                P.emit("tensor", "matmul", pres.ap,
                       ktok[:, 128 * dt8:128 * (dt8 + 1)],
                       vtile(p, 512 * h, 512), start=True, stop=True)
                vmm = P.mark("tensor", "T")
                P.wait("vector", "T", vmm)
                P.wait("vector", "V", v_m0)
                P.emit("vector", "scalar_tensor_tensor", Mpar(dt8, 0),
                       Mpar(dt8, 0), G128[h], pres.ap, ALU.mult, ALU.add)
                v_Lupd = P.mark("vector", "V")
                pres.release("V", v_Lupd)
        P.wait("gpsimd", "V", v_Lupd)
        d_li = P.dma("gpsimd", lg_i[:, :], B(O_M, 4096), sem="Lli")
        P.wait("gpsimd", "Lli", d_li)
        v_cc2 = P.emit("gpsimd", "collective_compute", "AllGather", ALU.bypass,
                       replica_groups=[list(range(NCORE))],
                       ins=[lg_i[:, :]], outs=[lg_o[:, :]], inc="C")

        gnormB = Fv(O_F2, 512)
        P.wait("sync", "V", v_Lupd)
        d_gn2 = P.dma("sync", chrow[0], gnorm, sem="Lch0")
        P.wait("tensor", "Lch0", d_gn2)
        pBh.twait()
        P.emit("tensor", "matmul", pBh.ap, ones[0:1, :], chrow[0],
               start=True, stop=True)
        v_mm = P.mark("tensor", "T")
        bc_state["last_mm"][0] = v_mm
        P.wait("scalar", "T", v_mm)
        P.emit("scalar", "copy", gnormB, pBh.ap)
        v_gnB = P.mark("scalar", "S")
        pBh.release("S", v_gnB)

        vg_proj(1)
        v_vg_all = v_g_ev[(3, NT - 1)]
        P.wait("vector", "S", max(v_vg_all, v_gnB))
        for p in range(NT):
            for h in range(H):
                P.emit("vector", "tensor_tensor", gtile(p, 512 * h, 512),
                       gtile(p, 512 * h, 512), gnormB, ALU.mult)
        v_gp = P.mark("vector", "V")

        lrecv = [B(O_F2 + 4096 * s, 4096) for s in range(2)]
        sps = [pAl, pAh, pBl, pBh, pCl, pCh, pDl, pDh]
        for s8 in sps:
            s8.free_at = None
        d_lr = {}
        v_lr_used = {}
        P.wait("gpsimd", "C", v_cc2)
        P.wait("gpsimd", "V", v_gp)
        d_lr[0] = P.dma("gpsimd", lrecv[0], lg_o[0:128, :], sem="Llr0")
        P.wait("vector", "S", v_vg_all)
        for j in range(NCORE):
            if j + 1 < NCORE:
                P.wait("gpsimd", "V", v_lr_used.get(j - 1, 0))
                d_lr[j + 1] = P.dma("gpsimd", lrecv[(j + 1) % 2],
                                    lg_o[128 * (j + 1):128 * (j + 2), :],
                                    sem="Llr%d" % ((j + 1) % 2))
            P.wait("vector", "Llr%d" % (j % 2), d_lr[j])
            for dt8 in range(8):
                h = dt8 // 2
                wcol = sinw_sb[:, 8 * h + j:8 * h + j + 1]
                src = lrecv[j % 2][:, 512 * dt8:512 * (dt8 + 1)]
                if j == 0:
                    P.emit("vector", "tensor_scalar", sps[dt8].ap, src,
                           wcol, None, ALU.mult)
                else:
                    P.emit("vector", "scalar_tensor_tensor", sps[dt8].ap, src,
                           wcol, sps[dt8].ap, ALU.mult, ALU.add)
            v_lr_used[j] = P.mark("vector", "V")
        P.wait("vector", "Lli", d_li)
        for dt8 in range(8):
            P.emit("vector", "tensor_copy", Mpar(dt8, 0), sps[dt8].ap)
        v_sin = P.mark("vector", "V")
        for s8 in sps:
            s8.release("V", v_sin)

        attn_sb2 = [attn_sb, attn_sbB]
        v_Mupd = {-1: v_sin}
        v_ob = {}
        at_cycle = [pAl, pBl]
        kv_cycle2 = [pAh, pBh]
        o_cycle = [pCl, pCh, pDh]
        for p in range(NT):
            rpar, wpar = p % 2, (p + 1) % 2
            pDl.twait()
            P.wait("tensor", "V", v_sin)
            for dt8 in range(8):
                h, half = dt8 // 2, dt8 % 2
                P.emit("tensor", "transpose", pT[:, 128 * dt8:128 * (dt8 + 1)],
                       kseg(p, h, half), id_sb)
            vt = P.mark("tensor", "T")
            P.wait("vector", "T", vt)
            for dt8 in range(8):
                h = dt8 // 2
                P.emit("vector", "tensor_scalar",
                       ktok[:, 128 * dt8:128 * (dt8 + 1)],
                       pT[:, 128 * dt8:128 * (dt8 + 1)], G128[h], None, ALU.mult)
            v_kt = P.mark("vector", "V")
            pDl.release("V", v_kt)
            vm = None
            for dt8 in range(8):
                h = dt8 // 2
                pres = kv_cycle2[dt8 % 2]
                pres.twait()
                P.wait("tensor", "V", v_kt)
                if p == 0:
                    P.wait("tensor", "S", v_v_ev[(h, p)])
                P.emit("tensor", "matmul", pres.ap,
                       ktok[:, 128 * dt8:128 * (dt8 + 1)],
                       vtile(p, 512 * h, 512), start=True, stop=True)
                vkv = P.mark("tensor", "T")
                P.wait("vector", "T", vkv)
                P.emit("vector", "scalar_tensor_tensor", Mpar(dt8, wpar),
                       Mpar(dt8, rpar), G128[h], pres.ap, ALU.mult, ALU.add)
                vm = P.mark("vector", "V")
                pres.release("V", vm)
            v_Mupd[p] = vm
            P.wait("tensor", "V", v_Mupd[p - 1])
            for h in range(H):
                at = at_cycle[h % 2]
                at.twait()
                P.emit("tensor", "matmul", at.ap[:, 0:128], kseg(p, h, 0),
                       qseg(p, h, 0), start=True, stop=False)
                P.emit("tensor", "matmul", at.ap[:, 0:128], kseg(p, h, 1),
                       qseg(p, h, 1), start=False, stop=True)
                v_at = P.mark("tensor", "T")
                P.wait("vector", "T", v_at)
                P.emit("vector", "tensor_tensor", attn_sb2[h % 2],
                       at.ap[:, 0:128], cau_sb, ALU.mult)
                v_am = P.mark("vector", "V")
                at.release("V", v_am)
                op = o_cycle[h % 3]
                op.twait()
                P.wait("tensor", "V", v_am)
                P.emit("tensor", "matmul", op.ap, attn_sb2[h % 2],
                       vtile(p, 512 * h, 512), start=True, stop=False)
                P.emit("tensor", "matmul", op.ap, qseg(p, h, 0),
                       Mpar(2 * h, rpar), start=False, stop=False)
                P.emit("tensor", "matmul", op.ap, qseg(p, h, 1),
                       Mpar(2 * h + 1, rpar), start=False, stop=True)
                v_omm = P.mark("tensor", "T")
                P.wait("scalar", "T", v_omm)
                P.emit("scalar", "copy", obv(p, 512 * h, 512), op.ap)
                v_ob[(p, h)] = P.mark("scalar", "S")
                op.release("S", v_ob[(p, h)])

        v_og = {}
        for p in range(NT):
            for h in range(H):
                P.wait("scalar", "V", v_gp if (p == 0 and h == 0) else 0)
                P.wait("scalar", "S", v_ob[(p, h)])
                P.emit("scalar", "activation", sq_scr[:, 0:512],
                       obv(p, 512 * h, 512), AF.Square,
                       accum_out=s_oiv[:, h:h + 1])
                vsq = P.mark("scalar", "S")
                P.wait("vector", "S", vsq)
                P.emit("vector", "tensor_scalar", s_oiv[:, h:h + 1],
                       s_oiv[:, h:h + 1], 1.0 / DV, EPS, ALU.mult, ALU.add)
                P.emit("vector", "reciprocal", s_oiv[:, h:h + 1],
                       s_oiv[:, h:h + 1])
                vv = P.mark("vector", "V")
                P.wait("scalar", "V", vv)
                P.emit("scalar", "sqrt", s_oiv[:, h:h + 1], s_oiv[:, h:h + 1])
                vs2 = P.mark("scalar", "S")
                P.wait("vector", "S", vs2)
                P.emit("vector", "scalar_tensor_tensor", obv(p, 512 * h, 512),
                       obv(p, 512 * h, 512), s_oiv[:, h:h + 1],
                       gtile(p, 512 * h, 512), ALU.mult, ALU.mult)
                v_og[p] = P.mark("vector", "V")
        v_og_all = v_og[NT - 1]

        v_ret_done0 = v_Mupd[NT - 1]
        P.wait("sync", "V", v_ret_done0)
        for j in range(4):
            v_gaB = bcast512(bc0[:, 512 * j:512 * (j + 1)], 8 + j,
                             row_math=gamath)

        woslot = [B(O_VB + 8192 * g, 8192) for g in range(4)]
        wtmp = Fv(O_F2, 512)
        v_ret_done = v_Mupd[NT - 1]
        d_wo = {}
        for g in range(4):
            if g < 2:
                P.wait("gpsimd", "S", v_ob[(NT - 1, H - 1)])
            else:
                P.wait("gpsimd", "V", v_og_all)
            d_wo[g] = P.dma("gpsimd",
                            woslot[g].rearrange("p (a m) -> p a m", m=2048),
                            wo_g[g].rearrange("(a p) m -> p a m", p=128),
                            sem="Lwo")

        d_x2 = {}
        P.wait("sync", "V", v_og_all)
        for tt in range(NT):
            d_x2[tt] = P.dma("sync", accv(tt, 0, 2048),
                             x_t[128 * tt:128 * (tt + 1), :], sem="Lxa")
        d_x2_all = d_x2[NT - 1]

        v_wo_used = {}
        wo_cycle = [pAl, pAh, pBl, pBh]
        wo_i = 0
        v_acc_wo = {}
        for tt in range(NT):
            P.wait("tensor", "V", v_og[tt])
            for grp in range(2):
                pDl.twait()
                for j in range(8):
                    kt2 = grp * 8 + j
                    P.emit("tensor", "transpose", pT[:, 128 * j:128 * (j + 1)],
                           obv(tt, 128 * kt2, 128), id_sb)
                vt = P.mark("tensor", "T")
                P.wait("vector", "T", vt)
                for j in range(8):
                    kt2 = grp * 8 + j
                    P.emit("vector", "tensor_copy", ogT[:, 128 * kt2:128 * (kt2 + 1)],
                           pT[:, 128 * j:128 * (j + 1)])
                v_tp = P.mark("vector", "V")
                pDl.release("V", v_tp)
            P.wait("tensor", "V", v_tp)
            for g in range(4):
                P.wait("tensor", "Lwo", d_wo[3])
                for nch in range(4):
                    pres = wo_cycle[wo_i % 4]
                    wo_i += 1
                    pres.twait()
                    for kg in range(4):
                        P.emit("tensor", "matmul", pres.ap,
                               ogT[:, 128 * (g * 4 + kg):128 * (g * 4 + kg + 1)],
                               woslot[g][:, 2048 * kg + 512 * nch:2048 * kg + 512 * (nch + 1)],
                               start=(kg == 0), stop=(kg == 3))
                    vmm = P.mark("tensor", "T")
                    if nch == 3 and tt == NT - 1:
                        v_wo_used[g] = vmm
                    P.wait("vector", "T", vmm)
                    P.wait("vector", "Lxa", d_x2_all)
                    if tt == 0 and g == 0 and nch == 0:
                        P.wait("vector", "S", v_gaB)
                    P.emit("vector", "tensor_tensor", wtmp, pres.ap,
                           bc0[:, 512 * nch:512 * (nch + 1)], ALU.mult)
                    P.emit("vector", "tensor_tensor",
                           accv(tt, 512 * nch, 512),
                           accv(tt, 512 * nch, 512), wtmp, ALU.add)
                    va = P.mark("vector", "V")
                    pres.release("V", va)
            v_acc_wo[tt] = va

        P.wait("sync", "T", v_wo_used[3])
        for j in range(4):
            bcast512(bc0[:, 512 * j:512 * (j + 1)], 16 + j, add1=True)
        for j in range(4):
            v_cbsb = bcast512(bc1[:, 512 * j:512 * (j + 1)], 12 + j)

        gate_bf = B(O_F2 + 2048, 512)
        P.wait("gpsimd", "T", v_wo_used[3])
        d_gate = P.dma("gpsimd",
                       gate_bf.rearrange("p (a m) -> p a m", m=32),
                       gatet.rearrange("(a p) m -> p a m", p=128), sem="Lgate")

        v_h2T = {}
        v_h2_done = {}
        v_comb = {}
        for t in range(NT):
            ht, hb = htmp[t % 2], hbuf[t % 2]
            xt = accv(t, 0, 2048)
            P.wait("vector", "V", v_acc_wo[t])
            P.emit("vector", "reduce_sum", s_mean[:, t:t + 1], xt, axis=AX.X)
            P.wait("scalar", "V", v_acc_wo[t])
            if t >= 1:
                P.wait("scalar", "V", v_var2_done[t - 1])
            else:
                P.wait("scalar", "T", v_wo_used[3])
            P.emit("scalar", "activation", sq_scr, xt, AF.Square,
                   accum_out=s_ssq[:, t:t + 1])
            v_sq = P.mark("scalar", "S")
            P.wait("vector", "S", v_sq)
            P.emit("vector", "tensor_scalar", s_mean[:, t:t + 1],
                   s_mean[:, t:t + 1], 1.0 / D, None, ALU.mult)
            P.emit("vector", "tensor_tensor", s_t1[:, t:t + 1],
                   s_mean[:, t:t + 1], s_mean[:, t:t + 1], ALU.mult)
            P.emit("vector", "tensor_scalar", s_inv[:, t:t + 1],
                   s_ssq[:, t:t + 1], 1.0 / D, None, ALU.mult)
            P.emit("vector", "tensor_tensor", s_inv[:, t:t + 1],
                   s_inv[:, t:t + 1], s_t1[:, t:t + 1], ALU.subtract)
            P.emit("vector", "tensor_scalar", s_inv[:, t:t + 1],
                   s_inv[:, t:t + 1], EPS, None, ALU.add)
            P.emit("vector", "reciprocal", s_inv[:, t:t + 1], s_inv[:, t:t + 1])
            v_st = P.mark("vector", "V")
            if t == 0:
                v_var2_done = {}
            v_var2_done[t] = v_st
            P.wait("scalar", "V", v_st)
            P.emit("scalar", "sqrt", s_inv[:, t:t + 1], s_inv[:, t:t + 1])
            v_si = P.mark("scalar", "S")
            P.wait("vector", "S", v_si)
            P.emit("vector", "tensor_scalar", s_nmi[:, t:t + 1],
                   s_mean[:, t:t + 1], s_inv[:, t:t + 1], -1.0,
                   ALU.mult, ALU.mult)
            v_nmi = P.mark("vector", "V")
            P.wait("scalar", "V", v_nmi)
            P.emit("scalar", "activation", ht, xt, AF.Identity,
                   bias=s_nmi[:, t:t + 1], scale=s_inv[:, t:t + 1])
            v_aff = P.mark("scalar", "S")
            P.wait("vector", "S", v_aff)
            P.wait("vector", "S", v_cbsb)
            P.emit("vector", "tensor_tensor", ht, ht, bc0, ALU.mult)
            P.emit("vector", "tensor_tensor", hb, ht, bc1, ALU.add)
            v_h2 = P.mark("vector", "V")
            v_h2_done[t] = v_h2
            P.wait("tensor", "V", v_h2)
            for grp in range(2):
                pDl.twait()
                for j in range(8):
                    kt2 = grp * 8 + j
                    P.emit("tensor", "transpose", pT[:, 128 * j:128 * (j + 1)],
                           hb[:, 128 * kt2:128 * (kt2 + 1)], id_sb)
                vt = P.mark("tensor", "T")
                P.wait("vector", "T", vt)
                for j in range(8):
                    kt2 = grp * 8 + j
                    P.emit("vector", "tensor_copy", h2T(kt2, 128 * t, 128),
                           pT[:, 128 * j:128 * (j + 1)])
                vv = P.mark("vector", "V")
                pDl.release("V", vv)
            v_h2T[t] = vv
            pCl.twait()
            P.wait("tensor", "Lgate", d_gate)
            P.wait("tensor", "V", vv)
            for kt2 in range(KT):
                P.emit("tensor", "matmul", pC_t[:, 0:32],
                       h2T(kt2, 128 * t, 128), gate_bf[:, 32 * kt2:32 * (kt2 + 1)],
                       start=(kt2 == 0), stop=(kt2 == KT - 1))
            v_lg = P.mark("tensor", "T")
            P.wait("vector", "T", v_lg)
            lg = pC_t[:, 0:32]
            P.emit("vector", "reduce_max", s_t1[:, t:t + 1], lg, axis=AX.X)
            P.emit("vector", "tensor_scalar", s_t1[:, t:t + 1],
                   s_t1[:, t:t + 1], -1.0, None, ALU.mult)
            v_rm = P.mark("vector", "V")
            P.wait("scalar", "V", v_rm)
            P.emit("scalar", "activation", r_p, lg, AF.Exp,
                   bias=s_t1[:, t:t + 1], accum_out=s_nmi[:, t:t + 1])
            v_exp = P.mark("scalar", "S")
            pCl.release("S", v_exp)
            P.wait("vector", "S", v_exp)
            P.emit("vector", "reciprocal", s_nmi[:, t:t + 1], s_nmi[:, t:t + 1])
            P.emit("vector", "tensor_scalar", r_p, r_p,
                   s_nmi[:, t:t + 1], None, ALU.mult)
            P.emit("vector", "tensor_copy", r_cur, r_p)
            for k in range(8):
                P.emit("vector", "reduce_max", r_mk[:, k:k + 1], r_cur, axis=AX.X)
                P.emit("vector", "tensor_scalar", r_tmp, r_cur,
                       r_mk[:, k:k + 1], None, ALU.is_equal)
                P.emit("vector", "scalar_tensor_tensor", r_cur, r_tmp,
                       -1e30, r_cur, ALU.mult, ALU.add)
            for k in range(8):
                P.emit("vector", "tensor_scalar", r_tmp, r_p,
                       r_mk[:, k:k + 1], 0.0, ALU.is_ge, ALU.add,
                       accum_out=r_cnt[:, k:k + 1])
            P.emit("vector", "tensor_scalar", r_a1[:, 0:8], r_cnt, 8.0, None,
                   ALU.is_ge)
            P.emit("vector", "tensor_tensor", r_a1[:, 0:8], r_a1[:, 0:8],
                   r_mk, ALU.mult)
            P.emit("vector", "reduce_max", s_t1[:, t:t + 1], r_a1[:, 0:8],
                   axis=AX.X)
            P.emit("vector", "tensor_scalar", r_tmp, r_p,
                   s_t1[:, t:t + 1], None, ALU.is_ge)
            P.emit("vector", "scalar_tensor_tensor", r_cur, r_p, 1.0, r_tmp,
                   ALU.mult, ALU.mult, accum_out=s_nmi[:, t:t + 1])
            P.emit("vector", "reciprocal", s_nmi[:, t:t + 1], s_nmi[:, t:t + 1])
            P.wait("vector", "S", v_alb)
            P.emit("vector", "tensor_tensor", s_nmi[:, t:t + 1],
                   s_nmi[:, t:t + 1], albB, ALU.mult)
            P.emit("vector", "tensor_scalar", r_a1, r_cur,
                   s_nmi[:, t:t + 1], TAU, ALU.mult, ALU.is_ge)
            P.emit("vector", "reduce_max", s_t1[:, t:t + 1], r_a1, axis=AX.X)
            P.emit("vector", "tensor_scalar", s_t1[:, t:t + 1],
                   s_t1[:, t:t + 1], 0.0, None, ALU.is_equal)
            P.emit("vector", "reduce_max", s_nmi[:, t:t + 1], r_p, axis=AX.X)
            P.emit("vector", "tensor_scalar", r_tmp, r_p,
                   s_nmi[:, t:t + 1], None, ALU.is_equal)
            P.emit("vector", "tensor_scalar", r_tmp, r_tmp,
                   s_t1[:, t:t + 1], None, ALU.mult)
            P.emit("vector", "tensor_tensor", r_a1, r_a1, r_tmp, ALU.max)
            P.emit("vector", "scalar_tensor_tensor", r_tmp, r_p, 1.0, r_a1,
                   ALU.mult, ALU.mult, accum_out=s_nmi[:, t:t + 1])
            P.emit("vector", "tensor_scalar", s_nmi[:, t:t + 1],
                   s_nmi[:, t:t + 1], 1e-12, None, ALU.max)
            P.emit("vector", "reciprocal", s_nmi[:, t:t + 1], s_nmi[:, t:t + 1])
            P.emit("vector", "tensor_scalar", comb[:, 32 * t:32 * (t + 1)],
                   r_tmp, s_nmi[:, t:t + 1], None, ALU.mult)
            v_comb[t] = P.mark("vector", "V")

        for j in range(4):
            v_rgb = bcast512(bc1[:, 512 * j:512 * (j + 1)], 20 + j,
                             row_math=rgmath)
        P.wait("vector", "S", v_rgb)
        v_accr = {}
        for tt in range(NT):
            P.emit("vector", "tensor_tensor", accv(tt, 0, 2048),
                   accv(tt, 0, 2048), bc1, ALU.mult)
            v_accr[tt] = P.mark("vector", "V")

        gu_sp = [B(O_F1 + 4096 * s, 4096) for s in range(2)]
        heT = B(O_F1 + 8192, 6144)
        sgb = [B(O_F1 + 14336, 512), B(O_F1 + 14848, 512)]
        dn_s = [B(O_VB + 3072 * s, 3072) for s in range(4)]

        gu_used = {}
        dn_used = {}
        d_gu = {}
        d_dn = {}

        def prefetch(e, pr):
            idx = e * 6 + pr
            P.wait("gpsimd", "T", gu_used.get(idx - 2, 0))
            if idx == 0:
                P.wait("gpsimd", "T", v_wo_used[2])
                P.wait("gpsimd", "V", v_h2T[NT - 1])
            d_gu[idx] = P.dma("gpsimd",
                              gu_sp[idx % 2].rearrange("p (b a m) -> p b a m", b=2, m=128),
                              gup[e, pr].rearrange("b (a p) m -> p b a m", p=128),
                              sem="Lgu%d" % (idx % 2))

        def prefetch_dn(e, nch):
            idx = e * 4 + nch
            P.wait("gpsimd", "T", dn_used.get(idx - 4, 0))
            if idx == 0:
                P.wait("gpsimd", "T", v_wo_used[0])
                P.wait("gpsimd", "T", v_wo_used[1])
            d_dn[idx] = P.dma("gpsimd",
                              dn_s[idx % 4].rearrange("p (a m) -> p a m", m=512),
                              dnp[e, nch].rearrange("(a p) m -> p a m", p=128),
                              sem="Ldn%d" % (idx % 4))

        for pr in range(2):
            prefetch(0, pr)
        for nch in range(4):
            prefetch_dn(0, nch)

        v_last_acc = None
        pe_cycle = [(pAl, pAh), (pBl, pBh), (pCl, pCh), (pDl, pDh)]
        v_he = {}
        sg_busy = {}
        for e in range(NEXP):
            pe_i = 0
            for pr in range(6):
                idx = e * 6 + pr
                sl = gu_sp[idx % 2]
                P.wait("tensor", "Lgu%d" % (idx % 2), d_gu[idx])
                if e == 0 and pr == 0:
                    P.wait("tensor", "V", v_h2T[NT - 1])
                for tch in range(2):
                    pg, pu = pe_cycle[pe_i % 4]
                    pe_i += 1
                    pg.twait()
                    pu.twait()
                    P.wait("tensor", "V", v_he.get((e - 1, pr, tch), 0))
                    for kt2 in range(KT):
                        P.emit("tensor", "matmul", pg.ap,
                               sl[:, 128 * kt2:128 * (kt2 + 1)],
                               h2T(kt2, 512 * tch, 512),
                               start=(kt2 == 0), stop=(kt2 == KT - 1))
                    for kt2 in range(KT):
                        P.emit("tensor", "matmul", pu.ap,
                               sl[:, 2048 + 128 * kt2:2048 + 128 * (kt2 + 1)],
                               h2T(kt2, 512 * tch, 512),
                               start=(kt2 == 0), stop=(kt2 == KT - 1))
                    vmm = P.mark("tensor", "T")
                    if tch == 1:
                        gu_used[idx] = vmm
                        if pr + 2 < 6:
                            prefetch(e, pr + 2)
                        elif e + 1 < NEXP:
                            prefetch(e + 1, pr + 2 - 6)
                    P.wait("scalar", "T", vmm)
                    sgi = pe_i % 2
                    sg = sgb[sgi]
                    P.wait("scalar", "V", sg_busy.get(sgi, 0))
                    P.emit("scalar", "activation", sg, pg.ap, AF.Silu)
                    vsg = P.mark("scalar", "S")
                    P.wait("vector", "S", vsg)
                    P.emit("vector", "tensor_tensor",
                           heT[:, 1024 * pr + 512 * tch:1024 * pr + 512 * (tch + 1)],
                           sg, pu.ap, ALU.mult)
                    vhe = P.mark("vector", "V")
                    sg_busy[sgi] = vhe
                    v_he[(e, pr, tch)] = vhe
                    pg.release("V", vhe)
                    pu.release("V", vhe)
            dn_cycle = [pAl, pAh, pBl, pBh]
            pe_i = 0
            for tt in range(NT):
                for nch in range(4):
                    idx = e * 4 + nch
                    pres = dn_cycle[pe_i % 4]
                    pe_i += 1
                    pres.twait()
                    P.wait("tensor", "Ldn%d" % (idx % 4), d_dn[idx])
                    if tt == 0:
                        for pr in range(6):
                            P.wait("tensor", "V", v_he[(e, pr, 0)])
                            P.wait("tensor", "V", v_he[(e, pr, 1)])
                    for kt2 in range(6):
                        P.emit("tensor", "matmul", pres.ap,
                               heT[:, 1024 * kt2 + 128 * tt:1024 * kt2 + 128 * (tt + 1)],
                               dn_s[idx % 4][:, 512 * kt2:512 * (kt2 + 1)],
                               start=(kt2 == 0), stop=(kt2 == 5))
                    vmm = P.mark("tensor", "T")
                    if tt == NT - 1:
                        dn_used[idx] = vmm
                        if e + 1 < NEXP:
                            prefetch_dn(e + 1, nch)
                    P.wait("vector", "T", vmm)
                    if e == 0:
                        P.wait("vector", "V", v_accr[tt])
                        P.wait("vector", "V", v_comb[NT - 1])
                    P.emit("vector", "scalar_tensor_tensor",
                           accv(tt, 512 * nch, 512), pres.ap,
                           comb[:, 32 * tt + e:32 * tt + e + 1],
                           accv(tt, 512 * nch, 512), ALU.mult, ALU.add)
                    va = P.mark("vector", "V")
                    pres.release("V", va)
            v_last_acc = va

        for j in range(4):
            v_gbB = bcast512(bc0[:, 512 * j:512 * (j + 1)], 20 + j,
                             row_math=gamath)
        P.wait("vector", "S", v_gbB)
        P.wait("vector", "V", v_last_acc)
        d_y = {}
        for tt in range(NT):
            P.emit("vector", "tensor_tensor", accv(tt, 0, 2048),
                   accv(tt, 0, 2048), bc0, ALU.mult)
            vo = P.mark("vector", "V")
            P.wait("sync", "V", vo)
            d_y[tt] = P.dma("sync", y[128 * tt:128 * (tt + 1), :],
                            accv(tt, 0, 2048), sem="Ly")
        P.wait("sync", "Ly", d_y[NT - 1])

        @block.tensor
        def _(e):
            for f in P.ops["tensor"]:
                f(e)

        @block.vector
        def _(e):
            for f in P.ops["vector"]:
                f(e)

        @block.scalar
        def _(e):
            for f in P.ops["scalar"]:
                f(e)

        @block.gpsimd
        def _(e):
            for f in P.ops["gpsimd"]:
                f(e)

        @block.sync
        def _(e):
            for f in P.ops["sync"]:
                f(e)

    return nc


_cache = {}


def _prep(inputs):
    f32 = np.float32
    x = np.ascontiguousarray(np.asarray(inputs["L0_attn_x"], f32).reshape(T, D))
    wq = np.asarray(inputs["L0_attn_Wq"], f32)
    wk = np.asarray(inputs["L0_attn_Wk"], f32)
    wv = np.asarray(inputs["L0_attn_Wv"], f32)
    wg = np.asarray(inputs["L0_attn_Wg"], f32)
    wo = np.asarray(inputs["L0_attn_Wo"], f32)
    gnw = np.asarray(inputs["L0_attn_gnorm_w"], f32).reshape(1, 512)
    gwt = np.asarray(inputs["L0_ffn_gate_weight"], f32)
    gu = np.asarray(inputs["L0_ffn_gate_up_proj"], f32)
    dn = np.asarray(inputs["L0_ffn_down_proj"], f32)
    alpha = np.asarray(inputs["L0_ffn_layer_alpha"], f32).reshape(1, 1)
    cv = np.asarray(inputs["L0_c"], f32).reshape(-1)
    wmod = np.asarray(inputs["L0_wmod"], f32)

    wq_m = np.ascontiguousarray(wq.T.reshape(D, 8, 128).transpose(1, 0, 2))
    wk_m = np.ascontiguousarray(wk.T.reshape(D, 8, 128).transpose(1, 0, 2))
    wv_n = np.ascontiguousarray(wv.T.reshape(D, 4, 512).transpose(1, 0, 2))
    wg_n = np.ascontiguousarray(wg.T.reshape(D, 4, 512).transpose(1, 0, 2))
    wo_g = np.ascontiguousarray(wo.T.reshape(4, 512, D))
    gu_t = gu.transpose(0, 2, 1).reshape(NEXP, D, 12, 128).transpose(0, 2, 1, 3)
    gup = np.ascontiguousarray(
        gu_t.reshape(NEXP, 2, 6, D, 128).transpose(0, 2, 1, 3, 4))
    dnp = np.ascontiguousarray(
        dn.transpose(0, 2, 1).reshape(NEXP, INTER, 4, 512).transpose(0, 2, 1, 3))
    gatet = np.ascontiguousarray(gwt.T)
    c_cols = np.ascontiguousarray(cv.reshape(16, 128).T)

    tglob = np.arange(T, dtype=f32)
    invf = (1.0 / (10000.0 ** (np.arange(0, DK, 2, dtype=f32) / DK))).astype(f32)
    freqs = (tglob[:, None] * invf[None, :]).astype(f32)
    cosb = np.cos(freqs).astype(f32).T
    sinb = np.sin(freqs).astype(f32).T
    ip = np.arange(T, dtype=np.float64) % 128
    tabs_all = np.zeros((NCORE, 2, H, 2, 128, TLOC), f32)
    for h in range(H):
        qs = (np.float64(GAMMA[h]) ** (ip + 1) * DK ** -0.5)
        ks = (np.float64(GAMMA[h]) ** (-ip - 1))
        for c in range(NCORE):
            sl = slice(TLOC * c, TLOC * (c + 1))
            tabs_all[c, 0, h, 0] = cosb[:, sl] * qs[None, sl]
            tabs_all[c, 0, h, 1] = sinb[:, sl] * qs[None, sl]
            tabs_all[c, 1, h, 0] = cosb[:, sl] * ks[None, sl]
            tabs_all[c, 1, h, 1] = sinb[:, sl] * ks[None, sl]

    identity = np.eye(128, dtype=BFNP)
    ii = np.arange(128)
    causal = (ii[None, :] >= ii[:, None]).astype(BFNP)

    sinw_all = np.zeros((NCORE, 128, 32), f32)
    for c in range(NCORE):
        for h in range(H):
            for j in range(NCORE):
                if j < c:
                    sinw_all[c, :, 8 * h + j] = np.float64(GAMMA[h]) ** (1024.0 * (c - 1 - j))

    per_core = []
    for c in range(NCORE):
        m = dict(
            x_t=np.ascontiguousarray(x[TLOC * c:TLOC * (c + 1)]),
            wq_m=wq_m, wk_m=wk_m, wv_n=wv_n, wg_n=wg_n, wo_g=wo_g,
            gup=gup, dnp=dnp, gatet=gatet,
            wmod_s=np.ascontiguousarray(
                wmod[:, 1536 * c:1536 * (c + 1)].reshape(KT, 128, 1536)),
            c_cols=c_cols, gnorm=gnw, alpha2=alpha,
            tabs=np.ascontiguousarray(tabs_all[c]),
            ident=identity, causalT=causal,
            sinw=np.ascontiguousarray(sinw_all[c]),
        )
        per_core.append(m)
    return per_core


def kernel(**inputs):
    per_core = _prep(inputs)
    if "nc" not in _cache:
        _cache["nc"] = build()
    nc = _cache["nc"]
    res = run_bass_kernel_spmd(nc, per_core, core_ids=list(range(NCORE)))
    out = np.concatenate([np.asarray(r["y"]) for r in res.results], axis=0)
    return out.reshape(1, T, D).astype(np.float32)




# version 05
# speedup vs baseline: 1.1994x; 1.1994x over previous
import contextlib
import numpy as np
import ml_dtypes
import concourse.bass as bass
import concourse.mybir as mybir
from concourse.bass_utils import run_bass_kernel_spmd

F32 = mybir.dt.float32
BF16 = mybir.dt.bfloat16
AF = mybir.ActivationFunctionType
ALU = mybir.AluOpType
AX = mybir.AxisListType
BFNP = ml_dtypes.bfloat16

NCORE = 8
T = 8192
TLOC = 1024
D = 2048
H = 4
DK = 256
DV = 512
KT = 16
NT = 8
NEXP = 32
INTER = 768
GAMMA = [float(1.0 - 2.0 ** (-5 - h)) for h in range(H)]
G128 = [float(np.float64(g) ** 128) for g in GAMMA]
EPS = 1e-5
TAU = 0.05

O_HT = 0
O_GP = 16384
O_QKOB = 32768
O_VB = 49152
O_F1 = 65536
O_F2 = 81920
O_M = 90112
O_BC0 = 94208
O_BC1 = 98304
O_ST = 102400
O_KTK = 104448
ACOLS = 105472


class Prog:
    ENGS = ("tensor", "vector", "scalar", "gpsimd", "sync")

    def __init__(self, sems):
        self.ops = {k: [] for k in self.ENGS}
        self.sems = sems
        self.ctr = {k: 0 for k in sems}
        self.last = {k: None for k in self.ENGS}
        self.last_kind = {k: None for k in self.ENGS}
        self.last_mark = {k: None for k in self.ENGS}

    def emit(self, eng, meth, *args, inc=None, incn=1, **kw):
        semh = self.sems[inc] if inc else None
        last = self.last
        dr = eng in ("vector", "scalar") and meth != "dma_start"

        def f(e, meth=meth, args=args, kw=kw, semh=semh, incn=incn, eng=eng, dr=dr):
            if dr:
                e.drain()
            ins = getattr(e, meth)(*args, **kw)
            last[eng] = ins
            if semh is not None:
                ins.then_inc(semh, incn)
        self.ops[eng].append(f)
        self.last_kind[eng] = "emit" if inc is None else "emit_inc"
        self.last_mark[eng] = inc
        if inc:
            self.ctr[inc] += incn
            return self.ctr[inc]
        return None

    def wait(self, eng, sem, v):
        if not v or v <= 0:
            return
        semh = self.sems[sem]

        def f(e, semh=semh, v=v):
            e.wait_ge(semh, v)
        self.ops[eng].append(f)

    def mark(self, eng, sem):
        if self.last_kind[eng] in ("mark", "emit_inc") and \
                self.last_mark[eng] == sem:
            return self.ctr[sem]
        assert self.last_kind[eng] in ("emit",), \
            f"mark({eng},{sem}) after {self.last_kind[eng]}"
        semh = self.sems[sem]
        last = self.last

        def f(e, semh=semh, eng=eng):
            last[eng].then_inc(semh, 1)
        self.ops[eng].append(f)
        self.last_kind[eng] = "mark"
        self.last_mark[eng] = sem
        self.ctr[sem] += 1
        return self.ctr[sem]

    def dma(self, queue, out, in_, sem=None):
        assert sem is not None, "dma needs an explicit slot semaphore"
        return self.emit(queue, "dma_start", inc=sem, incn=16, out=out, in_=in_)


class Psum:

    def __init__(self, P, ap):
        self.P, self.ap = P, ap
        self.free_at = None

    def twait(self):
        if self.free_at is not None:
            self.P.wait("tensor", *self.free_at)
            self.free_at = None

    def release(self, sem, val):
        self.free_at = (sem, val)


def build(dbg=()):
    nc = bass.Bass("TRN2", target_bir_lowering=False)

    def di(name, shape, dtype=F32):
        return nc.dram_tensor(name, shape, dtype, kind="ExternalInput").ap()

    x_t = di("x_t", [TLOC, D])
    wq_m = di("wq_m", [8, D, 128])
    wk_m = di("wk_m", [8, D, 128])
    wv_n = di("wv_n", [4, D, 512])
    wg_n = di("wg_n", [4, D, 512])
    wo_g = di("wo_g", [4, 512, D])
    gup = di("gup", [NEXP, 6, 2, D, 128])
    dnp = di("dnp", [NEXP, 4, INTER, 512])
    gatet = di("gatet", [D, 32])
    wmod_s = di("wmod_s", [KT, 128, 1536])
    c_cols = di("c_cols", [128, 16])
    gnorm = di("gnorm", [1, 512])
    alpha2 = di("alpha2", [1, 1])
    tabs = di("tabs", [2, H, 2, 128, TLOC])
    ident = di("ident", [128, 128], BF16)
    causalT = di("causalT", [128, 128], BF16)
    sinw = di("sinw", [128, 32])

    y = nc.dram_tensor("y", [TLOC, D], F32, kind="ExternalOutput").ap()
    dbg_out = {}
    for name, shape, dt in dbg:
        dbg_out[name] = nc.dram_tensor("dbg_" + name, list(shape), dt,
                                       kind="ExternalOutput").ap()

    modg_i = nc.dram_tensor("modg_i", [1, 1536], F32).ap()
    modg_o = nc.dram_tensor("modg_o", [NCORE, 1536], F32).ap()
    lg_i = nc.dram_tensor("lg_i", [128, 4096], BF16).ap()
    lg_o = nc.dram_tensor("lg_o", [NCORE * 128, 4096], BF16).ap()

    with contextlib.ExitStack() as ctx:
        en = ctx.enter_context
        AR = en(nc.sbuf_tensor("AR", [128, ACOLS], BF16))
        pA_t = en(nc.psum_tensor("pA", [128, 1024], F32))
        pB_t = en(nc.psum_tensor("pB", [128, 1024], F32))
        pC_t = en(nc.psum_tensor("pC", [128, 1024], F32))
        pD_t = en(nc.psum_tensor("pD", [128, 1024], F32))
        semnames = ["T", "V", "S", "C", "Lc", "Lx0", "Lx1", "Ltab",
                    "Lm0", "Lm1", "Lm2", "Lm3", "Lvg0", "Lvg1",
                    "Lwm0", "Lwm1", "Lgu0", "Lgu1",
                    "Ldn0", "Ldn1", "Ldn2", "Ldn3", "Lwo", "Lxa", "Ly",
                    "Llr0", "Llr1", "Lch0", "Lch1", "Lmi", "Lli", "Lgate"]
        semh = {s: en(nc.semaphore("sem" + s)) for s in semnames}
        block = en(nc.Block())
        P = Prog(semh)

        def B(o, n, p0=0, p1=128):
            return AR[p0:p1, o:o + n]

        def Fv(o, n, p0=0, p1=128):
            return AR[p0:p1, o:o + 2 * n].bitcast(F32)

        pAl, pAh = Psum(P, pA_t[:, 0:512]), Psum(P, pA_t[:, 512:1024])
        pBl, pBh = Psum(P, pB_t[:, 0:512]), Psum(P, pB_t[:, 512:1024])
        pCl, pCh = Psum(P, pC_t[:, 0:512]), Psum(P, pC_t[:, 512:1024])
        pDl, pDh = Psum(P, pD_t[:, 0:512]), Psum(P, pD_t[:, 512:1024])
        pT = pD_t[:, 0:512].bitcast(BF16)

        st_cur = [O_ST]

        def st_alloc(nbf):
            o = st_cur[0]
            st_cur[0] = (o + nbf + 15) // 16 * 16
            assert st_cur[0] <= O_ST + 2048, f"ST overflow {st_cur[0]}"
            return o

        ones = Fv(st_alloc(256), 128)
        csb = Fv(st_alloc(32), 16)
        silc = Fv(st_alloc(32), 16)
        id_sb = B(st_alloc(128), 128)
        cau_sb = B(st_alloc(128), 128)
        sinw_sb = Fv(st_alloc(64), 32)
        albB = Fv(st_alloc(16), 1)
        s_mean = Fv(st_alloc(16), 8)
        s_inv = Fv(st_alloc(16), 8)
        s_nmi = Fv(st_alloc(16), 8)
        s_ssq = Fv(st_alloc(16), 8)
        s_oiv = Fv(st_alloc(16), 8)
        s_t1 = Fv(st_alloc(16), 8)
        comb = Fv(st_alloc(512), 256)
        r_p = Fv(st_alloc(64), 32)
        r_cur = Fv(st_alloc(64), 32)
        r_tmp = Fv(st_alloc(64), 32)
        r_mk = Fv(st_alloc(16), 8)
        r_cnt = Fv(st_alloc(16), 8)
        r_a1 = Fv(st_alloc(64), 32)
        attn_sb = B(st_alloc(128), 128)
        attn_sbB = B(st_alloc(128), 128)

        chrow = [Fv(O_KTK, 512, 0, 1), Fv(O_M, 512, 0, 1)]
        rtmprow = Fv(O_M + 1024, 512, 0, 1)

        def hT(kt, c0, n):
            return B(O_HT + kt * 1024 + c0, n)

        def h2T(kt, c0, n):
            return B(O_QKOB + kt * 1024 + c0, n)

        def qseg(p, h, half):
            return B(O_QKOB + p * 2048 + h * 512 + half * 128, 128)

        def kseg(p, h, half):
            return B(O_QKOB + p * 2048 + h * 512 + 256 + half * 128, 128)

        def obv(p, c0, n):
            return B(O_QKOB + p * 2048 + c0, n)

        def vtile(p, c0, n):
            return B(O_VB + p * 2048 + c0, n)

        def gtile(p, c0, n):
            return B(O_GP + p * 2048 + c0, n)

        def accv(tt, c0, n):
            return Fv(4096 * tt + 2 * c0, n)

        def Mv(dt8, n=512):
            return B(O_M + dt8 * 512, n)

        ktok = B(O_KTK, 1024)
        ogT = B(O_M, 2048)

        P.emit("vector", "memset", ones, 1.0)
        v_ones = P.mark("vector", "V")

        P.dma("sync", csb, c_cols, sem="Lc")
        P.dma("sync", id_sb, ident, sem="Lc")
        P.dma("sync", cau_sb, causalT, sem="Lc")
        d_const = P.dma("sync", sinw_sb, sinw, sem="Lc")

        wslot = [Fv(O_VB, 1536), Fv(O_VB + 3072, 1536)]
        modloc = Fv(O_VB + 6144, 1536, 0, 1)

        P.wait("scalar", "Lc", d_const)
        v_silc = P.emit("scalar", "activation", silc, csb, AF.Silu, inc="S")

        mps = [pA_t[0:1, 0:512], pB_t[0:1, 0:512], pC_t[0:1, 0:512]]
        P.wait("tensor", "S", v_silc)
        t_after_k = {}
        for k in range(KT):
            if k >= 2:
                P.wait("gpsimd", "T", t_after_k[k - 2])
            d = P.dma("gpsimd", wslot[k % 2], wmod_s[k], sem="Lwm%d" % (k % 2))
            P.wait("tensor", "Lwm%d" % (k % 2), d)
            for nch in range(3):
                P.emit("tensor", "matmul", mps[nch], silc[:, k:k + 1],
                       wslot[k % 2][:, 512 * nch:512 * (nch + 1)],
                       start=(k == 0), stop=(k == KT - 1))
            t_after_k[k] = P.mark("tensor", "T")
        v_modmm = t_after_k[KT - 1]

        P.wait("scalar", "T", v_modmm)
        for nch in range(3):
            P.emit("scalar", "copy", modloc[:, 512 * nch:512 * (nch + 1)], mps[nch])
        v_modev = P.mark("scalar", "S")
        pAl.release("S", v_modev)
        pBl.release("S", v_modev)
        pCl.release("S", v_modev)

        P.wait("gpsimd", "S", v_modev)
        d_mi = P.dma("gpsimd", modg_i[:, :], modloc, sem="Lmi")
        P.wait("gpsimd", "Lmi", d_mi)
        v_cc1 = P.emit("gpsimd", "collective_compute", "AllGather", ALU.bypass,
                       replica_groups=[list(range(NCORE))],
                       ins=[modg_i[:, :]], outs=[modg_o[:, :]], inc="C")

        modflat = modg_o.rearrange("a b -> (a b)")
        bc_state = {"i": 0, "last_mm": {0: 0, 1: 0}}

        def bcast512(dst, chunk_idx, add1=False, row_math=None):
            i = bc_state["i"] % 2
            bc_state["i"] += 1
            row = chrow[i]
            P.wait("sync", "C", v_cc1)
            P.wait("sync", "T", bc_state["last_mm"][i])
            d = P.dma("sync", row,
                      modflat[512 * chunk_idx:512 * (chunk_idx + 1)].unsqueeze(0),
                      sem="Lch%d" % i)
            sem_mm, val_mm = "Lch%d" % i, d
            if row_math is not None:
                P.wait("vector", sem_mm, d)
                row_math(row)
                val_mm = P.mark("vector", "V")
                sem_mm = "V"
            P.wait("tensor", sem_mm, val_mm)
            P.wait("tensor", "V", v_ones)
            pBh.twait()
            P.emit("tensor", "matmul", pBh.ap, ones[0:1, :], row,
                   start=True, stop=True)
            mmv = P.mark("tensor", "T")
            bc_state["last_mm"][i] = mmv
            P.wait("scalar", "T", mmv)
            if add1:
                P.emit("scalar", "add", dst, pBh.ap, 1.0)
            else:
                P.emit("scalar", "copy", dst, pBh.ap)
            ev = P.mark("scalar", "S")
            pBh.release("S", ev)
            return ev

        def gamath(row):
            P.emit("vector", "tensor_scalar", rtmprow, row, 0.0, None, ALU.is_ge)
            P.emit("vector", "tensor_scalar", rtmprow, rtmprow, 2.0, -1.0,
                   ALU.mult, ALU.add)
            P.emit("vector", "tensor_tensor", row, row, rtmprow, ALU.mult)
            P.emit("vector", "tensor_scalar", row, row, 0.25, None, ALU.max)
            P.emit("vector", "tensor_tensor", row, row, rtmprow, ALU.mult)

        def rgmath(row):
            gamath(row)
            P.emit("vector", "reciprocal", row, row)

        bc0 = Fv(O_BC0, 2048)
        bc1 = Fv(O_BC1, 2048)

        d_al = P.dma("sync", chrow[0][:, 0:1], alpha2, sem="Lch0")
        P.wait("tensor", "Lch0", d_al)
        P.wait("tensor", "V", v_ones)
        pBh.twait()
        P.emit("tensor", "matmul", pB_t[:, 512:513], ones[0:1, :],
               chrow[0][:, 0:1], start=True, stop=True)
        v_mm = P.mark("tensor", "T")
        bc_state["last_mm"][0] = v_mm
        P.wait("scalar", "T", v_mm)
        P.emit("scalar", "copy", albB, pB_t[:, 512:513])
        v_alb = P.mark("scalar", "S")
        pBh.release("S", v_alb)

        for j in range(4):
            bcast512(bc0[:, 512 * j:512 * (j + 1)], 4 + j, add1=True)
        for j in range(4):
            v_casa = bcast512(bc1[:, 512 * j:512 * (j + 1)], j)

        xtile = [Fv(O_F2, 2048), Fv(O_F2 + 4096, 2048)]
        htmp = [Fv(O_F1, 2048), Fv(O_F1 + 4096, 2048)]
        hbuf = [B(O_F1 + 8192, 2048), B(O_F1 + 10240, 2048)]
        sq_scr = Fv(O_F1 + 12288, 2048)

        d_x = {0: P.dma("sync", xtile[0], x_t[0:128, :], sem="Lx0"),
               1: P.dma("sync", xtile[1], x_t[128:256, :], sem="Lx1")}
        x_free_at = {}
        v_h = {}
        tpos_res = {0: pDl, 1: pCl}
        tpos_bf = {0: pT, 1: pC_t[:, 0:512].bitcast(BF16)}

        v_h = {}
        x_free_at = {}
        vB_done = {}
        sC_done = {}

        def p1_A(t):
            xt = xtile[t % 2]
            xsem = "Lx%d" % (t % 2)
            P.wait("vector", xsem, d_x[t])
            P.emit("vector", "reduce_sum", s_mean[:, t:t + 1], xt, axis=AX.X)
            P.wait("scalar", xsem, d_x[t])
            P.emit("scalar", "activation", sq_scr, xt, AF.Square,
                   accum_out=s_ssq[:, t:t + 1])
            return P.mark("scalar", "S")

        sA_done = {0: None}
        sA_done[0] = p1_A(0)
        sA_done[1] = p1_A(1)
        for t in range(NT):
            xt, ht, hb = xtile[t % 2], htmp[t % 2], hbuf[t % 2]
            P.wait("vector", "S", sA_done[t])
            P.emit("vector", "tensor_scalar", s_mean[:, t:t + 1],
                   s_mean[:, t:t + 1], 1.0 / D, None, ALU.mult)
            P.emit("vector", "tensor_tensor", s_t1[:, t:t + 1],
                   s_mean[:, t:t + 1], s_mean[:, t:t + 1], ALU.mult)
            P.emit("vector", "tensor_scalar", s_inv[:, t:t + 1],
                   s_ssq[:, t:t + 1], 1.0 / D, None, ALU.mult)
            P.emit("vector", "tensor_tensor", s_inv[:, t:t + 1],
                   s_inv[:, t:t + 1], s_t1[:, t:t + 1], ALU.subtract)
            P.emit("vector", "tensor_scalar", s_inv[:, t:t + 1],
                   s_inv[:, t:t + 1], EPS, None, ALU.add)
            P.emit("vector", "reciprocal", s_inv[:, t:t + 1], s_inv[:, t:t + 1])
            vB = P.mark("vector", "V")
            P.wait("scalar", "V", vB)
            P.emit("scalar", "sqrt", s_inv[:, t:t + 1], s_inv[:, t:t + 1])
            sC = P.mark("scalar", "S")
            P.wait("vector", "S", sC)
            P.emit("vector", "tensor_scalar", s_nmi[:, t:t + 1],
                   s_mean[:, t:t + 1], s_inv[:, t:t + 1], -1.0,
                   ALU.mult, ALU.mult)
            vD = P.mark("vector", "V")
            P.wait("scalar", "V", vD)
            P.emit("scalar", "activation", ht, xt, AF.Identity,
                   bias=s_nmi[:, t:t + 1], scale=s_inv[:, t:t + 1])
            x_free_at[t] = P.mark("scalar", "S")
            if t + 2 < NT:
                P.wait("sync", "S", x_free_at[t])
                d_x[t + 2] = P.dma("sync", xtile[t % 2],
                                   x_t[128 * (t + 2):128 * (t + 3), :],
                                   sem="Lx%d" % (t % 2))
            if t + 2 < NT:
                sA_done[t + 2] = p1_A(t + 2)
            P.wait("vector", "S", x_free_at[t])
            P.wait("vector", "S", v_casa)
            P.emit("vector", "tensor_tensor", ht, ht, bc0, ALU.mult)
            P.emit("vector", "tensor_tensor", hb, ht, bc1, ALU.add)
            v_h[t] = P.mark("vector", "V")
            P.wait("tensor", "V", v_h[t])
            P.wait("tensor", "Lc", d_const)
            for grp in range(2):
                res = tpos_res[grp]
                pbf = tpos_bf[grp]
                res.twait()
                for j in range(8):
                    kt2 = grp * 8 + j
                    P.emit("tensor", "transpose", pbf[:, 128 * j:128 * (j + 1)],
                           hb[:, 128 * kt2:128 * (kt2 + 1)], id_sb)
                vt = P.mark("tensor", "T")
                P.wait("vector", "T", vt)
                for j in range(8):
                    kt2 = grp * 8 + j
                    P.emit("vector", "tensor_copy", hT(kt2, 128 * t, 128),
                           pbf[:, 128 * j:128 * (j + 1)])
                vv = P.mark("vector", "V")
                res.release("V", vv)
        v_hT_all = P.mark("vector", "V")

        mring = [B(O_F2 + 2048 * s, 2048) for s in range(4)]
        tabc = [Fv(O_F1, 1024), Fv(O_F1 + 2048, 1024)]
        rsc = [Fv(O_F1 + 4096, 1024), Fv(O_F1 + 6144, 1024)]
        mm_free = {}
        slot_i = 0
        v_rope_prev = 0
        d_tab_prev = 0

        for side in range(2):
            wsrc = wq_m if side == 0 else wk_m
            for h in range(H):
                P.wait("sync", "V", v_hT_all if (side == 0 and h == 0) else v_rope_prev)
                d_cos = P.dma("sync", tabc[0], tabs[side, h, 0], sem="Ltab")
                d_sin = P.dma("sync", tabc[1], tabs[side, h, 1], sem="Ltab")
                vps = []
                for half in range(2):
                    m = 2 * h + half
                    s4 = slot_i % 4
                    P.wait("gpsimd", "T", mm_free.get(s4, 0))
                    if slot_i == 0:
                        P.wait("gpsimd", "S", x_free_at[NT - 1])
                    dsl = P.dma("gpsimd",
                                mring[s4].rearrange("p (a m) -> p a m", m=128),
                                wsrc[m].rearrange("(a p) m -> p a m", p=128),
                                sem="Lm%d" % s4)
                    ptgt = [pAl, pAh] if half == 0 else [pBl, pBh]
                    P.wait("tensor", "Lm%d" % s4, dsl)
                    P.wait("tensor", "V", v_hT_all)
                    for tch in range(2):
                        ptgt[tch].twait()
                        for kt2 in range(KT):
                            P.emit("tensor", "matmul", ptgt[tch].ap,
                                   mring[s4][:, 128 * kt2:128 * (kt2 + 1)],
                                   hT(kt2, 512 * tch, 512),
                                   start=(kt2 == 0), stop=(kt2 == KT - 1))
                    vp = P.mark("tensor", "T")
                    mm_free[s4] = vp
                    vps.append(vp)
                    slot_i += 1
                P.wait("vector", "T", vps[-1])
                P.wait("vector", "Ltab", d_sin)
                q1 = pA_t[:, 0:1024]
                q2 = pB_t[:, 0:1024]
                segd = qseg if side == 0 else kseg
                P.emit("vector", "tensor_tensor", rsc[0], q1, tabc[0], ALU.mult)
                P.emit("vector", "tensor_tensor", rsc[1], q2, tabc[1], ALU.mult)
                for p in range(NT):
                    P.emit("vector", "tensor_tensor", segd(p, h, 0),
                           rsc[0][:, 128 * p:128 * (p + 1)],
                           rsc[1][:, 128 * p:128 * (p + 1)], ALU.subtract)
                P.emit("vector", "tensor_tensor", rsc[0], q2, tabc[0], ALU.mult)
                P.emit("vector", "tensor_tensor", rsc[1], q1, tabc[1], ALU.mult)
                for p in range(NT):
                    P.emit("vector", "tensor_tensor", segd(p, h, 1),
                           rsc[0][:, 128 * p:128 * (p + 1)],
                           rsc[1][:, 128 * p:128 * (p + 1)], ALU.add)
                v_rope_prev = P.mark("vector", "V")
                for pp in (pAl, pAh, pBl, pBh):
                    pp.release("V", v_rope_prev)
        v_rope_all = v_rope_prev

        vgslot = [B(O_F1 + 8192 * s, 8192) for s in range(2)]
        vg_free = {}
        vg_i = 0
        pcycle = [pAl, pAh, pBl, pBh]
        pc_i = 0
        v_g_ev = {}
        v_v_ev = {}
        P.wait("scalar", "Lmi", d_mi)

        def vg_proj(which):
            nonlocal vg_i, pc_i
            wsrc = wv_n if which == 0 else wg_n
            for nch in range(4):
                s2 = vg_i % 2
                if vg_i < 2:
                    P.wait("gpsimd", "V", v_rope_all)
                else:
                    P.wait("gpsimd", "T", vg_free[s2])
                dsl = P.dma("gpsimd",
                            vgslot[s2].rearrange("p (a m) -> p a m", m=512),
                            wsrc[nch].rearrange("(a p) m -> p a m", p=128),
                            sem="Lvg%d" % s2)
                P.wait("tensor", "Lvg%d" % s2, dsl)
                for tt in range(NT):
                    pres = pcycle[pc_i % 4]
                    pc_i += 1
                    pres.twait()
                    for kt2 in range(KT):
                        P.emit("tensor", "matmul", pres.ap,
                               hT(kt2, 128 * tt, 128),
                               vgslot[s2][:, 512 * kt2:512 * (kt2 + 1)],
                               start=(kt2 == 0), stop=(kt2 == KT - 1))
                    vmm = P.mark("tensor", "T")
                    if tt == NT - 1:
                        vg_free[s2] = vmm
                    P.wait("scalar", "T", vmm)
                    if which == 0:
                        P.emit("scalar", "copy",
                               vtile(tt, 512 * nch, 512), pres.ap)
                    else:
                        P.emit("scalar", "activation",
                               gtile(tt, 512 * nch, 512), pres.ap, AF.Silu)
                    ev = P.mark("scalar", "S")
                    pres.release("S", ev)
                    if which == 0:
                        v_v_ev[(nch, tt)] = ev
                    else:
                        v_g_ev[(nch, tt)] = ev
                vg_i += 1

        vg_proj(0)

        def Mpar(dt8, par, n=512):
            return B((O_M if par % 2 == 0 else O_F2) + dt8 * 512, n)

        P.wait("vector", "Lch0", P.ctr["Lch0"])
        P.wait("vector", "Lch1", P.ctr["Lch1"])
        P.wait("vector", "T", max(bc_state["last_mm"].values()))
        P.emit("vector", "memset", B(O_M, 4096), 0.0)
        v_m0 = P.mark("vector", "V")
        kv_cycle = [pAh, pBh, pCh, pDh]
        kv_i = 0
        v_Lupd = v_m0
        for p in range(NT):
            pDl.twait()
            P.wait("tensor", "V", v_rope_all)
            for dt8 in range(8):
                h, half = dt8 // 2, dt8 % 2
                P.emit("tensor", "transpose", pT[:, 128 * dt8:128 * (dt8 + 1)],
                       kseg(p, h, half), id_sb)
            vt = P.mark("tensor", "T")
            P.wait("vector", "T", vt)
            for dt8 in range(8):
                h = dt8 // 2
                P.emit("vector", "tensor_scalar",
                       ktok[:, 128 * dt8:128 * (dt8 + 1)],
                       pT[:, 128 * dt8:128 * (dt8 + 1)], G128[h], None, ALU.mult)
            v_kt = P.mark("vector", "V")
            pDl.release("V", v_kt)
            for dt8 in range(8):
                h, half = dt8 // 2, dt8 % 2
                pres = kv_cycle[kv_i % 4]
                kv_i += 1
                pres.twait()
                P.wait("tensor", "V", v_kt)
                P.wait("tensor", "S", v_v_ev[(h, p)])
                P.emit("tensor", "matmul", pres.ap,
                       ktok[:, 128 * dt8:128 * (dt8 + 1)],
                       vtile(p, 512 * h, 512), start=True, stop=True)
                vmm = P.mark("tensor", "T")
                P.wait("vector", "T", vmm)
                P.wait("vector", "V", v_m0)
                P.emit("vector", "scalar_tensor_tensor", Mpar(dt8, 0),
                       Mpar(dt8, 0), G128[h], pres.ap, ALU.mult, ALU.add)
                v_Lupd = P.mark("vector", "V")
                pres.release("V", v_Lupd)
        P.wait("gpsimd", "V", v_Lupd)
        d_li = P.dma("gpsimd", lg_i[:, :], B(O_M, 4096), sem="Lli")
        P.wait("gpsimd", "Lli", d_li)
        v_cc2 = P.emit("gpsimd", "collective_compute", "AllGather", ALU.bypass,
                       replica_groups=[list(range(NCORE))],
                       ins=[lg_i[:, :]], outs=[lg_o[:, :]], inc="C")

        gnormB = Fv(O_F2, 512)
        P.wait("sync", "V", v_Lupd)
        d_gn2 = P.dma("sync", chrow[0], gnorm, sem="Lch0")
        P.wait("tensor", "Lch0", d_gn2)
        pBh.twait()
        P.emit("tensor", "matmul", pBh.ap, ones[0:1, :], chrow[0],
               start=True, stop=True)
        v_mm = P.mark("tensor", "T")
        bc_state["last_mm"][0] = v_mm
        P.wait("scalar", "T", v_mm)
        P.emit("scalar", "copy", gnormB, pBh.ap)
        v_gnB = P.mark("scalar", "S")
        pBh.release("S", v_gnB)

        vg_proj(1)
        v_vg_all = v_g_ev[(3, NT - 1)]
        P.wait("vector", "S", max(v_vg_all, v_gnB))
        for p in range(NT):
            for h in range(H):
                P.emit("vector", "tensor_tensor", gtile(p, 512 * h, 512),
                       gtile(p, 512 * h, 512), gnormB, ALU.mult)
        v_gp = P.mark("vector", "V")

        lrecv = [B(O_F2 + 4096 * s, 4096) for s in range(2)]
        sps = [pAl, pAh, pBl, pBh, pCl, pCh, pDl, pDh]
        for s8 in sps:
            s8.free_at = None
        d_lr = {}
        v_lr_used = {}
        P.wait("gpsimd", "C", v_cc2)
        P.wait("gpsimd", "V", v_gp)
        d_lr[0] = P.dma("gpsimd", lrecv[0], lg_o[0:128, :], sem="Llr0")
        P.wait("vector", "S", v_vg_all)
        for j in range(NCORE):
            if j + 1 < NCORE:
                P.wait("gpsimd", "V", v_lr_used.get(j - 1, 0))
                d_lr[j + 1] = P.dma("gpsimd", lrecv[(j + 1) % 2],
                                    lg_o[128 * (j + 1):128 * (j + 2), :],
                                    sem="Llr%d" % ((j + 1) % 2))
            P.wait("vector", "Llr%d" % (j % 2), d_lr[j])
            for h in range(H):
                wcol = sinw_sb[:, 8 * h + j:8 * h + j + 1]
                srcw = lrecv[j % 2][:, 1024 * h:1024 * (h + 1)]
                full = [pA_t, pB_t, pC_t, pD_t][h][:, 0:1024]
                if j == 0:
                    P.emit("vector", "tensor_scalar", full, srcw,
                           wcol, None, ALU.mult)
                else:
                    P.emit("vector", "scalar_tensor_tensor", full, srcw,
                           wcol, full, ALU.mult, ALU.add)
            v_lr_used[j] = P.mark("vector", "V")
        P.wait("vector", "Lli", d_li)
        for h in range(H):
            P.emit("vector", "tensor_copy", Mpar(2 * h, 0, 1024),
                   [pA_t, pB_t, pC_t, pD_t][h][:, 0:1024])
        v_sin = P.mark("vector", "V")
        for s8 in sps:
            s8.release("V", v_sin)

        attn_sb2 = [attn_sb, attn_sbB]
        v_Mupd = {-1: v_sin}
        v_ob = {}
        at_cycle = [pAl, pBl]
        kv_cycle2 = [pAh, pBh]
        o_cycle = [pCl, pCh, pDh]
        for p in range(NT):
            rpar, wpar = p % 2, (p + 1) % 2
            pDl.twait()
            P.wait("tensor", "V", v_sin)
            for dt8 in range(8):
                h, half = dt8 // 2, dt8 % 2
                P.emit("tensor", "transpose", pT[:, 128 * dt8:128 * (dt8 + 1)],
                       kseg(p, h, half), id_sb)
            vt = P.mark("tensor", "T")
            P.wait("vector", "T", vt)
            for dt8 in range(8):
                h = dt8 // 2
                P.emit("vector", "tensor_scalar",
                       ktok[:, 128 * dt8:128 * (dt8 + 1)],
                       pT[:, 128 * dt8:128 * (dt8 + 1)], G128[h], None, ALU.mult)
            v_kt = P.mark("vector", "V")
            pDl.release("V", v_kt)
            vm = None
            for dt8 in range(8):
                h = dt8 // 2
                pres = kv_cycle2[dt8 % 2]
                pres.twait()
                P.wait("tensor", "V", v_kt)
                if p == 0:
                    P.wait("tensor", "S", v_v_ev[(h, p)])
                P.emit("tensor", "matmul", pres.ap,
                       ktok[:, 128 * dt8:128 * (dt8 + 1)],
                       vtile(p, 512 * h, 512), start=True, stop=True)
                vkv = P.mark("tensor", "T")
                P.wait("vector", "T", vkv)
                P.emit("vector", "scalar_tensor_tensor", Mpar(dt8, wpar),
                       Mpar(dt8, rpar), G128[h], pres.ap, ALU.mult, ALU.add)
                vm = P.mark("vector", "V")
                pres.release("V", vm)
            v_Mupd[p] = vm
            P.wait("tensor", "V", v_Mupd[p - 1])
            for h in range(H):
                at = at_cycle[h % 2]
                at.twait()
                P.emit("tensor", "matmul", at.ap[:, 0:128], kseg(p, h, 0),
                       qseg(p, h, 0), start=True, stop=False)
                P.emit("tensor", "matmul", at.ap[:, 0:128], kseg(p, h, 1),
                       qseg(p, h, 1), start=False, stop=True)
                v_at = P.mark("tensor", "T")
                P.wait("vector", "T", v_at)
                P.emit("vector", "tensor_tensor", attn_sb2[h % 2],
                       at.ap[:, 0:128], cau_sb, ALU.mult)
                v_am = P.mark("vector", "V")
                at.release("V", v_am)
                op = o_cycle[h % 3]
                op.twait()
                P.wait("tensor", "V", v_am)
                P.emit("tensor", "matmul", op.ap, attn_sb2[h % 2],
                       vtile(p, 512 * h, 512), start=True, stop=False)
                P.emit("tensor", "matmul", op.ap, qseg(p, h, 0),
                       Mpar(2 * h, rpar), start=False, stop=False)
                P.emit("tensor", "matmul", op.ap, qseg(p, h, 1),
                       Mpar(2 * h + 1, rpar), start=False, stop=True)
                v_omm = P.mark("tensor", "T")
                P.wait("scalar", "T", v_omm)
                P.emit("scalar", "copy", obv(p, 512 * h, 512), op.ap)
                v_ob[(p, h)] = P.mark("scalar", "S")
                op.release("S", v_ob[(p, h)])

        v_og = {}

        def p3b_sq(p):
            if p == 0:
                P.wait("scalar", "V", v_gp)
            for h in range(H):
                P.wait("scalar", "S", v_ob[(p, h)])
                P.emit("scalar", "activation", sq_scr[:, 0:512],
                       obv(p, 512 * h, 512), AF.Square,
                       accum_out=s_oiv[:, (p % 2) * 4 + h:(p % 2) * 4 + h + 1])
            return P.mark("scalar", "S")

        sqd = {0: p3b_sq(0)}
        for p in range(NT):
            c4 = s_oiv[:, (p % 2) * 4:(p % 2) * 4 + 4]
            if p + 1 < NT:
                sqd[p + 1] = p3b_sq(p + 1)
            P.wait("vector", "S", sqd[p])
            P.emit("vector", "tensor_scalar", c4, c4, 1.0 / DV, EPS,
                   ALU.mult, ALU.add)
            P.emit("vector", "reciprocal", c4, c4)
            vv = P.mark("vector", "V")
            P.wait("scalar", "V", vv)
            P.emit("scalar", "sqrt", c4, c4)
            vs2 = P.mark("scalar", "S")
            P.wait("vector", "S", vs2)
            for h in range(H):
                P.emit("vector", "scalar_tensor_tensor", obv(p, 512 * h, 512),
                       obv(p, 512 * h, 512),
                       s_oiv[:, (p % 2) * 4 + h:(p % 2) * 4 + h + 1],
                       gtile(p, 512 * h, 512), ALU.mult, ALU.mult)
            v_og[p] = P.mark("vector", "V")
        v_og_all = v_og[NT - 1]

        v_ret_done0 = v_Mupd[NT - 1]
        P.wait("sync", "V", v_ret_done0)
        for j in range(4):
            v_gaB = bcast512(bc0[:, 512 * j:512 * (j + 1)], 8 + j,
                             row_math=gamath)

        woslot = [B(O_VB + 8192 * g, 8192) for g in range(4)]
        wtmp = Fv(O_F2, 512)
        v_ret_done = v_Mupd[NT - 1]
        d_wo = {}
        for g in range(4):
            if g < 2:
                P.wait("gpsimd", "S", v_ob[(NT - 1, H - 1)])
            else:
                P.wait("gpsimd", "V", v_og_all)
            d_wo[g] = P.dma("gpsimd",
                            woslot[g].rearrange("p (a m) -> p a m", m=2048),
                            wo_g[g].rearrange("(a p) m -> p a m", p=128),
                            sem="Lwo")

        d_x2 = {}
        P.wait("sync", "V", v_og_all)
        for tt in range(NT):
            d_x2[tt] = P.dma("sync", accv(tt, 0, 2048),
                             x_t[128 * tt:128 * (tt + 1), :], sem="Lxa")
        d_x2_all = d_x2[NT - 1]

        v_wo_used = {}
        wo_cycle = [pAl, pAh, pBl, pBh]
        wo_i = 0
        v_acc_wo = {}
        for tt in range(NT):
            P.wait("tensor", "V", v_og[tt])
            for grp in range(2):
                pDl.twait()
                for j in range(8):
                    kt2 = grp * 8 + j
                    P.emit("tensor", "transpose", pT[:, 128 * j:128 * (j + 1)],
                           obv(tt, 128 * kt2, 128), id_sb)
                vt = P.mark("tensor", "T")
                P.wait("vector", "T", vt)
                for j in range(8):
                    kt2 = grp * 8 + j
                    P.emit("vector", "tensor_copy", ogT[:, 128 * kt2:128 * (kt2 + 1)],
                           pT[:, 128 * j:128 * (j + 1)])
                v_tp = P.mark("vector", "V")
                pDl.release("V", v_tp)
            P.wait("tensor", "V", v_tp)
            for g in range(4):
                P.wait("tensor", "Lwo", d_wo[3])
                for nch in range(4):
                    pres = wo_cycle[wo_i % 4]
                    wo_i += 1
                    pres.twait()
                    for kg in range(4):
                        P.emit("tensor", "matmul", pres.ap,
                               ogT[:, 128 * (g * 4 + kg):128 * (g * 4 + kg + 1)],
                               woslot[g][:, 2048 * kg + 512 * nch:2048 * kg + 512 * (nch + 1)],
                               start=(kg == 0), stop=(kg == 3))
                    vmm = P.mark("tensor", "T")
                    if nch == 3 and tt == NT - 1:
                        v_wo_used[g] = vmm
                    P.wait("vector", "T", vmm)
                    P.wait("vector", "Lxa", d_x2_all)
                    if tt == 0 and g == 0 and nch == 0:
                        P.wait("vector", "S", v_gaB)
                    P.emit("vector", "tensor_tensor", wtmp, pres.ap,
                           bc0[:, 512 * nch:512 * (nch + 1)], ALU.mult)
                    P.emit("vector", "tensor_tensor",
                           accv(tt, 512 * nch, 512),
                           accv(tt, 512 * nch, 512), wtmp, ALU.add)
                    va = P.mark("vector", "V")
                    pres.release("V", va)
            v_acc_wo[tt] = va

        P.wait("sync", "T", v_wo_used[3])
        for j in range(4):
            bcast512(bc0[:, 512 * j:512 * (j + 1)], 16 + j, add1=True)
        for j in range(4):
            v_cbsb = bcast512(bc1[:, 512 * j:512 * (j + 1)], 12 + j)

        gate_bf = B(O_F2 + 2048, 512)
        P.wait("gpsimd", "T", v_wo_used[3])
        d_gate = P.dma("gpsimd",
                       gate_bf.rearrange("p (a m) -> p a m", m=32),
                       gatet.rearrange("(a p) m -> p a m", p=128), sem="Lgate")

        v_h2T = {}
        v_h2_done = {}

        LGb = Fv(O_F2 + 4608, 256)
        RPb = Fv(O_F2 + 5632, 256)
        RCb = Fv(O_F2 + 6656, 256)
        RSb = Fv(O_M + 2048, 256)
        MKS = Fv(O_M + 3072, 64)
        CNTS = Fv(O_M + 3200, 64)
        R8a = Fv(O_M + 3328, 8)
        R8b = Fv(O_M + 3344, 8)
        R8c = Fv(O_M + 3360, 8)

        def ln2_A(t):
            xt = accv(t, 0, 2048)
            P.wait("vector", "V", v_acc_wo[t])
            P.emit("vector", "reduce_sum", s_mean[:, t:t + 1], xt, axis=AX.X)
            P.wait("scalar", "V", v_acc_wo[t])
            if t == 0:
                P.wait("scalar", "T", v_wo_used[3])
            P.emit("scalar", "activation", sq_scr, xt, AF.Square,
                   accum_out=s_ssq[:, t:t + 1])
            return P.mark("scalar", "S")

        sA2 = {0: ln2_A(0)}
        sA2[1] = ln2_A(1)
        for t in range(NT):
            ht, hb = htmp[t % 2], hbuf[t % 2]
            xt = accv(t, 0, 2048)
            P.wait("vector", "S", sA2[t])
            P.emit("vector", "tensor_scalar", s_mean[:, t:t + 1],
                   s_mean[:, t:t + 1], 1.0 / D, None, ALU.mult)
            P.emit("vector", "tensor_tensor", s_t1[:, t:t + 1],
                   s_mean[:, t:t + 1], s_mean[:, t:t + 1], ALU.mult)
            P.emit("vector", "tensor_scalar", s_inv[:, t:t + 1],
                   s_ssq[:, t:t + 1], 1.0 / D, None, ALU.mult)
            P.emit("vector", "tensor_tensor", s_inv[:, t:t + 1],
                   s_inv[:, t:t + 1], s_t1[:, t:t + 1], ALU.subtract)
            P.emit("vector", "tensor_scalar", s_inv[:, t:t + 1],
                   s_inv[:, t:t + 1], EPS, None, ALU.add)
            P.emit("vector", "reciprocal", s_inv[:, t:t + 1], s_inv[:, t:t + 1])
            vB = P.mark("vector", "V")
            P.wait("scalar", "V", vB)
            P.emit("scalar", "sqrt", s_inv[:, t:t + 1], s_inv[:, t:t + 1])
            sC = P.mark("scalar", "S")
            P.wait("vector", "S", sC)
            P.emit("vector", "tensor_scalar", s_nmi[:, t:t + 1],
                   s_mean[:, t:t + 1], s_inv[:, t:t + 1], -1.0,
                   ALU.mult, ALU.mult)
            vD = P.mark("vector", "V")
            P.wait("scalar", "V", vD)
            P.emit("scalar", "activation", ht, xt, AF.Identity,
                   bias=s_nmi[:, t:t + 1], scale=s_inv[:, t:t + 1])
            v_aff = P.mark("scalar", "S")
            if t + 2 < NT:
                sA2[t + 2] = ln2_A(t + 2)
            P.wait("vector", "S", v_aff)
            P.wait("vector", "S", v_cbsb)
            P.emit("vector", "tensor_tensor", ht, ht, bc0, ALU.mult)
            P.emit("vector", "tensor_tensor", hb, ht, bc1, ALU.add)
            v_h2 = P.mark("vector", "V")
            v_h2_done[t] = v_h2
            P.wait("tensor", "V", v_h2)
            for grp in range(2):
                pDl.twait()
                for j in range(8):
                    kt2 = grp * 8 + j
                    P.emit("tensor", "transpose", pT[:, 128 * j:128 * (j + 1)],
                           hb[:, 128 * kt2:128 * (kt2 + 1)], id_sb)
                vt = P.mark("tensor", "T")
                P.wait("vector", "T", vt)
                for j in range(8):
                    kt2 = grp * 8 + j
                    P.emit("vector", "tensor_copy", h2T(kt2, 128 * t, 128),
                           pT[:, 128 * j:128 * (j + 1)])
                vv = P.mark("vector", "V")
                pDl.release("V", vv)
            v_h2T[t] = vv
            pCl.twait()
            P.wait("tensor", "Lgate", d_gate)
            P.wait("tensor", "V", vv)
            for kt2 in range(KT):
                P.emit("tensor", "matmul", pC_t[:, 0:32],
                       h2T(kt2, 128 * t, 128), gate_bf[:, 32 * kt2:32 * (kt2 + 1)],
                       start=(kt2 == 0), stop=(kt2 == KT - 1))
            v_lg = P.mark("tensor", "T")
            P.wait("scalar", "T", v_lg)
            P.emit("scalar", "copy", LGb[:, 32 * t:32 * (t + 1)], pC_t[:, 0:32])
            v_lgc = P.mark("scalar", "S")
            pCl.release("S", v_lgc)
        v_lg_all = v_lgc

        for j in range(4):
            v_rgb = bcast512(bc1[:, 512 * j:512 * (j + 1)], 20 + j,
                             row_math=rgmath)
        P.wait("vector", "S", v_rgb)
        v_accr = {}
        for tt in range(NT):
            P.emit("vector", "tensor_tensor", accv(tt, 0, 2048),
                   accv(tt, 0, 2048), bc1, ALU.mult)
            v_accr[tt] = P.mark("vector", "V")

        def seg(ap):
            return ap.rearrange("p (a b) -> p a b", b=32)

        def col8(ap):
            return ap.rearrange("p (a b) -> p a b", b=1)

        def bc8(ap):
            return col8(ap).broadcast_to([128, 8, 32])

        def segred(dst8, src256, op):
            P.emit("vector", "tensor_reduce", col8(dst8), seg(src256),
                   axis=AX.X, op=op)

        MKS3 = MKS.rearrange("p (a b) -> p a b", b=8)
        P.wait("vector", "S", v_lg_all)
        segred(R8a, LGb, ALU.max)
        P.emit("vector", "tensor_tensor", seg(RPb), seg(LGb), bc8(R8a),
               ALU.subtract)
        v_sm = P.mark("vector", "V")
        P.wait("scalar", "V", v_sm)
        P.emit("scalar", "activation", RPb, RPb, AF.Exp)
        v_exp = P.mark("scalar", "S")
        P.wait("vector", "S", v_exp)
        segred(R8a, RPb, ALU.add)
        P.emit("vector", "reciprocal", R8a, R8a)
        P.emit("vector", "tensor_tensor", seg(RPb), seg(RPb), bc8(R8a),
               ALU.mult)
        P.emit("vector", "tensor_copy", RCb, RPb)
        for k in range(8):
            P.emit("vector", "tensor_reduce", MKS3[:, :, k:k + 1], seg(RCb),
                   axis=AX.X, op=ALU.max)
            P.emit("vector", "tensor_tensor", seg(RSb), seg(RCb),
                   MKS3[:, :, k:k + 1].broadcast_to([128, 8, 32]), ALU.is_equal)
            P.emit("vector", "scalar_tensor_tensor", RCb, RSb, -1e30, RCb,
                   ALU.mult, ALU.add)
        for k in range(8):
            P.emit("vector", "tensor_tensor", seg(RSb), seg(RPb),
                   MKS3[:, :, k:k + 1].broadcast_to([128, 8, 32]), ALU.is_ge)
            P.emit("vector", "tensor_reduce",
                   CNTS.rearrange("p (a b) -> p a b", b=8)[:, :, k:k + 1],
                   seg(RSb), axis=AX.X, op=ALU.add)
        P.emit("vector", "tensor_scalar", CNTS, CNTS, 8.0, None, ALU.is_ge)
        P.emit("vector", "tensor_tensor", CNTS, CNTS, MKS, ALU.mult)
        P.emit("vector", "tensor_reduce", col8(R8b),
               CNTS.rearrange("p (a b) -> p a b", b=8), axis=AX.X, op=ALU.max)
        P.emit("vector", "tensor_tensor", seg(RCb), seg(RPb), bc8(R8b),
               ALU.is_ge)
        P.emit("vector", "tensor_tensor", seg(RSb), seg(RPb), seg(RCb),
               ALU.mult)
        segred(R8b, RSb, ALU.add)
        P.emit("vector", "reciprocal", R8b, R8b)
        P.wait("vector", "S", v_alb)
        P.emit("vector", "tensor_scalar", R8b, R8b, albB, None, ALU.mult)
        P.emit("vector", "tensor_tensor", seg(RSb), seg(RSb), bc8(R8b),
               ALU.mult)
        P.emit("vector", "tensor_scalar", RSb, RSb, TAU, None, ALU.is_ge)
        segred(R8b, RSb, ALU.max)
        P.emit("vector", "tensor_scalar", R8b, R8b, 0.0, None, ALU.is_equal)
        segred(R8c, RPb, ALU.max)
        P.emit("vector", "tensor_tensor", seg(RCb), seg(RPb), bc8(R8c),
               ALU.is_equal)
        P.emit("vector", "tensor_tensor", seg(RCb), seg(RCb), bc8(R8b),
               ALU.mult)
        P.emit("vector", "tensor_tensor", RSb, RSb, RCb, ALU.max)
        P.emit("vector", "tensor_tensor", RSb, RPb, RSb, ALU.mult)
        segred(R8c, RSb, ALU.add)
        P.emit("vector", "tensor_scalar", R8c, R8c, 1e-12, None, ALU.max)
        P.emit("vector", "reciprocal", R8c, R8c)
        P.emit("vector", "tensor_tensor", seg(comb[:, 0:256]), seg(RSb),
               bc8(R8c), ALU.mult)
        v_comb_all = P.mark("vector", "V")

        gu_sp = [B(O_F1 + 4096 * s, 4096) for s in range(2)]
        heT = B(O_F1 + 8192, 6144)
        sgb = [B(O_F1 + 14336, 512), B(O_F1 + 14848, 512)]
        dn_s = [B(O_VB + 3072 * s, 3072) for s in range(4)]

        gu_used = {}
        dn_used = {}
        d_gu = {}
        d_dn = {}

        def prefetch(e, pr):
            idx = e * 6 + pr
            P.wait("gpsimd", "T", gu_used.get(idx - 2, 0))
            if idx == 0:
                P.wait("gpsimd", "T", v_wo_used[2])
                P.wait("gpsimd", "V", v_h2T[NT - 1])
            d_gu[idx] = P.dma("gpsimd",
                              gu_sp[idx % 2].rearrange("p (b a m) -> p b a m", b=2, m=128),
                              gup[e, pr].rearrange("b (a p) m -> p b a m", p=128),
                              sem="Lgu%d" % (idx % 2))

        def prefetch_dn(e, nch):
            idx = e * 4 + nch
            P.wait("gpsimd", "T", dn_used.get(idx - 4, 0))
            if idx == 0:
                P.wait("gpsimd", "T", v_wo_used[0])
                P.wait("gpsimd", "T", v_wo_used[1])
            d_dn[idx] = P.dma("gpsimd",
                              dn_s[idx % 4].rearrange("p (a m) -> p a m", m=512),
                              dnp[e, nch].rearrange("(a p) m -> p a m", p=128),
                              sem="Ldn%d" % (idx % 4))

        for pr in range(2):
            prefetch(0, pr)
        for nch in range(4):
            prefetch_dn(0, nch)

        v_last_acc = None
        pe_cycle = [(pAl, pAh), (pBl, pBh), (pCl, pCh), (pDl, pDh)]
        v_he = {}
        sg_busy = {}
        for e in range(NEXP):
            pe_i = 0
            for pr in range(6):
                idx = e * 6 + pr
                sl = gu_sp[idx % 2]
                P.wait("tensor", "Lgu%d" % (idx % 2), d_gu[idx])
                if e == 0 and pr == 0:
                    P.wait("tensor", "V", v_h2T[NT - 1])
                for tch in range(2):
                    pg, pu = pe_cycle[pe_i % 4]
                    pe_i += 1
                    pg.twait()
                    pu.twait()
                    P.wait("tensor", "V", v_he.get((e - 1, pr, tch), 0))
                    for kt2 in range(KT):
                        P.emit("tensor", "matmul", pg.ap,
                               sl[:, 128 * kt2:128 * (kt2 + 1)],
                               h2T(kt2, 512 * tch, 512),
                               start=(kt2 == 0), stop=(kt2 == KT - 1))
                    for kt2 in range(KT):
                        P.emit("tensor", "matmul", pu.ap,
                               sl[:, 2048 + 128 * kt2:2048 + 128 * (kt2 + 1)],
                               h2T(kt2, 512 * tch, 512),
                               start=(kt2 == 0), stop=(kt2 == KT - 1))
                    vmm = P.mark("tensor", "T")
                    if tch == 1:
                        gu_used[idx] = vmm
                        if pr + 2 < 6:
                            prefetch(e, pr + 2)
                        elif e + 1 < NEXP:
                            prefetch(e + 1, pr + 2 - 6)
                    P.wait("scalar", "T", vmm)
                    sgi = pe_i % 2
                    sg = sgb[sgi]
                    P.wait("scalar", "V", sg_busy.get(sgi, 0))
                    P.emit("scalar", "activation", sg, pg.ap, AF.Silu)
                    vsg = P.mark("scalar", "S")
                    P.wait("vector", "S", vsg)
                    P.emit("vector", "tensor_tensor",
                           heT[:, 1024 * pr + 512 * tch:1024 * pr + 512 * (tch + 1)],
                           sg, pu.ap, ALU.mult)
                    vhe = P.mark("vector", "V")
                    sg_busy[sgi] = vhe
                    v_he[(e, pr, tch)] = vhe
                    pg.release("V", vhe)
                    pu.release("V", vhe)
            dn_cycle = [pAl, pAh, pBl, pBh]
            pe_i = 0
            for tt in range(NT):
                for nch in range(4):
                    idx = e * 4 + nch
                    pres = dn_cycle[pe_i % 4]
                    pe_i += 1
                    pres.twait()
                    P.wait("tensor", "Ldn%d" % (idx % 4), d_dn[idx])
                    if tt == 0:
                        for pr in range(6):
                            P.wait("tensor", "V", v_he[(e, pr, 0)])
                            P.wait("tensor", "V", v_he[(e, pr, 1)])
                    for kt2 in range(6):
                        P.emit("tensor", "matmul", pres.ap,
                               heT[:, 1024 * kt2 + 128 * tt:1024 * kt2 + 128 * (tt + 1)],
                               dn_s[idx % 4][:, 512 * kt2:512 * (kt2 + 1)],
                               start=(kt2 == 0), stop=(kt2 == 5))
                    vmm = P.mark("tensor", "T")
                    if tt == NT - 1:
                        dn_used[idx] = vmm
                        if e + 1 < NEXP:
                            prefetch_dn(e + 1, nch)
                    P.wait("vector", "T", vmm)
                    if e == 0:
                        P.wait("vector", "V", v_accr[tt])
                        P.wait("vector", "V", v_comb_all)
                    P.emit("vector", "scalar_tensor_tensor",
                           accv(tt, 512 * nch, 512), pres.ap,
                           comb[:, 32 * tt + e:32 * tt + e + 1],
                           accv(tt, 512 * nch, 512), ALU.mult, ALU.add)
                    va = P.mark("vector", "V")
                    pres.release("V", va)
            v_last_acc = va

        for j in range(4):
            v_gbB = bcast512(bc0[:, 512 * j:512 * (j + 1)], 20 + j,
                             row_math=gamath)
        P.wait("vector", "S", v_gbB)
        P.wait("vector", "V", v_last_acc)
        d_y = {}
        for tt in range(NT):
            P.emit("vector", "tensor_tensor", accv(tt, 0, 2048),
                   accv(tt, 0, 2048), bc0, ALU.mult)
            vo = P.mark("vector", "V")
            P.wait("sync", "V", vo)
            d_y[tt] = P.dma("sync", y[128 * tt:128 * (tt + 1), :],
                            accv(tt, 0, 2048), sem="Ly")
        P.wait("sync", "Ly", d_y[NT - 1])

        @block.tensor
        def _(e):
            for f in P.ops["tensor"]:
                f(e)

        @block.vector
        def _(e):
            for f in P.ops["vector"]:
                f(e)

        @block.scalar
        def _(e):
            for f in P.ops["scalar"]:
                f(e)

        @block.gpsimd
        def _(e):
            for f in P.ops["gpsimd"]:
                f(e)

        @block.sync
        def _(e):
            for f in P.ops["sync"]:
                f(e)

    return nc


_cache = {}


def _prep(inputs):
    f32 = np.float32
    x = np.ascontiguousarray(np.asarray(inputs["L0_attn_x"], f32).reshape(T, D))
    wq = np.asarray(inputs["L0_attn_Wq"], f32)
    wk = np.asarray(inputs["L0_attn_Wk"], f32)
    wv = np.asarray(inputs["L0_attn_Wv"], f32)
    wg = np.asarray(inputs["L0_attn_Wg"], f32)
    wo = np.asarray(inputs["L0_attn_Wo"], f32)
    gnw = np.asarray(inputs["L0_attn_gnorm_w"], f32).reshape(1, 512)
    gwt = np.asarray(inputs["L0_ffn_gate_weight"], f32)
    gu = np.asarray(inputs["L0_ffn_gate_up_proj"], f32)
    dn = np.asarray(inputs["L0_ffn_down_proj"], f32)
    alpha = np.asarray(inputs["L0_ffn_layer_alpha"], f32).reshape(1, 1)
    cv = np.asarray(inputs["L0_c"], f32).reshape(-1)
    wmod = np.asarray(inputs["L0_wmod"], f32)

    wq_m = np.ascontiguousarray(wq.T.reshape(D, 8, 128).transpose(1, 0, 2))
    wk_m = np.ascontiguousarray(wk.T.reshape(D, 8, 128).transpose(1, 0, 2))
    wv_n = np.ascontiguousarray(wv.T.reshape(D, 4, 512).transpose(1, 0, 2))
    wg_n = np.ascontiguousarray(wg.T.reshape(D, 4, 512).transpose(1, 0, 2))
    wo_g = np.ascontiguousarray(wo.T.reshape(4, 512, D))
    gu_t = gu.transpose(0, 2, 1).reshape(NEXP, D, 12, 128).transpose(0, 2, 1, 3)
    gup = np.ascontiguousarray(
        gu_t.reshape(NEXP, 2, 6, D, 128).transpose(0, 2, 1, 3, 4))
    dnp = np.ascontiguousarray(
        dn.transpose(0, 2, 1).reshape(NEXP, INTER, 4, 512).transpose(0, 2, 1, 3))
    gatet = np.ascontiguousarray(gwt.T)
    c_cols = np.ascontiguousarray(cv.reshape(16, 128).T)

    tglob = np.arange(T, dtype=f32)
    invf = (1.0 / (10000.0 ** (np.arange(0, DK, 2, dtype=f32) / DK))).astype(f32)
    freqs = (tglob[:, None] * invf[None, :]).astype(f32)
    cosb = np.cos(freqs).astype(f32).T
    sinb = np.sin(freqs).astype(f32).T
    ip = np.arange(T, dtype=np.float64) % 128
    tabs_all = np.zeros((NCORE, 2, H, 2, 128, TLOC), f32)
    for h in range(H):
        qs = (np.float64(GAMMA[h]) ** (ip + 1) * DK ** -0.5)
        ks = (np.float64(GAMMA[h]) ** (-ip - 1))
        for c in range(NCORE):
            sl = slice(TLOC * c, TLOC * (c + 1))
            tabs_all[c, 0, h, 0] = cosb[:, sl] * qs[None, sl]
            tabs_all[c, 0, h, 1] = sinb[:, sl] * qs[None, sl]
            tabs_all[c, 1, h, 0] = cosb[:, sl] * ks[None, sl]
            tabs_all[c, 1, h, 1] = sinb[:, sl] * ks[None, sl]

    identity = np.eye(128, dtype=BFNP)
    ii = np.arange(128)
    causal = (ii[None, :] >= ii[:, None]).astype(BFNP)

    sinw_all = np.zeros((NCORE, 128, 32), f32)
    for c in range(NCORE):
        for h in range(H):
            for j in range(NCORE):
                if j < c:
                    sinw_all[c, :, 8 * h + j] = np.float64(GAMMA[h]) ** (1024.0 * (c - 1 - j))

    per_core = []
    for c in range(NCORE):
        m = dict(
            x_t=np.ascontiguousarray(x[TLOC * c:TLOC * (c + 1)]),
            wq_m=wq_m, wk_m=wk_m, wv_n=wv_n, wg_n=wg_n, wo_g=wo_g,
            gup=gup, dnp=dnp, gatet=gatet,
            wmod_s=np.ascontiguousarray(
                wmod[:, 1536 * c:1536 * (c + 1)].reshape(KT, 128, 1536)),
            c_cols=c_cols, gnorm=gnw, alpha2=alpha,
            tabs=np.ascontiguousarray(tabs_all[c]),
            ident=identity, causalT=causal,
            sinw=np.ascontiguousarray(sinw_all[c]),
        )
        per_core.append(m)
    return per_core


def kernel(**inputs):
    per_core = _prep(inputs)
    if "nc" not in _cache:
        _cache["nc"] = build()
    nc = _cache["nc"]
    res = run_bass_kernel_spmd(nc, per_core, core_ids=list(range(NCORE)))
    out = np.concatenate([np.asarray(r["y"]) for r in res.results], axis=0)
    return out.reshape(1, T, D).astype(np.float32)




# version 06
# speedup vs baseline: 1.2183x; 1.0158x over previous
import contextlib
import numpy as np
import ml_dtypes
import concourse.bass as bass
import concourse.mybir as mybir
from concourse.bass_utils import run_bass_kernel_spmd

F32 = mybir.dt.float32
BF16 = mybir.dt.bfloat16
AF = mybir.ActivationFunctionType
ALU = mybir.AluOpType
AX = mybir.AxisListType
BFNP = ml_dtypes.bfloat16

NCORE = 8
T = 8192
TLOC = 1024
D = 2048
H = 4
DK = 256
DV = 512
KT = 16
NT = 8
NEXP = 32
INTER = 768
GAMMA = [float(1.0 - 2.0 ** (-5 - h)) for h in range(H)]
G128 = [float(np.float64(g) ** 128) for g in GAMMA]
EPS = 1e-5
TAU = 0.05

O_HT = 0
O_GP = 16384
O_QKOB = 32768
O_VB = 49152
O_F1 = 65536
O_F2 = 81920
O_M = 90112
O_BC0 = 94208
O_BC1 = 98304
O_ST = 102400
O_KTK = 104448
ACOLS = 105472


class Prog:
    ENGS = ("tensor", "vector", "scalar", "gpsimd", "sync")

    def __init__(self, sems):
        self.ops = {k: [] for k in self.ENGS}
        self.sems = sems
        self.ctr = {k: 0 for k in sems}
        self.last = {k: None for k in self.ENGS}
        self.last_kind = {k: None for k in self.ENGS}
        self.last_mark = {k: None for k in self.ENGS}

    def emit(self, eng, meth, *args, inc=None, incn=1, **kw):
        semh = self.sems[inc] if inc else None
        last = self.last
        dr = eng in ("vector", "scalar") and meth != "dma_start"

        def f(e, meth=meth, args=args, kw=kw, semh=semh, incn=incn, eng=eng, dr=dr):
            if dr:
                e.drain()
            ins = getattr(e, meth)(*args, **kw)
            last[eng] = ins
            if semh is not None:
                ins.then_inc(semh, incn)
        self.ops[eng].append(f)
        self.last_kind[eng] = "emit" if inc is None else "emit_inc"
        self.last_mark[eng] = inc
        if inc:
            self.ctr[inc] += incn
            return self.ctr[inc]
        return None

    def wait(self, eng, sem, v):
        if not v or v <= 0:
            return
        semh = self.sems[sem]

        def f(e, semh=semh, v=v):
            e.wait_ge(semh, v)
        self.ops[eng].append(f)

    def mark(self, eng, sem):
        if self.last_kind[eng] in ("mark", "emit_inc") and \
                self.last_mark[eng] == sem:
            return self.ctr[sem]
        assert self.last_kind[eng] in ("emit",), \
            f"mark({eng},{sem}) after {self.last_kind[eng]}"
        semh = self.sems[sem]
        last = self.last

        def f(e, semh=semh, eng=eng):
            last[eng].then_inc(semh, 1)
        self.ops[eng].append(f)
        self.last_kind[eng] = "mark"
        self.last_mark[eng] = sem
        self.ctr[sem] += 1
        return self.ctr[sem]

    def dma(self, queue, out, in_, sem=None):
        assert sem is not None, "dma needs an explicit slot semaphore"
        return self.emit(queue, "dma_start", inc=sem, incn=16, out=out, in_=in_)


class Psum:

    def __init__(self, P, ap):
        self.P, self.ap = P, ap
        self.free_at = None

    def twait(self):
        if self.free_at is not None:
            self.P.wait("tensor", *self.free_at)
            self.free_at = None

    def release(self, sem, val):
        self.free_at = (sem, val)


def build(dbg=()):
    nc = bass.Bass("TRN2", target_bir_lowering=False)

    def di(name, shape, dtype=F32):
        return nc.dram_tensor(name, shape, dtype, kind="ExternalInput").ap()

    x_t = di("x_t", [TLOC, D])
    wq_m = di("wq_m", [8, D, 128])
    wk_m = di("wk_m", [8, D, 128])
    wv_n = di("wv_n", [4, D, 512])
    wg_n = di("wg_n", [4, D, 512])
    wo_g = di("wo_g", [4, 512, D])
    gup = di("gup", [NEXP, 6, 2, D, 128])
    dnp = di("dnp", [NEXP, 4, INTER, 512])
    gatet = di("gatet", [D, 32])
    wmod_s = di("wmod_s", [KT, 128, 1536])
    c_cols = di("c_cols", [128, 16])
    gnorm = di("gnorm", [1, 512])
    alpha2 = di("alpha2", [1, 1])
    tabs = di("tabs", [2, H, 2, 128, TLOC])
    ident = di("ident", [128, 128], BF16)
    causalT = di("causalT", [128, 128], BF16)
    sinw = di("sinw", [128, 32])

    y = nc.dram_tensor("y", [TLOC, D], F32, kind="ExternalOutput").ap()
    dbg_out = {}
    for name, shape, dt in dbg:
        dbg_out[name] = nc.dram_tensor("dbg_" + name, list(shape), dt,
                                       kind="ExternalOutput").ap()

    modg_i = nc.dram_tensor("modg_i", [1, 1536], F32).ap()
    modg_o = nc.dram_tensor("modg_o", [NCORE, 1536], F32).ap()
    lg_i = nc.dram_tensor("lg_i", [128, 4096], BF16).ap()
    lg_o = nc.dram_tensor("lg_o", [NCORE * 128, 4096], BF16).ap()

    with contextlib.ExitStack() as ctx:
        en = ctx.enter_context
        AR = en(nc.sbuf_tensor("AR", [128, ACOLS], BF16))
        pA_t = en(nc.psum_tensor("pA", [128, 1024], F32))
        pB_t = en(nc.psum_tensor("pB", [128, 1024], F32))
        pC_t = en(nc.psum_tensor("pC", [128, 1024], F32))
        pD_t = en(nc.psum_tensor("pD", [128, 1024], F32))
        semnames = ["T", "V", "S", "C", "Lc", "Lx0", "Lx1", "Ltab",
                    "Lm0", "Lm1", "Lm2", "Lm3", "Lvg0", "Lvg1", "Ltab0", "Ltab1",
                    "Lwm0", "Lwm1", "Lgu0", "Lgu1",
                    "Ldn0", "Ldn1", "Ldn2", "Ldn3", "Lwo", "Lxa", "Ly",
                    "Llr0", "Llr1", "Lch0", "Lch1", "Lmi", "Lli", "Lgate"]
        semh = {s: en(nc.semaphore("sem" + s)) for s in semnames}
        block = en(nc.Block())
        P = Prog(semh)

        def B(o, n, p0=0, p1=128):
            return AR[p0:p1, o:o + n]

        def Fv(o, n, p0=0, p1=128):
            return AR[p0:p1, o:o + 2 * n].bitcast(F32)

        pAl, pAh = Psum(P, pA_t[:, 0:512]), Psum(P, pA_t[:, 512:1024])
        pBl, pBh = Psum(P, pB_t[:, 0:512]), Psum(P, pB_t[:, 512:1024])
        pCl, pCh = Psum(P, pC_t[:, 0:512]), Psum(P, pC_t[:, 512:1024])
        pDl, pDh = Psum(P, pD_t[:, 0:512]), Psum(P, pD_t[:, 512:1024])
        pT = pD_t[:, 0:512].bitcast(BF16)

        st_cur = [O_ST]

        def st_alloc(nbf):
            o = st_cur[0]
            st_cur[0] = (o + nbf + 15) // 16 * 16
            assert st_cur[0] <= O_ST + 2048, f"ST overflow {st_cur[0]}"
            return o

        ones = Fv(st_alloc(256), 128)
        csb = Fv(st_alloc(32), 16)
        silc = Fv(st_alloc(32), 16)
        id_sb = B(st_alloc(128), 128)
        cau_sb = B(st_alloc(128), 128)
        sinw_sb = Fv(st_alloc(64), 32)
        albB = Fv(st_alloc(16), 1)
        s_mean = Fv(st_alloc(16), 8)
        s_inv = Fv(st_alloc(16), 8)
        s_nmi = Fv(st_alloc(16), 8)
        s_ssq = Fv(st_alloc(16), 8)
        s_oiv = Fv(st_alloc(16), 8)
        s_t1 = Fv(st_alloc(16), 8)
        comb = Fv(st_alloc(512), 256)
        r_p = Fv(st_alloc(64), 32)
        r_cur = Fv(st_alloc(64), 32)
        r_tmp = Fv(st_alloc(64), 32)
        r_mk = Fv(st_alloc(16), 8)
        r_cnt = Fv(st_alloc(16), 8)
        r_a1 = Fv(st_alloc(64), 32)
        attn_sb = B(st_alloc(128), 128)
        attn_sbB = B(st_alloc(128), 128)

        chrow = [Fv(O_KTK, 512, 0, 1), Fv(O_M, 512, 0, 1)]
        rtmprow = Fv(O_M + 1024, 512, 0, 1)

        def hT(kt, c0, n):
            return B(O_HT + kt * 1024 + c0, n)

        def h2T(kt, c0, n):
            return B(O_QKOB + kt * 1024 + c0, n)

        def qseg(p, h, half):
            return B(O_QKOB + p * 2048 + h * 512 + half * 128, 128)

        def kseg(p, h, half):
            return B(O_QKOB + p * 2048 + h * 512 + 256 + half * 128, 128)

        def obv(p, c0, n):
            return B(O_QKOB + p * 2048 + c0, n)

        def vtile(p, c0, n):
            return B(O_VB + p * 2048 + c0, n)

        def gtile(p, c0, n):
            return B(O_GP + p * 2048 + c0, n)

        def accv(tt, c0, n):
            return Fv(4096 * tt + 2 * c0, n)

        def Mv(dt8, n=512):
            return B(O_M + dt8 * 512, n)

        ktok = B(O_KTK, 1024)
        ogT = B(O_M, 2048)

        P.emit("vector", "memset", ones, 1.0)
        v_ones = P.mark("vector", "V")

        P.dma("sync", csb, c_cols, sem="Lc")
        P.dma("sync", id_sb, ident, sem="Lc")
        P.dma("sync", cau_sb, causalT, sem="Lc")
        d_const = P.dma("sync", sinw_sb, sinw, sem="Lc")

        wslot = [Fv(O_VB, 1536), Fv(O_VB + 3072, 1536)]
        modloc = Fv(O_VB + 6144, 1536, 0, 1)

        P.wait("scalar", "Lc", d_const)
        v_silc = P.emit("scalar", "activation", silc, csb, AF.Silu, inc="S")

        mps = [pA_t[0:1, 0:512], pB_t[0:1, 0:512], pC_t[0:1, 0:512]]
        P.wait("tensor", "S", v_silc)
        t_after_k = {}
        for k in range(KT):
            q = "gpsimd" if k % 2 == 0 else "sync"
            if k >= 2:
                P.wait(q, "T", t_after_k[k - 2])
            d = P.dma(q, wslot[k % 2], wmod_s[k], sem="Lwm%d" % (k % 2))
            P.wait("tensor", "Lwm%d" % (k % 2), d)
            for nch in range(3):
                P.emit("tensor", "matmul", mps[nch], silc[:, k:k + 1],
                       wslot[k % 2][:, 512 * nch:512 * (nch + 1)],
                       start=(k == 0), stop=(k == KT - 1))
            t_after_k[k] = P.mark("tensor", "T")
        v_modmm = t_after_k[KT - 1]

        P.wait("scalar", "T", v_modmm)
        for nch in range(3):
            P.emit("scalar", "copy", modloc[:, 512 * nch:512 * (nch + 1)], mps[nch])
        v_modev = P.mark("scalar", "S")
        pAl.release("S", v_modev)
        pBl.release("S", v_modev)
        pCl.release("S", v_modev)

        P.wait("gpsimd", "S", v_modev)
        d_mi = P.dma("gpsimd", modg_i[:, :], modloc, sem="Lmi")
        P.wait("gpsimd", "Lmi", d_mi)
        v_cc1 = P.emit("gpsimd", "collective_compute", "AllGather", ALU.bypass,
                       replica_groups=[list(range(NCORE))],
                       ins=[modg_i[:, :]], outs=[modg_o[:, :]], inc="C")

        modflat = modg_o.rearrange("a b -> (a b)")
        bc_state = {"i": 0, "last_mm": {0: 0, 1: 0}}

        def bcast512(dst, chunk_idx, add1=False, row_math=None):
            i = bc_state["i"] % 2
            bc_state["i"] += 1
            row = chrow[i]
            P.wait("sync", "C", v_cc1)
            P.wait("sync", "T", bc_state["last_mm"][i])
            d = P.dma("sync", row,
                      modflat[512 * chunk_idx:512 * (chunk_idx + 1)].unsqueeze(0),
                      sem="Lch%d" % i)
            sem_mm, val_mm = "Lch%d" % i, d
            if row_math is not None:
                P.wait("vector", sem_mm, d)
                row_math(row)
                val_mm = P.mark("vector", "V")
                sem_mm = "V"
            P.wait("tensor", sem_mm, val_mm)
            P.wait("tensor", "V", v_ones)
            pBh.twait()
            P.emit("tensor", "matmul", pBh.ap, ones[0:1, :], row,
                   start=True, stop=True)
            mmv = P.mark("tensor", "T")
            bc_state["last_mm"][i] = mmv
            P.wait("scalar", "T", mmv)
            if add1:
                P.emit("scalar", "add", dst, pBh.ap, 1.0)
            else:
                P.emit("scalar", "copy", dst, pBh.ap)
            ev = P.mark("scalar", "S")
            pBh.release("S", ev)
            return ev

        def gamath(row):
            P.emit("vector", "tensor_scalar", rtmprow, row, 0.0, None, ALU.is_ge)
            P.emit("vector", "tensor_scalar", rtmprow, rtmprow, 2.0, -1.0,
                   ALU.mult, ALU.add)
            P.emit("vector", "tensor_tensor", row, row, rtmprow, ALU.mult)
            P.emit("vector", "tensor_scalar", row, row, 0.25, None, ALU.max)
            P.emit("vector", "tensor_tensor", row, row, rtmprow, ALU.mult)

        def rgmath(row):
            gamath(row)
            P.emit("vector", "reciprocal", row, row)

        bc0 = Fv(O_BC0, 2048)
        bc1 = Fv(O_BC1, 2048)

        xtile = [Fv(O_F2, 2048), Fv(O_F2 + 4096, 2048)]
        d_x = {0: P.dma("sync", xtile[0], x_t[0:128, :], sem="Lx0"),
               1: P.dma("sync", xtile[1], x_t[128:256, :], sem="Lx1")}

        d_al = P.dma("sync", chrow[0][:, 0:1], alpha2, sem="Lch0")
        P.wait("tensor", "Lch0", d_al)
        P.wait("tensor", "V", v_ones)
        pBh.twait()
        P.emit("tensor", "matmul", pB_t[:, 512:513], ones[0:1, :],
               chrow[0][:, 0:1], start=True, stop=True)
        v_mm = P.mark("tensor", "T")
        bc_state["last_mm"][0] = v_mm
        P.wait("scalar", "T", v_mm)
        P.emit("scalar", "copy", albB, pB_t[:, 512:513])
        v_alb = P.mark("scalar", "S")
        pBh.release("S", v_alb)

        for j in range(4):
            bcast512(bc0[:, 512 * j:512 * (j + 1)], 4 + j, add1=True)
        for j in range(4):
            v_casa = bcast512(bc1[:, 512 * j:512 * (j + 1)], j)

        htmp = [Fv(O_F1, 2048), Fv(O_F1 + 4096, 2048)]
        hbuf = [B(O_F1 + 8192, 2048), B(O_F1 + 10240, 2048)]
        sq_scr = Fv(O_F1 + 12288, 2048)
        x_free_at = {}
        v_h = {}
        tpos_res = {0: pDl, 1: pCl}
        tpos_bf = {0: pT, 1: pC_t[:, 0:512].bitcast(BF16)}

        v_h = {}
        x_free_at = {}
        vB_done = {}
        sC_done = {}

        def p1_A(t):
            xt = xtile[t % 2]
            xsem = "Lx%d" % (t % 2)
            P.wait("vector", xsem, d_x[t])
            P.emit("vector", "reduce_sum", s_mean[:, t:t + 1], xt, axis=AX.X)
            P.wait("scalar", xsem, d_x[t])
            P.emit("scalar", "activation", sq_scr, xt, AF.Square,
                   accum_out=s_ssq[:, t:t + 1])
            return P.mark("scalar", "S")

        sA_done = {0: None}
        sA_done[0] = p1_A(0)
        sA_done[1] = p1_A(1)
        for t in range(NT):
            xt, ht, hb = xtile[t % 2], htmp[t % 2], hbuf[t % 2]
            P.wait("vector", "S", sA_done[t])
            P.emit("vector", "tensor_scalar", s_mean[:, t:t + 1],
                   s_mean[:, t:t + 1], 1.0 / D, None, ALU.mult)
            P.emit("vector", "tensor_tensor", s_t1[:, t:t + 1],
                   s_mean[:, t:t + 1], s_mean[:, t:t + 1], ALU.mult)
            P.emit("vector", "tensor_scalar", s_inv[:, t:t + 1],
                   s_ssq[:, t:t + 1], 1.0 / D, None, ALU.mult)
            P.emit("vector", "tensor_tensor", s_inv[:, t:t + 1],
                   s_inv[:, t:t + 1], s_t1[:, t:t + 1], ALU.subtract)
            P.emit("vector", "tensor_scalar", s_inv[:, t:t + 1],
                   s_inv[:, t:t + 1], EPS, None, ALU.add)
            P.emit("vector", "reciprocal", s_inv[:, t:t + 1], s_inv[:, t:t + 1])
            vB = P.mark("vector", "V")
            P.wait("scalar", "V", vB)
            P.emit("scalar", "sqrt", s_inv[:, t:t + 1], s_inv[:, t:t + 1])
            sC = P.mark("scalar", "S")
            P.wait("vector", "S", sC)
            P.emit("vector", "tensor_scalar", s_nmi[:, t:t + 1],
                   s_mean[:, t:t + 1], s_inv[:, t:t + 1], -1.0,
                   ALU.mult, ALU.mult)
            vD = P.mark("vector", "V")
            P.wait("scalar", "V", vD)
            P.emit("scalar", "activation", ht, xt, AF.Identity,
                   bias=s_nmi[:, t:t + 1], scale=s_inv[:, t:t + 1])
            x_free_at[t] = P.mark("scalar", "S")
            if t + 2 < NT:
                P.wait("sync", "S", x_free_at[t])
                d_x[t + 2] = P.dma("sync", xtile[t % 2],
                                   x_t[128 * (t + 2):128 * (t + 3), :],
                                   sem="Lx%d" % (t % 2))
            if t + 2 < NT:
                sA_done[t + 2] = p1_A(t + 2)
            P.wait("vector", "S", x_free_at[t])
            P.wait("vector", "S", v_casa)
            P.emit("vector", "tensor_tensor", ht, ht, bc0, ALU.mult)
            P.emit("vector", "tensor_tensor", hb, ht, bc1, ALU.add)
            v_h[t] = P.mark("vector", "V")
            P.wait("tensor", "V", v_h[t])
            P.wait("tensor", "Lc", d_const)
            for grp in range(2):
                res = tpos_res[grp]
                pbf = tpos_bf[grp]
                res.twait()
                for j in range(8):
                    kt2 = grp * 8 + j
                    P.emit("tensor", "transpose", pbf[:, 128 * j:128 * (j + 1)],
                           hb[:, 128 * kt2:128 * (kt2 + 1)], id_sb)
                vt = P.mark("tensor", "T")
                P.wait("vector", "T", vt)
                for j in range(8):
                    kt2 = grp * 8 + j
                    P.emit("vector", "tensor_copy", hT(kt2, 128 * t, 128),
                           pbf[:, 128 * j:128 * (j + 1)])
                vv = P.mark("vector", "V")
                res.release("V", vv)
        v_hT_all = P.mark("vector", "V")

        mring = [B(O_F2 + 2048 * s, 2048) for s in range(4)]
        mm_free = {}
        rope_done = {}
        slot_i = 0
        v_rope_prev = 0
        d_tab_prev = 0

        tabc2 = [[Fv(O_F1, 1024), Fv(O_F1 + 2048, 1024)],
                 [Fv(O_F1 + 8192, 1024), Fv(O_F1 + 10240, 1024)]]
        rsc2 = [[Fv(O_F1 + 4096, 1024), Fv(O_F1 + 6144, 1024)],
                [Fv(O_F1 + 12288, 1024), Fv(O_F1 + 14336, 1024)]]
        hi = 0
        for side in range(2):
            wsrc = wq_m if side == 0 else wk_m
            for h in range(H):
                par = hi % 2
                tabc = tabc2[par]
                rsc = rsc2[par]
                q1t, q2t = (pA_t, pB_t) if par == 0 else (pC_t, pD_t)
                pun = [(pAl, pAh), (pBl, pBh)] if par == 0 else [(pCl, pCh), (pDl, pDh)]
                hi += 1
                P.wait("sync", "V", v_hT_all if hi <= 2 else rope_done.get(hi - 3, 0))
                d_cos = P.dma("sync", tabc[0], tabs[side, h, 0],
                              sem="Ltab%d" % par)
                d_sin = P.dma("sync", tabc[1], tabs[side, h, 1],
                              sem="Ltab%d" % par)
                vps = []
                for half in range(2):
                    m = 2 * h + half
                    s4 = slot_i % 4
                    P.wait("gpsimd", "T", mm_free.get(s4, 0))
                    if slot_i == 0:
                        P.wait("gpsimd", "S", x_free_at[NT - 1])
                    dsl = P.dma("gpsimd",
                                mring[s4].rearrange("p (a m) -> p a m", m=128),
                                wsrc[m].rearrange("(a p) m -> p a m", p=128),
                                sem="Lm%d" % s4)
                    ptgt = pun[half]
                    P.wait("tensor", "Lm%d" % s4, dsl)
                    P.wait("tensor", "V", v_hT_all)
                    for tch in range(2):
                        ptgt[tch].twait()
                        for kt2 in range(KT):
                            P.emit("tensor", "matmul", ptgt[tch].ap,
                                   mring[s4][:, 128 * kt2:128 * (kt2 + 1)],
                                   hT(kt2, 512 * tch, 512),
                                   start=(kt2 == 0), stop=(kt2 == KT - 1))
                    vp = P.mark("tensor", "T")
                    mm_free[s4] = vp
                    vps.append(vp)
                    slot_i += 1
                P.wait("vector", "T", vps[-1])
                P.wait("vector", "Ltab%d" % par, d_sin)
                q1 = q1t[:, 0:1024]
                q2 = q2t[:, 0:1024]
                segd = qseg if side == 0 else kseg
                P.emit("vector", "tensor_tensor", rsc[0], q1, tabc[0], ALU.mult)
                P.emit("vector", "tensor_tensor", rsc[1], q2, tabc[1], ALU.mult)
                for p in range(NT):
                    P.emit("vector", "tensor_tensor", segd(p, h, 0),
                           rsc[0][:, 128 * p:128 * (p + 1)],
                           rsc[1][:, 128 * p:128 * (p + 1)], ALU.subtract)
                P.emit("vector", "tensor_tensor", rsc[0], q2, tabc[0], ALU.mult)
                P.emit("vector", "tensor_tensor", rsc[1], q1, tabc[1], ALU.mult)
                for p in range(NT):
                    P.emit("vector", "tensor_tensor", segd(p, h, 1),
                           rsc[0][:, 128 * p:128 * (p + 1)],
                           rsc[1][:, 128 * p:128 * (p + 1)], ALU.add)
                v_rope_prev = P.mark("vector", "V")
                rope_done[hi - 1] = v_rope_prev
                for pp in pun[0] + pun[1]:
                    pp.release("V", v_rope_prev)
        v_rope_all = v_rope_prev

        vgslot = [B(O_F1 + 8192 * s, 8192) for s in range(2)]
        vg_free = {}
        vg_i = 0
        pcycle = [pAl, pAh, pBl, pBh]
        pc_i = 0
        v_g_ev = {}
        v_v_ev = {}
        P.wait("scalar", "Lmi", d_mi)

        def vg_proj(which):
            nonlocal vg_i, pc_i
            wsrc = wv_n if which == 0 else wg_n
            for nch in range(4):
                s2 = vg_i % 2
                if vg_i < 2:
                    P.wait("gpsimd", "V", v_rope_all)
                else:
                    P.wait("gpsimd", "T", vg_free[s2])
                dsl = P.dma("gpsimd",
                            vgslot[s2].rearrange("p (a m) -> p a m", m=512),
                            wsrc[nch].rearrange("(a p) m -> p a m", p=128),
                            sem="Lvg%d" % s2)
                P.wait("tensor", "Lvg%d" % s2, dsl)
                for tt in range(NT):
                    pres = pcycle[pc_i % 4]
                    pc_i += 1
                    pres.twait()
                    for kt2 in range(KT):
                        P.emit("tensor", "matmul", pres.ap,
                               hT(kt2, 128 * tt, 128),
                               vgslot[s2][:, 512 * kt2:512 * (kt2 + 1)],
                               start=(kt2 == 0), stop=(kt2 == KT - 1))
                    vmm = P.mark("tensor", "T")
                    if tt == NT - 1:
                        vg_free[s2] = vmm
                    P.wait("scalar", "T", vmm)
                    if which == 0:
                        P.emit("scalar", "copy",
                               vtile(tt, 512 * nch, 512), pres.ap)
                        ev = P.mark("scalar", "S")
                        pres.release("S", ev)
                        v_v_ev[(nch, tt)] = ev
                    else:
                        P.emit("scalar", "activation",
                               gtile(tt, 512 * nch, 512), pres.ap, AF.Silu)
                        ev = P.mark("scalar", "S")
                        pres.release("S", ev)
                        P.wait("vector", "S", max(ev, v_gnB))
                        P.emit("vector", "tensor_tensor",
                               gtile(tt, 512 * nch, 512),
                               gtile(tt, 512 * nch, 512), gnormB, ALU.mult)
                        v_g_ev[(nch, tt)] = P.mark("vector", "V")
                vg_i += 1

        vg_proj(0)

        def Mpar(dt8, par, n=512):
            return B((O_M if par % 2 == 0 else O_F2) + dt8 * 512, n)

        P.wait("vector", "Lch0", P.ctr["Lch0"])
        P.wait("vector", "Lch1", P.ctr["Lch1"])
        P.wait("vector", "T", max(bc_state["last_mm"].values()))
        P.emit("vector", "memset", B(O_M, 4096), 0.0)
        v_m0 = P.mark("vector", "V")
        kv_cycle = [pAh, pBh, pCh, pDh]
        kv_i = 0
        v_Lupd = v_m0
        for p in range(NT):
            pDl.twait()
            P.wait("tensor", "V", v_rope_all)
            for dt8 in range(8):
                h, half = dt8 // 2, dt8 % 2
                P.emit("tensor", "transpose", pT[:, 128 * dt8:128 * (dt8 + 1)],
                       kseg(p, h, half), id_sb)
            vt = P.mark("tensor", "T")
            P.wait("vector", "T", vt)
            for dt8 in range(8):
                h = dt8 // 2
                P.emit("vector", "tensor_scalar",
                       ktok[:, 128 * dt8:128 * (dt8 + 1)],
                       pT[:, 128 * dt8:128 * (dt8 + 1)], G128[h], None, ALU.mult)
            v_kt = P.mark("vector", "V")
            pDl.release("V", v_kt)
            for dt8 in range(8):
                h, half = dt8 // 2, dt8 % 2
                pres = kv_cycle[kv_i % 4]
                kv_i += 1
                pres.twait()
                P.wait("tensor", "V", v_kt)
                P.wait("tensor", "S", v_v_ev[(h, p)])
                P.emit("tensor", "matmul", pres.ap,
                       ktok[:, 128 * dt8:128 * (dt8 + 1)],
                       vtile(p, 512 * h, 512), start=True, stop=True)
                vmm = P.mark("tensor", "T")
                P.wait("vector", "T", vmm)
                P.wait("vector", "V", v_m0)
                P.emit("vector", "scalar_tensor_tensor", Mpar(dt8, 0),
                       Mpar(dt8, 0), G128[h], pres.ap, ALU.mult, ALU.add)
                v_Lupd = P.mark("vector", "V")
                pres.release("V", v_Lupd)
        P.wait("gpsimd", "V", v_Lupd)
        d_li = P.dma("gpsimd", lg_i[:, :], B(O_M, 4096), sem="Lli")
        P.wait("gpsimd", "Lli", d_li)
        v_cc2 = P.emit("gpsimd", "collective_compute", "AllGather", ALU.bypass,
                       replica_groups=[list(range(NCORE))],
                       ins=[lg_i[:, :]], outs=[lg_o[:, :]], inc="C")

        gnormB = Fv(O_F2, 512)
        P.wait("sync", "V", v_Lupd)
        d_gn2 = P.dma("sync", chrow[0], gnorm, sem="Lch0")
        P.wait("tensor", "Lch0", d_gn2)
        pBh.twait()
        P.emit("tensor", "matmul", pBh.ap, ones[0:1, :], chrow[0],
               start=True, stop=True)
        v_mm = P.mark("tensor", "T")
        bc_state["last_mm"][0] = v_mm
        P.wait("scalar", "T", v_mm)
        P.emit("scalar", "copy", gnormB, pBh.ap)
        v_gnB = P.mark("scalar", "S")
        pBh.release("S", v_gnB)

        vg_proj(1)
        v_vg_all = v_g_ev[(3, NT - 1)]
        v_gp = v_vg_all

        woslot = [B(O_VB + 8192 * g, 8192) for g in range(4)]
        d_wo = {}
        for g in (2, 3):
            P.wait("gpsimd", "T", vg_free[g - 2])
            d_wo[g] = P.dma("gpsimd",
                            woslot[g].rearrange("p (a m) -> p a m", m=2048),
                            wo_g[g].rearrange("(a p) m -> p a m", p=128),
                            sem="Lwo")

        lrecv = [B(O_F2 + 4096 * s, 4096) for s in range(2)]
        sps = [pAl, pAh, pBl, pBh, pCl, pCh, pDl, pDh]
        for s8 in sps:
            s8.free_at = None
        d_lr = {}
        v_lr_used = {}
        P.wait("gpsimd", "C", v_cc2)
        P.wait("sync", "C", v_cc2)
        P.wait("gpsimd", "V", v_gp)
        P.wait("sync", "V", v_gp)
        d_lr[0] = P.dma("gpsimd", lrecv[0], lg_o[0:128, :], sem="Llr0")
        for j in range(NCORE):
            if j + 1 < NCORE:
                q = "gpsimd" if (j + 1) % 2 == 0 else "sync"
                P.wait(q, "V", v_lr_used.get(j - 1, 0))
                d_lr[j + 1] = P.dma(q, lrecv[(j + 1) % 2],
                                    lg_o[128 * (j + 1):128 * (j + 2), :],
                                    sem="Llr%d" % ((j + 1) % 2))
            P.wait("vector", "Llr%d" % (j % 2), d_lr[j])
            for h in range(H):
                wcol = sinw_sb[:, 8 * h + j:8 * h + j + 1]
                srcw = lrecv[j % 2][:, 1024 * h:1024 * (h + 1)]
                full = [pA_t, pB_t, pC_t, pD_t][h][:, 0:1024]
                if j == 0:
                    P.emit("vector", "tensor_scalar", full, srcw,
                           wcol, None, ALU.mult)
                else:
                    P.emit("vector", "scalar_tensor_tensor", full, srcw,
                           wcol, full, ALU.mult, ALU.add)
            v_lr_used[j] = P.mark("vector", "V")
        P.wait("vector", "Lli", d_li)
        for h in range(H):
            P.emit("vector", "tensor_copy", Mpar(2 * h, 0, 1024),
                   [pA_t, pB_t, pC_t, pD_t][h][:, 0:1024])
        v_sin = P.mark("vector", "V")
        for s8 in sps:
            s8.release("V", v_sin)

        attn_sb2 = [attn_sb, attn_sbB]
        v_Mupd = {-1: v_sin}
        v_ob = {}
        at_cycle = [pAl, pBl]
        kv_cycle2 = [pAh, pBh]
        o_cycle = [pCl, pCh, pDh]
        for p in range(NT):
            rpar, wpar = p % 2, (p + 1) % 2
            pDl.twait()
            P.wait("tensor", "V", v_sin)
            for dt8 in range(8):
                h, half = dt8 // 2, dt8 % 2
                P.emit("tensor", "transpose", pT[:, 128 * dt8:128 * (dt8 + 1)],
                       kseg(p, h, half), id_sb)
            vt = P.mark("tensor", "T")
            P.wait("vector", "T", vt)
            for dt8 in range(8):
                h = dt8 // 2
                P.emit("vector", "tensor_scalar",
                       ktok[:, 128 * dt8:128 * (dt8 + 1)],
                       pT[:, 128 * dt8:128 * (dt8 + 1)], G128[h], None, ALU.mult)
            v_kt = P.mark("vector", "V")
            pDl.release("V", v_kt)
            vm = None
            for dt8 in range(8):
                h = dt8 // 2
                pres = kv_cycle2[dt8 % 2]
                pres.twait()
                P.wait("tensor", "V", v_kt)
                if p == 0:
                    P.wait("tensor", "S", v_v_ev[(h, p)])
                P.emit("tensor", "matmul", pres.ap,
                       ktok[:, 128 * dt8:128 * (dt8 + 1)],
                       vtile(p, 512 * h, 512), start=True, stop=True)
                vkv = P.mark("tensor", "T")
                P.wait("vector", "T", vkv)
                P.emit("vector", "scalar_tensor_tensor", Mpar(dt8, wpar),
                       Mpar(dt8, rpar), G128[h], pres.ap, ALU.mult, ALU.add)
                vm = P.mark("vector", "V")
                pres.release("V", vm)
            v_Mupd[p] = vm
            P.wait("tensor", "V", v_Mupd[p - 1])
            for h in range(H):
                at = at_cycle[h % 2]
                at.twait()
                P.emit("tensor", "matmul", at.ap[:, 0:128], kseg(p, h, 0),
                       qseg(p, h, 0), start=True, stop=False)
                P.emit("tensor", "matmul", at.ap[:, 0:128], kseg(p, h, 1),
                       qseg(p, h, 1), start=False, stop=True)
                v_at = P.mark("tensor", "T")
                P.wait("vector", "T", v_at)
                P.emit("vector", "tensor_tensor", attn_sb2[h % 2],
                       at.ap[:, 0:128], cau_sb, ALU.mult)
                v_am = P.mark("vector", "V")
                at.release("V", v_am)
                op = o_cycle[h % 3]
                op.twait()
                P.wait("tensor", "V", v_am)
                P.emit("tensor", "matmul", op.ap, attn_sb2[h % 2],
                       vtile(p, 512 * h, 512), start=True, stop=False)
                P.emit("tensor", "matmul", op.ap, qseg(p, h, 0),
                       Mpar(2 * h, rpar), start=False, stop=False)
                P.emit("tensor", "matmul", op.ap, qseg(p, h, 1),
                       Mpar(2 * h + 1, rpar), start=False, stop=True)
                v_omm = P.mark("tensor", "T")
                P.wait("scalar", "T", v_omm)
                P.emit("scalar", "copy", obv(p, 512 * h, 512), op.ap)
                v_ob[(p, h)] = P.mark("scalar", "S")
                op.release("S", v_ob[(p, h)])

        v_og = {}
        RSb2 = Fv(O_M + 2048, 512)

        def p3b_sq(p):
            if p == 0:
                P.wait("scalar", "V", v_gp)
            P.wait("scalar", "V", v_og.get(p - 2, 0))
            for h in range(H):
                P.wait("scalar", "S", v_ob[(p, h)])
                P.emit("scalar", "activation", RSb2,
                       obv(p, 512 * h, 512), AF.Square,
                       accum_out=s_oiv[:, (p % 2) * 4 + h:(p % 2) * 4 + h + 1])
            return P.mark("scalar", "S")

        sqd = {0: p3b_sq(0)}
        for p in range(NT):
            c4 = s_oiv[:, (p % 2) * 4:(p % 2) * 4 + 4]
            if p + 1 < NT:
                sqd[p + 1] = p3b_sq(p + 1)
            P.wait("vector", "S", sqd[p])
            P.emit("vector", "tensor_scalar", c4, c4, 1.0 / DV, EPS,
                   ALU.mult, ALU.add)
            P.emit("vector", "reciprocal", c4, c4)
            vv = P.mark("vector", "V")
            P.wait("scalar", "V", vv)
            P.emit("scalar", "sqrt", c4, c4)
            vs2 = P.mark("scalar", "S")
            P.wait("vector", "S", vs2)
            for h in range(H):
                P.emit("vector", "scalar_tensor_tensor", obv(p, 512 * h, 512),
                       obv(p, 512 * h, 512),
                       s_oiv[:, (p % 2) * 4 + h:(p % 2) * 4 + h + 1],
                       gtile(p, 512 * h, 512), ALU.mult, ALU.mult)
            v_og[p] = P.mark("vector", "V")
        v_og_all = v_og[NT - 1]

        v_ret_done0 = v_Mupd[NT - 1]
        P.wait("sync", "V", v_ret_done0)
        for j in range(4):
            v_gaB = bcast512(bc0[:, 512 * j:512 * (j + 1)], 8 + j,
                             row_math=gamath)

        wtmp = Fv(O_F2, 512)
        v_ret_done = v_Mupd[NT - 1]
        for g in (0, 1):
            P.wait("gpsimd", "S", v_ob[(NT - 1, H - 1)])
            d_wo[g] = P.dma("gpsimd",
                            woslot[g].rearrange("p (a m) -> p a m", m=2048),
                            wo_g[g].rearrange("(a p) m -> p a m", p=128),
                            sem="Lwo")

        d_x2 = {}
        P.wait("sync", "V", v_og_all)
        for tt in range(NT):
            d_x2[tt] = P.dma("sync", accv(tt, 0, 2048),
                             x_t[128 * tt:128 * (tt + 1), :], sem="Lxa")
        d_x2_all = d_x2[NT - 1]

        v_wo_used = {}
        wo_cycle = [pAl, pAh, pBl, pBh]
        wo_i = 0
        v_acc_wo = {}
        for tt in range(NT):
            P.wait("tensor", "V", v_og[tt])
            for grp in range(2):
                pDl.twait()
                for j in range(8):
                    kt2 = grp * 8 + j
                    P.emit("tensor", "transpose", pT[:, 128 * j:128 * (j + 1)],
                           obv(tt, 128 * kt2, 128), id_sb)
                vt = P.mark("tensor", "T")
                P.wait("vector", "T", vt)
                for j in range(8):
                    kt2 = grp * 8 + j
                    P.emit("vector", "tensor_copy", ogT[:, 128 * kt2:128 * (kt2 + 1)],
                           pT[:, 128 * j:128 * (j + 1)])
                v_tp = P.mark("vector", "V")
                pDl.release("V", v_tp)
            P.wait("tensor", "V", v_tp)
            for g in range(4):
                P.wait("tensor", "Lwo", d_wo[1])
                for nch in range(4):
                    pres = wo_cycle[wo_i % 4]
                    wo_i += 1
                    pres.twait()
                    for kg in range(4):
                        P.emit("tensor", "matmul", pres.ap,
                               ogT[:, 128 * (g * 4 + kg):128 * (g * 4 + kg + 1)],
                               woslot[g][:, 2048 * kg + 512 * nch:2048 * kg + 512 * (nch + 1)],
                               start=(kg == 0), stop=(kg == 3))
                    vmm = P.mark("tensor", "T")
                    if nch == 3 and tt == NT - 1:
                        v_wo_used[g] = vmm
                    P.wait("vector", "T", vmm)
                    P.wait("vector", "Lxa", d_x2_all)
                    if tt == 0 and g == 0 and nch == 0:
                        P.wait("vector", "S", v_gaB)
                    P.emit("vector", "tensor_tensor", wtmp, pres.ap,
                           bc0[:, 512 * nch:512 * (nch + 1)], ALU.mult)
                    P.emit("vector", "tensor_tensor",
                           accv(tt, 512 * nch, 512),
                           accv(tt, 512 * nch, 512), wtmp, ALU.add)
                    va = P.mark("vector", "V")
                    pres.release("V", va)
            v_acc_wo[tt] = va

        P.wait("sync", "T", v_wo_used[3])
        for j in range(4):
            bcast512(bc0[:, 512 * j:512 * (j + 1)], 16 + j, add1=True)
        for j in range(4):
            v_cbsb = bcast512(bc1[:, 512 * j:512 * (j + 1)], 12 + j)

        gate_bf = B(O_F2 + 2048, 512)
        P.wait("gpsimd", "T", v_wo_used[3])
        d_gate = P.dma("gpsimd",
                       gate_bf.rearrange("p (a m) -> p a m", m=32),
                       gatet.rearrange("(a p) m -> p a m", p=128), sem="Lgate")

        v_h2T = {}
        v_h2_done = {}

        LGb = Fv(O_F2 + 4608, 256)
        RPb = Fv(O_F2 + 5632, 256)
        RCb = Fv(O_F2 + 6656, 256)
        RSb = Fv(O_M + 2048, 256)
        MKS = Fv(O_M + 3072, 64)
        CNTS = Fv(O_M + 3200, 64)
        R8a = Fv(O_M + 3328, 8)
        R8b = Fv(O_M + 3344, 8)
        R8c = Fv(O_M + 3360, 8)

        def ln2_A(t):
            xt = accv(t, 0, 2048)
            P.wait("vector", "V", v_acc_wo[t])
            P.emit("vector", "reduce_sum", s_mean[:, t:t + 1], xt, axis=AX.X)
            P.wait("scalar", "V", v_acc_wo[t])
            if t == 0:
                P.wait("scalar", "T", v_wo_used[3])
            P.emit("scalar", "activation", sq_scr, xt, AF.Square,
                   accum_out=s_ssq[:, t:t + 1])
            return P.mark("scalar", "S")

        sA2 = {0: ln2_A(0)}
        sA2[1] = ln2_A(1)
        for t in range(NT):
            ht, hb = htmp[t % 2], hbuf[t % 2]
            xt = accv(t, 0, 2048)
            P.wait("vector", "S", sA2[t])
            P.emit("vector", "tensor_scalar", s_mean[:, t:t + 1],
                   s_mean[:, t:t + 1], 1.0 / D, None, ALU.mult)
            P.emit("vector", "tensor_tensor", s_t1[:, t:t + 1],
                   s_mean[:, t:t + 1], s_mean[:, t:t + 1], ALU.mult)
            P.emit("vector", "tensor_scalar", s_inv[:, t:t + 1],
                   s_ssq[:, t:t + 1], 1.0 / D, None, ALU.mult)
            P.emit("vector", "tensor_tensor", s_inv[:, t:t + 1],
                   s_inv[:, t:t + 1], s_t1[:, t:t + 1], ALU.subtract)
            P.emit("vector", "tensor_scalar", s_inv[:, t:t + 1],
                   s_inv[:, t:t + 1], EPS, None, ALU.add)
            P.emit("vector", "reciprocal", s_inv[:, t:t + 1], s_inv[:, t:t + 1])
            vB = P.mark("vector", "V")
            P.wait("scalar", "V", vB)
            P.emit("scalar", "sqrt", s_inv[:, t:t + 1], s_inv[:, t:t + 1])
            sC = P.mark("scalar", "S")
            P.wait("vector", "S", sC)
            P.emit("vector", "tensor_scalar", s_nmi[:, t:t + 1],
                   s_mean[:, t:t + 1], s_inv[:, t:t + 1], -1.0,
                   ALU.mult, ALU.mult)
            vD = P.mark("vector", "V")
            P.wait("scalar", "V", vD)
            P.emit("scalar", "activation", ht, xt, AF.Identity,
                   bias=s_nmi[:, t:t + 1], scale=s_inv[:, t:t + 1])
            v_aff = P.mark("scalar", "S")
            if t + 2 < NT:
                sA2[t + 2] = ln2_A(t + 2)
            P.wait("vector", "S", v_aff)
            P.wait("vector", "S", v_cbsb)
            P.emit("vector", "tensor_tensor", ht, ht, bc0, ALU.mult)
            P.emit("vector", "tensor_tensor", hb, ht, bc1, ALU.add)
            v_h2 = P.mark("vector", "V")
            v_h2_done[t] = v_h2
            P.wait("tensor", "V", v_h2)
            for grp in range(2):
                pDl.twait()
                for j in range(8):
                    kt2 = grp * 8 + j
                    P.emit("tensor", "transpose", pT[:, 128 * j:128 * (j + 1)],
                           hb[:, 128 * kt2:128 * (kt2 + 1)], id_sb)
                vt = P.mark("tensor", "T")
                P.wait("vector", "T", vt)
                for j in range(8):
                    kt2 = grp * 8 + j
                    P.emit("vector", "tensor_copy", h2T(kt2, 128 * t, 128),
                           pT[:, 128 * j:128 * (j + 1)])
                vv = P.mark("vector", "V")
                pDl.release("V", vv)
            v_h2T[t] = vv
            pCl.twait()
            P.wait("tensor", "Lgate", d_gate)
            P.wait("tensor", "V", vv)
            for kt2 in range(KT):
                P.emit("tensor", "matmul", pC_t[:, 0:32],
                       h2T(kt2, 128 * t, 128), gate_bf[:, 32 * kt2:32 * (kt2 + 1)],
                       start=(kt2 == 0), stop=(kt2 == KT - 1))
            v_lg = P.mark("tensor", "T")
            P.wait("scalar", "T", v_lg)
            P.emit("scalar", "copy", LGb[:, 32 * t:32 * (t + 1)], pC_t[:, 0:32])
            v_lgc = P.mark("scalar", "S")
            pCl.release("S", v_lgc)
        v_lg_all = v_lgc

        for j in range(4):
            v_rgb = bcast512(bc1[:, 512 * j:512 * (j + 1)], 20 + j,
                             row_math=rgmath)
        P.wait("vector", "S", v_rgb)
        v_accr = {}
        for tt in range(NT):
            P.emit("vector", "tensor_tensor", accv(tt, 0, 2048),
                   accv(tt, 0, 2048), bc1, ALU.mult)
            v_accr[tt] = P.mark("vector", "V")

        def seg(ap):
            return ap.rearrange("p (a b) -> p a b", b=32)

        def col8(ap):
            return ap.rearrange("p (a b) -> p a b", b=1)

        def bc8(ap):
            return col8(ap).broadcast_to([128, 8, 32])

        def segred(dst8, src256, op):
            P.emit("vector", "tensor_reduce", col8(dst8), seg(src256),
                   axis=AX.X, op=op)

        MKS3 = MKS.rearrange("p (a b) -> p a b", b=8)
        P.wait("vector", "S", v_lg_all)
        segred(R8a, LGb, ALU.max)
        P.emit("vector", "tensor_tensor", seg(RPb), seg(LGb), bc8(R8a),
               ALU.subtract)
        v_sm = P.mark("vector", "V")
        P.wait("scalar", "V", v_sm)
        P.emit("scalar", "activation", RPb, RPb, AF.Exp)
        v_exp = P.mark("scalar", "S")
        P.wait("vector", "S", v_exp)
        segred(R8a, RPb, ALU.add)
        P.emit("vector", "reciprocal", R8a, R8a)
        P.emit("vector", "tensor_tensor", seg(RPb), seg(RPb), bc8(R8a),
               ALU.mult)
        P.emit("vector", "tensor_copy", RCb, RPb)
        for k in range(8):
            P.emit("vector", "tensor_reduce", MKS3[:, :, k:k + 1], seg(RCb),
                   axis=AX.X, op=ALU.max)
            P.emit("vector", "tensor_tensor", seg(RSb), seg(RCb),
                   MKS3[:, :, k:k + 1].broadcast_to([128, 8, 32]), ALU.is_equal)
            P.emit("vector", "scalar_tensor_tensor", RCb, RSb, -1e30, RCb,
                   ALU.mult, ALU.add)
        for k in range(8):
            P.emit("vector", "tensor_tensor", seg(RSb), seg(RPb),
                   MKS3[:, :, k:k + 1].broadcast_to([128, 8, 32]), ALU.is_ge)
            P.emit("vector", "tensor_reduce",
                   CNTS.rearrange("p (a b) -> p a b", b=8)[:, :, k:k + 1],
                   seg(RSb), axis=AX.X, op=ALU.add)
        P.emit("vector", "tensor_scalar", CNTS, CNTS, 8.0, None, ALU.is_ge)
        P.emit("vector", "tensor_tensor", CNTS, CNTS, MKS, ALU.mult)
        P.emit("vector", "tensor_reduce", col8(R8b),
               CNTS.rearrange("p (a b) -> p a b", b=8), axis=AX.X, op=ALU.max)
        P.emit("vector", "tensor_tensor", seg(RCb), seg(RPb), bc8(R8b),
               ALU.is_ge)
        P.emit("vector", "tensor_tensor", seg(RSb), seg(RPb), seg(RCb),
               ALU.mult)
        segred(R8b, RSb, ALU.add)
        P.emit("vector", "reciprocal", R8b, R8b)
        P.wait("vector", "S", v_alb)
        P.emit("vector", "tensor_scalar", R8b, R8b, albB, None, ALU.mult)
        P.emit("vector", "tensor_tensor", seg(RSb), seg(RSb), bc8(R8b),
               ALU.mult)
        P.emit("vector", "tensor_scalar", RSb, RSb, TAU, None, ALU.is_ge)
        segred(R8b, RSb, ALU.max)
        P.emit("vector", "tensor_scalar", R8b, R8b, 0.0, None, ALU.is_equal)
        segred(R8c, RPb, ALU.max)
        P.emit("vector", "tensor_tensor", seg(RCb), seg(RPb), bc8(R8c),
               ALU.is_equal)
        P.emit("vector", "tensor_tensor", seg(RCb), seg(RCb), bc8(R8b),
               ALU.mult)
        P.emit("vector", "tensor_tensor", RSb, RSb, RCb, ALU.max)
        P.emit("vector", "tensor_tensor", RSb, RPb, RSb, ALU.mult)
        segred(R8c, RSb, ALU.add)
        P.emit("vector", "tensor_scalar", R8c, R8c, 1e-12, None, ALU.max)
        P.emit("vector", "reciprocal", R8c, R8c)
        P.emit("vector", "tensor_tensor", seg(comb[:, 0:256]), seg(RSb),
               bc8(R8c), ALU.mult)
        v_comb_all = P.mark("vector", "V")

        gu_sp = [B(O_F1 + 4096 * s, 4096) for s in range(2)]
        heT = B(O_F1 + 8192, 6144)
        sgb = [B(O_F1 + 14336, 512), B(O_F1 + 14848, 512)]
        dn_s = [B(O_VB + 3072 * s, 3072) for s in range(4)]

        gu_used = {}
        dn_used = {}
        d_gu = {}
        d_dn = {}

        def prefetch(e, pr):
            idx = e * 6 + pr
            P.wait("gpsimd", "T", gu_used.get(idx - 2, 0))
            if idx == 0:
                P.wait("gpsimd", "T", v_wo_used[2])
                P.wait("gpsimd", "V", v_h2T[NT - 1])
            d_gu[idx] = P.dma("gpsimd",
                              gu_sp[idx % 2].rearrange("p (b a m) -> p b a m", b=2, m=128),
                              gup[e, pr].rearrange("b (a p) m -> p b a m", p=128),
                              sem="Lgu%d" % (idx % 2))

        def prefetch_dn(e, nch):
            idx = e * 4 + nch
            P.wait("gpsimd", "T", dn_used.get(idx - 4, 0))
            if idx == 0:
                P.wait("gpsimd", "T", v_wo_used[0])
                P.wait("gpsimd", "T", v_wo_used[1])
            d_dn[idx] = P.dma("gpsimd",
                              dn_s[idx % 4].rearrange("p (a m) -> p a m", m=512),
                              dnp[e, nch].rearrange("(a p) m -> p a m", p=128),
                              sem="Ldn%d" % (idx % 4))

        for pr in range(2):
            prefetch(0, pr)
        for nch in range(4):
            prefetch_dn(0, nch)

        v_last_acc = None
        pe_cycle = [(pAl, pAh), (pBl, pBh), (pCl, pCh), (pDl, pDh)]
        v_he = {}
        sg_busy = {}
        for e in range(NEXP):
            pe_i = 0
            for pr in range(6):
                idx = e * 6 + pr
                sl = gu_sp[idx % 2]
                P.wait("tensor", "Lgu%d" % (idx % 2), d_gu[idx])
                if e == 0 and pr == 0:
                    P.wait("tensor", "V", v_h2T[NT - 1])
                for tch in range(2):
                    pg, pu = pe_cycle[pe_i % 4]
                    pe_i += 1
                    pg.twait()
                    pu.twait()
                    P.wait("tensor", "V", v_he.get((e - 1, pr, tch), 0))
                    for kt2 in range(KT):
                        P.emit("tensor", "matmul", pg.ap,
                               sl[:, 128 * kt2:128 * (kt2 + 1)],
                               h2T(kt2, 512 * tch, 512),
                               start=(kt2 == 0), stop=(kt2 == KT - 1))
                    for kt2 in range(KT):
                        P.emit("tensor", "matmul", pu.ap,
                               sl[:, 2048 + 128 * kt2:2048 + 128 * (kt2 + 1)],
                               h2T(kt2, 512 * tch, 512),
                               start=(kt2 == 0), stop=(kt2 == KT - 1))
                    vmm = P.mark("tensor", "T")
                    if tch == 1:
                        gu_used[idx] = vmm
                        if pr + 2 < 6:
                            prefetch(e, pr + 2)
                        elif e + 1 < NEXP:
                            prefetch(e + 1, pr + 2 - 6)
                    P.wait("scalar", "T", vmm)
                    sgi = pe_i % 2
                    sg = sgb[sgi]
                    P.wait("scalar", "V", sg_busy.get(sgi, 0))
                    P.emit("scalar", "activation", sg, pg.ap, AF.Silu)
                    vsg = P.mark("scalar", "S")
                    P.wait("vector", "S", vsg)
                    P.emit("vector", "tensor_tensor",
                           heT[:, 1024 * pr + 512 * tch:1024 * pr + 512 * (tch + 1)],
                           sg, pu.ap, ALU.mult)
                    vhe = P.mark("vector", "V")
                    sg_busy[sgi] = vhe
                    v_he[(e, pr, tch)] = vhe
                    pg.release("V", vhe)
                    pu.release("V", vhe)
            dn_cycle = [pAl, pAh, pBl, pBh]
            pe_i = 0
            for tt in range(NT):
                for nch in range(4):
                    idx = e * 4 + nch
                    pres = dn_cycle[pe_i % 4]
                    pe_i += 1
                    pres.twait()
                    P.wait("tensor", "Ldn%d" % (idx % 4), d_dn[idx])
                    if tt == 0:
                        for pr in range(6):
                            P.wait("tensor", "V", v_he[(e, pr, 0)])
                            P.wait("tensor", "V", v_he[(e, pr, 1)])
                    for kt2 in range(6):
                        P.emit("tensor", "matmul", pres.ap,
                               heT[:, 1024 * kt2 + 128 * tt:1024 * kt2 + 128 * (tt + 1)],
                               dn_s[idx % 4][:, 512 * kt2:512 * (kt2 + 1)],
                               start=(kt2 == 0), stop=(kt2 == 5))
                    vmm = P.mark("tensor", "T")
                    if tt == NT - 1:
                        dn_used[idx] = vmm
                        if e + 1 < NEXP:
                            prefetch_dn(e + 1, nch)
                    P.wait("vector", "T", vmm)
                    if e == 0:
                        P.wait("vector", "V", v_accr[tt])
                        P.wait("vector", "V", v_comb_all)
                    P.emit("vector", "scalar_tensor_tensor",
                           accv(tt, 512 * nch, 512), pres.ap,
                           comb[:, 32 * tt + e:32 * tt + e + 1],
                           accv(tt, 512 * nch, 512), ALU.mult, ALU.add)
                    va = P.mark("vector", "V")
                    pres.release("V", va)
            v_last_acc = va

        for j in range(4):
            v_gbB = bcast512(bc0[:, 512 * j:512 * (j + 1)], 20 + j,
                             row_math=gamath)
        P.wait("vector", "S", v_gbB)
        P.wait("vector", "V", v_last_acc)
        d_y = {}
        for tt in range(NT):
            P.emit("vector", "tensor_tensor", accv(tt, 0, 2048),
                   accv(tt, 0, 2048), bc0, ALU.mult)
            vo = P.mark("vector", "V")
            P.wait("sync", "V", vo)
            d_y[tt] = P.dma("sync", y[128 * tt:128 * (tt + 1), :],
                            accv(tt, 0, 2048), sem="Ly")
        P.wait("sync", "Ly", d_y[NT - 1])

        @block.tensor
        def _(e):
            for f in P.ops["tensor"]:
                f(e)

        @block.vector
        def _(e):
            for f in P.ops["vector"]:
                f(e)

        @block.scalar
        def _(e):
            for f in P.ops["scalar"]:
                f(e)

        @block.gpsimd
        def _(e):
            for f in P.ops["gpsimd"]:
                f(e)

        @block.sync
        def _(e):
            for f in P.ops["sync"]:
                f(e)

    return nc


_cache = {}


def _prep(inputs):
    f32 = np.float32
    x = np.ascontiguousarray(np.asarray(inputs["L0_attn_x"], f32).reshape(T, D))
    wq = np.asarray(inputs["L0_attn_Wq"], f32)
    wk = np.asarray(inputs["L0_attn_Wk"], f32)
    wv = np.asarray(inputs["L0_attn_Wv"], f32)
    wg = np.asarray(inputs["L0_attn_Wg"], f32)
    wo = np.asarray(inputs["L0_attn_Wo"], f32)
    gnw = np.asarray(inputs["L0_attn_gnorm_w"], f32).reshape(1, 512)
    gwt = np.asarray(inputs["L0_ffn_gate_weight"], f32)
    gu = np.asarray(inputs["L0_ffn_gate_up_proj"], f32)
    dn = np.asarray(inputs["L0_ffn_down_proj"], f32)
    alpha = np.asarray(inputs["L0_ffn_layer_alpha"], f32).reshape(1, 1)
    cv = np.asarray(inputs["L0_c"], f32).reshape(-1)
    wmod = np.asarray(inputs["L0_wmod"], f32)

    wq_m = np.ascontiguousarray(wq.T.reshape(D, 8, 128).transpose(1, 0, 2))
    wk_m = np.ascontiguousarray(wk.T.reshape(D, 8, 128).transpose(1, 0, 2))
    wv_n = np.ascontiguousarray(wv.T.reshape(D, 4, 512).transpose(1, 0, 2))
    wg_n = np.ascontiguousarray(wg.T.reshape(D, 4, 512).transpose(1, 0, 2))
    wo_g = np.ascontiguousarray(wo.T.reshape(4, 512, D))
    gu_t = gu.transpose(0, 2, 1).reshape(NEXP, D, 12, 128).transpose(0, 2, 1, 3)
    gup = np.ascontiguousarray(
        gu_t.reshape(NEXP, 2, 6, D, 128).transpose(0, 2, 1, 3, 4))
    dnp = np.ascontiguousarray(
        dn.transpose(0, 2, 1).reshape(NEXP, INTER, 4, 512).transpose(0, 2, 1, 3))
    gatet = np.ascontiguousarray(gwt.T)
    c_cols = np.ascontiguousarray(cv.reshape(16, 128).T)

    tglob = np.arange(T, dtype=f32)
    invf = (1.0 / (10000.0 ** (np.arange(0, DK, 2, dtype=f32) / DK))).astype(f32)
    freqs = (tglob[:, None] * invf[None, :]).astype(f32)
    cosb = np.cos(freqs).astype(f32).T
    sinb = np.sin(freqs).astype(f32).T
    ip = np.arange(T, dtype=np.float64) % 128
    tabs_all = np.zeros((NCORE, 2, H, 2, 128, TLOC), f32)
    for h in range(H):
        qs = (np.float64(GAMMA[h]) ** (ip + 1) * DK ** -0.5)
        ks = (np.float64(GAMMA[h]) ** (-ip - 1))
        for c in range(NCORE):
            sl = slice(TLOC * c, TLOC * (c + 1))
            tabs_all[c, 0, h, 0] = cosb[:, sl] * qs[None, sl]
            tabs_all[c, 0, h, 1] = sinb[:, sl] * qs[None, sl]
            tabs_all[c, 1, h, 0] = cosb[:, sl] * ks[None, sl]
            tabs_all[c, 1, h, 1] = sinb[:, sl] * ks[None, sl]

    identity = np.eye(128, dtype=BFNP)
    ii = np.arange(128)
    causal = (ii[None, :] >= ii[:, None]).astype(BFNP)

    sinw_all = np.zeros((NCORE, 128, 32), f32)
    for c in range(NCORE):
        for h in range(H):
            for j in range(NCORE):
                if j < c:
                    sinw_all[c, :, 8 * h + j] = np.float64(GAMMA[h]) ** (1024.0 * (c - 1 - j))

    per_core = []
    for c in range(NCORE):
        m = dict(
            x_t=np.ascontiguousarray(x[TLOC * c:TLOC * (c + 1)]),
            wq_m=wq_m, wk_m=wk_m, wv_n=wv_n, wg_n=wg_n, wo_g=wo_g,
            gup=gup, dnp=dnp, gatet=gatet,
            wmod_s=np.ascontiguousarray(
                wmod[:, 1536 * c:1536 * (c + 1)].reshape(KT, 128, 1536)),
            c_cols=c_cols, gnorm=gnw, alpha2=alpha,
            tabs=np.ascontiguousarray(tabs_all[c]),
            ident=identity, causalT=causal,
            sinw=np.ascontiguousarray(sinw_all[c]),
        )
        per_core.append(m)
    return per_core


def kernel(**inputs):
    per_core = _prep(inputs)
    if "nc" not in _cache:
        _cache["nc"] = build()
    nc = _cache["nc"]
    res = run_bass_kernel_spmd(nc, per_core, core_ids=list(range(NCORE)))
    out = np.concatenate([np.asarray(r["y"]) for r in res.results], axis=0)
    return out.reshape(1, T, D).astype(np.float32)




# version 07
# speedup vs baseline: 1.2449x; 1.0218x over previous
import contextlib
import numpy as np
import ml_dtypes
import concourse.bass as bass
import concourse.mybir as mybir
from concourse.bass_utils import run_bass_kernel_spmd

F32 = mybir.dt.float32
BF16 = mybir.dt.bfloat16
AF = mybir.ActivationFunctionType
ALU = mybir.AluOpType
AX = mybir.AxisListType
BFNP = ml_dtypes.bfloat16

NCORE = 8
T = 8192
TLOC = 1024
D = 2048
H = 4
DK = 256
DV = 512
KT = 16
NT = 8
NEXP = 32
INTER = 768
GAMMA = [float(1.0 - 2.0 ** (-5 - h)) for h in range(H)]
G128 = [float(np.float64(g) ** 128) for g in GAMMA]
EPS = 1e-5
TAU = 0.05

O_HT = 0
O_GP = 16384
O_QKOB = 32768
O_VB = 49152
O_F1 = 65536
O_F2 = 81920
O_M = 90112
O_BC0 = 94208
O_BC1 = 98304
O_ST = 102400
O_KTK = 104448
ACOLS = 105472


class Prog:
    ENGS = ("tensor", "vector", "scalar", "gpsimd", "sync")

    def __init__(self, sems):
        self.ops = {k: [] for k in self.ENGS}
        self.sems = sems
        self.ctr = {k: 0 for k in sems}
        self.last = {k: None for k in self.ENGS}
        self.last_kind = {k: None for k in self.ENGS}
        self.last_mark = {k: None for k in self.ENGS}

    def emit(self, eng, meth, *args, inc=None, incn=1, **kw):
        semh = self.sems[inc] if inc else None
        last = self.last
        dr = eng in ("vector", "scalar") and meth != "dma_start"

        def f(e, meth=meth, args=args, kw=kw, semh=semh, incn=incn, eng=eng, dr=dr):
            if dr:
                e.drain()
            ins = getattr(e, meth)(*args, **kw)
            last[eng] = ins
            if semh is not None:
                ins.then_inc(semh, incn)
        self.ops[eng].append(f)
        self.last_kind[eng] = "emit" if inc is None else "emit_inc"
        self.last_mark[eng] = inc
        if inc:
            self.ctr[inc] += incn
            return self.ctr[inc]
        return None

    def wait(self, eng, sem, v):
        if not v or v <= 0:
            return
        semh = self.sems[sem]

        def f(e, semh=semh, v=v):
            e.wait_ge(semh, v)
        self.ops[eng].append(f)

    def mark(self, eng, sem):
        if self.last_kind[eng] in ("mark", "emit_inc") and \
                self.last_mark[eng] == sem:
            return self.ctr[sem]
        assert self.last_kind[eng] in ("emit",), \
            f"mark({eng},{sem}) after {self.last_kind[eng]}"
        semh = self.sems[sem]
        last = self.last

        def f(e, semh=semh, eng=eng):
            last[eng].then_inc(semh, 1)
        self.ops[eng].append(f)
        self.last_kind[eng] = "mark"
        self.last_mark[eng] = sem
        self.ctr[sem] += 1
        return self.ctr[sem]

    def dma(self, queue, out, in_, sem=None):
        assert sem is not None, "dma needs an explicit slot semaphore"
        return self.emit(queue, "dma_start", inc=sem, incn=16, out=out, in_=in_)


class Psum:

    def __init__(self, P, ap):
        self.P, self.ap = P, ap
        self.free_at = None

    def twait(self):
        if self.free_at is not None:
            self.P.wait("tensor", *self.free_at)
            self.free_at = None

    def release(self, sem, val):
        self.free_at = (sem, val)


def build(dbg=()):
    nc = bass.Bass("TRN2", target_bir_lowering=False)

    def di(name, shape, dtype=F32):
        return nc.dram_tensor(name, shape, dtype, kind="ExternalInput").ap()

    x_t = di("x_t", [TLOC, D])
    wq_m = di("wq_m", [8, D, 128])
    wk_m = di("wk_m", [8, D, 128])
    wv_n = di("wv_n", [4, D, 512])
    wg_n = di("wg_n", [4, D, 512])
    wo_g = di("wo_g", [4, 512, D])
    gup = di("gup", [NEXP, 6, 2, D, 128])
    dnp = di("dnp", [NEXP, 4, INTER, 512])
    gatet = di("gatet", [D, 32])
    wmod_s = di("wmod_s", [KT, 128, 1536])
    c_cols = di("c_cols", [128, 16])
    gnorm = di("gnorm", [1, 512])
    alpha2 = di("alpha2", [1, 1])
    tabs = di("tabs", [2, H, 2, 128, TLOC])
    ident = di("ident", [128, 128], BF16)
    causalT = di("causalT", [128, 128], BF16)
    sinw = di("sinw", [128, 32])

    y = nc.dram_tensor("y", [TLOC, D], F32, kind="ExternalOutput").ap()
    dbg_out = {}
    for name, shape, dt in dbg:
        dbg_out[name] = nc.dram_tensor("dbg_" + name, list(shape), dt,
                                       kind="ExternalOutput").ap()

    modg_i = nc.dram_tensor("modg_i", [1, 1536], F32).ap()
    modg_o = nc.dram_tensor("modg_o", [NCORE, 1536], F32).ap()
    lg_i = nc.dram_tensor("lg_i", [128, 4096], BF16).ap()
    lg_o = nc.dram_tensor("lg_o", [NCORE * 128, 4096], BF16).ap()

    with contextlib.ExitStack() as ctx:
        en = ctx.enter_context
        AR = en(nc.sbuf_tensor("AR", [128, ACOLS], BF16))
        pA_t = en(nc.psum_tensor("pA", [128, 1024], F32))
        pB_t = en(nc.psum_tensor("pB", [128, 1024], F32))
        pC_t = en(nc.psum_tensor("pC", [128, 1024], F32))
        pD_t = en(nc.psum_tensor("pD", [128, 1024], F32))
        semnames = ["T", "V", "S", "C", "Lc", "Lx0", "Lx1", "Ltab",
                    "Lm0", "Lm1", "Lm2", "Lm3", "Lvg0", "Lvg1", "Ltab0", "Ltab1",
                    "Lwm0", "Lwm1", "Lgu0", "Lgu1",
                    "Ldn0", "Ldn1", "Ldn2", "Ldn3", "Lwo", "Lxa", "Ly",
                    "Llr0", "Llr1", "Lch0", "Lch1", "Lmi", "Lli", "Lgate"]
        semh = {s: en(nc.semaphore("sem" + s)) for s in semnames}
        block = en(nc.Block())
        P = Prog(semh)

        def B(o, n, p0=0, p1=128):
            return AR[p0:p1, o:o + n]

        def Fv(o, n, p0=0, p1=128):
            return AR[p0:p1, o:o + 2 * n].bitcast(F32)

        pAl, pAh = Psum(P, pA_t[:, 0:512]), Psum(P, pA_t[:, 512:1024])
        pBl, pBh = Psum(P, pB_t[:, 0:512]), Psum(P, pB_t[:, 512:1024])
        pCl, pCh = Psum(P, pC_t[:, 0:512]), Psum(P, pC_t[:, 512:1024])
        pDl, pDh = Psum(P, pD_t[:, 0:512]), Psum(P, pD_t[:, 512:1024])
        pT = pD_t[:, 0:512].bitcast(BF16)

        st_cur = [O_ST]

        def st_alloc(nbf):
            o = st_cur[0]
            st_cur[0] = (o + nbf + 15) // 16 * 16
            assert st_cur[0] <= O_ST + 2048, f"ST overflow {st_cur[0]}"
            return o

        ones = Fv(st_alloc(256), 128)
        csb = Fv(st_alloc(32), 16)
        silc = Fv(st_alloc(32), 16)
        id_sb = B(st_alloc(128), 128)
        cau_sb = B(st_alloc(128), 128)
        sinw_sb = Fv(st_alloc(64), 32)
        albB = Fv(st_alloc(16), 1)
        s_mean = Fv(st_alloc(16), 8)
        s_inv = Fv(st_alloc(16), 8)
        s_nmi = Fv(st_alloc(16), 8)
        s_ssq = Fv(st_alloc(16), 8)
        s_oiv = Fv(st_alloc(16), 8)
        s_t1 = Fv(st_alloc(16), 8)
        comb = Fv(st_alloc(512), 256)
        r_p = Fv(st_alloc(64), 32)
        r_cur = Fv(st_alloc(64), 32)
        r_tmp = Fv(st_alloc(64), 32)
        r_mk = Fv(st_alloc(16), 8)
        r_cnt = Fv(st_alloc(16), 8)
        r_a1 = Fv(st_alloc(64), 32)
        attn_sb = B(st_alloc(128), 128)
        attn_sbB = B(st_alloc(128), 128)
        attn_sb4 = [B(O_HT + 8192 + 128 * i, 128) for i in range(8)]

        chrow = [Fv(O_KTK, 512, 0, 1), Fv(O_M, 512, 0, 1)]
        rtmprow = Fv(O_M + 1024, 512, 0, 1)

        def hT(kt, c0, n):
            return B(O_HT + kt * 1024 + c0, n)

        hT3 = B(O_HT, 16384).rearrange("p (a b) -> p a b", b=1024)
        h2T3 = B(O_QKOB, 16384).rearrange("p (a b) -> p a b", b=1024)

        def h2T(kt, c0, n):
            return B(O_QKOB + kt * 1024 + c0, n)

        def qseg(p, h, half):
            return B(O_QKOB + p * 2048 + h * 512 + half * 128, 128)

        def kseg(p, h, half):
            return B(O_QKOB + p * 2048 + h * 512 + 256 + half * 128, 128)

        def obv(p, c0, n):
            return B(O_QKOB + p * 2048 + c0, n)

        def vtile(p, c0, n):
            return B(O_VB + p * 2048 + c0, n)

        def gtile(p, c0, n):
            return B(O_GP + p * 2048 + c0, n)

        def accv(tt, c0, n):
            return Fv(4096 * tt + 2 * c0, n)

        def Mv(dt8, n=512):
            return B(O_M + dt8 * 512, n)

        ktok = B(O_KTK, 1024)
        ogT = B(O_M, 2048)

        P.emit("vector", "memset", ones, 1.0)
        v_ones = P.mark("vector", "V")

        P.dma("sync", csb, c_cols, sem="Lc")
        P.dma("sync", id_sb, ident, sem="Lc")
        P.dma("sync", cau_sb, causalT, sem="Lc")
        d_const = P.dma("sync", sinw_sb, sinw, sem="Lc")

        wslot = [Fv(O_VB, 1536), Fv(O_VB + 3072, 1536)]
        modloc = Fv(O_VB + 6144, 1536, 0, 1)

        P.wait("scalar", "Lc", d_const)
        v_silc = P.emit("scalar", "activation", silc, csb, AF.Silu, inc="S")

        mps = [pA_t[0:1, 0:512], pB_t[0:1, 0:512], pC_t[0:1, 0:512]]
        P.wait("tensor", "S", v_silc)
        t_after_k = {}
        for k in range(KT):
            q = "gpsimd" if k % 2 == 0 else "sync"
            if k >= 2:
                P.wait(q, "T", t_after_k[k - 2])
            d = P.dma(q, wslot[k % 2], wmod_s[k], sem="Lwm%d" % (k % 2))
            P.wait("tensor", "Lwm%d" % (k % 2), d)
            for nch in range(3):
                P.emit("tensor", "matmul", mps[nch], silc[:, k:k + 1],
                       wslot[k % 2][:, 512 * nch:512 * (nch + 1)],
                       start=(k == 0), stop=(k == KT - 1))
            t_after_k[k] = P.mark("tensor", "T")
        v_modmm = t_after_k[KT - 1]

        P.wait("scalar", "T", v_modmm)
        for nch in range(3):
            P.emit("scalar", "copy", modloc[:, 512 * nch:512 * (nch + 1)], mps[nch])
        v_modev = P.mark("scalar", "S")
        pAl.release("S", v_modev)
        pBl.release("S", v_modev)
        pCl.release("S", v_modev)

        P.wait("gpsimd", "S", v_modev)
        d_mi = P.dma("gpsimd", modg_i[:, :], modloc, sem="Lmi")
        P.wait("gpsimd", "Lmi", d_mi)
        v_cc1 = P.emit("gpsimd", "collective_compute", "AllGather", ALU.bypass,
                       replica_groups=[list(range(NCORE))],
                       ins=[modg_i[:, :]], outs=[modg_o[:, :]], inc="C")

        modflat = modg_o.rearrange("a b -> (a b)")
        bc_state = {"i": 0, "last_mm": {0: 0, 1: 0}}

        def bcast512(dst, chunk_idx, add1=False, row_math=None):
            i = bc_state["i"] % 2
            bc_state["i"] += 1
            row = chrow[i]
            P.wait("sync", "C", v_cc1)
            P.wait("sync", "T", bc_state["last_mm"][i])
            d = P.dma("sync", row,
                      modflat[512 * chunk_idx:512 * (chunk_idx + 1)].unsqueeze(0),
                      sem="Lch%d" % i)
            sem_mm, val_mm = "Lch%d" % i, d
            if row_math is not None:
                P.wait("vector", sem_mm, d)
                row_math(row)
                val_mm = P.mark("vector", "V")
                sem_mm = "V"
            P.wait("tensor", sem_mm, val_mm)
            P.wait("tensor", "V", v_ones)
            pBh.twait()
            P.emit("tensor", "matmul", pBh.ap, ones[0:1, :], row,
                   start=True, stop=True)
            mmv = P.mark("tensor", "T")
            bc_state["last_mm"][i] = mmv
            P.wait("scalar", "T", mmv)
            if add1:
                P.emit("scalar", "add", dst, pBh.ap, 1.0)
            else:
                P.emit("scalar", "copy", dst, pBh.ap)
            ev = P.mark("scalar", "S")
            pBh.release("S", ev)
            return ev

        def gamath(row):
            P.emit("vector", "tensor_scalar", rtmprow, row, 0.0, None, ALU.is_ge)
            P.emit("vector", "tensor_scalar", rtmprow, rtmprow, 2.0, -1.0,
                   ALU.mult, ALU.add)
            P.emit("vector", "tensor_tensor", row, row, rtmprow, ALU.mult)
            P.emit("vector", "tensor_scalar", row, row, 0.25, None, ALU.max)
            P.emit("vector", "tensor_tensor", row, row, rtmprow, ALU.mult)

        def rgmath(row):
            gamath(row)
            P.emit("vector", "reciprocal", row, row)

        bc0 = Fv(O_BC0, 2048)
        bc1 = Fv(O_BC1, 2048)

        xtile = [Fv(O_F2, 2048), Fv(O_F2 + 4096, 2048)]
        d_x = {0: P.dma("sync", xtile[0], x_t[0:128, :], sem="Lx0"),
               1: P.dma("sync", xtile[1], x_t[128:256, :], sem="Lx1")}

        d_al = P.dma("sync", chrow[0][:, 0:1], alpha2, sem="Lch0")
        P.wait("tensor", "Lch0", d_al)
        P.wait("tensor", "V", v_ones)
        pBh.twait()
        P.emit("tensor", "matmul", pB_t[:, 512:513], ones[0:1, :],
               chrow[0][:, 0:1], start=True, stop=True)
        v_mm = P.mark("tensor", "T")
        bc_state["last_mm"][0] = v_mm
        P.wait("scalar", "T", v_mm)
        P.emit("scalar", "copy", albB, pB_t[:, 512:513])
        v_alb = P.mark("scalar", "S")
        pBh.release("S", v_alb)

        for j in range(4):
            bcast512(bc0[:, 512 * j:512 * (j + 1)], 4 + j, add1=True)
        for j in range(4):
            v_casa = bcast512(bc1[:, 512 * j:512 * (j + 1)], j)

        htmp = [Fv(O_F1, 2048), Fv(O_F1 + 4096, 2048)]
        hbuf = [B(O_F1 + 8192, 2048), B(O_F1 + 10240, 2048)]
        sq_scr = Fv(O_F1 + 12288, 2048)
        x_free_at = {}
        v_h = {}
        tpos_res = {0: pDl, 1: pCl}
        tpos_bf = {0: pT, 1: pC_t[:, 0:512].bitcast(BF16)}

        v_h = {}
        x_free_at = {}
        vB_done = {}
        sC_done = {}

        def p1_A(t):
            xt = xtile[t % 2]
            xsem = "Lx%d" % (t % 2)
            P.wait("vector", xsem, d_x[t])
            P.emit("vector", "reduce_sum", s_mean[:, t:t + 1], xt, axis=AX.X)
            P.wait("scalar", xsem, d_x[t])
            P.emit("scalar", "activation", sq_scr, xt, AF.Square,
                   accum_out=s_ssq[:, t:t + 1])
            return P.mark("scalar", "S")

        sA_done = {0: None}
        sA_done[0] = p1_A(0)
        sA_done[1] = p1_A(1)
        for t in range(NT):
            xt, ht, hb = xtile[t % 2], htmp[t % 2], hbuf[t % 2]
            P.wait("vector", "S", sA_done[t])
            P.emit("vector", "tensor_scalar", s_mean[:, t:t + 1],
                   s_mean[:, t:t + 1], 1.0 / D, None, ALU.mult)
            P.emit("vector", "tensor_tensor", s_t1[:, t:t + 1],
                   s_mean[:, t:t + 1], s_mean[:, t:t + 1], ALU.mult)
            P.emit("vector", "tensor_scalar", s_inv[:, t:t + 1],
                   s_ssq[:, t:t + 1], 1.0 / D, None, ALU.mult)
            P.emit("vector", "tensor_tensor", s_inv[:, t:t + 1],
                   s_inv[:, t:t + 1], s_t1[:, t:t + 1], ALU.subtract)
            P.emit("vector", "tensor_scalar", s_inv[:, t:t + 1],
                   s_inv[:, t:t + 1], EPS, None, ALU.add)
            P.emit("vector", "reciprocal", s_inv[:, t:t + 1], s_inv[:, t:t + 1])
            vB = P.mark("vector", "V")
            P.wait("scalar", "V", vB)
            P.emit("scalar", "sqrt", s_inv[:, t:t + 1], s_inv[:, t:t + 1])
            sC = P.mark("scalar", "S")
            P.wait("vector", "S", sC)
            P.emit("vector", "tensor_scalar", s_nmi[:, t:t + 1],
                   s_mean[:, t:t + 1], s_inv[:, t:t + 1], -1.0,
                   ALU.mult, ALU.mult)
            vD = P.mark("vector", "V")
            P.wait("scalar", "V", vD)
            P.emit("scalar", "activation", ht, xt, AF.Identity,
                   bias=s_nmi[:, t:t + 1], scale=s_inv[:, t:t + 1])
            x_free_at[t] = P.mark("scalar", "S")
            if t + 2 < NT:
                P.wait("sync", "S", x_free_at[t])
                d_x[t + 2] = P.dma("sync", xtile[t % 2],
                                   x_t[128 * (t + 2):128 * (t + 3), :],
                                   sem="Lx%d" % (t % 2))
            if t + 2 < NT:
                sA_done[t + 2] = p1_A(t + 2)
            P.wait("vector", "S", x_free_at[t])
            P.wait("vector", "S", v_casa)
            P.emit("vector", "tensor_tensor", ht, ht, bc0, ALU.mult)
            P.emit("vector", "tensor_tensor", hb, ht, bc1, ALU.add)
            v_h[t] = P.mark("vector", "V")
            P.wait("tensor", "V", v_h[t])
            P.wait("tensor", "Lc", d_const)
            for grp in range(2):
                res = tpos_res[grp]
                pbf = tpos_bf[grp]
                res.twait()
                for j in range(8):
                    kt2 = grp * 8 + j
                    P.emit("tensor", "transpose", pbf[:, 128 * j:128 * (j + 1)],
                           hb[:, 128 * kt2:128 * (kt2 + 1)], id_sb)
                vt = P.mark("tensor", "T")
                P.wait("vector", "T", vt)
                P.emit("vector", "tensor_copy",
                       hT3[:, grp * 8:(grp + 1) * 8, 128 * t:128 * (t + 1)],
                       pbf.rearrange("p (a b) -> p a b", b=128))
                vv = P.mark("vector", "V")
                res.release("V", vv)
        v_hT_all = P.mark("vector", "V")

        mring = [B(O_F2 + 2048 * s, 2048) for s in range(4)]
        mm_free = {}
        rope_done = {}
        slot_i = 0
        v_rope_prev = 0
        d_tab_prev = 0

        tabc2 = [[Fv(O_F1, 1024), Fv(O_F1 + 2048, 1024)],
                 [Fv(O_F1 + 8192, 1024), Fv(O_F1 + 10240, 1024)]]
        rsc2 = [[Fv(O_F1 + 4096, 1024), Fv(O_F1 + 6144, 1024)],
                [Fv(O_F1 + 12288, 1024), Fv(O_F1 + 14336, 1024)]]
        hi = 0
        for side in range(2):
            wsrc = wq_m if side == 0 else wk_m
            for h in range(H):
                par = hi % 2
                tabc = tabc2[par]
                rsc = rsc2[par]
                q1t, q2t = (pA_t, pB_t) if par == 0 else (pC_t, pD_t)
                pun = [(pAl, pAh), (pBl, pBh)] if par == 0 else [(pCl, pCh), (pDl, pDh)]
                hi += 1
                P.wait("sync", "V", v_hT_all if hi <= 2 else rope_done.get(hi - 3, 0))
                d_cos = P.dma("sync", tabc[0], tabs[side, h, 0],
                              sem="Ltab%d" % par)
                d_sin = P.dma("sync", tabc[1], tabs[side, h, 1],
                              sem="Ltab%d" % par)
                vps = []
                for half in range(2):
                    m = 2 * h + half
                    s4 = slot_i % 4
                    P.wait("gpsimd", "T", mm_free.get(s4, 0))
                    if slot_i == 0:
                        P.wait("gpsimd", "S", x_free_at[NT - 1])
                    dsl = P.dma("gpsimd",
                                mring[s4].rearrange("p (a m) -> p a m", m=128),
                                wsrc[m].rearrange("(a p) m -> p a m", p=128),
                                sem="Lm%d" % s4)
                    ptgt = pun[half]
                    P.wait("tensor", "Lm%d" % s4, dsl)
                    P.wait("tensor", "V", v_hT_all)
                    for tch in range(2):
                        ptgt[tch].twait()
                        for kt2 in range(KT):
                            P.emit("tensor", "matmul", ptgt[tch].ap,
                                   mring[s4][:, 128 * kt2:128 * (kt2 + 1)],
                                   hT(kt2, 512 * tch, 512),
                                   start=(kt2 == 0), stop=(kt2 == KT - 1))
                    vp = P.mark("tensor", "T")
                    mm_free[s4] = vp
                    vps.append(vp)
                    slot_i += 1
                P.wait("vector", "T", vps[-1])
                P.wait("vector", "Ltab%d" % par, d_sin)
                q1 = q1t[:, 0:1024]
                q2 = q2t[:, 0:1024]
                segd = qseg if side == 0 else kseg
                P.emit("vector", "tensor_tensor", rsc[0], q1, tabc[0], ALU.mult)
                P.emit("vector", "tensor_tensor", rsc[1], q2, tabc[1], ALU.mult)
                for p in range(NT):
                    P.emit("vector", "tensor_tensor", segd(p, h, 0),
                           rsc[0][:, 128 * p:128 * (p + 1)],
                           rsc[1][:, 128 * p:128 * (p + 1)], ALU.subtract)
                P.emit("vector", "tensor_tensor", rsc[0], q2, tabc[0], ALU.mult)
                P.emit("vector", "tensor_tensor", rsc[1], q1, tabc[1], ALU.mult)
                for p in range(NT):
                    P.emit("vector", "tensor_tensor", segd(p, h, 1),
                           rsc[0][:, 128 * p:128 * (p + 1)],
                           rsc[1][:, 128 * p:128 * (p + 1)], ALU.add)
                v_rope_prev = P.mark("vector", "V")
                rope_done[hi - 1] = v_rope_prev
                for pp in pun[0] + pun[1]:
                    pp.release("V", v_rope_prev)
        v_rope_all = v_rope_prev

        vgslot = [B(O_F1 + 8192 * s, 8192) for s in range(2)]
        vg_free = {}
        vg_i = 0
        pcycle = [pAl, pAh, pBl, pBh]
        pc_i = 0
        v_g_ev = {}
        v_v_ev = {}
        P.wait("scalar", "Lmi", d_mi)

        def vg_proj(which):
            nonlocal vg_i, pc_i
            wsrc = wv_n if which == 0 else wg_n
            for nch in range(4):
                s2 = vg_i % 2
                if vg_i < 2:
                    P.wait("gpsimd", "V", v_rope_all)
                else:
                    P.wait("gpsimd", "T", vg_free[s2])
                dsl = P.dma("gpsimd",
                            vgslot[s2].rearrange("p (a m) -> p a m", m=512),
                            wsrc[nch].rearrange("(a p) m -> p a m", p=128),
                            sem="Lvg%d" % s2)
                P.wait("tensor", "Lvg%d" % s2, dsl)
                for tt in range(NT):
                    pres = pcycle[pc_i % 4]
                    pc_i += 1
                    pres.twait()
                    for kt2 in range(KT):
                        P.emit("tensor", "matmul", pres.ap,
                               hT(kt2, 128 * tt, 128),
                               vgslot[s2][:, 512 * kt2:512 * (kt2 + 1)],
                               start=(kt2 == 0), stop=(kt2 == KT - 1))
                    vmm = P.mark("tensor", "T")
                    if tt == NT - 1:
                        vg_free[s2] = vmm
                    P.wait("scalar", "T", vmm)
                    if which == 0:
                        P.emit("scalar", "copy",
                               vtile(tt, 512 * nch, 512), pres.ap)
                        ev = P.mark("scalar", "S")
                        pres.release("S", ev)
                        v_v_ev[(nch, tt)] = ev
                    else:
                        P.emit("scalar", "activation",
                               gtile(tt, 512 * nch, 512), pres.ap, AF.Silu)
                        ev = P.mark("scalar", "S")
                        pres.release("S", ev)
                        P.wait("vector", "S", max(ev, v_gnB))
                        P.emit("vector", "tensor_tensor",
                               gtile(tt, 512 * nch, 512),
                               gtile(tt, 512 * nch, 512), gnormB, ALU.mult)
                        v_g_ev[(nch, tt)] = P.mark("vector", "V")
                vg_i += 1

        vg_proj(0)

        def Mpar(dt8, par, n=512):
            return B((O_M if par % 2 == 0 else O_F2) + dt8 * 512, n)

        P.wait("vector", "Lch0", P.ctr["Lch0"])
        P.wait("vector", "Lch1", P.ctr["Lch1"])
        P.wait("vector", "T", max(bc_state["last_mm"].values()))
        P.emit("vector", "memset", B(O_M, 4096), 0.0)
        v_m0 = P.mark("vector", "V")
        kv_cycle = [pAh, pBh, pCh, pDh]
        kv_i = 0
        v_Lupd = v_m0
        for p in range(NT):
            pDl.twait()
            P.wait("tensor", "V", v_rope_all)
            for dt8 in range(8):
                h, half = dt8 // 2, dt8 % 2
                P.emit("tensor", "transpose", pT[:, 128 * dt8:128 * (dt8 + 1)],
                       kseg(p, h, half), id_sb)
            vt = P.mark("tensor", "T")
            P.wait("vector", "T", vt)
            for h in range(H):
                P.emit("vector", "tensor_scalar",
                       ktok[:, 256 * h:256 * (h + 1)],
                       pT[:, 256 * h:256 * (h + 1)], G128[h], None, ALU.mult)
            v_kt = P.mark("vector", "V")
            pDl.release("V", v_kt)
            for dt8 in range(8):
                h, half = dt8 // 2, dt8 % 2
                pres = kv_cycle[kv_i % 4]
                kv_i += 1
                pres.twait()
                P.wait("tensor", "V", v_kt)
                P.wait("tensor", "S", v_v_ev[(h, p)])
                P.emit("tensor", "matmul", pres.ap,
                       ktok[:, 128 * dt8:128 * (dt8 + 1)],
                       vtile(p, 512 * h, 512), start=True, stop=True)
                vmm = P.mark("tensor", "T")
                P.wait("vector", "T", vmm)
                P.wait("vector", "V", v_m0)
                P.emit("vector", "scalar_tensor_tensor", Mpar(dt8, 0),
                       Mpar(dt8, 0), G128[h], pres.ap, ALU.mult, ALU.add)
                v_Lupd = P.mark("vector", "V")
                pres.release("V", v_Lupd)
        P.wait("gpsimd", "V", v_Lupd)
        d_li = P.dma("gpsimd", lg_i[:, :], B(O_M, 4096), sem="Lli")
        P.wait("gpsimd", "Lli", d_li)
        v_cc2 = P.emit("gpsimd", "collective_compute", "AllGather", ALU.bypass,
                       replica_groups=[list(range(NCORE))],
                       ins=[lg_i[:, :]], outs=[lg_o[:, :]], inc="C")

        gnormB = Fv(O_F2, 512)
        P.wait("sync", "V", v_Lupd)
        d_gn2 = P.dma("sync", chrow[0], gnorm, sem="Lch0")
        P.wait("tensor", "Lch0", d_gn2)
        pBh.twait()
        P.emit("tensor", "matmul", pBh.ap, ones[0:1, :], chrow[0],
               start=True, stop=True)
        v_mm = P.mark("tensor", "T")
        bc_state["last_mm"][0] = v_mm
        P.wait("scalar", "T", v_mm)
        P.emit("scalar", "copy", gnormB, pBh.ap)
        v_gnB = P.mark("scalar", "S")
        pBh.release("S", v_gnB)

        vg_proj(1)
        v_vg_all = v_g_ev[(3, NT - 1)]
        v_gp = v_vg_all

        woslot = [B(O_VB + 8192 * g, 8192) for g in range(4)]
        d_wo = {}
        for g in (2, 3):
            P.wait("gpsimd", "T", vg_free[g - 2])
            d_wo[g] = P.dma("gpsimd",
                            woslot[g].rearrange("p (a m) -> p a m", m=2048),
                            wo_g[g].rearrange("(a p) m -> p a m", p=128),
                            sem="Lwo")

        lrecv = [B(O_F2 + 4096 * s, 4096) for s in range(2)]
        sacc = Fv(O_HT, 4096)
        d_lr = {}
        v_lr_used = {}
        P.wait("gpsimd", "C", v_cc2)
        P.wait("sync", "C", v_cc2)
        P.wait("gpsimd", "V", v_gp)
        P.wait("sync", "V", v_gp)
        d_lr[0] = P.dma("gpsimd", lrecv[0], lg_o[0:128, :], sem="Llr0")
        for j in range(NCORE):
            if j + 1 < NCORE:
                q = "gpsimd" if (j + 1) % 2 == 0 else "sync"
                P.wait(q, "V", v_lr_used.get(j - 1, 0))
                d_lr[j + 1] = P.dma(q, lrecv[(j + 1) % 2],
                                    lg_o[128 * (j + 1):128 * (j + 2), :],
                                    sem="Llr%d" % ((j + 1) % 2))
            P.wait("vector", "Llr%d" % (j % 2), d_lr[j])
            if j == NCORE - 1:
                P.wait("vector", "Lli", d_li)
            for h in range(H):
                wcol = sinw_sb[:, 8 * h + j:8 * h + j + 1]
                srcw = lrecv[j % 2][:, 1024 * h:1024 * (h + 1)]
                acc4 = sacc[:, 1024 * h:1024 * (h + 1)]
                if j == 0:
                    P.emit("vector", "tensor_scalar", acc4, srcw,
                           wcol, None, ALU.mult)
                elif j == NCORE - 1:
                    P.emit("vector", "scalar_tensor_tensor",
                           Mpar(2 * h, 0, 1024), srcw, wcol, acc4,
                           ALU.mult, ALU.add)
                else:
                    P.emit("vector", "scalar_tensor_tensor", acc4, srcw,
                           wcol, acc4, ALU.mult, ALU.add)
            v_lr_used[j] = P.mark("vector", "V")
        v_sin = v_lr_used[NCORE - 1]

        attn_sb2 = [attn_sb, attn_sbB]
        v_Mupd = {-1: v_sin}
        v_ob = {}
        at_cycle = [pAl, pBl]
        kv_cycle2 = [pAh, pBh]
        o_cycle = [pCl, pCh, pDh]
        odo = {}
        for p in range(NT):
            rpar, wpar = p % 2, (p + 1) % 2
            pDl.twait()
            for dt8 in range(8):
                h, half = dt8 // 2, dt8 % 2
                P.emit("tensor", "transpose", pT[:, 128 * dt8:128 * (dt8 + 1)],
                       kseg(p, h, half), id_sb)
            vt = P.mark("tensor", "T")
            P.wait("vector", "T", vt)
            for h in range(H):
                P.emit("vector", "tensor_scalar",
                       ktok[:, 256 * h:256 * (h + 1)],
                       pT[:, 256 * h:256 * (h + 1)], G128[h], None, ALU.mult)
            v_kt = P.mark("vector", "V")
            pDl.release("V", v_kt)
            amarks = {}
            for h in range(H):
                at = at_cycle[h % 2]
                at.twait()
                P.emit("tensor", "matmul", at.ap[:, 0:128], kseg(p, h, 0),
                       qseg(p, h, 0), start=True, stop=False)
                P.emit("tensor", "matmul", at.ap[:, 0:128], kseg(p, h, 1),
                       qseg(p, h, 1), start=False, stop=True)
                v_at = P.mark("tensor", "T")
                P.wait("vector", "T", v_at)
                P.emit("vector", "tensor_tensor", attn_sb4[(p % 2) * 4 + h],
                       at.ap[:, 0:128], cau_sb, ALU.mult)
                amarks[h] = P.mark("vector", "V")
                at.release("V", amarks[h])

            def emit_o(p=p, rpar=rpar, amarks=amarks):
                P.wait("tensor", "V", v_Mupd[p - 1])
                for h in range(H):
                    op = o_cycle[h % 3]
                    op.twait()
                    P.wait("tensor", "V", amarks[h])
                    P.emit("tensor", "matmul", op.ap, attn_sb4[(p % 2) * 4 + h],
                           vtile(p, 512 * h, 512), start=True, stop=False)
                    P.emit("tensor", "matmul", op.ap, qseg(p, h, 0),
                           Mpar(2 * h, rpar), start=False, stop=False)
                    P.emit("tensor", "matmul", op.ap, qseg(p, h, 1),
                           Mpar(2 * h + 1, rpar), start=False, stop=True)
                    v_omm = P.mark("tensor", "T")
                    P.wait("scalar", "T", v_omm)
                    P.emit("scalar", "copy", obv(p, 512 * h, 512), op.ap)
                    v_ob[(p, h)] = P.mark("scalar", "S")
                    op.release("S", v_ob[(p, h)])

            odo[p] = emit_o
            if p >= 1:
                odo[p - 1]()
            vm = None
            for dt8 in range(8):
                h = dt8 // 2
                pres = kv_cycle2[dt8 % 2]
                pres.twait()
                P.wait("tensor", "V", v_kt)
                if p == 0:
                    P.wait("tensor", "S", v_v_ev[(h, p)])
                P.emit("tensor", "matmul", pres.ap,
                       ktok[:, 128 * dt8:128 * (dt8 + 1)],
                       vtile(p, 512 * h, 512), start=True, stop=True)
                vkv = P.mark("tensor", "T")
                P.wait("vector", "T", vkv)
                P.emit("vector", "scalar_tensor_tensor", Mpar(dt8, wpar),
                       Mpar(dt8, rpar), G128[h], pres.ap, ALU.mult, ALU.add)
                vm = P.mark("vector", "V")
                pres.release("V", vm)
            v_Mupd[p] = vm
        odo[NT - 1]()

        v_og = {}
        RSb2 = Fv(O_HT + 9216, 512)

        def p3b_sq(p):
            if p == 0:
                P.wait("scalar", "V", v_gp)
            P.wait("scalar", "V", v_og.get(p - 2, 0))
            for h in range(H):
                P.wait("scalar", "S", v_ob[(p, h)])
                P.emit("scalar", "activation", RSb2,
                       obv(p, 512 * h, 512), AF.Square,
                       accum_out=s_oiv[:, (p % 2) * 4 + h:(p % 2) * 4 + h + 1])
            return P.mark("scalar", "S")

        sqd = {0: p3b_sq(0)}
        for p in range(NT):
            c4 = s_oiv[:, (p % 2) * 4:(p % 2) * 4 + 4]
            if p + 1 < NT:
                sqd[p + 1] = p3b_sq(p + 1)
            P.wait("vector", "S", sqd[p])
            P.emit("vector", "tensor_scalar", c4, c4, 1.0 / DV, EPS,
                   ALU.mult, ALU.add)
            P.emit("vector", "reciprocal", c4, c4)
            vv = P.mark("vector", "V")
            P.wait("scalar", "V", vv)
            P.emit("scalar", "sqrt", c4, c4)
            vs2 = P.mark("scalar", "S")
            P.wait("vector", "S", vs2)
            for h in range(H):
                P.emit("vector", "scalar_tensor_tensor", obv(p, 512 * h, 512),
                       obv(p, 512 * h, 512),
                       s_oiv[:, (p % 2) * 4 + h:(p % 2) * 4 + h + 1],
                       gtile(p, 512 * h, 512), ALU.mult, ALU.mult)
            v_og[p] = P.mark("vector", "V")
        v_og_all = v_og[NT - 1]

        P.wait("sync", "S", v_ob[(NT - 1, H - 1)])
        for j in range(4):
            v_gaB = bcast512(bc0[:, 512 * j:512 * (j + 1)], 8 + j,
                             row_math=gamath)

        wtmp = Fv(O_F2, 512)
        v_ret_done = v_Mupd[NT - 1]
        for g in (0, 1):
            P.wait("gpsimd", "S", v_ob[(NT - 1, H - 1)])
            d_wo[g] = P.dma("gpsimd",
                            woslot[g].rearrange("p (a m) -> p a m", m=2048),
                            wo_g[g].rearrange("(a p) m -> p a m", p=128),
                            sem="Lwo")

        d_x2 = {}
        P.wait("sync", "V", v_og_all)
        for tt in range(NT):
            d_x2[tt] = P.dma("sync", accv(tt, 0, 2048),
                             x_t[128 * tt:128 * (tt + 1), :], sem="Lxa")
        d_x2_all = d_x2[NT - 1]

        v_wo_used = {}
        wo_cycle = [pAl, pAh, pBl, pBh]
        wo_i = 0
        v_acc_wo = {}
        for tt in range(NT):
            P.wait("tensor", "V", v_og[tt])
            for grp in range(2):
                pDl.twait()
                for j in range(8):
                    kt2 = grp * 8 + j
                    P.emit("tensor", "transpose", pT[:, 128 * j:128 * (j + 1)],
                           obv(tt, 128 * kt2, 128), id_sb)
                vt = P.mark("tensor", "T")
                P.wait("vector", "T", vt)
                P.emit("vector", "tensor_copy",
                       ogT[:, 1024 * grp:1024 * (grp + 1)], pT)
                v_tp = P.mark("vector", "V")
                pDl.release("V", v_tp)
            P.wait("tensor", "V", v_tp)
            for g in range(4):
                P.wait("tensor", "Lwo", d_wo[1])
                for nch in range(4):
                    pres = wo_cycle[wo_i % 4]
                    wo_i += 1
                    pres.twait()
                    for kg in range(4):
                        P.emit("tensor", "matmul", pres.ap,
                               ogT[:, 128 * (g * 4 + kg):128 * (g * 4 + kg + 1)],
                               woslot[g][:, 2048 * kg + 512 * nch:2048 * kg + 512 * (nch + 1)],
                               start=(kg == 0), stop=(kg == 3))
                    vmm = P.mark("tensor", "T")
                    if nch == 3 and tt == NT - 1:
                        v_wo_used[g] = vmm
                    P.wait("vector", "T", vmm)
                    P.wait("vector", "Lxa", d_x2_all)
                    if tt == 0 and g == 0 and nch == 0:
                        P.wait("vector", "S", v_gaB)
                    P.emit("vector", "tensor_tensor", wtmp, pres.ap,
                           bc0[:, 512 * nch:512 * (nch + 1)], ALU.mult)
                    P.emit("vector", "tensor_tensor",
                           accv(tt, 512 * nch, 512),
                           accv(tt, 512 * nch, 512), wtmp, ALU.add)
                    va = P.mark("vector", "V")
                    pres.release("V", va)
            v_acc_wo[tt] = va

        P.wait("sync", "T", v_wo_used[3])
        for j in range(4):
            bcast512(bc0[:, 512 * j:512 * (j + 1)], 16 + j, add1=True)
        for j in range(4):
            v_cbsb = bcast512(bc1[:, 512 * j:512 * (j + 1)], 12 + j)

        gate_bf = B(O_F2 + 2048, 512)
        P.wait("gpsimd", "T", v_wo_used[3])
        d_gate = P.dma("gpsimd",
                       gate_bf.rearrange("p (a m) -> p a m", m=32),
                       gatet.rearrange("(a p) m -> p a m", p=128), sem="Lgate")

        v_h2T = {}
        v_h2_done = {}

        LGb = Fv(O_F2 + 4608, 256)
        RPb = Fv(O_F2 + 5632, 256)
        RCb = Fv(O_F2 + 6656, 256)
        RSb = Fv(O_M + 2048, 256)
        MKS = Fv(O_M + 3072, 64)
        CNTS = Fv(O_M + 3200, 64)
        R8a = Fv(O_M + 3328, 8)
        R8b = Fv(O_M + 3344, 8)
        R8c = Fv(O_M + 3360, 8)

        def ln2_A(t):
            xt = accv(t, 0, 2048)
            P.wait("vector", "V", v_acc_wo[t])
            P.emit("vector", "reduce_sum", s_mean[:, t:t + 1], xt, axis=AX.X)
            P.wait("scalar", "V", v_acc_wo[t])
            if t == 0:
                P.wait("scalar", "T", v_wo_used[3])
            P.emit("scalar", "activation", sq_scr, xt, AF.Square,
                   accum_out=s_ssq[:, t:t + 1])
            return P.mark("scalar", "S")

        sA2 = {0: ln2_A(0)}
        sA2[1] = ln2_A(1)
        for t in range(NT):
            ht, hb = htmp[t % 2], hbuf[t % 2]
            xt = accv(t, 0, 2048)
            P.wait("vector", "S", sA2[t])
            P.emit("vector", "tensor_scalar", s_mean[:, t:t + 1],
                   s_mean[:, t:t + 1], 1.0 / D, None, ALU.mult)
            P.emit("vector", "tensor_tensor", s_t1[:, t:t + 1],
                   s_mean[:, t:t + 1], s_mean[:, t:t + 1], ALU.mult)
            P.emit("vector", "tensor_scalar", s_inv[:, t:t + 1],
                   s_ssq[:, t:t + 1], 1.0 / D, None, ALU.mult)
            P.emit("vector", "tensor_tensor", s_inv[:, t:t + 1],
                   s_inv[:, t:t + 1], s_t1[:, t:t + 1], ALU.subtract)
            P.emit("vector", "tensor_scalar", s_inv[:, t:t + 1],
                   s_inv[:, t:t + 1], EPS, None, ALU.add)
            P.emit("vector", "reciprocal", s_inv[:, t:t + 1], s_inv[:, t:t + 1])
            vB = P.mark("vector", "V")
            P.wait("scalar", "V", vB)
            P.emit("scalar", "sqrt", s_inv[:, t:t + 1], s_inv[:, t:t + 1])
            sC = P.mark("scalar", "S")
            P.wait("vector", "S", sC)
            P.emit("vector", "tensor_scalar", s_nmi[:, t:t + 1],
                   s_mean[:, t:t + 1], s_inv[:, t:t + 1], -1.0,
                   ALU.mult, ALU.mult)
            vD = P.mark("vector", "V")
            P.wait("scalar", "V", vD)
            P.emit("scalar", "activation", ht, xt, AF.Identity,
                   bias=s_nmi[:, t:t + 1], scale=s_inv[:, t:t + 1])
            v_aff = P.mark("scalar", "S")
            if t + 2 < NT:
                sA2[t + 2] = ln2_A(t + 2)
            P.wait("vector", "S", v_aff)
            P.wait("vector", "S", v_cbsb)
            P.emit("vector", "tensor_tensor", ht, ht, bc0, ALU.mult)
            P.emit("vector", "tensor_tensor", hb, ht, bc1, ALU.add)
            v_h2 = P.mark("vector", "V")
            v_h2_done[t] = v_h2
            P.wait("tensor", "V", v_h2)
            for grp in range(2):
                pDl.twait()
                for j in range(8):
                    kt2 = grp * 8 + j
                    P.emit("tensor", "transpose", pT[:, 128 * j:128 * (j + 1)],
                           hb[:, 128 * kt2:128 * (kt2 + 1)], id_sb)
                vt = P.mark("tensor", "T")
                P.wait("vector", "T", vt)
                P.emit("vector", "tensor_copy",
                       h2T3[:, grp * 8:(grp + 1) * 8, 128 * t:128 * (t + 1)],
                       pT.rearrange("p (a b) -> p a b", b=128))
                vv = P.mark("vector", "V")
                pDl.release("V", vv)
            v_h2T[t] = vv
            pCl.twait()
            P.wait("tensor", "Lgate", d_gate)
            P.wait("tensor", "V", vv)
            for kt2 in range(KT):
                P.emit("tensor", "matmul", pC_t[:, 0:32],
                       h2T(kt2, 128 * t, 128), gate_bf[:, 32 * kt2:32 * (kt2 + 1)],
                       start=(kt2 == 0), stop=(kt2 == KT - 1))
            v_lg = P.mark("tensor", "T")
            P.wait("scalar", "T", v_lg)
            P.emit("scalar", "copy", LGb[:, 32 * t:32 * (t + 1)], pC_t[:, 0:32])
            v_lgc = P.mark("scalar", "S")
            pCl.release("S", v_lgc)
        v_lg_all = v_lgc

        for j in range(4):
            v_rgb = bcast512(bc1[:, 512 * j:512 * (j + 1)], 20 + j,
                             row_math=rgmath)
        P.wait("vector", "S", v_rgb)
        v_accr = {}
        for tt in range(NT):
            P.emit("vector", "tensor_tensor", accv(tt, 0, 2048),
                   accv(tt, 0, 2048), bc1, ALU.mult)
            v_accr[tt] = P.mark("vector", "V")

        def seg(ap):
            return ap.rearrange("p (a b) -> p a b", b=32)

        def col8(ap):
            return ap.rearrange("p (a b) -> p a b", b=1)

        def bc8(ap):
            return col8(ap).broadcast_to([128, 8, 32])

        def segred(dst8, src256, op):
            P.emit("vector", "tensor_reduce", col8(dst8), seg(src256),
                   axis=AX.X, op=op)

        MKS3 = MKS.rearrange("p (a b) -> p a b", b=8)
        P.wait("vector", "S", v_lg_all)
        segred(R8a, LGb, ALU.max)
        P.emit("vector", "tensor_tensor", seg(RPb), seg(LGb), bc8(R8a),
               ALU.subtract)
        v_sm = P.mark("vector", "V")
        P.wait("scalar", "V", v_sm)
        P.emit("scalar", "activation", RPb, RPb, AF.Exp)
        v_exp = P.mark("scalar", "S")
        P.wait("vector", "S", v_exp)
        segred(R8a, RPb, ALU.add)
        P.emit("vector", "reciprocal", R8a, R8a)
        P.emit("vector", "tensor_tensor", seg(RPb), seg(RPb), bc8(R8a),
               ALU.mult)
        P.emit("vector", "tensor_copy", RCb, RPb)
        for k in range(8):
            P.emit("vector", "tensor_reduce", MKS3[:, :, k:k + 1], seg(RCb),
                   axis=AX.X, op=ALU.max)
            P.emit("vector", "tensor_tensor", seg(RSb), seg(RCb),
                   MKS3[:, :, k:k + 1].broadcast_to([128, 8, 32]), ALU.is_equal)
            P.emit("vector", "scalar_tensor_tensor", RCb, RSb, -1e30, RCb,
                   ALU.mult, ALU.add)
        for k in range(8):
            P.emit("vector", "tensor_tensor", seg(RSb), seg(RPb),
                   MKS3[:, :, k:k + 1].broadcast_to([128, 8, 32]), ALU.is_ge)
            P.emit("vector", "tensor_reduce",
                   CNTS.rearrange("p (a b) -> p a b", b=8)[:, :, k:k + 1],
                   seg(RSb), axis=AX.X, op=ALU.add)
        P.emit("vector", "tensor_scalar", CNTS, CNTS, 8.0, None, ALU.is_ge)
        P.emit("vector", "tensor_tensor", CNTS, CNTS, MKS, ALU.mult)
        P.emit("vector", "tensor_reduce", col8(R8b),
               CNTS.rearrange("p (a b) -> p a b", b=8), axis=AX.X, op=ALU.max)
        P.emit("vector", "tensor_tensor", seg(RCb), seg(RPb), bc8(R8b),
               ALU.is_ge)
        P.emit("vector", "tensor_tensor", seg(RSb), seg(RPb), seg(RCb),
               ALU.mult)
        segred(R8b, RSb, ALU.add)
        P.emit("vector", "reciprocal", R8b, R8b)
        P.wait("vector", "S", v_alb)
        P.emit("vector", "tensor_scalar", R8b, R8b, albB, None, ALU.mult)
        P.emit("vector", "tensor_tensor", seg(RSb), seg(RSb), bc8(R8b),
               ALU.mult)
        P.emit("vector", "tensor_scalar", RSb, RSb, TAU, None, ALU.is_ge)
        segred(R8b, RSb, ALU.max)
        P.emit("vector", "tensor_scalar", R8b, R8b, 0.0, None, ALU.is_equal)
        segred(R8c, RPb, ALU.max)
        P.emit("vector", "tensor_tensor", seg(RCb), seg(RPb), bc8(R8c),
               ALU.is_equal)
        P.emit("vector", "tensor_tensor", seg(RCb), seg(RCb), bc8(R8b),
               ALU.mult)
        P.emit("vector", "tensor_tensor", RSb, RSb, RCb, ALU.max)
        P.emit("vector", "tensor_tensor", RSb, RPb, RSb, ALU.mult)
        segred(R8c, RSb, ALU.add)
        P.emit("vector", "tensor_scalar", R8c, R8c, 1e-12, None, ALU.max)
        P.emit("vector", "reciprocal", R8c, R8c)
        P.emit("vector", "tensor_tensor", seg(comb[:, 0:256]), seg(RSb),
               bc8(R8c), ALU.mult)
        v_comb_all = P.mark("vector", "V")

        gu_sp = [B(O_F1 + 4096 * s, 4096) for s in range(2)]
        heT = B(O_F1 + 8192, 6144)
        sgb = [B(O_F1 + 14336, 512), B(O_F1 + 14848, 512)]
        dn_s = [B(O_VB + 3072 * s, 3072) for s in range(4)]

        gu_used = {}
        dn_used = {}
        d_gu = {}
        d_dn = {}

        def prefetch(e, pr):
            idx = e * 6 + pr
            P.wait("gpsimd", "T", gu_used.get(idx - 2, 0))
            if idx == 0:
                P.wait("gpsimd", "T", v_wo_used[2])
                P.wait("gpsimd", "V", v_h2T[NT - 1])
            d_gu[idx] = P.dma("gpsimd",
                              gu_sp[idx % 2].rearrange("p (b a m) -> p b a m", b=2, m=128),
                              gup[e, pr].rearrange("b (a p) m -> p b a m", p=128),
                              sem="Lgu%d" % (idx % 2))

        def prefetch_dn(e, nch):
            idx = e * 4 + nch
            P.wait("gpsimd", "T", dn_used.get(idx - 4, 0))
            if idx == 0:
                P.wait("gpsimd", "T", v_wo_used[0])
                P.wait("gpsimd", "T", v_wo_used[1])
            d_dn[idx] = P.dma("gpsimd",
                              dn_s[idx % 4].rearrange("p (a m) -> p a m", m=512),
                              dnp[e, nch].rearrange("(a p) m -> p a m", p=128),
                              sem="Ldn%d" % (idx % 4))

        for pr in range(2):
            prefetch(0, pr)
        for nch in range(4):
            prefetch_dn(0, nch)

        v_last_acc = None
        pe_cycle = [(pAl, pAh), (pBl, pBh), (pCl, pCh), (pDl, pDh)]
        v_he = {}
        sg_busy = {}
        for e in range(NEXP):
            pe_i = 0
            for pr in range(6):
                idx = e * 6 + pr
                sl = gu_sp[idx % 2]
                P.wait("tensor", "Lgu%d" % (idx % 2), d_gu[idx])
                if e == 0 and pr == 0:
                    P.wait("tensor", "V", v_h2T[NT - 1])
                for tch in range(2):
                    pg, pu = pe_cycle[pe_i % 4]
                    pe_i += 1
                    pg.twait()
                    pu.twait()
                    P.wait("tensor", "V", v_he.get((e - 1, pr, tch), 0))
                    for kt2 in range(KT):
                        P.emit("tensor", "matmul", pg.ap,
                               sl[:, 128 * kt2:128 * (kt2 + 1)],
                               h2T(kt2, 512 * tch, 512),
                               start=(kt2 == 0), stop=(kt2 == KT - 1))
                    for kt2 in range(KT):
                        P.emit("tensor", "matmul", pu.ap,
                               sl[:, 2048 + 128 * kt2:2048 + 128 * (kt2 + 1)],
                               h2T(kt2, 512 * tch, 512),
                               start=(kt2 == 0), stop=(kt2 == KT - 1))
                    vmm = P.mark("tensor", "T")
                    if tch == 1:
                        gu_used[idx] = vmm
                        if pr + 2 < 6:
                            prefetch(e, pr + 2)
                        elif e + 1 < NEXP:
                            prefetch(e + 1, pr + 2 - 6)
                    P.wait("scalar", "T", vmm)
                    sgi = pe_i % 2
                    sg = sgb[sgi]
                    P.wait("scalar", "V", sg_busy.get(sgi, 0))
                    P.emit("scalar", "activation", sg, pg.ap, AF.Silu)
                    vsg = P.mark("scalar", "S")
                    P.wait("vector", "S", vsg)
                    P.emit("vector", "tensor_tensor",
                           heT[:, 1024 * pr + 512 * tch:1024 * pr + 512 * (tch + 1)],
                           sg, pu.ap, ALU.mult)
                    vhe = P.mark("vector", "V")
                    sg_busy[sgi] = vhe
                    v_he[(e, pr, tch)] = vhe
                    pg.release("V", vhe)
                    pu.release("V", vhe)
            dn_cycle = [pAl, pAh, pBl, pBh]
            pe_i = 0
            for tt in range(NT):
                for nch in range(4):
                    idx = e * 4 + nch
                    pres = dn_cycle[pe_i % 4]
                    pe_i += 1
                    pres.twait()
                    P.wait("tensor", "Ldn%d" % (idx % 4), d_dn[idx])
                    if tt == 0:
                        for pr in range(6):
                            P.wait("tensor", "V", v_he[(e, pr, 0)])
                            P.wait("tensor", "V", v_he[(e, pr, 1)])
                    for kt2 in range(6):
                        P.emit("tensor", "matmul", pres.ap,
                               heT[:, 1024 * kt2 + 128 * tt:1024 * kt2 + 128 * (tt + 1)],
                               dn_s[idx % 4][:, 512 * kt2:512 * (kt2 + 1)],
                               start=(kt2 == 0), stop=(kt2 == 5))
                    vmm = P.mark("tensor", "T")
                    if tt == NT - 1:
                        dn_used[idx] = vmm
                        if e + 1 < NEXP:
                            prefetch_dn(e + 1, nch)
                    P.wait("vector", "T", vmm)
                    if e == 0:
                        P.wait("vector", "V", v_accr[tt])
                        P.wait("vector", "V", v_comb_all)
                    P.emit("vector", "scalar_tensor_tensor",
                           accv(tt, 512 * nch, 512), pres.ap,
                           comb[:, 32 * tt + e:32 * tt + e + 1],
                           accv(tt, 512 * nch, 512), ALU.mult, ALU.add)
                    va = P.mark("vector", "V")
                    pres.release("V", va)
            v_last_acc = va

        for j in range(4):
            v_gbB = bcast512(bc0[:, 512 * j:512 * (j + 1)], 20 + j,
                             row_math=gamath)
        P.wait("vector", "S", v_gbB)
        P.wait("vector", "V", v_last_acc)
        d_y = {}
        for tt in range(NT):
            P.emit("vector", "tensor_tensor", accv(tt, 0, 2048),
                   accv(tt, 0, 2048), bc0, ALU.mult)
            vo = P.mark("vector", "V")
            P.wait("sync", "V", vo)
            d_y[tt] = P.dma("sync", y[128 * tt:128 * (tt + 1), :],
                            accv(tt, 0, 2048), sem="Ly")
        P.wait("sync", "Ly", d_y[NT - 1])

        @block.tensor
        def _(e):
            for f in P.ops["tensor"]:
                f(e)

        @block.vector
        def _(e):
            for f in P.ops["vector"]:
                f(e)

        @block.scalar
        def _(e):
            for f in P.ops["scalar"]:
                f(e)

        @block.gpsimd
        def _(e):
            for f in P.ops["gpsimd"]:
                f(e)

        @block.sync
        def _(e):
            for f in P.ops["sync"]:
                f(e)

    return nc


_cache = {}


def _prep(inputs):
    f32 = np.float32
    x = np.ascontiguousarray(np.asarray(inputs["L0_attn_x"], f32).reshape(T, D))
    wq = np.asarray(inputs["L0_attn_Wq"], f32)
    wk = np.asarray(inputs["L0_attn_Wk"], f32)
    wv = np.asarray(inputs["L0_attn_Wv"], f32)
    wg = np.asarray(inputs["L0_attn_Wg"], f32)
    wo = np.asarray(inputs["L0_attn_Wo"], f32)
    gnw = np.asarray(inputs["L0_attn_gnorm_w"], f32).reshape(1, 512)
    gwt = np.asarray(inputs["L0_ffn_gate_weight"], f32)
    gu = np.asarray(inputs["L0_ffn_gate_up_proj"], f32)
    dn = np.asarray(inputs["L0_ffn_down_proj"], f32)
    alpha = np.asarray(inputs["L0_ffn_layer_alpha"], f32).reshape(1, 1)
    cv = np.asarray(inputs["L0_c"], f32).reshape(-1)
    wmod = np.asarray(inputs["L0_wmod"], f32)

    wq_m = np.ascontiguousarray(wq.T.reshape(D, 8, 128).transpose(1, 0, 2))
    wk_m = np.ascontiguousarray(wk.T.reshape(D, 8, 128).transpose(1, 0, 2))
    wv_n = np.ascontiguousarray(wv.T.reshape(D, 4, 512).transpose(1, 0, 2))
    wg_n = np.ascontiguousarray(wg.T.reshape(D, 4, 512).transpose(1, 0, 2))
    wo_g = np.ascontiguousarray(wo.T.reshape(4, 512, D))
    gu_t = gu.transpose(0, 2, 1).reshape(NEXP, D, 12, 128).transpose(0, 2, 1, 3)
    gup = np.ascontiguousarray(
        gu_t.reshape(NEXP, 2, 6, D, 128).transpose(0, 2, 1, 3, 4))
    dnp = np.ascontiguousarray(
        dn.transpose(0, 2, 1).reshape(NEXP, INTER, 4, 512).transpose(0, 2, 1, 3))
    gatet = np.ascontiguousarray(gwt.T)
    c_cols = np.ascontiguousarray(cv.reshape(16, 128).T)

    tglob = np.arange(T, dtype=f32)
    invf = (1.0 / (10000.0 ** (np.arange(0, DK, 2, dtype=f32) / DK))).astype(f32)
    freqs = (tglob[:, None] * invf[None, :]).astype(f32)
    cosb = np.cos(freqs).astype(f32).T
    sinb = np.sin(freqs).astype(f32).T
    ip = np.arange(T, dtype=np.float64) % 128
    tabs_all = np.zeros((NCORE, 2, H, 2, 128, TLOC), f32)
    for h in range(H):
        qs = (np.float64(GAMMA[h]) ** (ip + 1) * DK ** -0.5)
        ks = (np.float64(GAMMA[h]) ** (-ip - 1))
        for c in range(NCORE):
            sl = slice(TLOC * c, TLOC * (c + 1))
            tabs_all[c, 0, h, 0] = cosb[:, sl] * qs[None, sl]
            tabs_all[c, 0, h, 1] = sinb[:, sl] * qs[None, sl]
            tabs_all[c, 1, h, 0] = cosb[:, sl] * ks[None, sl]
            tabs_all[c, 1, h, 1] = sinb[:, sl] * ks[None, sl]

    identity = np.eye(128, dtype=BFNP)
    ii = np.arange(128)
    causal = (ii[None, :] >= ii[:, None]).astype(BFNP)

    sinw_all = np.zeros((NCORE, 128, 32), f32)
    for c in range(NCORE):
        for h in range(H):
            for j in range(NCORE):
                if j < c:
                    sinw_all[c, :, 8 * h + j] = np.float64(GAMMA[h]) ** (1024.0 * (c - 1 - j))

    per_core = []
    for c in range(NCORE):
        m = dict(
            x_t=np.ascontiguousarray(x[TLOC * c:TLOC * (c + 1)]),
            wq_m=wq_m, wk_m=wk_m, wv_n=wv_n, wg_n=wg_n, wo_g=wo_g,
            gup=gup, dnp=dnp, gatet=gatet,
            wmod_s=np.ascontiguousarray(
                wmod[:, 1536 * c:1536 * (c + 1)].reshape(KT, 128, 1536)),
            c_cols=c_cols, gnorm=gnw, alpha2=alpha,
            tabs=np.ascontiguousarray(tabs_all[c]),
            ident=identity, causalT=causal,
            sinw=np.ascontiguousarray(sinw_all[c]),
        )
        per_core.append(m)
    return per_core


def kernel(**inputs):
    per_core = _prep(inputs)
    if "nc" not in _cache:
        _cache["nc"] = build()
    nc = _cache["nc"]
    res = run_bass_kernel_spmd(nc, per_core, core_ids=list(range(NCORE)))
    out = np.concatenate([np.asarray(r["y"]) for r in res.results], axis=0)
    return out.reshape(1, T, D).astype(np.float32)




# version 08
# speedup vs baseline: 1.2593x; 1.0116x over previous
import contextlib
import numpy as np
import ml_dtypes
import concourse.bass as bass
import concourse.mybir as mybir
from concourse.bass_utils import run_bass_kernel_spmd

F32 = mybir.dt.float32
BF16 = mybir.dt.bfloat16
AF = mybir.ActivationFunctionType
ALU = mybir.AluOpType
AX = mybir.AxisListType
BFNP = ml_dtypes.bfloat16

NCORE = 8
T = 8192
TLOC = 1024
D = 2048
H = 4
DK = 256
DV = 512
KT = 16
NT = 8
NEXP = 32
INTER = 768
GAMMA = [float(1.0 - 2.0 ** (-5 - h)) for h in range(H)]
G128 = [float(np.float64(g) ** 128) for g in GAMMA]
EPS = 1e-5
TAU = 0.05

O_HT = 0
O_GP = 16384
O_QKOB = 32768
O_VB = 49152
O_F1 = 65536
O_F2 = 81920
O_M = 90112
O_BC0 = 94208
O_BC1 = 98304
O_ST = 102400
O_KTK = 104448
ACOLS = 105472


class Prog:
    ENGS = ("tensor", "vector", "scalar", "gpsimd", "sync")

    def __init__(self, sems):
        self.ops = {k: [] for k in self.ENGS}
        self.sems = sems
        self.ctr = {k: 0 for k in sems}
        self.last = {k: None for k in self.ENGS}
        self.last_kind = {k: None for k in self.ENGS}
        self.last_mark = {k: None for k in self.ENGS}

    def emit(self, eng, meth, *args, inc=None, incn=1, **kw):
        semh = self.sems[inc] if inc else None
        last = self.last
        dr = eng in ("vector", "scalar") and meth != "dma_start"

        def f(e, meth=meth, args=args, kw=kw, semh=semh, incn=incn, eng=eng, dr=dr):
            if dr:
                e.drain()
            ins = getattr(e, meth)(*args, **kw)
            last[eng] = ins
            if semh is not None:
                ins.then_inc(semh, incn)
        self.ops[eng].append(f)
        self.last_kind[eng] = "emit" if inc is None else "emit_inc"
        self.last_mark[eng] = inc
        if inc:
            self.ctr[inc] += incn
            return self.ctr[inc]
        return None

    def wait(self, eng, sem, v):
        if not v or v <= 0:
            return
        semh = self.sems[sem]

        def f(e, semh=semh, v=v):
            e.wait_ge(semh, v)
        self.ops[eng].append(f)

    def mark(self, eng, sem):
        if self.last_kind[eng] in ("mark", "emit_inc") and \
                self.last_mark[eng] == sem:
            return self.ctr[sem]
        assert self.last_kind[eng] in ("emit",), \
            f"mark({eng},{sem}) after {self.last_kind[eng]}"
        semh = self.sems[sem]
        last = self.last

        def f(e, semh=semh, eng=eng):
            last[eng].then_inc(semh, 1)
        self.ops[eng].append(f)
        self.last_kind[eng] = "mark"
        self.last_mark[eng] = sem
        self.ctr[sem] += 1
        return self.ctr[sem]

    def dma(self, queue, out, in_, sem=None):
        assert sem is not None, "dma needs an explicit slot semaphore"
        return self.emit(queue, "dma_start", inc=sem, incn=16, out=out, in_=in_)


class Psum:

    def __init__(self, P, ap):
        self.P, self.ap = P, ap
        self.free_at = None

    def twait(self):
        if self.free_at is not None:
            self.P.wait("tensor", *self.free_at)
            self.free_at = None

    def release(self, sem, val):
        self.free_at = (sem, val)


def build(dbg=()):
    nc = bass.Bass("TRN2", target_bir_lowering=False)

    def di(name, shape, dtype=F32):
        return nc.dram_tensor(name, shape, dtype, kind="ExternalInput").ap()

    x_t = di("x_t", [TLOC, D])
    wq_m = di("wq_m", [8, D, 128])
    wk_m = di("wk_m", [8, D, 128])
    wv_n = di("wv_n", [4, D, 512])
    wg_n = di("wg_n", [4, D, 512])
    wo_g = di("wo_g", [4, 512, D])
    gup = di("gup", [NEXP, 6, 2, D, 128])
    dnp = di("dnp", [NEXP, 4, INTER, 512])
    gatet = di("gatet", [D, 32])
    wmod_s = di("wmod_s", [KT, 128, 1536])
    c_cols = di("c_cols", [128, 16])
    gnorm = di("gnorm", [1, 512])
    alpha2 = di("alpha2", [1, 1])
    tabs = di("tabs", [2, H, 2, 128, TLOC])
    ident = di("ident", [128, 128], BF16)
    causalT = di("causalT", [128, 128], BF16)
    sinw = di("sinw", [128, 32])

    y = nc.dram_tensor("y", [TLOC, D], F32, kind="ExternalOutput").ap()
    dbg_out = {}
    for name, shape, dt in dbg:
        dbg_out[name] = nc.dram_tensor("dbg_" + name, list(shape), dt,
                                       kind="ExternalOutput").ap()

    modg_i = nc.dram_tensor("modg_i", [1, 1536], F32).ap()
    modg_o = nc.dram_tensor("modg_o", [NCORE, 1536], F32).ap()
    lg_i = nc.dram_tensor("lg_i", [128, 4096], BF16).ap()
    lg_o = nc.dram_tensor("lg_o", [NCORE * 128, 4096], BF16).ap()

    with contextlib.ExitStack() as ctx:
        en = ctx.enter_context
        AR = en(nc.sbuf_tensor("AR", [128, ACOLS], BF16))
        pA_t = en(nc.psum_tensor("pA", [128, 1024], F32))
        pB_t = en(nc.psum_tensor("pB", [128, 1024], F32))
        pC_t = en(nc.psum_tensor("pC", [128, 1024], F32))
        pD_t = en(nc.psum_tensor("pD", [128, 1024], F32))
        semnames = ["T", "V", "S", "C", "Lc", "Lx0", "Lx1", "Lx2", "Lx3", "Ltab",
                    "Lm0", "Lm1", "Lm2", "Lm3", "Lvg0", "Lvg1", "Ltab0", "Ltab1",
                    "Lwm0", "Lwm1", "Lgu0", "Lgu1",
                    "Ldn0", "Ldn1", "Ldn2", "Ldn3", "Lwo", "Lxa", "Ly",
                    "Llr0", "Llr1", "Lch0", "Lch1", "Lmi", "Lli", "Lgate"]
        semh = {s: en(nc.semaphore("sem" + s)) for s in semnames}
        block = en(nc.Block())
        P = Prog(semh)

        def B(o, n, p0=0, p1=128):
            return AR[p0:p1, o:o + n]

        def Fv(o, n, p0=0, p1=128):
            return AR[p0:p1, o:o + 2 * n].bitcast(F32)

        pAl, pAh = Psum(P, pA_t[:, 0:512]), Psum(P, pA_t[:, 512:1024])
        pBl, pBh = Psum(P, pB_t[:, 0:512]), Psum(P, pB_t[:, 512:1024])
        pCl, pCh = Psum(P, pC_t[:, 0:512]), Psum(P, pC_t[:, 512:1024])
        pDl, pDh = Psum(P, pD_t[:, 0:512]), Psum(P, pD_t[:, 512:1024])
        pT = pD_t[:, 0:512].bitcast(BF16)

        st_cur = [O_ST]

        def st_alloc(nbf):
            o = st_cur[0]
            st_cur[0] = (o + nbf + 15) // 16 * 16
            assert st_cur[0] <= O_ST + 2048, f"ST overflow {st_cur[0]}"
            return o

        ones = Fv(st_alloc(256), 128)
        csb = Fv(st_alloc(32), 16)
        silc = Fv(st_alloc(32), 16)
        id_sb = B(st_alloc(128), 128)
        cau_sb = B(st_alloc(128), 128)
        sinw_sb = Fv(st_alloc(64), 32)
        albB = Fv(st_alloc(16), 1)
        s_mean = Fv(st_alloc(16), 8)
        s_inv = Fv(st_alloc(16), 8)
        s_nmi = Fv(st_alloc(16), 8)
        s_ssq = Fv(st_alloc(16), 8)
        s_oiv = Fv(st_alloc(16), 8)
        s_t1 = Fv(st_alloc(16), 8)
        comb = Fv(st_alloc(512), 256)
        r_p = Fv(st_alloc(64), 32)
        r_cur = Fv(st_alloc(64), 32)
        r_tmp = Fv(st_alloc(64), 32)
        r_mk = Fv(st_alloc(16), 8)
        r_cnt = Fv(st_alloc(16), 8)
        r_a1 = Fv(st_alloc(64), 32)
        attn_sb = B(st_alloc(128), 128)
        attn_sbB = B(st_alloc(128), 128)
        attn_sb4 = [B(O_HT + 8192 + 128 * i, 128) for i in range(8)]

        chrow = [Fv(O_KTK, 512, 0, 1), Fv(O_M, 512, 0, 1)]
        rtmprow = Fv(O_M + 1024, 512, 0, 1)
        rtmprow_w = Fv(O_HT + 14336, 1024, 0, 1)
        rtmprow_w2 = Fv(O_M, 1024, 0, 1)

        def hT(kt, c0, n):
            return B(O_HT + kt * 1024 + c0, n)

        hT3 = B(O_HT, 16384).rearrange("p (a b) -> p a b", b=1024)
        h2T3 = B(O_QKOB, 16384).rearrange("p (a b) -> p a b", b=1024)

        def h2T(kt, c0, n):
            return B(O_QKOB + kt * 1024 + c0, n)

        def qseg(p, h, half):
            return B(O_QKOB + p * 2048 + h * 512 + half * 128, 128)

        def kseg(p, h, half):
            return B(O_QKOB + p * 2048 + h * 512 + 256 + half * 128, 128)

        def obv(p, c0, n):
            return B(O_QKOB + p * 2048 + c0, n)

        def vtile(p, c0, n):
            return B(O_VB + p * 2048 + c0, n)

        def gtile(p, c0, n):
            return B(O_GP + p * 2048 + c0, n)

        def accv(tt, c0, n):
            return Fv(4096 * tt + 2 * c0, n)

        def Mv(dt8, n=512):
            return B(O_M + dt8 * 512, n)

        ktok = B(O_KTK, 1024)
        ogT = B(O_M, 2048)

        P.emit("vector", "memset", ones, 1.0)
        v_ones = P.mark("vector", "V")

        P.dma("sync", csb, c_cols, sem="Lc")
        P.dma("sync", id_sb, ident, sem="Lc")
        P.dma("sync", cau_sb, causalT, sem="Lc")
        d_const = P.dma("sync", sinw_sb, sinw, sem="Lc")

        wslot = [Fv(O_VB, 1536), Fv(O_VB + 3072, 1536)]
        modloc = Fv(O_VB + 6144, 1536, 0, 1)

        P.wait("scalar", "Lc", d_const)
        v_silc = P.emit("scalar", "activation", silc, csb, AF.Silu, inc="S")

        mps = [pA_t[0:1, 0:512], pB_t[0:1, 0:512], pC_t[0:1, 0:512]]
        P.wait("tensor", "S", v_silc)
        t_after_k = {}
        for k in range(KT):
            q = "gpsimd" if k % 2 == 0 else "sync"
            if k >= 2:
                P.wait(q, "T", t_after_k[k - 2])
            d = P.dma(q, wslot[k % 2], wmod_s[k], sem="Lwm%d" % (k % 2))
            P.wait("tensor", "Lwm%d" % (k % 2), d)
            for nch in range(3):
                P.emit("tensor", "matmul", mps[nch], silc[:, k:k + 1],
                       wslot[k % 2][:, 512 * nch:512 * (nch + 1)],
                       start=(k == 0), stop=(k == KT - 1))
            t_after_k[k] = P.mark("tensor", "T")
        v_modmm = t_after_k[KT - 1]

        P.wait("scalar", "T", v_modmm)
        for nch in range(3):
            P.emit("scalar", "copy", modloc[:, 512 * nch:512 * (nch + 1)], mps[nch])
        v_modev = P.mark("scalar", "S")
        pAl.release("S", v_modev)
        pBl.release("S", v_modev)
        pCl.release("S", v_modev)

        P.wait("gpsimd", "S", v_modev)
        d_mi = P.dma("gpsimd", modg_i[:, :], modloc, sem="Lmi")
        P.wait("gpsimd", "Lmi", d_mi)
        v_cc1 = P.emit("gpsimd", "collective_compute", "AllGather", ALU.bypass,
                       replica_groups=[list(range(NCORE))],
                       ins=[modg_i[:, :]], outs=[modg_o[:, :]], inc="C")

        modflat = modg_o.rearrange("a b -> (a b)")
        bc_state = {"i": 0, "last_mm": {0: 0, 1: 0}}
        bb_state = {"last_mm": 0}

        def bcast_batch(dsts, chunk0, row, gates=(), row_math=None, add1=False):
            n = len(dsts)
            P.wait("sync", "C", v_cc1)
            for sem, val in gates:
                P.wait("sync", sem, val)
            P.wait("sync", "T", bb_state["last_mm"])
            d = P.dma("sync", row[:, 0:512 * n],
                      modflat[512 * chunk0:512 * (chunk0 + n)].unsqueeze(0),
                      sem="Lch1")
            sem_mm, val_mm = "Lch1", d
            if row_math is not None:
                P.wait("vector", sem_mm, d)
                row_math(row[:, 0:512 * n])
                val_mm = P.mark("vector", "V")
                sem_mm = "V"
            P.wait("tensor", sem_mm, val_mm)
            P.wait("tensor", "V", v_ones)
            evs = []
            for i in range(n):
                pres = pBh if i % 2 == 0 else pAh
                pres.twait()
                P.emit("tensor", "matmul", pres.ap, ones[0:1, :],
                       row[:, 512 * i:512 * (i + 1)], start=True, stop=True)
                mmv = P.mark("tensor", "T")
                P.wait("scalar", "T", mmv)
                if add1:
                    P.emit("scalar", "add", dsts[i], pres.ap, 1.0)
                else:
                    P.emit("scalar", "copy", dsts[i], pres.ap)
                ev = P.mark("scalar", "S")
                pres.release("S", ev)
                evs.append(ev)
            bb_state["last_mm"] = P.ctr["T"]
            return evs[-1]

        def gamath_w(row, scr_late=False):
            for c0 in range(0, row.shape[1], 1024):
                half = row[:, c0:c0 + 1024]
                scr = (rtmprow_w2 if scr_late else rtmprow_w)[:, 0:1024]
                P.emit("vector", "tensor_scalar", scr, half, 0.0, None,
                       ALU.is_ge)
                P.emit("vector", "tensor_scalar", scr, scr, 2.0, -1.0,
                       ALU.mult, ALU.add)
                P.emit("vector", "tensor_tensor", half, half, scr, ALU.mult)
                P.emit("vector", "tensor_scalar", half, half, 0.25, None,
                       ALU.max)
                P.emit("vector", "tensor_tensor", half, half, scr, ALU.mult)

        def gamath_w_late(row):
            gamath_w(row, scr_late=True)

        def rgmath_w(row):
            gamath_w(row, scr_late=True)
            P.emit("vector", "reciprocal", row, row)

        def bcast512(dst, chunk_idx, add1=False, row_math=None):
            i = bc_state["i"] % 2
            bc_state["i"] += 1
            row = chrow[i]
            P.wait("sync", "C", v_cc1)
            P.wait("sync", "T", bc_state["last_mm"][i])
            d = P.dma("sync", row,
                      modflat[512 * chunk_idx:512 * (chunk_idx + 1)].unsqueeze(0),
                      sem="Lch%d" % i)
            sem_mm, val_mm = "Lch%d" % i, d
            if row_math is not None:
                P.wait("vector", sem_mm, d)
                row_math(row)
                val_mm = P.mark("vector", "V")
                sem_mm = "V"
            P.wait("tensor", sem_mm, val_mm)
            P.wait("tensor", "V", v_ones)
            pBh.twait()
            P.emit("tensor", "matmul", pBh.ap, ones[0:1, :], row,
                   start=True, stop=True)
            mmv = P.mark("tensor", "T")
            bc_state["last_mm"][i] = mmv
            P.wait("scalar", "T", mmv)
            if add1:
                P.emit("scalar", "add", dst, pBh.ap, 1.0)
            else:
                P.emit("scalar", "copy", dst, pBh.ap)
            ev = P.mark("scalar", "S")
            pBh.release("S", ev)
            return ev

        def gamath(row):
            P.emit("vector", "tensor_scalar", rtmprow, row, 0.0, None, ALU.is_ge)
            P.emit("vector", "tensor_scalar", rtmprow, rtmprow, 2.0, -1.0,
                   ALU.mult, ALU.add)
            P.emit("vector", "tensor_tensor", row, row, rtmprow, ALU.mult)
            P.emit("vector", "tensor_scalar", row, row, 0.25, None, ALU.max)
            P.emit("vector", "tensor_tensor", row, row, rtmprow, ALU.mult)

        def rgmath(row):
            gamath(row)
            P.emit("vector", "reciprocal", row, row)

        bc0 = Fv(O_BC0, 2048)
        bc1 = Fv(O_BC1, 2048)

        xtile = [Fv(O_F2, 2048), Fv(O_F2 + 4096, 2048),
                 Fv(O_QKOB, 2048), Fv(O_QKOB + 4096, 2048)]
        d_x = {t: P.dma("sync", xtile[t], x_t[128 * t:128 * (t + 1), :],
                        sem="Lx%d" % t) for t in range(4)}

        d_al = P.dma("sync", chrow[0][:, 0:1], alpha2, sem="Lch0")
        P.wait("tensor", "Lch0", d_al)
        P.wait("tensor", "V", v_ones)
        pBh.twait()
        P.emit("tensor", "matmul", pB_t[:, 512:513], ones[0:1, :],
               chrow[0][:, 0:1], start=True, stop=True)
        v_mm = P.mark("tensor", "T")
        bc_state["last_mm"][0] = v_mm
        P.wait("scalar", "T", v_mm)
        P.emit("scalar", "copy", albB, pB_t[:, 512:513])
        v_alb = P.mark("scalar", "S")
        pBh.release("S", v_alb)

        carow = Fv(O_VB + 9216, 2048, 0, 1)
        bcast_batch([bc0[:, 512 * j:512 * (j + 1)] for j in range(4)], 4,
                    carow, add1=True)
        v_casa = bcast_batch([bc1[:, 512 * j:512 * (j + 1)] for j in range(4)],
                             0, carow)

        htmp = [Fv(O_F1, 2048), Fv(O_F1 + 4096, 2048)]
        hbuf = [B(O_F1 + 8192, 2048), B(O_F1 + 10240, 2048)]
        sq_scr = Fv(O_F1 + 12288, 2048)
        x_free_at = {}
        v_h = {}
        tpos_res = {0: pDl, 1: pCl}
        tpos_bf = {0: pT, 1: pC_t[:, 0:512].bitcast(BF16)}

        v_h = {}
        x_free_at = {}
        v_FG = {}

        def p1_A(t):
            xt = xtile[t % 4]
            xsem = "Lx%d" % (t % 4)
            P.wait("vector", xsem, d_x[t])
            P.emit("vector", "reduce_sum", s_mean[:, t:t + 1], xt, axis=AX.X)
            P.wait("scalar", xsem, d_x[t])
            P.emit("scalar", "activation", sq_scr, xt, AF.Square,
                   accum_out=s_ssq[:, t:t + 1])
            return P.mark("scalar", "S")

        sA_done = {}
        sA_done[0] = p1_A(0)
        sA_done[1] = p1_A(1)
        FGd = {}
        for t in range(NT):
            xt, ht, hb = xtile[t % 4], htmp[t % 2], hbuf[t % 2]
            P.wait("vector", "S", sA_done[t])
            P.emit("vector", "tensor_scalar", s_mean[:, t:t + 1],
                   s_mean[:, t:t + 1], 1.0 / D, None, ALU.mult)
            P.emit("vector", "tensor_tensor", s_t1[:, t:t + 1],
                   s_mean[:, t:t + 1], s_mean[:, t:t + 1], ALU.mult)
            P.emit("vector", "tensor_scalar", s_inv[:, t:t + 1],
                   s_ssq[:, t:t + 1], 1.0 / D, None, ALU.mult)
            P.emit("vector", "tensor_tensor", s_inv[:, t:t + 1],
                   s_inv[:, t:t + 1], s_t1[:, t:t + 1], ALU.subtract)
            P.emit("vector", "tensor_scalar", s_inv[:, t:t + 1],
                   s_inv[:, t:t + 1], EPS, None, ALU.add)
            P.emit("vector", "reciprocal", s_inv[:, t:t + 1], s_inv[:, t:t + 1])
            vB = P.mark("vector", "V")
            P.wait("scalar", "V", vB)
            P.emit("scalar", "sqrt", s_inv[:, t:t + 1], s_inv[:, t:t + 1])
            sC = P.mark("scalar", "S")
            P.wait("vector", "S", sC)
            P.emit("vector", "tensor_scalar", s_nmi[:, t:t + 1],
                   s_mean[:, t:t + 1], s_inv[:, t:t + 1], -1.0,
                   ALU.mult, ALU.mult)
            vD = P.mark("vector", "V")
            P.wait("scalar", "V", vD)
            P.wait("scalar", "V", v_FG.get(t - 2, 0))
            P.emit("scalar", "activation", ht, xt, AF.Identity,
                   bias=s_nmi[:, t:t + 1], scale=s_inv[:, t:t + 1])
            x_free_at[t] = P.mark("scalar", "S")
            if t + 4 < NT:
                P.wait("sync", "S", x_free_at[t])
                d_x[t + 4] = P.dma("sync", xtile[t % 4],
                                   x_t[128 * (t + 4):128 * (t + 5), :],
                                   sem="Lx%d" % (t % 4))
            if t + 2 < NT:
                sA_done[t + 2] = p1_A(t + 2)

            def FG_t(t=t, ht=ht, hb=hb):
                P.wait("vector", "S", x_free_at[t])
                P.wait("vector", "S", v_casa)
                P.emit("vector", "tensor_tensor", ht, ht, bc0, ALU.mult)
                P.emit("vector", "tensor_tensor", hb, ht, bc1, ALU.add)
                v_h[t] = P.mark("vector", "V")
                P.wait("tensor", "V", v_h[t])
                P.wait("tensor", "Lc", d_const)
                for grp in range(2):
                    res = tpos_res[grp]
                    pbf = tpos_bf[grp]
                    res.twait()
                    for j in range(8):
                        kt2 = grp * 8 + j
                        P.emit("tensor", "transpose",
                               pbf[:, 128 * j:128 * (j + 1)],
                               hb[:, 128 * kt2:128 * (kt2 + 1)], id_sb)
                    vt = P.mark("tensor", "T")
                    P.wait("vector", "T", vt)
                    P.emit("vector", "tensor_copy",
                           hT3[:, grp * 8:(grp + 1) * 8, 128 * t:128 * (t + 1)],
                           pbf.rearrange("p (a b) -> p a b", b=128))
                    vv = P.mark("vector", "V")
                    res.release("V", vv)
                v_FG[t] = vv

            FGd[t] = FG_t
            if t >= 1:
                FGd[t - 1]()
        FGd[NT - 1]()
        v_hT_all = P.mark("vector", "V")

        mring = [B(O_F2 + 2048 * s, 2048) for s in range(4)]
        mm_free = {}
        rope_done = {}
        slot_i = 0
        v_rope_prev = 0
        d_tab_prev = 0

        tabc2 = [[Fv(O_F1, 1024), Fv(O_F1 + 2048, 1024)],
                 [Fv(O_F1 + 8192, 1024), Fv(O_F1 + 10240, 1024)]]
        rsc2 = [[Fv(O_F1 + 4096, 1024), Fv(O_F1 + 6144, 1024)],
                [Fv(O_F1 + 12288, 1024), Fv(O_F1 + 14336, 1024)]]
        hi = 0
        for side in range(2):
            wsrc = wq_m if side == 0 else wk_m
            for h in range(H):
                par = hi % 2
                tabc = tabc2[par]
                rsc = rsc2[par]
                q1t, q2t = (pA_t, pB_t) if par == 0 else (pC_t, pD_t)
                pun = [(pAl, pAh), (pBl, pBh)] if par == 0 else [(pCl, pCh), (pDl, pDh)]
                hi += 1
                P.wait("sync", "V", v_hT_all if hi <= 2 else rope_done.get(hi - 3, 0))
                d_cos = P.dma("sync", tabc[0], tabs[side, h, 0],
                              sem="Ltab%d" % par)
                d_sin = P.dma("sync", tabc[1], tabs[side, h, 1],
                              sem="Ltab%d" % par)
                vps = []
                for half in range(2):
                    m = 2 * h + half
                    s4 = slot_i % 4
                    P.wait("gpsimd", "T", mm_free.get(s4, 0))
                    if slot_i == 0:
                        P.wait("gpsimd", "S", x_free_at[NT - 1])
                    dsl = P.dma("gpsimd",
                                mring[s4].rearrange("p (a m) -> p a m", m=128),
                                wsrc[m].rearrange("(a p) m -> p a m", p=128),
                                sem="Lm%d" % s4)
                    ptgt = pun[half]
                    P.wait("tensor", "Lm%d" % s4, dsl)
                    P.wait("tensor", "V", v_hT_all)
                    for tch in range(2):
                        ptgt[tch].twait()
                        for kt2 in range(KT):
                            P.emit("tensor", "matmul", ptgt[tch].ap,
                                   mring[s4][:, 128 * kt2:128 * (kt2 + 1)],
                                   hT(kt2, 512 * tch, 512),
                                   start=(kt2 == 0), stop=(kt2 == KT - 1))
                    vp = P.mark("tensor", "T")
                    mm_free[s4] = vp
                    vps.append(vp)
                    slot_i += 1
                P.wait("vector", "T", vps[-1])
                P.wait("vector", "Ltab%d" % par, d_sin)
                q1 = q1t[:, 0:1024]
                q2 = q2t[:, 0:1024]
                segd = qseg if side == 0 else kseg
                P.emit("vector", "tensor_tensor", rsc[0], q1, tabc[0], ALU.mult)
                P.emit("vector", "tensor_tensor", rsc[1], q2, tabc[1], ALU.mult)
                for p in range(NT):
                    P.emit("vector", "tensor_tensor", segd(p, h, 0),
                           rsc[0][:, 128 * p:128 * (p + 1)],
                           rsc[1][:, 128 * p:128 * (p + 1)], ALU.subtract)
                P.emit("vector", "tensor_tensor", rsc[0], q2, tabc[0], ALU.mult)
                P.emit("vector", "tensor_tensor", rsc[1], q1, tabc[1], ALU.mult)
                for p in range(NT):
                    P.emit("vector", "tensor_tensor", segd(p, h, 1),
                           rsc[0][:, 128 * p:128 * (p + 1)],
                           rsc[1][:, 128 * p:128 * (p + 1)], ALU.add)
                v_rope_prev = P.mark("vector", "V")
                rope_done[hi - 1] = v_rope_prev
                for pp in pun[0] + pun[1]:
                    pp.release("V", v_rope_prev)
        v_rope_all = v_rope_prev

        vgslot = [B(O_F1 + 8192 * s, 8192) for s in range(2)]
        vg_free = {}
        vg_i = 0
        pcycle = [pAl, pAh, pBl, pBh]
        pc_i = 0
        v_g_ev = {}
        v_v_ev = {}
        P.wait("scalar", "Lmi", d_mi)

        def vg_proj(which):
            nonlocal vg_i, pc_i
            wsrc = wv_n if which == 0 else wg_n
            for nch in range(4):
                s2 = vg_i % 2
                if vg_i < 2:
                    P.wait("gpsimd", "V", v_rope_all)
                else:
                    P.wait("gpsimd", "T", vg_free[s2])
                dsl = P.dma("gpsimd",
                            vgslot[s2].rearrange("p (a m) -> p a m", m=512),
                            wsrc[nch].rearrange("(a p) m -> p a m", p=128),
                            sem="Lvg%d" % s2)
                P.wait("tensor", "Lvg%d" % s2, dsl)
                for tt in range(NT):
                    pres = pcycle[pc_i % 4]
                    pc_i += 1
                    pres.twait()
                    for kt2 in range(KT):
                        P.emit("tensor", "matmul", pres.ap,
                               hT(kt2, 128 * tt, 128),
                               vgslot[s2][:, 512 * kt2:512 * (kt2 + 1)],
                               start=(kt2 == 0), stop=(kt2 == KT - 1))
                    vmm = P.mark("tensor", "T")
                    if tt == NT - 1:
                        vg_free[s2] = vmm
                    P.wait("scalar", "T", vmm)
                    if which == 0:
                        P.emit("scalar", "copy",
                               vtile(tt, 512 * nch, 512), pres.ap)
                        ev = P.mark("scalar", "S")
                        pres.release("S", ev)
                        v_v_ev[(nch, tt)] = ev
                    else:
                        P.emit("scalar", "activation",
                               gtile(tt, 512 * nch, 512), pres.ap, AF.Silu)
                        ev = P.mark("scalar", "S")
                        pres.release("S", ev)
                        P.wait("vector", "S", max(ev, v_gnB))
                        P.emit("vector", "tensor_tensor",
                               gtile(tt, 512 * nch, 512),
                               gtile(tt, 512 * nch, 512), gnormB, ALU.mult)
                        v_g_ev[(nch, tt)] = P.mark("vector", "V")
                vg_i += 1

        vg_proj(0)

        def Mpar(dt8, par, n=512):
            return B((O_M if par % 2 == 0 else O_F2) + dt8 * 512, n)

        P.wait("vector", "Lch0", P.ctr["Lch0"])
        P.wait("vector", "Lch1", P.ctr["Lch1"])
        P.wait("vector", "T", max(bc_state["last_mm"].values()))
        P.emit("vector", "memset", B(O_M, 4096), 0.0)
        v_m0 = P.mark("vector", "V")
        kv_cycle = [pAh, pBh, pCh, pDh]
        kv_i = 0
        v_Lupd = v_m0
        for p in range(NT):
            pDl.twait()
            P.wait("tensor", "V", v_rope_all)
            for dt8 in range(8):
                h, half = dt8 // 2, dt8 % 2
                P.emit("tensor", "transpose", pT[:, 128 * dt8:128 * (dt8 + 1)],
                       kseg(p, h, half), id_sb)
            vt = P.mark("tensor", "T")
            P.wait("vector", "T", vt)
            for h in range(H):
                P.emit("vector", "tensor_scalar",
                       ktok[:, 256 * h:256 * (h + 1)],
                       pT[:, 256 * h:256 * (h + 1)], G128[h], None, ALU.mult)
            v_kt = P.mark("vector", "V")
            pDl.release("V", v_kt)
            for dt8 in range(8):
                h, half = dt8 // 2, dt8 % 2
                pres = kv_cycle[kv_i % 4]
                kv_i += 1
                pres.twait()
                P.wait("tensor", "V", v_kt)
                P.wait("tensor", "S", v_v_ev[(h, p)])
                P.emit("tensor", "matmul", pres.ap,
                       ktok[:, 128 * dt8:128 * (dt8 + 1)],
                       vtile(p, 512 * h, 512), start=True, stop=True)
                vmm = P.mark("tensor", "T")
                P.wait("vector", "T", vmm)
                P.wait("vector", "V", v_m0)
                P.emit("vector", "scalar_tensor_tensor", Mpar(dt8, 0),
                       Mpar(dt8, 0), G128[h], pres.ap, ALU.mult, ALU.add)
                v_Lupd = P.mark("vector", "V")
                pres.release("V", v_Lupd)
        P.wait("gpsimd", "V", v_Lupd)
        d_li = P.dma("gpsimd", lg_i[:, :], B(O_M, 4096), sem="Lli")
        P.wait("gpsimd", "Lli", d_li)
        v_cc2 = P.emit("gpsimd", "collective_compute", "AllGather", ALU.bypass,
                       replica_groups=[list(range(NCORE))],
                       ins=[lg_i[:, :]], outs=[lg_o[:, :]], inc="C")

        gnormB = Fv(O_F2, 512)
        P.wait("sync", "V", v_Lupd)
        d_gn2 = P.dma("sync", chrow[0], gnorm, sem="Lch0")
        P.wait("tensor", "Lch0", d_gn2)
        pBh.twait()
        P.emit("tensor", "matmul", pBh.ap, ones[0:1, :], chrow[0],
               start=True, stop=True)
        v_mm = P.mark("tensor", "T")
        bc_state["last_mm"][0] = v_mm
        P.wait("scalar", "T", v_mm)
        P.emit("scalar", "copy", gnormB, pBh.ap)
        v_gnB = P.mark("scalar", "S")
        pBh.release("S", v_gnB)

        vg_proj(1)
        v_vg_all = v_g_ev[(3, NT - 1)]
        v_gp = v_vg_all

        woslot = [B(O_VB + 8192 * g, 8192) for g in range(4)]
        d_wo = {}
        for g in (2, 3):
            P.wait("gpsimd", "T", vg_free[g - 2])
            d_wo[g] = P.dma("gpsimd",
                            woslot[g].rearrange("p (a m) -> p a m", m=2048),
                            wo_g[g].rearrange("(a p) m -> p a m", p=128),
                            sem="Lwo")

        garow = Fv(O_HT + 10240, 2048, 0, 1)
        v_gaB = bcast_batch([bc0[:, 512 * j:512 * (j + 1)] for j in range(4)],
                            8, garow, gates=(("T", vg_free[1]),),
                            row_math=gamath_w)
        v_sbB = bcast_batch([bc1[:, 512 * j:512 * (j + 1)] for j in range(4)],
                            12, garow, gates=(("T", vg_free[1]),))

        lrecv = [B(O_F2 + 4096 * s, 4096) for s in range(2)]
        sacc = Fv(O_HT, 4096)
        d_lr = {}
        v_lr_used = {}
        P.wait("gpsimd", "C", v_cc2)
        P.wait("sync", "C", v_cc2)
        P.wait("gpsimd", "V", v_gp)
        P.wait("sync", "V", v_gp)
        d_lr[0] = P.dma("gpsimd", lrecv[0], lg_o[0:128, :], sem="Llr0")
        for j in range(NCORE):
            if j + 1 < NCORE:
                q = "gpsimd" if (j + 1) % 2 == 0 else "sync"
                P.wait(q, "V", v_lr_used.get(j - 1, 0))
                d_lr[j + 1] = P.dma(q, lrecv[(j + 1) % 2],
                                    lg_o[128 * (j + 1):128 * (j + 2), :],
                                    sem="Llr%d" % ((j + 1) % 2))
            P.wait("vector", "Llr%d" % (j % 2), d_lr[j])
            if j == NCORE - 1:
                P.wait("vector", "Lli", d_li)
            for h in range(H):
                wcol = sinw_sb[:, 8 * h + j:8 * h + j + 1]
                srcw = lrecv[j % 2][:, 1024 * h:1024 * (h + 1)]
                acc4 = sacc[:, 1024 * h:1024 * (h + 1)]
                if j == 0:
                    P.emit("vector", "tensor_scalar", acc4, srcw,
                           wcol, None, ALU.mult)
                elif j == NCORE - 1:
                    P.emit("vector", "scalar_tensor_tensor",
                           Mpar(2 * h, 0, 1024), srcw, wcol, acc4,
                           ALU.mult, ALU.add)
                else:
                    P.emit("vector", "scalar_tensor_tensor", acc4, srcw,
                           wcol, acc4, ALU.mult, ALU.add)
            v_lr_used[j] = P.mark("vector", "V")
        v_sin = v_lr_used[NCORE - 1]

        attn_sb2 = [attn_sb, attn_sbB]
        v_Mupd = {-1: v_sin}
        v_ob = {}
        at_cycle = [pAl, pBl]
        kv_cycle2 = [pAh, pBh]
        o_cycle = [pCl, pCh, pDh]
        odo = {}
        for p in range(NT):
            rpar, wpar = p % 2, (p + 1) % 2
            pDl.twait()
            for dt8 in range(8):
                h, half = dt8 // 2, dt8 % 2
                P.emit("tensor", "transpose", pT[:, 128 * dt8:128 * (dt8 + 1)],
                       kseg(p, h, half), id_sb)
            vt = P.mark("tensor", "T")
            P.wait("vector", "T", vt)
            for h in range(H):
                P.emit("vector", "tensor_scalar",
                       ktok[:, 256 * h:256 * (h + 1)],
                       pT[:, 256 * h:256 * (h + 1)], G128[h], None, ALU.mult)
            v_kt = P.mark("vector", "V")
            pDl.release("V", v_kt)
            amarks = {}
            for h in range(H):
                at = at_cycle[h % 2]
                at.twait()
                P.emit("tensor", "matmul", at.ap[:, 0:128], kseg(p, h, 0),
                       qseg(p, h, 0), start=True, stop=False)
                P.emit("tensor", "matmul", at.ap[:, 0:128], kseg(p, h, 1),
                       qseg(p, h, 1), start=False, stop=True)
                v_at = P.mark("tensor", "T")
                P.wait("vector", "T", v_at)
                P.emit("vector", "tensor_tensor", attn_sb4[(p % 2) * 4 + h],
                       at.ap[:, 0:128], cau_sb, ALU.mult)
                amarks[h] = P.mark("vector", "V")
                at.release("V", amarks[h])

            def emit_o(p=p, rpar=rpar, amarks=amarks):
                P.wait("tensor", "V", v_Mupd[p - 1])
                for h in range(H):
                    op = o_cycle[h % 3]
                    op.twait()
                    P.wait("tensor", "V", amarks[h])
                    P.emit("tensor", "matmul", op.ap, attn_sb4[(p % 2) * 4 + h],
                           vtile(p, 512 * h, 512), start=True, stop=False)
                    P.emit("tensor", "matmul", op.ap, qseg(p, h, 0),
                           Mpar(2 * h, rpar), start=False, stop=False)
                    P.emit("tensor", "matmul", op.ap, qseg(p, h, 1),
                           Mpar(2 * h + 1, rpar), start=False, stop=True)
                    v_omm = P.mark("tensor", "T")
                    P.wait("scalar", "T", v_omm)
                    P.emit("scalar", "copy", obv(p, 512 * h, 512), op.ap)
                    v_ob[(p, h)] = P.mark("scalar", "S")
                    op.release("S", v_ob[(p, h)])

            odo[p] = emit_o
            if p >= 1:
                odo[p - 1]()
            vm = None
            for dt8 in range(8):
                h = dt8 // 2
                pres = kv_cycle2[dt8 % 2]
                pres.twait()
                P.wait("tensor", "V", v_kt)
                if p == 0:
                    P.wait("tensor", "S", v_v_ev[(h, p)])
                P.emit("tensor", "matmul", pres.ap,
                       ktok[:, 128 * dt8:128 * (dt8 + 1)],
                       vtile(p, 512 * h, 512), start=True, stop=True)
                vkv = P.mark("tensor", "T")
                P.wait("vector", "T", vkv)
                P.emit("vector", "scalar_tensor_tensor", Mpar(dt8, wpar),
                       Mpar(dt8, rpar), G128[h], pres.ap, ALU.mult, ALU.add)
                vm = P.mark("vector", "V")
                pres.release("V", vm)
            v_Mupd[p] = vm
        odo[NT - 1]()

        def tap(name, off, ncols, sem, val):
            if name in dbg_out:
                P.wait("sync", sem, val)
                P.wait("sync", "Ly", P.ctr["Ly"])
                d = P.dma("sync", dbg_out[name], B(off, ncols), sem="Ly")
                for eng in ("vector", "scalar", "tensor", "gpsimd"):
                    P.wait(eng, "Ly", d)

        tap("hT", O_HT, 16384, "V", v_hT_all)
        tap("gp", O_GP, 16384, "V", v_gp)
        v_og = {}
        RSb2 = Fv(O_HT + 9216, 512)

        def p3b_sq(p):
            if p == 0:
                P.wait("scalar", "V", v_gp)
            P.wait("scalar", "V", v_og.get(p - 2, 0))
            for h in range(H):
                P.wait("scalar", "S", v_ob[(p, h)])
                P.emit("scalar", "activation", RSb2,
                       obv(p, 512 * h, 512), AF.Square,
                       accum_out=s_oiv[:, (p % 2) * 4 + h:(p % 2) * 4 + h + 1])
            return P.mark("scalar", "S")

        sqd = {0: p3b_sq(0)}
        for p in range(NT):
            c4 = s_oiv[:, (p % 2) * 4:(p % 2) * 4 + 4]
            if p + 1 < NT:
                sqd[p + 1] = p3b_sq(p + 1)
            P.wait("vector", "S", sqd[p])
            P.emit("vector", "tensor_scalar", c4, c4, 1.0 / DV, EPS,
                   ALU.mult, ALU.add)
            P.emit("vector", "reciprocal", c4, c4)
            vv = P.mark("vector", "V")
            P.wait("scalar", "V", vv)
            P.emit("scalar", "sqrt", c4, c4)
            vs2 = P.mark("scalar", "S")
            P.wait("vector", "S", vs2)
            for h in range(H):
                P.emit("vector", "scalar_tensor_tensor", obv(p, 512 * h, 512),
                       obv(p, 512 * h, 512),
                       s_oiv[:, (p % 2) * 4 + h:(p % 2) * 4 + h + 1],
                       gtile(p, 512 * h, 512), ALU.mult, ALU.mult)
            v_og[p] = P.mark("vector", "V")
        v_og_all = v_og[NT - 1]


        wtmp = Fv(O_F2, 512)
        v_ret_done = v_Mupd[NT - 1]
        for g in (0, 1):
            P.wait("gpsimd", "S", v_ob[(NT - 1, H - 1)])
            d_wo[g] = P.dma("gpsimd",
                            woslot[g].rearrange("p (a m) -> p a m", m=2048),
                            wo_g[g].rearrange("(a p) m -> p a m", p=128),
                            sem="Lwo")

        d_x2 = {}
        P.wait("sync", "V", v_og_all)
        for tt in range(NT):
            d_x2[tt] = P.dma("sync", accv(tt, 0, 2048),
                             x_t[128 * tt:128 * (tt + 1), :], sem="Lxa")
        d_x2_all = d_x2[NT - 1]

        v_wo_used = {}
        wo_cycle = [pAl, pAh, pBl, pBh]
        wo_i = 0
        v_acc_wo = {}
        for tt in range(NT):
            P.wait("tensor", "V", v_og[tt])
            for grp in range(2):
                pDl.twait()
                for j in range(8):
                    kt2 = grp * 8 + j
                    P.emit("tensor", "transpose", pT[:, 128 * j:128 * (j + 1)],
                           obv(tt, 128 * kt2, 128), id_sb)
                vt = P.mark("tensor", "T")
                P.wait("vector", "T", vt)
                P.emit("vector", "tensor_copy",
                       ogT[:, 1024 * grp:1024 * (grp + 1)], pT)
                v_tp = P.mark("vector", "V")
                pDl.release("V", v_tp)
            P.wait("tensor", "V", v_tp)
            for g in range(4):
                P.wait("tensor", "Lwo", d_wo[1])
                for nch in range(4):
                    pres = wo_cycle[wo_i % 4]
                    wo_i += 1
                    pres.twait()
                    for kg in range(4):
                        P.emit("tensor", "matmul", pres.ap,
                               ogT[:, 128 * (g * 4 + kg):128 * (g * 4 + kg + 1)],
                               woslot[g][:, 2048 * kg + 512 * nch:2048 * kg + 512 * (nch + 1)],
                               start=(kg == 0), stop=(kg == 3))
                    vmm = P.mark("tensor", "T")
                    if nch == 3 and tt == NT - 1:
                        v_wo_used[g] = vmm
                    P.wait("vector", "T", vmm)
                    P.wait("vector", "Lxa", d_x2_all)
                    if tt == 0 and g == 0 and nch == 0:
                        P.wait("vector", "S", v_gaB)
                    P.emit("vector", "tensor_tensor", wtmp, pres.ap,
                           bc0[:, 512 * nch:512 * (nch + 1)], ALU.mult)
                    P.emit("vector", "tensor_tensor",
                           accv(tt, 512 * nch, 512),
                           accv(tt, 512 * nch, 512), wtmp, ALU.add)
                    va = P.mark("vector", "V")
                    pres.release("V", va)
            v_acc_wo[tt] = va

        tap("og", O_QKOB, 16384, "V", v_og_all)
        if "x2" in dbg_out:
            P.wait("sync", "V", v_acc_wo[NT - 1])
            P.wait("sync", "Ly", P.ctr["Ly"])
            d = P.dma("sync", dbg_out["x2"], Fv(0, 16384), sem="Ly")
            for eng in ("vector", "scalar", "tensor", "gpsimd"):
                P.wait(eng, "Ly", d)
        cbrow = Fv(O_M, 2048, 0, 1)
        v_cbsb = bcast_batch([bc0[:, 512 * j:512 * (j + 1)] for j in range(4)],
                             16, cbrow, gates=(("V", v_acc_wo[NT - 1]),),
                             add1=True)

        gate_bf = B(O_F2 + 2048, 512)
        P.wait("gpsimd", "T", v_wo_used[3])
        d_gate = P.dma("gpsimd",
                       gate_bf.rearrange("p (a m) -> p a m", m=32),
                       gatet.rearrange("(a p) m -> p a m", p=128), sem="Lgate")

        v_h2T = {}
        v_h2_done = {}

        LGb = Fv(O_F2 + 4608, 256)
        RPb = Fv(O_F2 + 5632, 256)
        RCb = Fv(O_F2 + 6656, 256)
        RSb = Fv(O_M + 2048, 256)
        MKS = Fv(O_M + 3072, 64)
        CNTS = Fv(O_M + 3200, 64)
        R8a = Fv(O_M + 3328, 8)
        R8b = Fv(O_M + 3344, 8)
        R8c = Fv(O_M + 3360, 8)

        def ln2_A(t):
            xt = accv(t, 0, 2048)
            P.wait("vector", "V", v_acc_wo[t])
            P.emit("vector", "reduce_sum", s_mean[:, t:t + 1], xt, axis=AX.X)
            P.wait("scalar", "V", v_acc_wo[t])
            if t == 0:
                P.wait("scalar", "T", v_wo_used[3])
            P.emit("scalar", "activation", sq_scr, xt, AF.Square,
                   accum_out=s_ssq[:, t:t + 1])
            return P.mark("scalar", "S")

        sA2 = {0: ln2_A(0)}
        sA2[1] = ln2_A(1)
        for t in range(NT):
            ht, hb = htmp[t % 2], hbuf[t % 2]
            xt = accv(t, 0, 2048)
            P.wait("vector", "S", sA2[t])
            P.emit("vector", "tensor_scalar", s_mean[:, t:t + 1],
                   s_mean[:, t:t + 1], 1.0 / D, None, ALU.mult)
            P.emit("vector", "tensor_tensor", s_t1[:, t:t + 1],
                   s_mean[:, t:t + 1], s_mean[:, t:t + 1], ALU.mult)
            P.emit("vector", "tensor_scalar", s_inv[:, t:t + 1],
                   s_ssq[:, t:t + 1], 1.0 / D, None, ALU.mult)
            P.emit("vector", "tensor_tensor", s_inv[:, t:t + 1],
                   s_inv[:, t:t + 1], s_t1[:, t:t + 1], ALU.subtract)
            P.emit("vector", "tensor_scalar", s_inv[:, t:t + 1],
                   s_inv[:, t:t + 1], EPS, None, ALU.add)
            P.emit("vector", "reciprocal", s_inv[:, t:t + 1], s_inv[:, t:t + 1])
            vB = P.mark("vector", "V")
            P.wait("scalar", "V", vB)
            P.emit("scalar", "sqrt", s_inv[:, t:t + 1], s_inv[:, t:t + 1])
            sC = P.mark("scalar", "S")
            P.wait("vector", "S", sC)
            P.emit("vector", "tensor_scalar", s_nmi[:, t:t + 1],
                   s_mean[:, t:t + 1], s_inv[:, t:t + 1], -1.0,
                   ALU.mult, ALU.mult)
            vD = P.mark("vector", "V")
            P.wait("scalar", "V", vD)
            P.emit("scalar", "activation", ht, xt, AF.Identity,
                   bias=s_nmi[:, t:t + 1], scale=s_inv[:, t:t + 1])
            v_aff = P.mark("scalar", "S")
            if t + 2 < NT:
                sA2[t + 2] = ln2_A(t + 2)
            P.wait("vector", "S", v_aff)
            P.wait("vector", "S", v_cbsb)
            P.wait("vector", "S", v_sbB)
            P.emit("vector", "tensor_tensor", ht, ht, bc0, ALU.mult)
            P.emit("vector", "tensor_tensor", hb, ht, bc1, ALU.add)
            v_h2 = P.mark("vector", "V")
            v_h2_done[t] = v_h2
            P.wait("tensor", "V", v_h2)
            for grp in range(2):
                pDl.twait()
                for j in range(8):
                    kt2 = grp * 8 + j
                    P.emit("tensor", "transpose", pT[:, 128 * j:128 * (j + 1)],
                           hb[:, 128 * kt2:128 * (kt2 + 1)], id_sb)
                vt = P.mark("tensor", "T")
                P.wait("vector", "T", vt)
                P.emit("vector", "tensor_copy",
                       h2T3[:, grp * 8:(grp + 1) * 8, 128 * t:128 * (t + 1)],
                       pT.rearrange("p (a b) -> p a b", b=128))
                vv = P.mark("vector", "V")
                pDl.release("V", vv)
            v_h2T[t] = vv
            pCl.twait()
            P.wait("tensor", "Lgate", d_gate)
            P.wait("tensor", "V", vv)
            for kt2 in range(KT):
                P.emit("tensor", "matmul", pC_t[:, 0:32],
                       h2T(kt2, 128 * t, 128), gate_bf[:, 32 * kt2:32 * (kt2 + 1)],
                       start=(kt2 == 0), stop=(kt2 == KT - 1))
            v_lg = P.mark("tensor", "T")
            v_lg_mm_last = v_lg
            P.wait("scalar", "T", v_lg)
            P.emit("scalar", "copy", LGb[:, 32 * t:32 * (t + 1)], pC_t[:, 0:32])
            v_lgc = P.mark("scalar", "S")
            pCl.release("S", v_lgc)
        v_lg_all = v_lgc

        tap("h2T", O_QKOB, 16384, "V", v_h2T[NT - 1])
        rgrow = Fv(O_F2, 2048, 0, 1)
        v_rgb = bcast_batch([bc1[:, 512 * j:512 * (j + 1)] for j in range(4)],
                            20, rgrow, gates=(("V", v_h2_done[NT - 1]),
                                              ("T", v_lg_mm_last)),
                            row_math=rgmath_w)
        P.wait("vector", "S", v_rgb)
        v_accr = {}
        for tt in range(NT):
            P.emit("vector", "tensor_tensor", accv(tt, 0, 2048),
                   accv(tt, 0, 2048), bc1, ALU.mult)
            v_accr[tt] = P.mark("vector", "V")

        def seg(ap):
            return ap.rearrange("p (a b) -> p a b", b=32)

        def col8(ap):
            return ap.rearrange("p (a b) -> p a b", b=1)

        def bc8(ap):
            return col8(ap).broadcast_to([128, 8, 32])

        def segred(dst8, src256, op):
            P.emit("vector", "tensor_reduce", col8(dst8), seg(src256),
                   axis=AX.X, op=op)

        MKS3 = MKS.rearrange("p (a b) -> p a b", b=8)
        P.wait("vector", "S", v_lg_all)
        segred(R8a, LGb, ALU.max)
        P.emit("vector", "tensor_tensor", seg(RPb), seg(LGb), bc8(R8a),
               ALU.subtract)
        v_sm = P.mark("vector", "V")
        P.wait("scalar", "V", v_sm)
        P.emit("scalar", "activation", RPb, RPb, AF.Exp)
        v_exp = P.mark("scalar", "S")
        P.wait("vector", "S", v_exp)
        segred(R8a, RPb, ALU.add)
        P.emit("vector", "reciprocal", R8a, R8a)
        P.emit("vector", "tensor_tensor", seg(RPb), seg(RPb), bc8(R8a),
               ALU.mult)
        P.emit("vector", "tensor_copy", RCb, RPb)
        for k in range(8):
            P.emit("vector", "tensor_reduce", MKS3[:, :, k:k + 1], seg(RCb),
                   axis=AX.X, op=ALU.max)
            P.emit("vector", "tensor_tensor", seg(RSb), seg(RCb),
                   MKS3[:, :, k:k + 1].broadcast_to([128, 8, 32]), ALU.is_equal)
            P.emit("vector", "scalar_tensor_tensor", RCb, RSb, -1e30, RCb,
                   ALU.mult, ALU.add)
        for k in range(8):
            P.emit("vector", "tensor_tensor", seg(RSb), seg(RPb),
                   MKS3[:, :, k:k + 1].broadcast_to([128, 8, 32]), ALU.is_ge)
            P.emit("vector", "tensor_reduce",
                   CNTS.rearrange("p (a b) -> p a b", b=8)[:, :, k:k + 1],
                   seg(RSb), axis=AX.X, op=ALU.add)
        P.emit("vector", "tensor_scalar", CNTS, CNTS, 8.0, None, ALU.is_ge)
        P.emit("vector", "tensor_tensor", CNTS, CNTS, MKS, ALU.mult)
        P.emit("vector", "tensor_reduce", col8(R8b),
               CNTS.rearrange("p (a b) -> p a b", b=8), axis=AX.X, op=ALU.max)
        P.emit("vector", "tensor_tensor", seg(RCb), seg(RPb), bc8(R8b),
               ALU.is_ge)
        P.emit("vector", "tensor_tensor", seg(RSb), seg(RPb), seg(RCb),
               ALU.mult)
        segred(R8b, RSb, ALU.add)
        P.emit("vector", "reciprocal", R8b, R8b)
        P.wait("vector", "S", v_alb)
        P.emit("vector", "tensor_scalar", R8b, R8b, albB, None, ALU.mult)
        P.emit("vector", "tensor_tensor", seg(RSb), seg(RSb), bc8(R8b),
               ALU.mult)
        P.emit("vector", "tensor_scalar", RSb, RSb, TAU, None, ALU.is_ge)
        segred(R8b, RSb, ALU.max)
        P.emit("vector", "tensor_scalar", R8b, R8b, 0.0, None, ALU.is_equal)
        segred(R8c, RPb, ALU.max)
        P.emit("vector", "tensor_tensor", seg(RCb), seg(RPb), bc8(R8c),
               ALU.is_equal)
        P.emit("vector", "tensor_tensor", seg(RCb), seg(RCb), bc8(R8b),
               ALU.mult)
        P.emit("vector", "tensor_tensor", RSb, RSb, RCb, ALU.max)
        P.emit("vector", "tensor_tensor", RSb, RPb, RSb, ALU.mult)
        segred(R8c, RSb, ALU.add)
        P.emit("vector", "tensor_scalar", R8c, R8c, 1e-12, None, ALU.max)
        P.emit("vector", "reciprocal", R8c, R8c)
        P.emit("vector", "tensor_tensor", seg(comb[:, 0:256]), seg(RSb),
               bc8(R8c), ALU.mult)
        v_comb_all = P.mark("vector", "V")
        if "comb" in dbg_out:
            P.wait("sync", "V", v_comb_all)
            P.wait("sync", "Ly", P.ctr["Ly"])
            d = P.dma("sync", dbg_out["comb"], comb, sem="Ly")
            for eng in ("vector", "scalar", "tensor", "gpsimd"):
                P.wait(eng, "Ly", d)

        gu_sp = [B(O_F1 + 4096 * s, 4096) for s in range(2)]
        heT = B(O_F1 + 8192, 6144)
        sgb = [B(O_F1 + 14336, 512), B(O_F1 + 14848, 512)]
        dn_s = [B(O_VB + 3072 * s, 3072) for s in range(4)]

        gu_used = {}
        dn_used = {}
        d_gu = {}
        d_dn = {}

        def prefetch(e, pr):
            idx = e * 6 + pr
            P.wait("gpsimd", "T", gu_used.get(idx - 2, 0))
            if idx == 0:
                P.wait("gpsimd", "T", v_wo_used[2])
                P.wait("gpsimd", "V", v_h2T[NT - 1])
            d_gu[idx] = P.dma("gpsimd",
                              gu_sp[idx % 2].rearrange("p (b a m) -> p b a m", b=2, m=128),
                              gup[e, pr].rearrange("b (a p) m -> p b a m", p=128),
                              sem="Lgu%d" % (idx % 2))

        def prefetch_dn(e, nch):
            idx = e * 4 + nch
            P.wait("gpsimd", "T", dn_used.get(idx - 4, 0))
            if idx == 0:
                P.wait("gpsimd", "T", v_wo_used[0])
                P.wait("gpsimd", "T", v_wo_used[1])
            d_dn[idx] = P.dma("gpsimd",
                              dn_s[idx % 4].rearrange("p (a m) -> p a m", m=512),
                              dnp[e, nch].rearrange("(a p) m -> p a m", p=128),
                              sem="Ldn%d" % (idx % 4))

        for pr in range(2):
            prefetch(0, pr)
        for nch in range(4):
            prefetch_dn(0, nch)

        v_last_acc = None
        pe_cycle = [(pAl, pAh), (pBl, pBh), (pCl, pCh), (pDl, pDh)]
        v_he = {}
        sg_busy = {}
        for e in range(NEXP):
            pe_i = 0
            for pr in range(6):
                idx = e * 6 + pr
                sl = gu_sp[idx % 2]
                P.wait("tensor", "Lgu%d" % (idx % 2), d_gu[idx])
                if e == 0 and pr == 0:
                    P.wait("tensor", "V", v_h2T[NT - 1])
                for tch in range(2):
                    pg, pu = pe_cycle[pe_i % 4]
                    pe_i += 1
                    pg.twait()
                    pu.twait()
                    P.wait("tensor", "V", v_he.get((e - 1, pr, tch), 0))
                    for kt2 in range(KT):
                        P.emit("tensor", "matmul", pg.ap,
                               sl[:, 128 * kt2:128 * (kt2 + 1)],
                               h2T(kt2, 512 * tch, 512),
                               start=(kt2 == 0), stop=(kt2 == KT - 1))
                    for kt2 in range(KT):
                        P.emit("tensor", "matmul", pu.ap,
                               sl[:, 2048 + 128 * kt2:2048 + 128 * (kt2 + 1)],
                               h2T(kt2, 512 * tch, 512),
                               start=(kt2 == 0), stop=(kt2 == KT - 1))
                    vmm = P.mark("tensor", "T")
                    if tch == 1:
                        gu_used[idx] = vmm
                        if pr + 2 < 6:
                            prefetch(e, pr + 2)
                        elif e + 1 < NEXP:
                            prefetch(e + 1, pr + 2 - 6)
                    P.wait("scalar", "T", vmm)
                    sgi = pe_i % 2
                    sg = sgb[sgi]
                    P.wait("scalar", "V", sg_busy.get(sgi, 0))
                    P.emit("scalar", "activation", sg, pg.ap, AF.Silu)
                    vsg = P.mark("scalar", "S")
                    P.wait("vector", "S", vsg)
                    P.emit("vector", "tensor_tensor",
                           heT[:, 1024 * pr + 512 * tch:1024 * pr + 512 * (tch + 1)],
                           sg, pu.ap, ALU.mult)
                    vhe = P.mark("vector", "V")
                    sg_busy[sgi] = vhe
                    v_he[(e, pr, tch)] = vhe
                    pg.release("V", vhe)
                    pu.release("V", vhe)
            dn_cycle = [pAl, pAh, pBl, pBh]
            pe_i = 0
            for tt in range(NT):
                for nch in range(4):
                    idx = e * 4 + nch
                    pres = dn_cycle[pe_i % 4]
                    pe_i += 1
                    pres.twait()
                    P.wait("tensor", "Ldn%d" % (idx % 4), d_dn[idx])
                    if tt == 0:
                        for pr in range(6):
                            P.wait("tensor", "V", v_he[(e, pr, 0)])
                            P.wait("tensor", "V", v_he[(e, pr, 1)])
                    for kt2 in range(6):
                        P.emit("tensor", "matmul", pres.ap,
                               heT[:, 1024 * kt2 + 128 * tt:1024 * kt2 + 128 * (tt + 1)],
                               dn_s[idx % 4][:, 512 * kt2:512 * (kt2 + 1)],
                               start=(kt2 == 0), stop=(kt2 == 5))
                    vmm = P.mark("tensor", "T")
                    if tt == NT - 1:
                        dn_used[idx] = vmm
                        if e + 1 < NEXP:
                            prefetch_dn(e + 1, nch)
                    P.wait("vector", "T", vmm)
                    if e == 0:
                        P.wait("vector", "V", v_accr[tt])
                        P.wait("vector", "V", v_comb_all)
                    P.emit("vector", "scalar_tensor_tensor",
                           accv(tt, 512 * nch, 512), pres.ap,
                           comb[:, 32 * tt + e:32 * tt + e + 1],
                           accv(tt, 512 * nch, 512), ALU.mult, ALU.add)
                    va = P.mark("vector", "V")
                    pres.release("V", va)
            v_last_acc = va

        v_gbB = bcast_batch([bc0[:, 512 * j:512 * (j + 1)] for j in range(4)],
                            20, rgrow, gates=(("V", v_h2_done[NT - 1]),),
                            row_math=gamath_w_late)
        P.wait("vector", "S", v_gbB)
        P.wait("vector", "V", v_last_acc)
        d_y = {}
        for tt in range(NT):
            P.emit("vector", "tensor_tensor", accv(tt, 0, 2048),
                   accv(tt, 0, 2048), bc0, ALU.mult)
            vo = P.mark("vector", "V")
            P.wait("sync", "V", vo)
            d_y[tt] = P.dma("sync", y[128 * tt:128 * (tt + 1), :],
                            accv(tt, 0, 2048), sem="Ly")
        P.wait("sync", "Ly", d_y[NT - 1])

        @block.tensor
        def _(e):
            for f in P.ops["tensor"]:
                f(e)

        @block.vector
        def _(e):
            for f in P.ops["vector"]:
                f(e)

        @block.scalar
        def _(e):
            for f in P.ops["scalar"]:
                f(e)

        @block.gpsimd
        def _(e):
            for f in P.ops["gpsimd"]:
                f(e)

        @block.sync
        def _(e):
            for f in P.ops["sync"]:
                f(e)

    return nc


_cache = {}


def _prep(inputs):
    f32 = np.float32
    x = np.ascontiguousarray(np.asarray(inputs["L0_attn_x"], f32).reshape(T, D))
    wq = np.asarray(inputs["L0_attn_Wq"], f32)
    wk = np.asarray(inputs["L0_attn_Wk"], f32)
    wv = np.asarray(inputs["L0_attn_Wv"], f32)
    wg = np.asarray(inputs["L0_attn_Wg"], f32)
    wo = np.asarray(inputs["L0_attn_Wo"], f32)
    gnw = np.asarray(inputs["L0_attn_gnorm_w"], f32).reshape(1, 512)
    gwt = np.asarray(inputs["L0_ffn_gate_weight"], f32)
    gu = np.asarray(inputs["L0_ffn_gate_up_proj"], f32)
    dn = np.asarray(inputs["L0_ffn_down_proj"], f32)
    alpha = np.asarray(inputs["L0_ffn_layer_alpha"], f32).reshape(1, 1)
    cv = np.asarray(inputs["L0_c"], f32).reshape(-1)
    wmod = np.asarray(inputs["L0_wmod"], f32)

    wq_m = np.ascontiguousarray(wq.T.reshape(D, 8, 128).transpose(1, 0, 2))
    wk_m = np.ascontiguousarray(wk.T.reshape(D, 8, 128).transpose(1, 0, 2))
    wv_n = np.ascontiguousarray(wv.T.reshape(D, 4, 512).transpose(1, 0, 2))
    wg_n = np.ascontiguousarray(wg.T.reshape(D, 4, 512).transpose(1, 0, 2))
    wo_g = np.ascontiguousarray(wo.T.reshape(4, 512, D))
    gu_t = gu.transpose(0, 2, 1).reshape(NEXP, D, 12, 128).transpose(0, 2, 1, 3)
    gup = np.ascontiguousarray(
        gu_t.reshape(NEXP, 2, 6, D, 128).transpose(0, 2, 1, 3, 4))
    dnp = np.ascontiguousarray(
        dn.transpose(0, 2, 1).reshape(NEXP, INTER, 4, 512).transpose(0, 2, 1, 3))
    gatet = np.ascontiguousarray(gwt.T)
    c_cols = np.ascontiguousarray(cv.reshape(16, 128).T)

    tglob = np.arange(T, dtype=f32)
    invf = (1.0 / (10000.0 ** (np.arange(0, DK, 2, dtype=f32) / DK))).astype(f32)
    freqs = (tglob[:, None] * invf[None, :]).astype(f32)
    cosb = np.cos(freqs).astype(f32).T
    sinb = np.sin(freqs).astype(f32).T
    ip = np.arange(T, dtype=np.float64) % 128
    tabs_all = np.zeros((NCORE, 2, H, 2, 128, TLOC), f32)
    for h in range(H):
        qs = (np.float64(GAMMA[h]) ** (ip + 1) * DK ** -0.5)
        ks = (np.float64(GAMMA[h]) ** (-ip - 1))
        for c in range(NCORE):
            sl = slice(TLOC * c, TLOC * (c + 1))
            tabs_all[c, 0, h, 0] = cosb[:, sl] * qs[None, sl]
            tabs_all[c, 0, h, 1] = sinb[:, sl] * qs[None, sl]
            tabs_all[c, 1, h, 0] = cosb[:, sl] * ks[None, sl]
            tabs_all[c, 1, h, 1] = sinb[:, sl] * ks[None, sl]

    identity = np.eye(128, dtype=BFNP)
    ii = np.arange(128)
    causal = (ii[None, :] >= ii[:, None]).astype(BFNP)

    sinw_all = np.zeros((NCORE, 128, 32), f32)
    for c in range(NCORE):
        for h in range(H):
            for j in range(NCORE):
                if j < c:
                    sinw_all[c, :, 8 * h + j] = np.float64(GAMMA[h]) ** (1024.0 * (c - 1 - j))

    per_core = []
    for c in range(NCORE):
        m = dict(
            x_t=np.ascontiguousarray(x[TLOC * c:TLOC * (c + 1)]),
            wq_m=wq_m, wk_m=wk_m, wv_n=wv_n, wg_n=wg_n, wo_g=wo_g,
            gup=gup, dnp=dnp, gatet=gatet,
            wmod_s=np.ascontiguousarray(
                wmod[:, 1536 * c:1536 * (c + 1)].reshape(KT, 128, 1536)),
            c_cols=c_cols, gnorm=gnw, alpha2=alpha,
            tabs=np.ascontiguousarray(tabs_all[c]),
            ident=identity, causalT=causal,
            sinw=np.ascontiguousarray(sinw_all[c]),
        )
        per_core.append(m)
    return per_core


def kernel(**inputs):
    per_core = _prep(inputs)
    if "nc" not in _cache:
        _cache["nc"] = build()
    nc = _cache["nc"]
    res = run_bass_kernel_spmd(nc, per_core, core_ids=list(range(NCORE)))
    out = np.concatenate([np.asarray(r["y"]) for r in res.results], axis=0)
    return out.reshape(1, T, D).astype(np.float32)




# version 09 (final)
# speedup vs baseline: 1.2709x; 1.0092x over previous
import contextlib
import numpy as np
import ml_dtypes
import concourse.bass as bass
import concourse.mybir as mybir
from concourse.bass_utils import run_bass_kernel_spmd

F32 = mybir.dt.float32
BF16 = mybir.dt.bfloat16
AF = mybir.ActivationFunctionType
ALU = mybir.AluOpType
AX = mybir.AxisListType
BFNP = ml_dtypes.bfloat16

NCORE = 8
T = 8192
TLOC = 1024
D = 2048
H = 4
DK = 256
DV = 512
KT = 16
NT = 8
NEXP = 32
INTER = 768
GAMMA = [float(1.0 - 2.0 ** (-5 - h)) for h in range(H)]
G128 = [float(np.float64(g) ** 128) for g in GAMMA]
EPS = 1e-5
TAU = 0.05

O_HT = 0
O_GP = 16384
O_QKOB = 32768
O_VB = 49152
O_F1 = 65536
O_F2 = 81920
O_M = 90112
O_BC0 = 94208
O_BC1 = 98304
O_ST = 102400
O_KTK = 104448
ACOLS = 105472


class Prog:
    ENGS = ("tensor", "vector", "scalar", "gpsimd", "sync")

    def __init__(self, sems):
        self.ops = {k: [] for k in self.ENGS}
        self.sems = sems
        self.ctr = {k: 0 for k in sems}
        self.last = {k: None for k in self.ENGS}
        self.last_kind = {k: None for k in self.ENGS}
        self.last_mark = {k: None for k in self.ENGS}

    def emit(self, eng, meth, *args, inc=None, incn=1, **kw):
        semh = self.sems[inc] if inc else None
        last = self.last
        dr = eng in ("vector", "scalar") and meth != "dma_start"

        def f(e, meth=meth, args=args, kw=kw, semh=semh, incn=incn, eng=eng, dr=dr):
            if dr:
                e.drain()
            ins = getattr(e, meth)(*args, **kw)
            last[eng] = ins
            if semh is not None:
                ins.then_inc(semh, incn)
        self.ops[eng].append(f)
        self.last_kind[eng] = "emit" if inc is None else "emit_inc"
        self.last_mark[eng] = inc
        if inc:
            self.ctr[inc] += incn
            return self.ctr[inc]
        return None

    def wait(self, eng, sem, v):
        if not v or v <= 0:
            return
        semh = self.sems[sem]

        def f(e, semh=semh, v=v):
            e.wait_ge(semh, v)
        self.ops[eng].append(f)

    def mark(self, eng, sem):
        if self.last_kind[eng] in ("mark", "emit_inc") and \
                self.last_mark[eng] == sem:
            return self.ctr[sem]
        assert self.last_kind[eng] in ("emit",), \
            f"mark({eng},{sem}) after {self.last_kind[eng]}"
        semh = self.sems[sem]
        last = self.last

        def f(e, semh=semh, eng=eng):
            last[eng].then_inc(semh, 1)
        self.ops[eng].append(f)
        self.last_kind[eng] = "mark"
        self.last_mark[eng] = sem
        self.ctr[sem] += 1
        return self.ctr[sem]

    def dma(self, queue, out, in_, sem=None):
        assert sem is not None, "dma needs an explicit slot semaphore"
        return self.emit(queue, "dma_start", inc=sem, incn=16, out=out, in_=in_)


class Psum:

    def __init__(self, P, ap):
        self.P, self.ap = P, ap
        self.free_at = None

    def twait(self):
        if self.free_at is not None:
            self.P.wait("tensor", *self.free_at)
            self.free_at = None

    def release(self, sem, val):
        self.free_at = (sem, val)


def build(dbg=()):
    nc = bass.Bass("TRN2", target_bir_lowering=False)

    def di(name, shape, dtype=F32):
        return nc.dram_tensor(name, shape, dtype, kind="ExternalInput").ap()

    x_t = di("x_t", [TLOC, D])
    wq_m = di("wq_m", [8, D, 128])
    wk_m = di("wk_m", [8, D, 128])
    wv_n = di("wv_n", [4, D, 512])
    wg_n = di("wg_n", [4, D, 512])
    wo_g = di("wo_g", [4, 512, D])
    gup = di("gup", [NEXP, 6, 2, D, 128])
    dnp = di("dnp", [NEXP, 4, INTER, 512])
    gatet = di("gatet", [D, 32])
    wmod_s = di("wmod_s", [KT, 128, 1536])
    c_cols = di("c_cols", [128, 16])
    gnorm = di("gnorm", [1, 512])
    alpha2 = di("alpha2", [1, 1])
    tabs = di("tabs", [2, H, 2, 128, TLOC])
    ident = di("ident", [128, 128], BF16)
    causalT = di("causalT", [128, 128], BF16)
    sinw = di("sinw", [128, 32])

    y = nc.dram_tensor("y", [TLOC, D], F32, kind="ExternalOutput").ap()
    dbg_out = {}
    for name, shape, dt in dbg:
        dbg_out[name] = nc.dram_tensor("dbg_" + name, list(shape), dt,
                                       kind="ExternalOutput").ap()

    modg_i = nc.dram_tensor("modg_i", [1, 1536], F32).ap()
    modg_o = nc.dram_tensor("modg_o", [NCORE, 1536], F32).ap()
    lg_i = nc.dram_tensor("lg_i", [128, 4096], BF16).ap()
    lg_o = nc.dram_tensor("lg_o", [NCORE * 128, 4096], BF16).ap()

    with contextlib.ExitStack() as ctx:
        en = ctx.enter_context
        AR = en(nc.sbuf_tensor("AR", [128, ACOLS], BF16))
        pA_t = en(nc.psum_tensor("pA", [128, 1024], F32))
        pB_t = en(nc.psum_tensor("pB", [128, 1024], F32))
        pC_t = en(nc.psum_tensor("pC", [128, 1024], F32))
        pD_t = en(nc.psum_tensor("pD", [128, 1024], F32))
        semnames = ["T", "V", "S", "C", "Lc", "Lx0", "Lx1", "Lx2", "Lx3", "Ltab",
                    "Lm0", "Lm1", "Lm2", "Lm3", "Lvg0", "Lvg1", "Ltab0", "Ltab1",
                    "Lwm0", "Lwm1", "Lwm2", "Lwm3", "Lgu0", "Lgu1",
                    "Ldn0", "Ldn1", "Ldn2", "Ldn3", "Lwo", "Lxa", "Ly",
                    "Llr0", "Llr1", "Lch0", "Lch1", "Lmi", "Lli", "Lgate"]
        semh = {s: en(nc.semaphore("sem" + s)) for s in semnames}
        block = en(nc.Block())
        P = Prog(semh)

        def B(o, n, p0=0, p1=128):
            return AR[p0:p1, o:o + n]

        def Fv(o, n, p0=0, p1=128):
            return AR[p0:p1, o:o + 2 * n].bitcast(F32)

        pAl, pAh = Psum(P, pA_t[:, 0:512]), Psum(P, pA_t[:, 512:1024])
        pBl, pBh = Psum(P, pB_t[:, 0:512]), Psum(P, pB_t[:, 512:1024])
        pCl, pCh = Psum(P, pC_t[:, 0:512]), Psum(P, pC_t[:, 512:1024])
        pDl, pDh = Psum(P, pD_t[:, 0:512]), Psum(P, pD_t[:, 512:1024])
        pT = pD_t[:, 0:512].bitcast(BF16)

        st_cur = [O_ST]

        def st_alloc(nbf):
            o = st_cur[0]
            st_cur[0] = (o + nbf + 15) // 16 * 16
            assert st_cur[0] <= O_ST + 2048, f"ST overflow {st_cur[0]}"
            return o

        ones = Fv(st_alloc(256), 128)
        csb = Fv(st_alloc(32), 16)
        silc = Fv(st_alloc(32), 16)
        id_sb = B(st_alloc(128), 128)
        cau_sb = B(st_alloc(128), 128)
        sinw_sb = Fv(st_alloc(64), 32)
        albB = Fv(st_alloc(16), 1)
        s_mean = Fv(st_alloc(16), 8)
        s_inv = Fv(st_alloc(16), 8)
        s_nmi = Fv(st_alloc(16), 8)
        s_ssq = Fv(st_alloc(16), 8)
        s_oiv = Fv(st_alloc(16), 8)
        s_t1 = Fv(st_alloc(16), 8)
        comb = Fv(st_alloc(512), 256)
        r_p = Fv(st_alloc(64), 32)
        r_cur = Fv(st_alloc(64), 32)
        r_tmp = Fv(st_alloc(64), 32)
        r_mk = Fv(st_alloc(16), 8)
        r_cnt = Fv(st_alloc(16), 8)
        r_a1 = Fv(st_alloc(64), 32)
        attn_sb = B(st_alloc(128), 128)
        attn_sbB = B(st_alloc(128), 128)
        attn_sb4 = [B(O_HT + 10240 + 128 * i, 128) for i in range(8)]

        chrow = [Fv(O_KTK, 512, 0, 1), Fv(O_M, 512, 0, 1)]
        rtmprow = Fv(O_M + 1024, 512, 0, 1)
        rtmprow_w = Fv(O_HT + 15360, 512, 0, 1)
        rtmprow_w2 = Fv(O_M, 1024, 0, 1)

        def hT(kt, c0, n):
            return B(O_HT + kt * 1024 + c0, n)

        hT3 = B(O_HT, 16384).rearrange("p (a b) -> p a b", b=1024)
        h2T3 = B(O_QKOB, 16384).rearrange("p (a b) -> p a b", b=1024)

        def h2T(kt, c0, n):
            return B(O_QKOB + kt * 1024 + c0, n)

        def qseg(p, h, half):
            return B(O_QKOB + p * 2048 + h * 512 + half * 128, 128)

        def kseg(p, h, half):
            return B(O_QKOB + p * 2048 + h * 512 + 256 + half * 128, 128)

        def obv(p, c0, n):
            return B(O_QKOB + p * 2048 + c0, n)

        def vtile(p, c0, n):
            return B(O_VB + p * 2048 + c0, n)

        def gtile(p, c0, n):
            return B(O_GP + p * 2048 + c0, n)

        def accv(tt, c0, n):
            return Fv(4096 * tt + 2 * c0, n)

        def Mv(dt8, n=512):
            return B(O_M + dt8 * 512, n)

        ktok = B(O_KTK, 1024)
        ogT = B(O_M, 2048)

        P.emit("vector", "memset", ones, 1.0)
        v_ones = P.mark("vector", "V")

        P.dma("sync", csb, c_cols, sem="Lc")
        P.dma("sync", id_sb, ident, sem="Lc")
        P.dma("sync", cau_sb, causalT, sem="Lc")
        d_const = P.dma("sync", sinw_sb, sinw, sem="Lc")

        wslot = [Fv(O_VB + 3072 * s, 1536) for s in range(4)]
        modloc = Fv(O_VB + 12288, 1536, 0, 1)

        P.wait("scalar", "Lc", d_const)
        v_silc = P.emit("scalar", "activation", silc, csb, AF.Silu, inc="S")

        mps = [pA_t[0:1, 0:512], pB_t[0:1, 0:512], pC_t[0:1, 0:512]]
        P.wait("tensor", "S", v_silc)
        t_after_k = {}
        for k in range(KT):
            q = "gpsimd" if k % 2 == 0 else "sync"
            if k >= 4:
                P.wait(q, "T", t_after_k[k - 4])
            d = P.dma(q, wslot[k % 4], wmod_s[k], sem="Lwm%d" % (k % 4))
            P.wait("tensor", "Lwm%d" % (k % 4), d)
            for nch in range(3):
                P.emit("tensor", "matmul", mps[nch], silc[:, k:k + 1],
                       wslot[k % 4][:, 512 * nch:512 * (nch + 1)],
                       start=(k == 0), stop=(k == KT - 1))
            t_after_k[k] = P.mark("tensor", "T")
        v_modmm = t_after_k[KT - 1]

        P.wait("scalar", "T", v_modmm)
        for nch in range(3):
            P.emit("scalar", "copy", modloc[:, 512 * nch:512 * (nch + 1)], mps[nch])
        v_modev = P.mark("scalar", "S")
        pAl.release("S", v_modev)
        pBl.release("S", v_modev)
        pCl.release("S", v_modev)

        P.wait("gpsimd", "S", v_modev)
        d_mi = P.dma("gpsimd", modg_i[:, :], modloc, sem="Lmi")
        P.wait("gpsimd", "Lmi", d_mi)
        v_cc1 = P.emit("gpsimd", "collective_compute", "AllGather", ALU.bypass,
                       replica_groups=[list(range(NCORE))],
                       ins=[modg_i[:, :]], outs=[modg_o[:, :]], inc="C")

        modflat = modg_o.rearrange("a b -> (a b)")
        bc_state = {"i": 0, "last_mm": {0: 0, 1: 0}}
        bb_state = {"last_mm": 0}

        def bcast_batch(dsts, chunk0, row, gates=(), row_math=None, add1=False):
            n = len(dsts)
            P.wait("sync", "C", v_cc1)
            for sem, val in gates:
                P.wait("sync", sem, val)
            P.wait("sync", "T", bb_state["last_mm"])
            d = P.dma("sync", row[:, 0:512 * n],
                      modflat[512 * chunk0:512 * (chunk0 + n)].unsqueeze(0),
                      sem="Lch1")
            sem_mm, val_mm = "Lch1", d
            if row_math is not None:
                P.wait("vector", sem_mm, d)
                row_math(row[:, 0:512 * n])
                val_mm = P.mark("vector", "V")
                sem_mm = "V"
            P.wait("tensor", sem_mm, val_mm)
            P.wait("tensor", "V", v_ones)
            evs = []
            for i in range(n):
                pres = pBh if i % 2 == 0 else pAh
                pres.twait()
                P.emit("tensor", "matmul", pres.ap, ones[0:1, :],
                       row[:, 512 * i:512 * (i + 1)], start=True, stop=True)
                mmv = P.mark("tensor", "T")
                P.wait("scalar", "T", mmv)
                if add1:
                    P.emit("scalar", "add", dsts[i], pres.ap, 1.0)
                else:
                    P.emit("scalar", "copy", dsts[i], pres.ap)
                ev = P.mark("scalar", "S")
                pres.release("S", ev)
                evs.append(ev)
            bb_state["last_mm"] = P.ctr["T"]
            return evs[-1]

        def gamath_w(row, scr_late=False):
            for c0 in range(0, row.shape[1], 512):
                half = row[:, c0:c0 + 512]
                scr = (rtmprow_w2 if scr_late else rtmprow_w)[:, 0:512]
                P.emit("vector", "tensor_scalar", scr, half, 0.0, None,
                       ALU.is_ge)
                P.emit("vector", "tensor_scalar", scr, scr, 2.0, -1.0,
                       ALU.mult, ALU.add)
                P.emit("vector", "tensor_tensor", half, half, scr, ALU.mult)
                P.emit("vector", "tensor_scalar", half, half, 0.25, None,
                       ALU.max)
                P.emit("vector", "tensor_tensor", half, half, scr, ALU.mult)

        def gamath_w_late(row):
            gamath_w(row, scr_late=True)

        def rgmath_w(row):
            gamath_w(row, scr_late=True)
            P.emit("vector", "reciprocal", row, row)

        def bcast512(dst, chunk_idx, add1=False, row_math=None):
            i = bc_state["i"] % 2
            bc_state["i"] += 1
            row = chrow[i]
            P.wait("sync", "C", v_cc1)
            P.wait("sync", "T", bc_state["last_mm"][i])
            d = P.dma("sync", row,
                      modflat[512 * chunk_idx:512 * (chunk_idx + 1)].unsqueeze(0),
                      sem="Lch%d" % i)
            sem_mm, val_mm = "Lch%d" % i, d
            if row_math is not None:
                P.wait("vector", sem_mm, d)
                row_math(row)
                val_mm = P.mark("vector", "V")
                sem_mm = "V"
            P.wait("tensor", sem_mm, val_mm)
            P.wait("tensor", "V", v_ones)
            pBh.twait()
            P.emit("tensor", "matmul", pBh.ap, ones[0:1, :], row,
                   start=True, stop=True)
            mmv = P.mark("tensor", "T")
            bc_state["last_mm"][i] = mmv
            P.wait("scalar", "T", mmv)
            if add1:
                P.emit("scalar", "add", dst, pBh.ap, 1.0)
            else:
                P.emit("scalar", "copy", dst, pBh.ap)
            ev = P.mark("scalar", "S")
            pBh.release("S", ev)
            return ev

        def gamath(row):
            P.emit("vector", "tensor_scalar", rtmprow, row, 0.0, None, ALU.is_ge)
            P.emit("vector", "tensor_scalar", rtmprow, rtmprow, 2.0, -1.0,
                   ALU.mult, ALU.add)
            P.emit("vector", "tensor_tensor", row, row, rtmprow, ALU.mult)
            P.emit("vector", "tensor_scalar", row, row, 0.25, None, ALU.max)
            P.emit("vector", "tensor_tensor", row, row, rtmprow, ALU.mult)

        def rgmath(row):
            gamath(row)
            P.emit("vector", "reciprocal", row, row)

        bc0 = Fv(O_BC0, 2048)
        bc1 = Fv(O_BC1, 2048)

        xtile = [Fv(O_F2, 2048), Fv(O_F2 + 4096, 2048),
                 Fv(O_QKOB, 2048), Fv(O_QKOB + 4096, 2048)]
        d_x = {t: P.dma("sync", xtile[t], x_t[128 * t:128 * (t + 1), :],
                        sem="Lx%d" % t) for t in range(4)}

        d_al = P.dma("sync", chrow[0][:, 0:1], alpha2, sem="Lch0")
        P.wait("tensor", "Lch0", d_al)
        P.wait("tensor", "V", v_ones)
        pBh.twait()
        P.emit("tensor", "matmul", pB_t[:, 512:513], ones[0:1, :],
               chrow[0][:, 0:1], start=True, stop=True)
        v_mm = P.mark("tensor", "T")
        bc_state["last_mm"][0] = v_mm
        P.wait("scalar", "T", v_mm)
        P.emit("scalar", "copy", albB, pB_t[:, 512:513])
        v_alb = P.mark("scalar", "S")
        pBh.release("S", v_alb)

        carow = Fv(O_VB, 2048, 0, 1)
        bcast_batch([bc0[:, 512 * j:512 * (j + 1)] for j in range(4)], 4,
                    carow, add1=True)
        v_casa = bcast_batch([bc1[:, 512 * j:512 * (j + 1)] for j in range(4)],
                             0, carow)

        htmp = [Fv(O_F1, 2048), Fv(O_F1 + 4096, 2048)]
        hbuf = [B(O_F1 + 8192, 2048), B(O_F1 + 10240, 2048)]
        sq_scr = Fv(O_F1 + 12288, 2048)
        x_free_at = {}
        v_h = {}
        tpos_res = {0: pDl, 1: pCl}
        tpos_bf = {0: pT, 1: pC_t[:, 0:512].bitcast(BF16)}

        v_h = {}
        x_free_at = {}
        v_FG = {}

        def p1_A(t):
            xt = xtile[t % 4]
            xsem = "Lx%d" % (t % 4)
            P.wait("vector", xsem, d_x[t])
            P.emit("vector", "reduce_sum", s_mean[:, t:t + 1], xt, axis=AX.X)
            P.wait("scalar", xsem, d_x[t])
            P.emit("scalar", "activation", sq_scr, xt, AF.Square,
                   accum_out=s_ssq[:, t:t + 1])
            return P.mark("scalar", "S")

        sA_done = {}
        sA_done[0] = p1_A(0)
        sA_done[1] = p1_A(1)
        FGd = {}
        for t in range(NT):
            xt, ht, hb = xtile[t % 4], htmp[t % 2], hbuf[t % 2]
            P.wait("vector", "S", sA_done[t])
            P.emit("vector", "tensor_scalar", s_mean[:, t:t + 1],
                   s_mean[:, t:t + 1], 1.0 / D, None, ALU.mult)
            P.emit("vector", "tensor_tensor", s_t1[:, t:t + 1],
                   s_mean[:, t:t + 1], s_mean[:, t:t + 1], ALU.mult)
            P.emit("vector", "tensor_scalar", s_inv[:, t:t + 1],
                   s_ssq[:, t:t + 1], 1.0 / D, None, ALU.mult)
            P.emit("vector", "tensor_tensor", s_inv[:, t:t + 1],
                   s_inv[:, t:t + 1], s_t1[:, t:t + 1], ALU.subtract)
            P.emit("vector", "tensor_scalar", s_inv[:, t:t + 1],
                   s_inv[:, t:t + 1], EPS, None, ALU.add)
            P.emit("vector", "reciprocal", s_inv[:, t:t + 1], s_inv[:, t:t + 1])
            vB = P.mark("vector", "V")
            P.wait("scalar", "V", vB)
            P.emit("scalar", "sqrt", s_inv[:, t:t + 1], s_inv[:, t:t + 1])
            sC = P.mark("scalar", "S")
            P.wait("vector", "S", sC)
            P.emit("vector", "tensor_scalar", s_nmi[:, t:t + 1],
                   s_mean[:, t:t + 1], s_inv[:, t:t + 1], -1.0,
                   ALU.mult, ALU.mult)
            vD = P.mark("vector", "V")
            P.wait("scalar", "V", vD)
            P.wait("scalar", "S", v_FG.get(t - 2, 0))
            P.emit("scalar", "activation", ht, xt, AF.Identity,
                   bias=s_nmi[:, t:t + 1], scale=s_inv[:, t:t + 1])
            x_free_at[t] = P.mark("scalar", "S")
            if t + 4 < NT:
                P.wait("sync", "S", x_free_at[t])
                d_x[t + 4] = P.dma("sync", xtile[t % 4],
                                   x_t[128 * (t + 4):128 * (t + 5), :],
                                   sem="Lx%d" % (t % 4))
            if t + 2 < NT:
                sA_done[t + 2] = p1_A(t + 2)

            def FG_t(t=t, ht=ht, hb=hb):
                P.wait("vector", "S", x_free_at[t])
                P.wait("vector", "S", v_casa)
                P.emit("vector", "tensor_tensor", ht, ht, bc0, ALU.mult)
                P.emit("vector", "tensor_tensor", hb, ht, bc1, ALU.add)
                v_h[t] = P.mark("vector", "V")
                P.wait("tensor", "V", v_h[t])
                P.wait("tensor", "Lc", d_const)
                for grp in range(2):
                    res = tpos_res[grp]
                    pbf = tpos_bf[grp]
                    res.twait()
                    for j in range(8):
                        kt2 = grp * 8 + j
                        P.emit("tensor", "transpose",
                               pbf[:, 128 * j:128 * (j + 1)],
                               hb[:, 128 * kt2:128 * (kt2 + 1)], id_sb)
                    vt = P.mark("tensor", "T")
                    P.wait("scalar", "T", vt)
                    P.emit("scalar", "copy",
                           hT3[:, grp * 8:(grp + 1) * 8, 128 * t:128 * (t + 1)],
                           pbf.rearrange("p (a b) -> p a b", b=128))
                    vv = P.mark("scalar", "S")
                    res.release("S", vv)
                v_FG[t] = vv

            FGd[t] = FG_t
            if t >= 1:
                FGd[t - 1]()
        FGd[NT - 1]()
        v_hT_all_s = v_FG[NT - 1]
        P.wait("vector", "S", v_hT_all_s)
        v_hT_all = P.emit("vector", "nop", inc="V")

        mring = [B(O_F2 + 2048 * s, 2048) for s in range(4)]
        mm_free = {}
        rope_done = {}
        slot_i = 0
        v_rope_prev = 0
        d_tab_prev = 0

        tabc2 = [[Fv(O_F1, 1024), Fv(O_F1 + 2048, 1024)],
                 [Fv(O_F1 + 8192, 1024), Fv(O_F1 + 10240, 1024)]]
        rsc2 = [[Fv(O_F1 + 4096, 1024), Fv(O_F1 + 6144, 1024)],
                [Fv(O_F1 + 12288, 1024), Fv(O_F1 + 14336, 1024)]]
        hi = 0
        for side in range(2):
            wsrc = wq_m if side == 0 else wk_m
            for h in range(H):
                par = hi % 2
                tabc = tabc2[par]
                rsc = rsc2[par]
                q1t, q2t = (pA_t, pB_t) if par == 0 else (pC_t, pD_t)
                pun = [(pAl, pAh), (pBl, pBh)] if par == 0 else [(pCl, pCh), (pDl, pDh)]
                hi += 1
                P.wait("sync", "V", v_hT_all if hi <= 2 else rope_done.get(hi - 3, 0))
                d_cos = P.dma("sync", tabc[0], tabs[side, h, 0],
                              sem="Ltab%d" % par)
                d_sin = P.dma("sync", tabc[1], tabs[side, h, 1],
                              sem="Ltab%d" % par)
                vps = []
                for half in range(2):
                    m = 2 * h + half
                    s4 = slot_i % 4
                    P.wait("gpsimd", "T", mm_free.get(s4, 0))
                    if slot_i == 0:
                        P.wait("gpsimd", "S", x_free_at[NT - 1])
                    dsl = P.dma("gpsimd",
                                mring[s4].rearrange("p (a m) -> p a m", m=128),
                                wsrc[m].rearrange("(a p) m -> p a m", p=128),
                                sem="Lm%d" % s4)
                    ptgt = pun[half]
                    P.wait("tensor", "Lm%d" % s4, dsl)
                    P.wait("tensor", "V", v_hT_all)
                    for tch in range(2):
                        ptgt[tch].twait()
                        for kt2 in range(KT):
                            P.emit("tensor", "matmul", ptgt[tch].ap,
                                   mring[s4][:, 128 * kt2:128 * (kt2 + 1)],
                                   hT(kt2, 512 * tch, 512),
                                   start=(kt2 == 0), stop=(kt2 == KT - 1))
                    vp = P.mark("tensor", "T")
                    mm_free[s4] = vp
                    vps.append(vp)
                    slot_i += 1
                P.wait("vector", "T", vps[-1])
                P.wait("vector", "Ltab%d" % par, d_sin)
                q1 = q1t[:, 0:1024]
                q2 = q2t[:, 0:1024]
                segd = qseg if side == 0 else kseg
                P.emit("vector", "tensor_tensor", rsc[0], q1, tabc[0], ALU.mult)
                P.emit("vector", "tensor_tensor", rsc[1], q2, tabc[1], ALU.mult)
                for p in range(NT):
                    P.emit("vector", "tensor_tensor", segd(p, h, 0),
                           rsc[0][:, 128 * p:128 * (p + 1)],
                           rsc[1][:, 128 * p:128 * (p + 1)], ALU.subtract)
                P.emit("vector", "tensor_tensor", rsc[0], q2, tabc[0], ALU.mult)
                P.emit("vector", "tensor_tensor", rsc[1], q1, tabc[1], ALU.mult)
                for p in range(NT):
                    P.emit("vector", "tensor_tensor", segd(p, h, 1),
                           rsc[0][:, 128 * p:128 * (p + 1)],
                           rsc[1][:, 128 * p:128 * (p + 1)], ALU.add)
                v_rope_prev = P.mark("vector", "V")
                rope_done[hi - 1] = v_rope_prev
                for pp in pun[0] + pun[1]:
                    pp.release("V", v_rope_prev)
        v_rope_all = v_rope_prev

        vgslot = [B(O_F1 + 8192 * s, 8192) for s in range(2)]
        vg_free = {}
        vg_i = 0
        pcycle = [pAl, pAh, pBl, pBh]
        pc_i = 0
        v_g_ev = {}
        v_v_ev = {}
        P.wait("scalar", "Lmi", d_mi)

        def vg_proj(which, nchs=(0, 1, 2, 3), hook=None):
            nonlocal vg_i, pc_i
            wsrc = wv_n if which == 0 else wg_n
            for nch in nchs:
                s2 = vg_i % 2
                if vg_i < 2:
                    P.wait("gpsimd", "V", v_rope_all)
                else:
                    P.wait("gpsimd", "T", vg_free[s2])
                dsl = P.dma("gpsimd",
                            vgslot[s2].rearrange("p (a m) -> p a m", m=512),
                            wsrc[nch].rearrange("(a p) m -> p a m", p=128),
                            sem="Lvg%d" % s2)
                P.wait("tensor", "Lvg%d" % s2, dsl)
                for tt in range(NT):
                    pres = pcycle[pc_i % 4]
                    pc_i += 1
                    pres.twait()
                    for kt2 in range(KT):
                        P.emit("tensor", "matmul", pres.ap,
                               hT(kt2, 128 * tt, 128),
                               vgslot[s2][:, 512 * kt2:512 * (kt2 + 1)],
                               start=(kt2 == 0), stop=(kt2 == KT - 1))
                    vmm = P.mark("tensor", "T")
                    if tt == NT - 1:
                        vg_free[s2] = vmm
                    P.wait("scalar", "T", vmm)
                    if which == 0:
                        P.emit("scalar", "copy",
                               vtile(tt, 512 * nch, 512), pres.ap)
                        ev = P.mark("scalar", "S")
                        pres.release("S", ev)
                        v_v_ev[(nch, tt)] = ev
                        if hook is not None:
                            hook(tt)
                    else:
                        P.emit("scalar", "activation",
                               gtile(tt, 512 * nch, 512), pres.ap, AF.Silu)
                        ev = P.mark("scalar", "S")
                        pres.release("S", ev)
                        P.wait("vector", "S", ev)
                        P.wait("vector", "T", v_gnB_t)
                        P.emit("vector", "tensor_tensor",
                               gtile(tt, 512 * nch, 512),
                               gtile(tt, 512 * nch, 512), gnormB, ALU.mult)
                        v_g_ev[(nch, tt)] = P.mark("vector", "V")
                vg_i += 1

        def Mpar(dt8, par, n=512):
            return B((O_M if par % 2 == 0 else O_F2) + dt8 * 512, n)

        P.wait("vector", "Lch0", P.ctr["Lch0"])
        P.wait("vector", "Lch1", P.ctr["Lch1"])
        P.wait("vector", "T", max(bc_state["last_mm"].values()))
        P.emit("vector", "memset", B(O_M, 4096), 0.0)
        v_m0 = P.mark("vector", "V")
        kv_cycle = [pCh, pDh]
        kv_i = [0]
        lstate = {"v": v_m0}

        def emit_L_pair(p):
            pDl.twait()
            P.wait("tensor", "V", v_rope_all)
            for dt8 in range(8):
                h, half = dt8 // 2, dt8 % 2
                P.emit("tensor", "transpose", pT[:, 128 * dt8:128 * (dt8 + 1)],
                       kseg(p, h, half), id_sb)
            vt = P.mark("tensor", "T")
            P.wait("vector", "T", vt)
            for h in range(H):
                P.emit("vector", "tensor_scalar",
                       ktok[:, 256 * h:256 * (h + 1)],
                       pT[:, 256 * h:256 * (h + 1)], G128[h], None, ALU.mult)
            v_kt = P.mark("vector", "V")
            pDl.release("V", v_kt)
            for dt8 in range(8):
                h = dt8 // 2
                pres = kv_cycle[kv_i[0] % 2]
                kv_i[0] += 1
                pres.twait()
                P.wait("tensor", "V", v_kt)
                P.wait("tensor", "S", v_v_ev[(h, p)])
                P.emit("tensor", "matmul", pres.ap,
                       ktok[:, 128 * dt8:128 * (dt8 + 1)],
                       vtile(p, 512 * h, 512), start=True, stop=True)
                vmm = P.mark("tensor", "T")
                P.wait("vector", "T", vmm)
                P.wait("vector", "V", v_m0)
                P.emit("vector", "scalar_tensor_tensor", Mpar(dt8, 0),
                       Mpar(dt8, 0), G128[h], pres.ap, ALU.mult, ALU.add)
                lstate["v"] = P.mark("vector", "V")
                pres.release("V", lstate["v"])

        vg_proj(0, nchs=(0, 1, 2))
        vg_proj(0, nchs=(3,), hook=emit_L_pair)
        v_Lupd = lstate["v"]
        P.wait("gpsimd", "V", v_Lupd)
        d_li = P.dma("gpsimd", lg_i[:, :], B(O_M, 4096), sem="Lli")
        P.wait("gpsimd", "Lli", d_li)
        v_cc2 = P.emit("gpsimd", "collective_compute", "AllGather", ALU.bypass,
                       replica_groups=[list(range(NCORE))],
                       ins=[lg_i[:, :]], outs=[lg_o[:, :]], inc="C")

        P.wait("sync", "V", v_Lupd)
        d_gn2 = P.dma("sync", chrow[0], gnorm, sem="Lch0")
        P.wait("tensor", "Lch0", d_gn2)
        pCl.twait()
        P.emit("tensor", "matmul", pCl.ap, ones[0:1, :], chrow[0],
               start=True, stop=True)
        v_gnB_t = P.mark("tensor", "T")
        bc_state["last_mm"][0] = v_gnB_t
        gnormB = pCl.ap

        vg_proj(1)
        v_vg_all = v_g_ev[(3, NT - 1)]
        v_gp = v_vg_all
        pCl.release("V", v_gp)

        woslot = [B(O_VB + 8192 * g, 8192) for g in range(4)]
        d_wo = {}
        for g in (2, 3):
            P.wait("gpsimd", "T", vg_free[g - 2])
            d_wo[g] = P.dma("gpsimd",
                            woslot[g].rearrange("p (a m) -> p a m", m=2048),
                            wo_g[g].rearrange("(a p) m -> p a m", p=128),
                            sem="Lwo")

        garow = Fv(O_HT + 11264, 2048, 0, 1)
        v_gaB = bcast_batch([bc0[:, 512 * j:512 * (j + 1)] for j in range(4)],
                            8, garow, gates=(("T", vg_free[1]),),
                            row_math=gamath_w)
        v_sbB = bcast_batch([bc1[:, 512 * j:512 * (j + 1)] for j in range(4)],
                            12, garow, gates=(("T", vg_free[1]),))

        lrecv = [B(O_F2 + 4096 * s, 4096) for s in range(2)]
        sacc = Fv(O_HT, 4096)
        d_lr = {}
        v_lr_used = {}
        P.wait("gpsimd", "C", v_cc2)
        P.wait("sync", "C", v_cc2)
        d_lr[0] = P.dma("gpsimd", lrecv[0], lg_o[0:128, :], sem="Llr0")
        for j in range(NCORE):
            if j + 1 < NCORE:
                q = "gpsimd" if (j + 1) % 2 == 0 else "sync"
                P.wait(q, "V", v_lr_used.get(j - 1, 0))
                d_lr[j + 1] = P.dma(q, lrecv[(j + 1) % 2],
                                    lg_o[128 * (j + 1):128 * (j + 2), :],
                                    sem="Llr%d" % ((j + 1) % 2))
            P.wait("vector", "Llr%d" % (j % 2), d_lr[j])
            if j == NCORE - 1:
                P.wait("vector", "Lli", d_li)
            for h in range(H):
                wcol = sinw_sb[:, 8 * h + j:8 * h + j + 1]
                srcw = lrecv[j % 2][:, 1024 * h:1024 * (h + 1)]
                acc4 = sacc[:, 1024 * h:1024 * (h + 1)]
                if j == 0:
                    P.emit("vector", "tensor_scalar", acc4, srcw,
                           wcol, None, ALU.mult)
                elif j == NCORE - 1:
                    P.emit("vector", "scalar_tensor_tensor",
                           Mpar(2 * h, 0, 1024), srcw, wcol, acc4,
                           ALU.mult, ALU.add)
                else:
                    P.emit("vector", "scalar_tensor_tensor", acc4, srcw,
                           wcol, acc4, ALU.mult, ALU.add)
            v_lr_used[j] = P.mark("vector", "V")
        v_sin = v_lr_used[NCORE - 1]

        attn_sb2 = [attn_sb, attn_sbB]
        v_Mupd = {-1: v_sin}
        v_ob = {}
        at_cycle = [pAl, pBl]
        kv_cycle2 = [pAh, pBh]
        o_cycle = [pCl, pCh, pDh]
        odo = {}
        for p in range(NT):
            rpar, wpar = p % 2, (p + 1) % 2
            pDl.twait()
            for dt8 in range(8):
                h, half = dt8 // 2, dt8 % 2
                P.emit("tensor", "transpose", pT[:, 128 * dt8:128 * (dt8 + 1)],
                       kseg(p, h, half), id_sb)
            vt = P.mark("tensor", "T")
            P.wait("vector", "T", vt)
            for h in range(H):
                P.emit("vector", "tensor_scalar",
                       ktok[:, 256 * h:256 * (h + 1)],
                       pT[:, 256 * h:256 * (h + 1)], G128[h], None, ALU.mult)
            v_kt = P.mark("vector", "V")
            pDl.release("V", v_kt)
            amarks = {}
            for h in range(H):
                at = at_cycle[h % 2]
                at.twait()
                P.emit("tensor", "matmul", at.ap[:, 0:128], kseg(p, h, 0),
                       qseg(p, h, 0), start=True, stop=False)
                P.emit("tensor", "matmul", at.ap[:, 0:128], kseg(p, h, 1),
                       qseg(p, h, 1), start=False, stop=True)
                v_at = P.mark("tensor", "T")
                P.wait("vector", "T", v_at)
                P.emit("vector", "tensor_tensor", attn_sb4[(p % 2) * 4 + h],
                       at.ap[:, 0:128], cau_sb, ALU.mult)
                amarks[h] = P.mark("vector", "V")
                at.release("V", amarks[h])

            def emit_o(p=p, rpar=rpar, amarks=amarks):
                P.wait("tensor", "V", v_Mupd[p - 1])
                for h in range(H):
                    op = o_cycle[h % 3]
                    op.twait()
                    P.wait("tensor", "V", amarks[h])
                    P.emit("tensor", "matmul", op.ap, attn_sb4[(p % 2) * 4 + h],
                           vtile(p, 512 * h, 512), start=True, stop=False)
                    P.emit("tensor", "matmul", op.ap, qseg(p, h, 0),
                           Mpar(2 * h, rpar), start=False, stop=False)
                    P.emit("tensor", "matmul", op.ap, qseg(p, h, 1),
                           Mpar(2 * h + 1, rpar), start=False, stop=True)
                    v_omm = P.mark("tensor", "T")
                    P.wait("scalar", "T", v_omm)
                    P.emit("scalar", "copy", obv(p, 512 * h, 512), op.ap)
                    v_ob[(p, h)] = P.mark("scalar", "S")
                    op.release("S", v_ob[(p, h)])

            odo[p] = emit_o
            if p >= 1:
                odo[p - 1]()
            vm = None
            for dt8 in range(8):
                h = dt8 // 2
                pres = kv_cycle2[dt8 % 2]
                pres.twait()
                P.wait("tensor", "V", v_kt)
                if p == 0:
                    P.wait("tensor", "S", v_v_ev[(h, p)])
                P.emit("tensor", "matmul", pres.ap,
                       ktok[:, 128 * dt8:128 * (dt8 + 1)],
                       vtile(p, 512 * h, 512), start=True, stop=True)
                vkv = P.mark("tensor", "T")
                P.wait("vector", "T", vkv)
                P.emit("vector", "scalar_tensor_tensor", Mpar(dt8, wpar),
                       Mpar(dt8, rpar), G128[h], pres.ap, ALU.mult, ALU.add)
                vm = P.mark("vector", "V")
                pres.release("V", vm)
            v_Mupd[p] = vm
        odo[NT - 1]()

        def tap(name, off, ncols, sem, val):
            if name in dbg_out:
                P.wait("sync", sem, val)
                P.wait("sync", "Ly", P.ctr["Ly"])
                d = P.dma("sync", dbg_out[name], B(off, ncols), sem="Ly")
                for eng in ("vector", "scalar", "tensor", "gpsimd"):
                    P.wait(eng, "Ly", d)

        tap("hT", O_HT, 16384, "V", v_hT_all)
        tap("gp", O_GP, 16384, "V", v_gp)
        v_og = {}
        RSb2 = Fv(O_HT + 9216, 512)

        def p3b_sq(p):
            if p == 0:
                P.wait("scalar", "V", v_gp)
            P.wait("scalar", "V", v_og.get(p - 2, 0))
            for h in range(H):
                P.wait("scalar", "S", v_ob[(p, h)])
                P.emit("scalar", "activation", RSb2,
                       obv(p, 512 * h, 512), AF.Square,
                       accum_out=s_oiv[:, (p % 2) * 4 + h:(p % 2) * 4 + h + 1])
            return P.mark("scalar", "S")

        sqd = {0: p3b_sq(0)}
        for p in range(NT):
            c4 = s_oiv[:, (p % 2) * 4:(p % 2) * 4 + 4]
            if p + 1 < NT:
                sqd[p + 1] = p3b_sq(p + 1)
            P.wait("vector", "S", sqd[p])
            P.emit("vector", "tensor_scalar", c4, c4, 1.0 / DV, EPS,
                   ALU.mult, ALU.add)
            P.emit("vector", "reciprocal", c4, c4)
            vv = P.mark("vector", "V")
            P.wait("scalar", "V", vv)
            P.emit("scalar", "sqrt", c4, c4)
            vs2 = P.mark("scalar", "S")
            P.wait("vector", "S", vs2)
            for h in range(H):
                P.emit("vector", "scalar_tensor_tensor", obv(p, 512 * h, 512),
                       obv(p, 512 * h, 512),
                       s_oiv[:, (p % 2) * 4 + h:(p % 2) * 4 + h + 1],
                       gtile(p, 512 * h, 512), ALU.mult, ALU.mult)
            v_og[p] = P.mark("vector", "V")
        v_og_all = v_og[NT - 1]


        wtmp = Fv(O_F2, 512)
        v_ret_done = v_Mupd[NT - 1]
        for g in (0, 1):
            P.wait("gpsimd", "S", v_ob[(NT - 1, H - 1)])
            d_wo[g] = P.dma("gpsimd",
                            woslot[g].rearrange("p (a m) -> p a m", m=2048),
                            wo_g[g].rearrange("(a p) m -> p a m", p=128),
                            sem="Lwo")

        d_x2 = {}
        P.wait("sync", "V", v_og_all)
        for tt in range(NT):
            d_x2[tt] = P.dma("sync", accv(tt, 0, 2048),
                             x_t[128 * tt:128 * (tt + 1), :], sem="Lxa")
        d_x2_all = d_x2[NT - 1]

        v_wo_used = {}
        wo_cycle = [pAl, pAh, pBl, pBh]
        wo_i = 0
        v_acc_wo = {}
        for tt in range(NT):
            P.wait("tensor", "V", v_og[tt])
            for grp in range(2):
                pDl.twait()
                for j in range(8):
                    kt2 = grp * 8 + j
                    P.emit("tensor", "transpose", pT[:, 128 * j:128 * (j + 1)],
                           obv(tt, 128 * kt2, 128), id_sb)
                vt = P.mark("tensor", "T")
                P.wait("vector", "T", vt)
                P.emit("vector", "tensor_copy",
                       ogT[:, 1024 * grp:1024 * (grp + 1)], pT)
                v_tp = P.mark("vector", "V")
                pDl.release("V", v_tp)
            P.wait("tensor", "V", v_tp)
            for g in range(4):
                P.wait("tensor", "Lwo", d_wo[1])
                for nch in range(4):
                    pres = wo_cycle[wo_i % 4]
                    wo_i += 1
                    pres.twait()
                    for kg in range(4):
                        P.emit("tensor", "matmul", pres.ap,
                               ogT[:, 128 * (g * 4 + kg):128 * (g * 4 + kg + 1)],
                               woslot[g][:, 2048 * kg + 512 * nch:2048 * kg + 512 * (nch + 1)],
                               start=(kg == 0), stop=(kg == 3))
                    vmm = P.mark("tensor", "T")
                    if nch == 3 and tt == NT - 1:
                        v_wo_used[g] = vmm
                    P.wait("vector", "T", vmm)
                    P.wait("vector", "Lxa", d_x2_all)
                    if tt == 0 and g == 0 and nch == 0:
                        P.wait("vector", "S", v_gaB)
                    P.emit("vector", "tensor_tensor", wtmp, pres.ap,
                           bc0[:, 512 * nch:512 * (nch + 1)], ALU.mult)
                    P.emit("vector", "tensor_tensor",
                           accv(tt, 512 * nch, 512),
                           accv(tt, 512 * nch, 512), wtmp, ALU.add)
                    va = P.mark("vector", "V")
                    pres.release("V", va)
            v_acc_wo[tt] = va

        tap("og", O_QKOB, 16384, "V", v_og_all)
        if "x2" in dbg_out:
            P.wait("sync", "V", v_acc_wo[NT - 1])
            P.wait("sync", "Ly", P.ctr["Ly"])
            d = P.dma("sync", dbg_out["x2"], Fv(0, 16384), sem="Ly")
            for eng in ("vector", "scalar", "tensor", "gpsimd"):
                P.wait(eng, "Ly", d)
        cbrow = Fv(O_M, 2048, 0, 1)
        v_cbsb = bcast_batch([bc0[:, 512 * j:512 * (j + 1)] for j in range(4)],
                             16, cbrow, gates=(("V", v_acc_wo[NT - 1]),),
                             add1=True)

        gate_bf = B(O_F2 + 2048, 512)
        P.wait("gpsimd", "T", v_wo_used[3])
        d_gate = P.dma("gpsimd",
                       gate_bf.rearrange("p (a m) -> p a m", m=32),
                       gatet.rearrange("(a p) m -> p a m", p=128), sem="Lgate")

        v_h2T = {}
        v_h2_done = {}

        LGb = Fv(O_F2 + 4608, 256)
        RPb = Fv(O_F2 + 5632, 256)
        RCb = Fv(O_F2 + 6656, 256)
        RSb = Fv(O_M + 2048, 256)
        MKS = Fv(O_M + 3072, 64)
        CNTS = Fv(O_M + 3200, 64)
        R8a = Fv(O_M + 3328, 8)
        R8b = Fv(O_M + 3344, 8)
        R8c = Fv(O_M + 3360, 8)

        def ln2_A(t):
            xt = accv(t, 0, 2048)
            P.wait("vector", "V", v_acc_wo[t])
            P.emit("vector", "reduce_sum", s_mean[:, t:t + 1], xt, axis=AX.X)
            P.wait("scalar", "V", v_acc_wo[t])
            if t == 0:
                P.wait("scalar", "T", v_wo_used[3])
            P.emit("scalar", "activation", sq_scr, xt, AF.Square,
                   accum_out=s_ssq[:, t:t + 1])
            return P.mark("scalar", "S")

        sA2 = {0: ln2_A(0)}
        sA2[1] = ln2_A(1)
        for t in range(NT):
            ht, hb = htmp[t % 2], hbuf[t % 2]
            xt = accv(t, 0, 2048)
            P.wait("vector", "S", sA2[t])
            P.emit("vector", "tensor_scalar", s_mean[:, t:t + 1],
                   s_mean[:, t:t + 1], 1.0 / D, None, ALU.mult)
            P.emit("vector", "tensor_tensor", s_t1[:, t:t + 1],
                   s_mean[:, t:t + 1], s_mean[:, t:t + 1], ALU.mult)
            P.emit("vector", "tensor_scalar", s_inv[:, t:t + 1],
                   s_ssq[:, t:t + 1], 1.0 / D, None, ALU.mult)
            P.emit("vector", "tensor_tensor", s_inv[:, t:t + 1],
                   s_inv[:, t:t + 1], s_t1[:, t:t + 1], ALU.subtract)
            P.emit("vector", "tensor_scalar", s_inv[:, t:t + 1],
                   s_inv[:, t:t + 1], EPS, None, ALU.add)
            P.emit("vector", "reciprocal", s_inv[:, t:t + 1], s_inv[:, t:t + 1])
            vB = P.mark("vector", "V")
            P.wait("scalar", "V", vB)
            P.emit("scalar", "sqrt", s_inv[:, t:t + 1], s_inv[:, t:t + 1])
            sC = P.mark("scalar", "S")
            P.wait("vector", "S", sC)
            P.emit("vector", "tensor_scalar", s_nmi[:, t:t + 1],
                   s_mean[:, t:t + 1], s_inv[:, t:t + 1], -1.0,
                   ALU.mult, ALU.mult)
            vD = P.mark("vector", "V")
            P.wait("scalar", "V", vD)
            P.emit("scalar", "activation", ht, xt, AF.Identity,
                   bias=s_nmi[:, t:t + 1], scale=s_inv[:, t:t + 1])
            v_aff = P.mark("scalar", "S")
            if t + 2 < NT:
                sA2[t + 2] = ln2_A(t + 2)
            P.wait("vector", "S", v_aff)
            P.wait("vector", "S", v_cbsb)
            P.wait("vector", "S", v_sbB)
            P.emit("vector", "tensor_tensor", ht, ht, bc0, ALU.mult)
            P.emit("vector", "tensor_tensor", hb, ht, bc1, ALU.add)
            v_h2 = P.mark("vector", "V")
            v_h2_done[t] = v_h2
            P.wait("tensor", "V", v_h2)
            for grp in range(2):
                pDl.twait()
                for j in range(8):
                    kt2 = grp * 8 + j
                    P.emit("tensor", "transpose", pT[:, 128 * j:128 * (j + 1)],
                           hb[:, 128 * kt2:128 * (kt2 + 1)], id_sb)
                vt = P.mark("tensor", "T")
                P.wait("scalar", "T", vt)
                P.emit("scalar", "copy",
                       h2T3[:, grp * 8:(grp + 1) * 8, 128 * t:128 * (t + 1)],
                       pT.rearrange("p (a b) -> p a b", b=128))
                vv = P.mark("scalar", "S")
                pDl.release("S", vv)
            v_h2T[t] = vv
            pCl.twait()
            P.wait("tensor", "Lgate", d_gate)
            P.wait("tensor", "S", vv)
            for kt2 in range(KT):
                P.emit("tensor", "matmul", pC_t[:, 0:32],
                       h2T(kt2, 128 * t, 128), gate_bf[:, 32 * kt2:32 * (kt2 + 1)],
                       start=(kt2 == 0), stop=(kt2 == KT - 1))
            v_lg = P.mark("tensor", "T")
            v_lg_mm_last = v_lg
            P.wait("scalar", "T", v_lg)
            P.emit("scalar", "copy", LGb[:, 32 * t:32 * (t + 1)], pC_t[:, 0:32])
            v_lgc = P.mark("scalar", "S")
            pCl.release("S", v_lgc)
        v_lg_all = v_lgc

        tap("h2T", O_QKOB, 16384, "V", v_h2T[NT - 1])
        rgrow = Fv(O_F2, 2048, 0, 1)
        v_rgb = bcast_batch([bc1[:, 512 * j:512 * (j + 1)] for j in range(4)],
                            20, rgrow, gates=(("V", v_h2_done[NT - 1]),
                                              ("T", v_lg_mm_last)),
                            row_math=rgmath_w)
        P.wait("vector", "S", v_rgb)
        v_accr = {}
        for tt in range(NT):
            P.emit("vector", "tensor_tensor", accv(tt, 0, 2048),
                   accv(tt, 0, 2048), bc1, ALU.mult)
            v_accr[tt] = P.mark("vector", "V")

        def seg(ap):
            return ap.rearrange("p (a b) -> p a b", b=32)

        def col8(ap):
            return ap.rearrange("p (a b) -> p a b", b=1)

        def bc8(ap):
            return col8(ap).broadcast_to([128, 8, 32])

        def segred(dst8, src256, op):
            P.emit("vector", "tensor_reduce", col8(dst8), seg(src256),
                   axis=AX.X, op=op)

        MKS3 = MKS.rearrange("p (a b) -> p a b", b=8)
        P.wait("vector", "S", v_lg_all)
        segred(R8a, LGb, ALU.max)
        P.emit("vector", "tensor_tensor", seg(RPb), seg(LGb), bc8(R8a),
               ALU.subtract)
        v_sm = P.mark("vector", "V")
        P.wait("scalar", "V", v_sm)
        P.emit("scalar", "activation", RPb, RPb, AF.Exp)
        v_exp = P.mark("scalar", "S")
        P.wait("vector", "S", v_exp)
        segred(R8a, RPb, ALU.add)
        P.emit("vector", "reciprocal", R8a, R8a)
        P.emit("vector", "tensor_tensor", seg(RPb), seg(RPb), bc8(R8a),
               ALU.mult)
        P.emit("vector", "tensor_copy", RCb, RPb)
        for k in range(8):
            P.emit("vector", "tensor_reduce", MKS3[:, :, k:k + 1], seg(RCb),
                   axis=AX.X, op=ALU.max)
            P.emit("vector", "tensor_tensor", seg(RSb), seg(RCb),
                   MKS3[:, :, k:k + 1].broadcast_to([128, 8, 32]), ALU.is_equal)
            P.emit("vector", "scalar_tensor_tensor", RCb, RSb, -1e30, RCb,
                   ALU.mult, ALU.add)
        for k in range(8):
            P.emit("vector", "tensor_tensor", seg(RSb), seg(RPb),
                   MKS3[:, :, k:k + 1].broadcast_to([128, 8, 32]), ALU.is_ge)
            P.emit("vector", "tensor_reduce",
                   CNTS.rearrange("p (a b) -> p a b", b=8)[:, :, k:k + 1],
                   seg(RSb), axis=AX.X, op=ALU.add)
        P.emit("vector", "tensor_scalar", CNTS, CNTS, 8.0, None, ALU.is_ge)
        P.emit("vector", "tensor_tensor", CNTS, CNTS, MKS, ALU.mult)
        P.emit("vector", "tensor_reduce", col8(R8b),
               CNTS.rearrange("p (a b) -> p a b", b=8), axis=AX.X, op=ALU.max)
        P.emit("vector", "tensor_tensor", seg(RCb), seg(RPb), bc8(R8b),
               ALU.is_ge)
        P.emit("vector", "tensor_tensor", seg(RSb), seg(RPb), seg(RCb),
               ALU.mult)
        segred(R8b, RSb, ALU.add)
        P.emit("vector", "reciprocal", R8b, R8b)
        P.wait("vector", "S", v_alb)
        P.emit("vector", "tensor_scalar", R8b, R8b, albB, None, ALU.mult)
        P.emit("vector", "tensor_tensor", seg(RSb), seg(RSb), bc8(R8b),
               ALU.mult)
        P.emit("vector", "tensor_scalar", RSb, RSb, TAU, None, ALU.is_ge)
        segred(R8b, RSb, ALU.max)
        P.emit("vector", "tensor_scalar", R8b, R8b, 0.0, None, ALU.is_equal)
        segred(R8c, RPb, ALU.max)
        P.emit("vector", "tensor_tensor", seg(RCb), seg(RPb), bc8(R8c),
               ALU.is_equal)
        P.emit("vector", "tensor_tensor", seg(RCb), seg(RCb), bc8(R8b),
               ALU.mult)
        P.emit("vector", "tensor_tensor", RSb, RSb, RCb, ALU.max)
        P.emit("vector", "tensor_tensor", RSb, RPb, RSb, ALU.mult)
        segred(R8c, RSb, ALU.add)
        P.emit("vector", "tensor_scalar", R8c, R8c, 1e-12, None, ALU.max)
        P.emit("vector", "reciprocal", R8c, R8c)
        P.emit("vector", "tensor_tensor", seg(comb[:, 0:256]), seg(RSb),
               bc8(R8c), ALU.mult)
        v_comb_all = P.mark("vector", "V")
        if "comb" in dbg_out:
            P.wait("sync", "V", v_comb_all)
            P.wait("sync", "Ly", P.ctr["Ly"])
            d = P.dma("sync", dbg_out["comb"], comb, sem="Ly")
            for eng in ("vector", "scalar", "tensor", "gpsimd"):
                P.wait(eng, "Ly", d)

        gu_sp = [B(O_F1 + 4096 * s, 4096) for s in range(2)]
        heT = B(O_F1 + 8192, 6144)
        sgb = [B(O_F1 + 14336, 512), B(O_F1 + 14848, 512)]
        dn_s = [B(O_VB + 3072 * s, 3072) for s in range(4)]

        gu_used = {}
        dn_used = {}
        d_gu = {}
        d_dn = {}

        def prefetch(e, pr):
            idx = e * 6 + pr
            P.wait("gpsimd", "T", gu_used.get(idx - 2, 0))
            if idx == 0:
                P.wait("gpsimd", "T", v_wo_used[2])
                P.wait("gpsimd", "S", v_h2T[NT - 1])
            d_gu[idx] = P.dma("gpsimd",
                              gu_sp[idx % 2].rearrange("p (b a m) -> p b a m", b=2, m=128),
                              gup[e, pr].rearrange("b (a p) m -> p b a m", p=128),
                              sem="Lgu%d" % (idx % 2))

        def prefetch_dn(e, nch):
            idx = e * 4 + nch
            P.wait("gpsimd", "T", dn_used.get(idx - 4, 0))
            if idx == 0:
                P.wait("gpsimd", "T", v_wo_used[0])
                P.wait("gpsimd", "T", v_wo_used[1])
            d_dn[idx] = P.dma("gpsimd",
                              dn_s[idx % 4].rearrange("p (a m) -> p a m", m=512),
                              dnp[e, nch].rearrange("(a p) m -> p a m", p=128),
                              sem="Ldn%d" % (idx % 4))

        for pr in range(2):
            prefetch(0, pr)
        for nch in range(4):
            prefetch_dn(0, nch)

        v_last_acc = None
        pe_cycle = [(pAl, pAh), (pBl, pBh), (pCl, pCh), (pDl, pDh)]
        v_he = {}
        sg_busy = {}
        for e in range(NEXP):
            pe_i = 0
            for pr in range(6):
                idx = e * 6 + pr
                sl = gu_sp[idx % 2]
                P.wait("tensor", "Lgu%d" % (idx % 2), d_gu[idx])
                if e == 0 and pr == 0:
                    P.wait("tensor", "S", v_h2T[NT - 1])
                for tch in range(2):
                    pg, pu = pe_cycle[pe_i % 4]
                    pe_i += 1
                    pg.twait()
                    pu.twait()
                    P.wait("tensor", "V", v_he.get((e - 1, pr, tch), 0))
                    for kt2 in range(KT):
                        P.emit("tensor", "matmul", pg.ap,
                               sl[:, 128 * kt2:128 * (kt2 + 1)],
                               h2T(kt2, 512 * tch, 512),
                               start=(kt2 == 0), stop=(kt2 == KT - 1))
                    for kt2 in range(KT):
                        P.emit("tensor", "matmul", pu.ap,
                               sl[:, 2048 + 128 * kt2:2048 + 128 * (kt2 + 1)],
                               h2T(kt2, 512 * tch, 512),
                               start=(kt2 == 0), stop=(kt2 == KT - 1))
                    vmm = P.mark("tensor", "T")
                    if tch == 1:
                        gu_used[idx] = vmm
                        if pr + 2 < 6:
                            prefetch(e, pr + 2)
                        elif e + 1 < NEXP:
                            prefetch(e + 1, pr + 2 - 6)
                    P.wait("scalar", "T", vmm)
                    sgi = pe_i % 2
                    sg = sgb[sgi]
                    P.wait("scalar", "V", sg_busy.get(sgi, 0))
                    P.emit("scalar", "activation", sg, pg.ap, AF.Silu)
                    vsg = P.mark("scalar", "S")
                    P.wait("vector", "S", vsg)
                    P.emit("vector", "tensor_tensor",
                           heT[:, 1024 * pr + 512 * tch:1024 * pr + 512 * (tch + 1)],
                           sg, pu.ap, ALU.mult)
                    vhe = P.mark("vector", "V")
                    sg_busy[sgi] = vhe
                    v_he[(e, pr, tch)] = vhe
                    pg.release("V", vhe)
                    pu.release("V", vhe)
            dn_cycle = [pAl, pAh, pBl, pBh]
            pe_i = 0
            for tt in range(NT):
                for nch in range(4):
                    idx = e * 4 + nch
                    pres = dn_cycle[pe_i % 4]
                    pe_i += 1
                    pres.twait()
                    P.wait("tensor", "Ldn%d" % (idx % 4), d_dn[idx])
                    if tt == 0:
                        for pr in range(6):
                            P.wait("tensor", "V", v_he[(e, pr, 0)])
                            P.wait("tensor", "V", v_he[(e, pr, 1)])
                    for kt2 in range(6):
                        P.emit("tensor", "matmul", pres.ap,
                               heT[:, 1024 * kt2 + 128 * tt:1024 * kt2 + 128 * (tt + 1)],
                               dn_s[idx % 4][:, 512 * kt2:512 * (kt2 + 1)],
                               start=(kt2 == 0), stop=(kt2 == 5))
                    vmm = P.mark("tensor", "T")
                    if tt == NT - 1:
                        dn_used[idx] = vmm
                        if e + 1 < NEXP:
                            prefetch_dn(e + 1, nch)
                    P.wait("vector", "T", vmm)
                    if e == 0:
                        P.wait("vector", "V", v_accr[tt])
                        P.wait("vector", "V", v_comb_all)
                    P.emit("vector", "scalar_tensor_tensor",
                           accv(tt, 512 * nch, 512), pres.ap,
                           comb[:, 32 * tt + e:32 * tt + e + 1],
                           accv(tt, 512 * nch, 512), ALU.mult, ALU.add)
                    va = P.mark("vector", "V")
                    pres.release("V", va)
            v_last_acc = va

        v_gbB = bcast_batch([bc0[:, 512 * j:512 * (j + 1)] for j in range(4)],
                            20, rgrow, gates=(("V", v_h2_done[NT - 1]),),
                            row_math=gamath_w_late)
        P.wait("vector", "S", v_gbB)
        P.wait("vector", "V", v_last_acc)
        d_y = {}
        for tt in range(NT):
            P.emit("vector", "tensor_tensor", accv(tt, 0, 2048),
                   accv(tt, 0, 2048), bc0, ALU.mult)
            vo = P.mark("vector", "V")
            P.wait("sync", "V", vo)
            d_y[tt] = P.dma("sync", y[128 * tt:128 * (tt + 1), :],
                            accv(tt, 0, 2048), sem="Ly")
        P.wait("sync", "Ly", d_y[NT - 1])

        @block.tensor
        def _(e):
            for f in P.ops["tensor"]:
                f(e)

        @block.vector
        def _(e):
            for f in P.ops["vector"]:
                f(e)

        @block.scalar
        def _(e):
            for f in P.ops["scalar"]:
                f(e)

        @block.gpsimd
        def _(e):
            for f in P.ops["gpsimd"]:
                f(e)

        @block.sync
        def _(e):
            for f in P.ops["sync"]:
                f(e)

    return nc


_cache = {}


def _prep(inputs):
    f32 = np.float32
    x = np.ascontiguousarray(np.asarray(inputs["L0_attn_x"], f32).reshape(T, D))
    wq = np.asarray(inputs["L0_attn_Wq"], f32)
    wk = np.asarray(inputs["L0_attn_Wk"], f32)
    wv = np.asarray(inputs["L0_attn_Wv"], f32)
    wg = np.asarray(inputs["L0_attn_Wg"], f32)
    wo = np.asarray(inputs["L0_attn_Wo"], f32)
    gnw = np.asarray(inputs["L0_attn_gnorm_w"], f32).reshape(1, 512)
    gwt = np.asarray(inputs["L0_ffn_gate_weight"], f32)
    gu = np.asarray(inputs["L0_ffn_gate_up_proj"], f32)
    dn = np.asarray(inputs["L0_ffn_down_proj"], f32)
    alpha = np.asarray(inputs["L0_ffn_layer_alpha"], f32).reshape(1, 1)
    cv = np.asarray(inputs["L0_c"], f32).reshape(-1)
    wmod = np.asarray(inputs["L0_wmod"], f32)

    wq_m = np.ascontiguousarray(wq.T.reshape(D, 8, 128).transpose(1, 0, 2))
    wk_m = np.ascontiguousarray(wk.T.reshape(D, 8, 128).transpose(1, 0, 2))
    wv_n = np.ascontiguousarray(wv.T.reshape(D, 4, 512).transpose(1, 0, 2))
    wg_n = np.ascontiguousarray(wg.T.reshape(D, 4, 512).transpose(1, 0, 2))
    wo_g = np.ascontiguousarray(wo.T.reshape(4, 512, D))
    gu_t = gu.transpose(0, 2, 1).reshape(NEXP, D, 12, 128).transpose(0, 2, 1, 3)
    gup = np.ascontiguousarray(
        gu_t.reshape(NEXP, 2, 6, D, 128).transpose(0, 2, 1, 3, 4))
    dnp = np.ascontiguousarray(
        dn.transpose(0, 2, 1).reshape(NEXP, INTER, 4, 512).transpose(0, 2, 1, 3))
    gatet = np.ascontiguousarray(gwt.T)
    c_cols = np.ascontiguousarray(cv.reshape(16, 128).T)

    tglob = np.arange(T, dtype=f32)
    invf = (1.0 / (10000.0 ** (np.arange(0, DK, 2, dtype=f32) / DK))).astype(f32)
    freqs = (tglob[:, None] * invf[None, :]).astype(f32)
    cosb = np.cos(freqs).astype(f32).T
    sinb = np.sin(freqs).astype(f32).T
    ip = np.arange(T, dtype=np.float64) % 128
    tabs_all = np.zeros((NCORE, 2, H, 2, 128, TLOC), f32)
    for h in range(H):
        qs = (np.float64(GAMMA[h]) ** (ip + 1) * DK ** -0.5)
        ks = (np.float64(GAMMA[h]) ** (-ip - 1))
        for c in range(NCORE):
            sl = slice(TLOC * c, TLOC * (c + 1))
            tabs_all[c, 0, h, 0] = cosb[:, sl] * qs[None, sl]
            tabs_all[c, 0, h, 1] = sinb[:, sl] * qs[None, sl]
            tabs_all[c, 1, h, 0] = cosb[:, sl] * ks[None, sl]
            tabs_all[c, 1, h, 1] = sinb[:, sl] * ks[None, sl]

    identity = np.eye(128, dtype=BFNP)
    ii = np.arange(128)
    causal = (ii[None, :] >= ii[:, None]).astype(BFNP)

    sinw_all = np.zeros((NCORE, 128, 32), f32)
    for c in range(NCORE):
        for h in range(H):
            for j in range(NCORE):
                if j < c:
                    sinw_all[c, :, 8 * h + j] = np.float64(GAMMA[h]) ** (1024.0 * (c - 1 - j))

    per_core = []
    for c in range(NCORE):
        m = dict(
            x_t=np.ascontiguousarray(x[TLOC * c:TLOC * (c + 1)]),
            wq_m=wq_m, wk_m=wk_m, wv_n=wv_n, wg_n=wg_n, wo_g=wo_g,
            gup=gup, dnp=dnp, gatet=gatet,
            wmod_s=np.ascontiguousarray(
                wmod[:, 1536 * c:1536 * (c + 1)].reshape(KT, 128, 1536)),
            c_cols=c_cols, gnorm=gnw, alpha2=alpha,
            tabs=np.ascontiguousarray(tabs_all[c]),
            ident=identity, causalT=causal,
            sinw=np.ascontiguousarray(sinw_all[c]),
        )
        per_core.append(m)
    return per_core


def kernel(**inputs):
    per_core = _prep(inputs)
    if "nc" not in _cache:
        _cache["nc"] = build()
    nc = _cache["nc"]
    res = run_bass_kernel_spmd(nc, per_core, core_ids=list(range(NCORE)))
    out = np.concatenate([np.asarray(r["y"]) for r in res.results], axis=0)
    return out.reshape(1, T, D).astype(np.float32)




# version 11
# speedup vs baseline: 1.2709x; 1.0000x over previous
import contextlib
import numpy as np
import ml_dtypes
import concourse.bass as bass
import concourse.mybir as mybir
from concourse.bass_utils import run_bass_kernel_spmd

F32 = mybir.dt.float32
BF16 = mybir.dt.bfloat16
AF = mybir.ActivationFunctionType
ALU = mybir.AluOpType
AX = mybir.AxisListType
BFNP = ml_dtypes.bfloat16

NCORE = 8
T = 8192
TLOC = 1024
D = 2048
H = 4
DK = 256
DV = 512
KT = 16
NT = 8
NEXP = 32
INTER = 768
GAMMA = [float(1.0 - 2.0 ** (-5 - h)) for h in range(H)]
G128 = [float(np.float64(g) ** 128) for g in GAMMA]
EPS = 1e-5
TAU = 0.05

O_HT = 0
O_GP = 16384
O_QKOB = 32768
O_VB = 49152
O_F1 = 65536
O_F2 = 81920
O_M = 90112
O_BC0 = 94208
O_BC1 = 98304
O_ST = 102400
O_KTK = 104448
ACOLS = 105472


class Prog:
    ENGS = ("tensor", "vector", "scalar", "gpsimd", "sync")

    def __init__(self, sems):
        self.ops = {k: [] for k in self.ENGS}
        self.sems = sems
        self.ctr = {k: 0 for k in sems}
        self.last = {k: None for k in self.ENGS}
        self.last_kind = {k: None for k in self.ENGS}
        self.last_mark = {k: None for k in self.ENGS}

    def emit(self, eng, meth, *args, inc=None, incn=1, **kw):
        semh = self.sems[inc] if inc else None
        last = self.last
        dr = eng in ("vector", "scalar") and meth != "dma_start"

        def f(e, meth=meth, args=args, kw=kw, semh=semh, incn=incn, eng=eng, dr=dr):
            if dr:
                e.drain()
            ins = getattr(e, meth)(*args, **kw)
            last[eng] = ins
            if semh is not None:
                ins.then_inc(semh, incn)
        self.ops[eng].append(f)
        self.last_kind[eng] = "emit" if inc is None else "emit_inc"
        self.last_mark[eng] = inc
        if inc:
            self.ctr[inc] += incn
            return self.ctr[inc]
        return None

    def wait(self, eng, sem, v):
        if not v or v <= 0:
            return
        semh = self.sems[sem]

        def f(e, semh=semh, v=v):
            e.wait_ge(semh, v)
        self.ops[eng].append(f)

    def mark(self, eng, sem):
        if self.last_kind[eng] in ("mark", "emit_inc") and \
                self.last_mark[eng] == sem:
            return self.ctr[sem]
        assert self.last_kind[eng] in ("emit",), \
            f"mark({eng},{sem}) after {self.last_kind[eng]}"
        semh = self.sems[sem]
        last = self.last

        def f(e, semh=semh, eng=eng):
            last[eng].then_inc(semh, 1)
        self.ops[eng].append(f)
        self.last_kind[eng] = "mark"
        self.last_mark[eng] = sem
        self.ctr[sem] += 1
        return self.ctr[sem]

    def dma(self, queue, out, in_, sem=None):
        assert sem is not None, "dma needs an explicit slot semaphore"
        return self.emit(queue, "dma_start", inc=sem, incn=16, out=out, in_=in_)


class Psum:

    def __init__(self, P, ap):
        self.P, self.ap = P, ap
        self.free_at = None

    def twait(self):
        if self.free_at is not None:
            self.P.wait("tensor", *self.free_at)
            self.free_at = None

    def release(self, sem, val):
        self.free_at = (sem, val)


def build(dbg=()):
    nc = bass.Bass("TRN2", target_bir_lowering=False)

    def di(name, shape, dtype=F32):
        return nc.dram_tensor(name, shape, dtype, kind="ExternalInput").ap()

    x_t = di("x_t", [TLOC, D])
    wq_m = di("wq_m", [8, D, 128])
    wk_m = di("wk_m", [8, D, 128])
    wv_n = di("wv_n", [4, D, 512])
    wg_n = di("wg_n", [4, D, 512])
    wo_g = di("wo_g", [4, 512, D])
    gup = di("gup", [NEXP, 6, 2, D, 128])
    dnp = di("dnp", [NEXP, 4, INTER, 512])
    gatet = di("gatet", [D, 32])
    wmod_s = di("wmod_s", [KT, 128, 1536])
    c_cols = di("c_cols", [128, 16])
    gnorm = di("gnorm", [1, 512])
    alpha2 = di("alpha2", [1, 1])
    tabs = di("tabs", [2, H, 2, 128, TLOC])
    ident = di("ident", [128, 128], BF16)
    causalT = di("causalT", [128, 128], BF16)
    sinw = di("sinw", [128, 32])

    y = nc.dram_tensor("y", [TLOC, D], F32, kind="ExternalOutput").ap()
    dbg_out = {}
    for name, shape, dt in dbg:
        dbg_out[name] = nc.dram_tensor("dbg_" + name, list(shape), dt,
                                       kind="ExternalOutput").ap()

    modg_i = nc.dram_tensor("modg_i", [1, 1536], F32).ap()
    modg_o = nc.dram_tensor("modg_o", [NCORE, 1536], F32).ap()
    lg_i = nc.dram_tensor("lg_i", [128, 4096], BF16).ap()
    lg_o = nc.dram_tensor("lg_o", [NCORE * 128, 4096], BF16).ap()

    with contextlib.ExitStack() as ctx:
        en = ctx.enter_context
        AR = en(nc.sbuf_tensor("AR", [128, ACOLS], BF16))
        pA_t = en(nc.psum_tensor("pA", [128, 1024], F32))
        pB_t = en(nc.psum_tensor("pB", [128, 1024], F32))
        pC_t = en(nc.psum_tensor("pC", [128, 1024], F32))
        pD_t = en(nc.psum_tensor("pD", [128, 1024], F32))
        semnames = ["T", "V", "S", "C", "Lc", "Lx0", "Lx1", "Lx2", "Lx3", "Ltab",
                    "Lm0", "Lm1", "Lm2", "Lm3", "Lvg0", "Lvg1", "Ltab0", "Ltab1",
                    "Lwm0", "Lwm1", "Lwm2", "Lwm3", "Lgu0", "Lgu1",
                    "Ldn0", "Ldn1", "Ldn2", "Ldn3", "Lwo", "Lxa", "Ly",
                    "Llr0", "Llr1", "Lch0", "Lch1", "Lmi", "Lli", "Lgate"]
        semh = {s: en(nc.semaphore("sem" + s)) for s in semnames}
        block = en(nc.Block())
        P = Prog(semh)

        def B(o, n, p0=0, p1=128):
            return AR[p0:p1, o:o + n]

        def Fv(o, n, p0=0, p1=128):
            return AR[p0:p1, o:o + 2 * n].bitcast(F32)

        pAl, pAh = Psum(P, pA_t[:, 0:512]), Psum(P, pA_t[:, 512:1024])
        pBl, pBh = Psum(P, pB_t[:, 0:512]), Psum(P, pB_t[:, 512:1024])
        pCl, pCh = Psum(P, pC_t[:, 0:512]), Psum(P, pC_t[:, 512:1024])
        pDl, pDh = Psum(P, pD_t[:, 0:512]), Psum(P, pD_t[:, 512:1024])
        pT = pD_t[:, 0:512].bitcast(BF16)

        st_cur = [O_ST]

        def st_alloc(nbf):
            o = st_cur[0]
            st_cur[0] = (o + nbf + 15) // 16 * 16
            assert st_cur[0] <= O_ST + 2048, f"ST overflow {st_cur[0]}"
            return o

        ones = Fv(st_alloc(256), 128)
        csb = Fv(st_alloc(32), 16)
        silc = Fv(st_alloc(32), 16)
        id_sb = B(st_alloc(128), 128)
        cau_sb = B(st_alloc(128), 128)
        sinw_sb = Fv(st_alloc(64), 32)
        albB = Fv(st_alloc(16), 1)
        s_mean = Fv(st_alloc(16), 8)
        s_inv = Fv(st_alloc(16), 8)
        s_nmi = Fv(st_alloc(16), 8)
        s_ssq = Fv(st_alloc(16), 8)
        s_oiv = Fv(st_alloc(16), 8)
        s_t1 = Fv(st_alloc(16), 8)
        comb = Fv(st_alloc(512), 256)
        r_p = Fv(st_alloc(64), 32)
        r_cur = Fv(st_alloc(64), 32)
        r_tmp = Fv(st_alloc(64), 32)
        r_mk = Fv(st_alloc(16), 8)
        r_cnt = Fv(st_alloc(16), 8)
        r_a1 = Fv(st_alloc(64), 32)
        attn_sb = B(st_alloc(128), 128)
        attn_sbB = B(st_alloc(128), 128)
        attn_sb4 = [B(O_HT + 10240 + 128 * i, 128) for i in range(8)]

        chrow = [Fv(O_KTK, 512, 0, 1), Fv(O_M, 512, 0, 1)]
        rtmprow = Fv(O_M + 1024, 512, 0, 1)
        rtmprow_w = Fv(O_HT + 15360, 512, 0, 1)
        rtmprow_w2 = Fv(O_M, 1024, 0, 1)

        def hT(kt, c0, n):
            return B(O_HT + kt * 1024 + c0, n)

        hT3 = B(O_HT, 16384).rearrange("p (a b) -> p a b", b=1024)
        h2T3 = B(O_QKOB, 16384).rearrange("p (a b) -> p a b", b=1024)

        def h2T(kt, c0, n):
            return B(O_QKOB + kt * 1024 + c0, n)

        def qseg(p, h, half):
            return B(O_QKOB + p * 2048 + h * 512 + half * 128, 128)

        def kseg(p, h, half):
            return B(O_QKOB + p * 2048 + h * 512 + 256 + half * 128, 128)

        def obv(p, c0, n):
            return B(O_QKOB + p * 2048 + c0, n)

        def vtile(p, c0, n):
            return B(O_VB + p * 2048 + c0, n)

        def gtile(p, c0, n):
            return B(O_GP + p * 2048 + c0, n)

        def accv(tt, c0, n):
            return Fv(4096 * tt + 2 * c0, n)

        def Mv(dt8, n=512):
            return B(O_M + dt8 * 512, n)

        ktok = B(O_KTK, 1024)
        ogT = B(O_M, 2048)

        P.emit("vector", "memset", ones, 1.0)
        v_ones = P.mark("vector", "V")

        P.dma("sync", csb, c_cols, sem="Lc")
        P.dma("sync", id_sb, ident, sem="Lc")
        P.dma("sync", cau_sb, causalT, sem="Lc")
        d_const = P.dma("sync", sinw_sb, sinw, sem="Lc")

        wslot = [Fv(O_VB + 3072 * s, 1536) for s in range(4)]
        modloc = Fv(O_VB + 12288, 1536, 0, 1)

        P.wait("scalar", "Lc", d_const)
        v_silc = P.emit("scalar", "activation", silc, csb, AF.Silu, inc="S")

        mps = [pA_t[0:1, 0:512], pB_t[0:1, 0:512], pC_t[0:1, 0:512]]
        P.wait("tensor", "S", v_silc)
        t_after_k = {}
        for k in range(KT):
            q = "gpsimd" if k % 2 == 0 else "sync"
            if k >= 4:
                P.wait(q, "T", t_after_k[k - 4])
            d = P.dma(q, wslot[k % 4], wmod_s[k], sem="Lwm%d" % (k % 4))
            P.wait("tensor", "Lwm%d" % (k % 4), d)
            for nch in range(3):
                P.emit("tensor", "matmul", mps[nch], silc[:, k:k + 1],
                       wslot[k % 4][:, 512 * nch:512 * (nch + 1)],
                       start=(k == 0), stop=(k == KT - 1))
            t_after_k[k] = P.mark("tensor", "T")
        v_modmm = t_after_k[KT - 1]

        P.wait("scalar", "T", v_modmm)
        for nch in range(3):
            P.emit("scalar", "copy", modloc[:, 512 * nch:512 * (nch + 1)], mps[nch])
        v_modev = P.mark("scalar", "S")
        pAl.release("S", v_modev)
        pBl.release("S", v_modev)
        pCl.release("S", v_modev)

        P.wait("gpsimd", "S", v_modev)
        d_mi = P.dma("gpsimd", modg_i[:, :], modloc, sem="Lmi")
        P.wait("gpsimd", "Lmi", d_mi)
        v_cc1 = P.emit("gpsimd", "collective_compute", "AllGather", ALU.bypass,
                       replica_groups=[list(range(NCORE))],
                       ins=[modg_i[:, :]], outs=[modg_o[:, :]], inc="C")

        modflat = modg_o.rearrange("a b -> (a b)")
        bc_state = {"i": 0, "last_mm": {0: 0, 1: 0}}
        bb_state = {"last_mm": 0}

        def bcast_batch(dsts, chunk0, row, gates=(), row_math=None, add1=False):
            n = len(dsts)
            P.wait("sync", "C", v_cc1)
            for sem, val in gates:
                P.wait("sync", sem, val)
            P.wait("sync", "T", bb_state["last_mm"])
            d = P.dma("sync", row[:, 0:512 * n],
                      modflat[512 * chunk0:512 * (chunk0 + n)].unsqueeze(0),
                      sem="Lch1")
            sem_mm, val_mm = "Lch1", d
            if row_math is not None:
                P.wait("vector", sem_mm, d)
                row_math(row[:, 0:512 * n])
                val_mm = P.mark("vector", "V")
                sem_mm = "V"
            P.wait("tensor", sem_mm, val_mm)
            P.wait("tensor", "V", v_ones)
            evs = []
            for i in range(n):
                pres = pBh if i % 2 == 0 else pAh
                pres.twait()
                P.emit("tensor", "matmul", pres.ap, ones[0:1, :],
                       row[:, 512 * i:512 * (i + 1)], start=True, stop=True)
                mmv = P.mark("tensor", "T")
                P.wait("scalar", "T", mmv)
                if add1:
                    P.emit("scalar", "add", dsts[i], pres.ap, 1.0)
                else:
                    P.emit("scalar", "copy", dsts[i], pres.ap)
                ev = P.mark("scalar", "S")
                pres.release("S", ev)
                evs.append(ev)
            bb_state["last_mm"] = P.ctr["T"]
            return evs[-1]

        def gamath_w(row, scr_late=False):
            for c0 in range(0, row.shape[1], 512):
                half = row[:, c0:c0 + 512]
                scr = (rtmprow_w2 if scr_late else rtmprow_w)[:, 0:512]
                P.emit("vector", "tensor_scalar", scr, half, 0.0, None,
                       ALU.is_ge)
                P.emit("vector", "tensor_scalar", scr, scr, 2.0, -1.0,
                       ALU.mult, ALU.add)
                P.emit("vector", "tensor_tensor", half, half, scr, ALU.mult)
                P.emit("vector", "tensor_scalar", half, half, 0.25, None,
                       ALU.max)
                P.emit("vector", "tensor_tensor", half, half, scr, ALU.mult)

        def gamath_w_late(row):
            gamath_w(row, scr_late=True)

        def rgmath_w(row):
            gamath_w(row, scr_late=True)
            P.emit("vector", "reciprocal", row, row)

        def bcast512(dst, chunk_idx, add1=False, row_math=None):
            i = bc_state["i"] % 2
            bc_state["i"] += 1
            row = chrow[i]
            P.wait("sync", "C", v_cc1)
            P.wait("sync", "T", bc_state["last_mm"][i])
            d = P.dma("sync", row,
                      modflat[512 * chunk_idx:512 * (chunk_idx + 1)].unsqueeze(0),
                      sem="Lch%d" % i)
            sem_mm, val_mm = "Lch%d" % i, d
            if row_math is not None:
                P.wait("vector", sem_mm, d)
                row_math(row)
                val_mm = P.mark("vector", "V")
                sem_mm = "V"
            P.wait("tensor", sem_mm, val_mm)
            P.wait("tensor", "V", v_ones)
            pBh.twait()
            P.emit("tensor", "matmul", pBh.ap, ones[0:1, :], row,
                   start=True, stop=True)
            mmv = P.mark("tensor", "T")
            bc_state["last_mm"][i] = mmv
            P.wait("scalar", "T", mmv)
            if add1:
                P.emit("scalar", "add", dst, pBh.ap, 1.0)
            else:
                P.emit("scalar", "copy", dst, pBh.ap)
            ev = P.mark("scalar", "S")
            pBh.release("S", ev)
            return ev

        def gamath(row):
            P.emit("vector", "tensor_scalar", rtmprow, row, 0.0, None, ALU.is_ge)
            P.emit("vector", "tensor_scalar", rtmprow, rtmprow, 2.0, -1.0,
                   ALU.mult, ALU.add)
            P.emit("vector", "tensor_tensor", row, row, rtmprow, ALU.mult)
            P.emit("vector", "tensor_scalar", row, row, 0.25, None, ALU.max)
            P.emit("vector", "tensor_tensor", row, row, rtmprow, ALU.mult)

        def rgmath(row):
            gamath(row)
            P.emit("vector", "reciprocal", row, row)

        bc0 = Fv(O_BC0, 2048)
        bc1 = Fv(O_BC1, 2048)

        xtile = [Fv(O_F2, 2048), Fv(O_F2 + 4096, 2048),
                 Fv(O_QKOB, 2048), Fv(O_QKOB + 4096, 2048)]
        d_x = {t: P.dma("sync", xtile[t], x_t[128 * t:128 * (t + 1), :],
                        sem="Lx%d" % t) for t in range(4)}

        d_al = P.dma("sync", chrow[0][:, 0:1], alpha2, sem="Lch0")
        P.wait("tensor", "Lch0", d_al)
        P.wait("tensor", "V", v_ones)
        pBh.twait()
        P.emit("tensor", "matmul", pB_t[:, 512:513], ones[0:1, :],
               chrow[0][:, 0:1], start=True, stop=True)
        v_mm = P.mark("tensor", "T")
        bc_state["last_mm"][0] = v_mm
        P.wait("scalar", "T", v_mm)
        P.emit("scalar", "copy", albB, pB_t[:, 512:513])
        v_alb = P.mark("scalar", "S")
        pBh.release("S", v_alb)

        carow = Fv(O_VB, 2048, 0, 1)
        bcast_batch([bc0[:, 512 * j:512 * (j + 1)] for j in range(4)], 4,
                    carow, add1=True)
        v_casa = bcast_batch([bc1[:, 512 * j:512 * (j + 1)] for j in range(4)],
                             0, carow)

        htmp = [Fv(O_F1, 2048), Fv(O_F1 + 4096, 2048)]
        hbuf = [B(O_F1 + 8192, 2048), B(O_F1 + 10240, 2048)]
        sq_scr = Fv(O_F1 + 12288, 2048)
        x_free_at = {}
        v_h = {}
        tpos_res = {0: pDl, 1: pCl}
        tpos_bf = {0: pT, 1: pC_t[:, 0:512].bitcast(BF16)}

        v_h = {}
        x_free_at = {}
        v_FG = {}

        def p1_A(t):
            xt = xtile[t % 4]
            xsem = "Lx%d" % (t % 4)
            P.wait("vector", xsem, d_x[t])
            P.emit("vector", "reduce_sum", s_mean[:, t:t + 1], xt, axis=AX.X)
            P.wait("scalar", xsem, d_x[t])
            P.emit("scalar", "activation", sq_scr, xt, AF.Square,
                   accum_out=s_ssq[:, t:t + 1])
            return P.mark("scalar", "S")

        sA_done = {}
        sA_done[0] = p1_A(0)
        sA_done[1] = p1_A(1)
        FGd = {}
        for t in range(NT):
            xt, ht, hb = xtile[t % 4], htmp[t % 2], hbuf[t % 2]
            P.wait("vector", "S", sA_done[t])
            P.emit("vector", "tensor_scalar", s_mean[:, t:t + 1],
                   s_mean[:, t:t + 1], 1.0 / D, None, ALU.mult)
            P.emit("vector", "tensor_tensor", s_t1[:, t:t + 1],
                   s_mean[:, t:t + 1], s_mean[:, t:t + 1], ALU.mult)
            P.emit("vector", "tensor_scalar", s_inv[:, t:t + 1],
                   s_ssq[:, t:t + 1], 1.0 / D, None, ALU.mult)
            P.emit("vector", "tensor_tensor", s_inv[:, t:t + 1],
                   s_inv[:, t:t + 1], s_t1[:, t:t + 1], ALU.subtract)
            P.emit("vector", "tensor_scalar", s_inv[:, t:t + 1],
                   s_inv[:, t:t + 1], EPS, None, ALU.add)
            P.emit("vector", "reciprocal", s_inv[:, t:t + 1], s_inv[:, t:t + 1])
            vB = P.mark("vector", "V")
            P.wait("scalar", "V", vB)
            P.emit("scalar", "sqrt", s_inv[:, t:t + 1], s_inv[:, t:t + 1])
            sC = P.mark("scalar", "S")
            P.wait("vector", "S", sC)
            P.emit("vector", "tensor_scalar", s_nmi[:, t:t + 1],
                   s_mean[:, t:t + 1], s_inv[:, t:t + 1], -1.0,
                   ALU.mult, ALU.mult)
            vD = P.mark("vector", "V")
            P.wait("scalar", "V", vD)
            P.wait("scalar", "S", v_FG.get(t - 2, 0))
            P.emit("scalar", "activation", ht, xt, AF.Identity,
                   bias=s_nmi[:, t:t + 1], scale=s_inv[:, t:t + 1])
            x_free_at[t] = P.mark("scalar", "S")
            if t + 4 < NT:
                P.wait("sync", "S", x_free_at[t])
                d_x[t + 4] = P.dma("sync", xtile[t % 4],
                                   x_t[128 * (t + 4):128 * (t + 5), :],
                                   sem="Lx%d" % (t % 4))
            if t + 2 < NT:
                sA_done[t + 2] = p1_A(t + 2)

            def FG_t(t=t, ht=ht, hb=hb):
                P.wait("vector", "S", x_free_at[t])
                P.wait("vector", "S", v_casa)
                P.emit("vector", "tensor_tensor", ht, ht, bc0, ALU.mult)
                P.emit("vector", "tensor_tensor", hb, ht, bc1, ALU.add)
                v_h[t] = P.mark("vector", "V")
                P.wait("tensor", "V", v_h[t])
                P.wait("tensor", "Lc", d_const)
                for grp in range(2):
                    res = tpos_res[grp]
                    pbf = tpos_bf[grp]
                    res.twait()
                    for j in range(8):
                        kt2 = grp * 8 + j
                        P.emit("tensor", "transpose",
                               pbf[:, 128 * j:128 * (j + 1)],
                               hb[:, 128 * kt2:128 * (kt2 + 1)], id_sb)
                    vt = P.mark("tensor", "T")
                    P.wait("scalar", "T", vt)
                    P.emit("scalar", "copy",
                           hT3[:, grp * 8:(grp + 1) * 8, 128 * t:128 * (t + 1)],
                           pbf.rearrange("p (a b) -> p a b", b=128))
                    vv = P.mark("scalar", "S")
                    res.release("S", vv)
                v_FG[t] = vv

            FGd[t] = FG_t
            if t >= 1:
                FGd[t - 1]()
        FGd[NT - 1]()
        v_hT_all_s = v_FG[NT - 1]
        P.wait("vector", "S", v_hT_all_s)
        v_hT_all = P.emit("vector", "nop", inc="V")

        mring = [B(O_F2 + 2048 * s, 2048) for s in range(4)]
        mm_free = {}
        rope_done = {}
        slot_i = 0
        v_rope_prev = 0
        d_tab_prev = 0

        tabc2 = [[Fv(O_F1, 1024), Fv(O_F1 + 2048, 1024)],
                 [Fv(O_F1 + 8192, 1024), Fv(O_F1 + 10240, 1024)]]
        rsc2 = [[Fv(O_F1 + 4096, 1024), Fv(O_F1 + 6144, 1024)],
                [Fv(O_F1 + 12288, 1024), Fv(O_F1 + 14336, 1024)]]
        hi = 0
        for side in range(2):
            wsrc = wq_m if side == 0 else wk_m
            for h in range(H):
                par = hi % 2
                tabc = tabc2[par]
                rsc = rsc2[par]
                q1t, q2t = (pA_t, pB_t) if par == 0 else (pC_t, pD_t)
                pun = [(pAl, pAh), (pBl, pBh)] if par == 0 else [(pCl, pCh), (pDl, pDh)]
                hi += 1
                P.wait("sync", "V", v_hT_all if hi <= 2 else rope_done.get(hi - 3, 0))
                d_cos = P.dma("sync", tabc[0], tabs[side, h, 0],
                              sem="Ltab%d" % par)
                d_sin = P.dma("sync", tabc[1], tabs[side, h, 1],
                              sem="Ltab%d" % par)
                vps = []
                for half in range(2):
                    m = 2 * h + half
                    s4 = slot_i % 4
                    P.wait("gpsimd", "T", mm_free.get(s4, 0))
                    if slot_i == 0:
                        P.wait("gpsimd", "S", x_free_at[NT - 1])
                    dsl = P.dma("gpsimd",
                                mring[s4].rearrange("p (a m) -> p a m", m=128),
                                wsrc[m].rearrange("(a p) m -> p a m", p=128),
                                sem="Lm%d" % s4)
                    ptgt = pun[half]
                    P.wait("tensor", "Lm%d" % s4, dsl)
                    P.wait("tensor", "V", v_hT_all)
                    for tch in range(2):
                        ptgt[tch].twait()
                        for kt2 in range(KT):
                            P.emit("tensor", "matmul", ptgt[tch].ap,
                                   mring[s4][:, 128 * kt2:128 * (kt2 + 1)],
                                   hT(kt2, 512 * tch, 512),
                                   start=(kt2 == 0), stop=(kt2 == KT - 1))
                    vp = P.mark("tensor", "T")
                    mm_free[s4] = vp
                    vps.append(vp)
                    slot_i += 1
                P.wait("vector", "T", vps[-1])
                P.wait("vector", "Ltab%d" % par, d_sin)
                q1 = q1t[:, 0:1024]
                q2 = q2t[:, 0:1024]
                segd = qseg if side == 0 else kseg
                P.emit("vector", "tensor_tensor", rsc[0], q1, tabc[0], ALU.mult)
                P.emit("vector", "tensor_tensor", rsc[1], q2, tabc[1], ALU.mult)
                for p in range(NT):
                    P.emit("vector", "tensor_tensor", segd(p, h, 0),
                           rsc[0][:, 128 * p:128 * (p + 1)],
                           rsc[1][:, 128 * p:128 * (p + 1)], ALU.subtract)
                P.emit("vector", "tensor_tensor", rsc[0], q2, tabc[0], ALU.mult)
                P.emit("vector", "tensor_tensor", rsc[1], q1, tabc[1], ALU.mult)
                for p in range(NT):
                    P.emit("vector", "tensor_tensor", segd(p, h, 1),
                           rsc[0][:, 128 * p:128 * (p + 1)],
                           rsc[1][:, 128 * p:128 * (p + 1)], ALU.add)
                v_rope_prev = P.mark("vector", "V")
                rope_done[hi - 1] = v_rope_prev
                for pp in pun[0] + pun[1]:
                    pp.release("V", v_rope_prev)
        v_rope_all = v_rope_prev

        vgslot = [B(O_F1 + 8192 * s, 8192) for s in range(2)]
        vg_free = {}
        vg_i = 0
        pcycle = [pAl, pAh, pBl, pBh]
        pc_i = 0
        v_g_ev = {}
        v_v_ev = {}
        P.wait("scalar", "Lmi", d_mi)

        def vg_proj(which, nchs=(0, 1, 2, 3), hook=None):
            nonlocal vg_i, pc_i
            wsrc = wv_n if which == 0 else wg_n
            for nch in nchs:
                s2 = vg_i % 2
                if vg_i < 2:
                    P.wait("gpsimd", "V", v_rope_all)
                else:
                    P.wait("gpsimd", "T", vg_free[s2])
                dsl = P.dma("gpsimd",
                            vgslot[s2].rearrange("p (a m) -> p a m", m=512),
                            wsrc[nch].rearrange("(a p) m -> p a m", p=128),
                            sem="Lvg%d" % s2)
                P.wait("tensor", "Lvg%d" % s2, dsl)
                for tt in range(NT):
                    pres = pcycle[pc_i % 4]
                    pc_i += 1
                    pres.twait()
                    for kt2 in range(KT):
                        P.emit("tensor", "matmul", pres.ap,
                               hT(kt2, 128 * tt, 128),
                               vgslot[s2][:, 512 * kt2:512 * (kt2 + 1)],
                               start=(kt2 == 0), stop=(kt2 == KT - 1))
                    vmm = P.mark("tensor", "T")
                    if tt == NT - 1:
                        vg_free[s2] = vmm
                    P.wait("scalar", "T", vmm)
                    if which == 0:
                        P.emit("scalar", "copy",
                               vtile(tt, 512 * nch, 512), pres.ap)
                        ev = P.mark("scalar", "S")
                        pres.release("S", ev)
                        v_v_ev[(nch, tt)] = ev
                        if hook is not None:
                            hook(tt)
                    else:
                        P.emit("scalar", "activation",
                               gtile(tt, 512 * nch, 512), pres.ap, AF.Silu)
                        ev = P.mark("scalar", "S")
                        pres.release("S", ev)
                        P.wait("vector", "S", ev)
                        P.wait("vector", "T", v_gnB_t)
                        P.emit("vector", "tensor_tensor",
                               gtile(tt, 512 * nch, 512),
                               gtile(tt, 512 * nch, 512), gnormB, ALU.mult)
                        v_g_ev[(nch, tt)] = P.mark("vector", "V")
                vg_i += 1

        def Mpar(dt8, par, n=512):
            return B((O_M if par % 2 == 0 else O_F2) + dt8 * 512, n)

        P.wait("vector", "Lch0", P.ctr["Lch0"])
        P.wait("vector", "Lch1", P.ctr["Lch1"])
        P.wait("vector", "T", max(bc_state["last_mm"].values()))
        P.emit("vector", "memset", B(O_M, 4096), 0.0)
        v_m0 = P.mark("vector", "V")
        kv_cycle = [pCh, pDh]
        kv_i = [0]
        lstate = {"v": v_m0}

        def emit_L_pair(p):
            pDl.twait()
            P.wait("tensor", "V", v_rope_all)
            for dt8 in range(8):
                h, half = dt8 // 2, dt8 % 2
                P.emit("tensor", "transpose", pT[:, 128 * dt8:128 * (dt8 + 1)],
                       kseg(p, h, half), id_sb)
            vt = P.mark("tensor", "T")
            P.wait("vector", "T", vt)
            for h in range(H):
                P.emit("vector", "tensor_scalar",
                       ktok[:, 256 * h:256 * (h + 1)],
                       pT[:, 256 * h:256 * (h + 1)], G128[h], None, ALU.mult)
            v_kt = P.mark("vector", "V")
            pDl.release("V", v_kt)
            for dt8 in range(8):
                h = dt8 // 2
                pres = kv_cycle[kv_i[0] % 2]
                kv_i[0] += 1
                pres.twait()
                P.wait("tensor", "V", v_kt)
                P.wait("tensor", "S", v_v_ev[(h, p)])
                P.emit("tensor", "matmul", pres.ap,
                       ktok[:, 128 * dt8:128 * (dt8 + 1)],
                       vtile(p, 512 * h, 512), start=True, stop=True)
                vmm = P.mark("tensor", "T")
                P.wait("vector", "T", vmm)
                P.wait("vector", "V", v_m0)
                P.emit("vector", "scalar_tensor_tensor", Mpar(dt8, 0),
                       Mpar(dt8, 0), G128[h], pres.ap, ALU.mult, ALU.add)
                lstate["v"] = P.mark("vector", "V")
                pres.release("V", lstate["v"])

        vg_proj(0, nchs=(0, 1, 2))
        vg_proj(0, nchs=(3,), hook=emit_L_pair)
        v_Lupd = lstate["v"]
        P.wait("gpsimd", "V", v_Lupd)
        d_li = P.dma("gpsimd", lg_i[:, :], B(O_M, 4096), sem="Lli")
        P.wait("gpsimd", "Lli", d_li)
        v_cc2 = P.emit("gpsimd", "collective_compute", "AllGather", ALU.bypass,
                       replica_groups=[list(range(NCORE))],
                       ins=[lg_i[:, :]], outs=[lg_o[:, :]], inc="C")

        P.wait("sync", "V", v_Lupd)
        d_gn2 = P.dma("sync", chrow[0], gnorm, sem="Lch0")
        P.wait("tensor", "Lch0", d_gn2)
        pCl.twait()
        P.emit("tensor", "matmul", pCl.ap, ones[0:1, :], chrow[0],
               start=True, stop=True)
        v_gnB_t = P.mark("tensor", "T")
        bc_state["last_mm"][0] = v_gnB_t
        gnormB = pCl.ap

        vg_proj(1)
        v_vg_all = v_g_ev[(3, NT - 1)]
        v_gp = v_vg_all
        pCl.release("V", v_gp)

        woslot = [B(O_VB + 8192 * g, 8192) for g in range(4)]
        d_wo = {}
        for g in (2, 3):
            P.wait("gpsimd", "T", vg_free[g - 2])
            d_wo[g] = P.dma("gpsimd",
                            woslot[g].rearrange("p (a m) -> p a m", m=2048),
                            wo_g[g].rearrange("(a p) m -> p a m", p=128),
                            sem="Lwo")

        garow = Fv(O_HT + 11264, 2048, 0, 1)
        v_gaB = bcast_batch([bc0[:, 512 * j:512 * (j + 1)] for j in range(4)],
                            8, garow, gates=(("T", vg_free[1]),),
                            row_math=gamath_w)
        v_sbB = bcast_batch([bc1[:, 512 * j:512 * (j + 1)] for j in range(4)],
                            12, garow, gates=(("T", vg_free[1]),))

        lrecv = [B(O_F2 + 4096 * s, 4096) for s in range(2)]
        sacc = Fv(O_HT, 4096)
        d_lr = {}
        v_lr_used = {}
        P.wait("gpsimd", "C", v_cc2)
        P.wait("sync", "C", v_cc2)
        d_lr[0] = P.dma("gpsimd", lrecv[0], lg_o[0:128, :], sem="Llr0")
        for j in range(NCORE):
            if j + 1 < NCORE:
                q = "gpsimd" if (j + 1) % 2 == 0 else "sync"
                P.wait(q, "V", v_lr_used.get(j - 1, 0))
                d_lr[j + 1] = P.dma(q, lrecv[(j + 1) % 2],
                                    lg_o[128 * (j + 1):128 * (j + 2), :],
                                    sem="Llr%d" % ((j + 1) % 2))
            P.wait("vector", "Llr%d" % (j % 2), d_lr[j])
            if j == NCORE - 1:
                P.wait("vector", "Lli", d_li)
            for h in range(H):
                wcol = sinw_sb[:, 8 * h + j:8 * h + j + 1]
                srcw = lrecv[j % 2][:, 1024 * h:1024 * (h + 1)]
                acc4 = sacc[:, 1024 * h:1024 * (h + 1)]
                if j == 0:
                    P.emit("vector", "tensor_scalar", acc4, srcw,
                           wcol, None, ALU.mult)
                elif j == NCORE - 1:
                    P.emit("vector", "scalar_tensor_tensor",
                           Mpar(2 * h, 0, 1024), srcw, wcol, acc4,
                           ALU.mult, ALU.add)
                else:
                    P.emit("vector", "scalar_tensor_tensor", acc4, srcw,
                           wcol, acc4, ALU.mult, ALU.add)
            v_lr_used[j] = P.mark("vector", "V")
        v_sin = v_lr_used[NCORE - 1]

        attn_sb2 = [attn_sb, attn_sbB]
        v_Mupd = {-1: v_sin}
        v_ob = {}
        at_cycle = [pAl, pBl]
        kv_cycle2 = [pAh, pBh]
        o_cycle = [pCl, pCh, pDh]
        odo = {}
        for p in range(NT):
            rpar, wpar = p % 2, (p + 1) % 2
            pDl.twait()
            for dt8 in range(8):
                h, half = dt8 // 2, dt8 % 2
                P.emit("tensor", "transpose", pT[:, 128 * dt8:128 * (dt8 + 1)],
                       kseg(p, h, half), id_sb)
            vt = P.mark("tensor", "T")
            P.wait("vector", "T", vt)
            for h in range(H):
                P.emit("vector", "tensor_scalar",
                       ktok[:, 256 * h:256 * (h + 1)],
                       pT[:, 256 * h:256 * (h + 1)], G128[h], None, ALU.mult)
            v_kt = P.mark("vector", "V")
            pDl.release("V", v_kt)
            amarks = {}
            for h in range(H):
                at = at_cycle[h % 2]
                at.twait()
                P.emit("tensor", "matmul", at.ap[:, 0:128], kseg(p, h, 0),
                       qseg(p, h, 0), start=True, stop=False)
                P.emit("tensor", "matmul", at.ap[:, 0:128], kseg(p, h, 1),
                       qseg(p, h, 1), start=False, stop=True)
                v_at = P.mark("tensor", "T")
                P.wait("vector", "T", v_at)
                P.emit("vector", "tensor_tensor", attn_sb4[(p % 2) * 4 + h],
                       at.ap[:, 0:128], cau_sb, ALU.mult)
                amarks[h] = P.mark("vector", "V")
                at.release("V", amarks[h])

            def emit_o(p=p, rpar=rpar, amarks=amarks):
                P.wait("tensor", "V", v_Mupd[p - 1])
                for h in range(H):
                    op = o_cycle[h % 3]
                    op.twait()
                    P.wait("tensor", "V", amarks[h])
                    P.emit("tensor", "matmul", op.ap, attn_sb4[(p % 2) * 4 + h],
                           vtile(p, 512 * h, 512), start=True, stop=False)
                    P.emit("tensor", "matmul", op.ap, qseg(p, h, 0),
                           Mpar(2 * h, rpar), start=False, stop=False)
                    P.emit("tensor", "matmul", op.ap, qseg(p, h, 1),
                           Mpar(2 * h + 1, rpar), start=False, stop=True)
                    v_omm = P.mark("tensor", "T")
                    P.wait("scalar", "T", v_omm)
                    P.emit("scalar", "copy", obv(p, 512 * h, 512), op.ap)
                    v_ob[(p, h)] = P.mark("scalar", "S")
                    op.release("S", v_ob[(p, h)])

            odo[p] = emit_o
            if p >= 1:
                odo[p - 1]()
            vm = None
            for dt8 in range(8):
                h = dt8 // 2
                pres = kv_cycle2[dt8 % 2]
                pres.twait()
                P.wait("tensor", "V", v_kt)
                if p == 0:
                    P.wait("tensor", "S", v_v_ev[(h, p)])
                P.emit("tensor", "matmul", pres.ap,
                       ktok[:, 128 * dt8:128 * (dt8 + 1)],
                       vtile(p, 512 * h, 512), start=True, stop=True)
                vkv = P.mark("tensor", "T")
                P.wait("vector", "T", vkv)
                P.emit("vector", "scalar_tensor_tensor", Mpar(dt8, wpar),
                       Mpar(dt8, rpar), G128[h], pres.ap, ALU.mult, ALU.add)
                vm = P.mark("vector", "V")
                pres.release("V", vm)
            v_Mupd[p] = vm
        odo[NT - 1]()

        def tap(name, off, ncols, sem, val):
            if name in dbg_out:
                P.wait("sync", sem, val)
                P.wait("sync", "Ly", P.ctr["Ly"])
                d = P.dma("sync", dbg_out[name], B(off, ncols), sem="Ly")
                for eng in ("vector", "scalar", "tensor", "gpsimd"):
                    P.wait(eng, "Ly", d)

        tap("hT", O_HT, 16384, "V", v_hT_all)
        tap("gp", O_GP, 16384, "V", v_gp)
        v_og = {}
        RSb2 = Fv(O_HT + 9216, 512)

        def p3b_sq(p):
            if p == 0:
                P.wait("scalar", "V", v_gp)
            P.wait("scalar", "V", v_og.get(p - 2, 0))
            for h in range(H):
                P.wait("scalar", "S", v_ob[(p, h)])
                P.emit("scalar", "activation", RSb2,
                       obv(p, 512 * h, 512), AF.Square,
                       accum_out=s_oiv[:, (p % 2) * 4 + h:(p % 2) * 4 + h + 1])
            return P.mark("scalar", "S")

        sqd = {0: p3b_sq(0)}
        for p in range(NT):
            c4 = s_oiv[:, (p % 2) * 4:(p % 2) * 4 + 4]
            if p + 1 < NT:
                sqd[p + 1] = p3b_sq(p + 1)
            P.wait("vector", "S", sqd[p])
            P.emit("vector", "tensor_scalar", c4, c4, 1.0 / DV, EPS,
                   ALU.mult, ALU.add)
            P.emit("vector", "reciprocal", c4, c4)
            vv = P.mark("vector", "V")
            P.wait("scalar", "V", vv)
            P.emit("scalar", "sqrt", c4, c4)
            vs2 = P.mark("scalar", "S")
            P.wait("vector", "S", vs2)
            for h in range(H):
                P.emit("vector", "scalar_tensor_tensor", obv(p, 512 * h, 512),
                       obv(p, 512 * h, 512),
                       s_oiv[:, (p % 2) * 4 + h:(p % 2) * 4 + h + 1],
                       gtile(p, 512 * h, 512), ALU.mult, ALU.mult)
            v_og[p] = P.mark("vector", "V")
        v_og_all = v_og[NT - 1]


        wtmp = Fv(O_F2, 512)
        v_ret_done = v_Mupd[NT - 1]
        for g in (0, 1):
            P.wait("gpsimd", "S", v_ob[(NT - 1, H - 1)])
            d_wo[g] = P.dma("gpsimd",
                            woslot[g].rearrange("p (a m) -> p a m", m=2048),
                            wo_g[g].rearrange("(a p) m -> p a m", p=128),
                            sem="Lwo")

        d_x2 = {}
        P.wait("sync", "V", v_og_all)
        for tt in range(NT):
            d_x2[tt] = P.dma("sync", accv(tt, 0, 2048),
                             x_t[128 * tt:128 * (tt + 1), :], sem="Lxa")
        d_x2_all = d_x2[NT - 1]

        v_wo_used = {}
        wo_cycle = [pAl, pAh, pBl, pBh]
        wo_i = 0
        v_acc_wo = {}
        for tt in range(NT):
            P.wait("tensor", "V", v_og[tt])
            for grp in range(2):
                pDl.twait()
                for j in range(8):
                    kt2 = grp * 8 + j
                    P.emit("tensor", "transpose", pT[:, 128 * j:128 * (j + 1)],
                           obv(tt, 128 * kt2, 128), id_sb)
                vt = P.mark("tensor", "T")
                P.wait("vector", "T", vt)
                P.emit("vector", "tensor_copy",
                       ogT[:, 1024 * grp:1024 * (grp + 1)], pT)
                v_tp = P.mark("vector", "V")
                pDl.release("V", v_tp)
            P.wait("tensor", "V", v_tp)
            for g in range(4):
                P.wait("tensor", "Lwo", d_wo[1])
                for nch in range(4):
                    pres = wo_cycle[wo_i % 4]
                    wo_i += 1
                    pres.twait()
                    for kg in range(4):
                        P.emit("tensor", "matmul", pres.ap,
                               ogT[:, 128 * (g * 4 + kg):128 * (g * 4 + kg + 1)],
                               woslot[g][:, 2048 * kg + 512 * nch:2048 * kg + 512 * (nch + 1)],
                               start=(kg == 0), stop=(kg == 3))
                    vmm = P.mark("tensor", "T")
                    if nch == 3 and tt == NT - 1:
                        v_wo_used[g] = vmm
                    P.wait("vector", "T", vmm)
                    P.wait("vector", "Lxa", d_x2_all)
                    if tt == 0 and g == 0 and nch == 0:
                        P.wait("vector", "S", v_gaB)
                    P.emit("vector", "tensor_tensor", wtmp, pres.ap,
                           bc0[:, 512 * nch:512 * (nch + 1)], ALU.mult)
                    P.emit("vector", "tensor_tensor",
                           accv(tt, 512 * nch, 512),
                           accv(tt, 512 * nch, 512), wtmp, ALU.add)
                    va = P.mark("vector", "V")
                    pres.release("V", va)
            v_acc_wo[tt] = va

        tap("og", O_QKOB, 16384, "V", v_og_all)
        if "x2" in dbg_out:
            P.wait("sync", "V", v_acc_wo[NT - 1])
            P.wait("sync", "Ly", P.ctr["Ly"])
            d = P.dma("sync", dbg_out["x2"], Fv(0, 16384), sem="Ly")
            for eng in ("vector", "scalar", "tensor", "gpsimd"):
                P.wait(eng, "Ly", d)
        cbrow = Fv(O_M, 2048, 0, 1)
        v_cbsb = bcast_batch([bc0[:, 512 * j:512 * (j + 1)] for j in range(4)],
                             16, cbrow, gates=(("V", v_acc_wo[NT - 1]),),
                             add1=True)

        gate_bf = B(O_F2 + 2048, 512)
        P.wait("gpsimd", "T", v_wo_used[3])
        d_gate = P.dma("gpsimd",
                       gate_bf.rearrange("p (a m) -> p a m", m=32),
                       gatet.rearrange("(a p) m -> p a m", p=128), sem="Lgate")

        v_h2T = {}
        v_h2_done = {}

        LGb = Fv(O_F2 + 4608, 256)
        RPb = Fv(O_F2 + 5632, 256)
        RCb = Fv(O_F2 + 6656, 256)
        RSb = Fv(O_M + 2048, 256)
        MKS = Fv(O_M + 3072, 64)
        CNTS = Fv(O_M + 3200, 64)
        R8a = Fv(O_M + 3328, 8)
        R8b = Fv(O_M + 3344, 8)
        R8c = Fv(O_M + 3360, 8)

        def ln2_A(t):
            xt = accv(t, 0, 2048)
            P.wait("vector", "V", v_acc_wo[t])
            P.emit("vector", "reduce_sum", s_mean[:, t:t + 1], xt, axis=AX.X)
            P.wait("scalar", "V", v_acc_wo[t])
            if t == 0:
                P.wait("scalar", "T", v_wo_used[3])
            P.emit("scalar", "activation", sq_scr, xt, AF.Square,
                   accum_out=s_ssq[:, t:t + 1])
            return P.mark("scalar", "S")

        sA2 = {0: ln2_A(0)}
        sA2[1] = ln2_A(1)
        for t in range(NT):
            ht, hb = htmp[t % 2], hbuf[t % 2]
            xt = accv(t, 0, 2048)
            P.wait("vector", "S", sA2[t])
            P.emit("vector", "tensor_scalar", s_mean[:, t:t + 1],
                   s_mean[:, t:t + 1], 1.0 / D, None, ALU.mult)
            P.emit("vector", "tensor_tensor", s_t1[:, t:t + 1],
                   s_mean[:, t:t + 1], s_mean[:, t:t + 1], ALU.mult)
            P.emit("vector", "tensor_scalar", s_inv[:, t:t + 1],
                   s_ssq[:, t:t + 1], 1.0 / D, None, ALU.mult)
            P.emit("vector", "tensor_tensor", s_inv[:, t:t + 1],
                   s_inv[:, t:t + 1], s_t1[:, t:t + 1], ALU.subtract)
            P.emit("vector", "tensor_scalar", s_inv[:, t:t + 1],
                   s_inv[:, t:t + 1], EPS, None, ALU.add)
            P.emit("vector", "reciprocal", s_inv[:, t:t + 1], s_inv[:, t:t + 1])
            vB = P.mark("vector", "V")
            P.wait("scalar", "V", vB)
            P.emit("scalar", "sqrt", s_inv[:, t:t + 1], s_inv[:, t:t + 1])
            sC = P.mark("scalar", "S")
            P.wait("vector", "S", sC)
            P.emit("vector", "tensor_scalar", s_nmi[:, t:t + 1],
                   s_mean[:, t:t + 1], s_inv[:, t:t + 1], -1.0,
                   ALU.mult, ALU.mult)
            vD = P.mark("vector", "V")
            P.wait("scalar", "V", vD)
            P.emit("scalar", "activation", ht, xt, AF.Identity,
                   bias=s_nmi[:, t:t + 1], scale=s_inv[:, t:t + 1])
            v_aff = P.mark("scalar", "S")
            if t + 2 < NT:
                sA2[t + 2] = ln2_A(t + 2)
            P.wait("vector", "S", v_aff)
            P.wait("vector", "S", v_cbsb)
            P.wait("vector", "S", v_sbB)
            P.emit("vector", "tensor_tensor", ht, ht, bc0, ALU.mult)
            P.emit("vector", "tensor_tensor", hb, ht, bc1, ALU.add)
            v_h2 = P.mark("vector", "V")
            v_h2_done[t] = v_h2
            P.wait("tensor", "V", v_h2)
            for grp in range(2):
                pDl.twait()
                for j in range(8):
                    kt2 = grp * 8 + j
                    P.emit("tensor", "transpose", pT[:, 128 * j:128 * (j + 1)],
                           hb[:, 128 * kt2:128 * (kt2 + 1)], id_sb)
                vt = P.mark("tensor", "T")
                P.wait("scalar", "T", vt)
                P.emit("scalar", "copy",
                       h2T3[:, grp * 8:(grp + 1) * 8, 128 * t:128 * (t + 1)],
                       pT.rearrange("p (a b) -> p a b", b=128))
                vv = P.mark("scalar", "S")
                pDl.release("S", vv)
            v_h2T[t] = vv
            pCl.twait()
            P.wait("tensor", "Lgate", d_gate)
            P.wait("tensor", "S", vv)
            for kt2 in range(KT):
                P.emit("tensor", "matmul", pC_t[:, 0:32],
                       h2T(kt2, 128 * t, 128), gate_bf[:, 32 * kt2:32 * (kt2 + 1)],
                       start=(kt2 == 0), stop=(kt2 == KT - 1))
            v_lg = P.mark("tensor", "T")
            v_lg_mm_last = v_lg
            P.wait("scalar", "T", v_lg)
            P.emit("scalar", "copy", LGb[:, 32 * t:32 * (t + 1)], pC_t[:, 0:32])
            v_lgc = P.mark("scalar", "S")
            pCl.release("S", v_lgc)
        v_lg_all = v_lgc

        tap("h2T", O_QKOB, 16384, "V", v_h2T[NT - 1])
        rgrow = Fv(O_F2, 2048, 0, 1)
        v_rgb = bcast_batch([bc1[:, 512 * j:512 * (j + 1)] for j in range(4)],
                            20, rgrow, gates=(("V", v_h2_done[NT - 1]),
                                              ("T", v_lg_mm_last)),
                            row_math=rgmath_w)
        P.wait("vector", "S", v_rgb)
        v_accr = {}
        for tt in range(NT):
            P.emit("vector", "tensor_tensor", accv(tt, 0, 2048),
                   accv(tt, 0, 2048), bc1, ALU.mult)
            v_accr[tt] = P.mark("vector", "V")

        def seg(ap):
            return ap.rearrange("p (a b) -> p a b", b=32)

        def col8(ap):
            return ap.rearrange("p (a b) -> p a b", b=1)

        def bc8(ap):
            return col8(ap).broadcast_to([128, 8, 32])

        def segred(dst8, src256, op):
            P.emit("vector", "tensor_reduce", col8(dst8), seg(src256),
                   axis=AX.X, op=op)

        MKS3 = MKS.rearrange("p (a b) -> p a b", b=8)
        P.wait("vector", "S", v_lg_all)
        segred(R8a, LGb, ALU.max)
        P.emit("vector", "tensor_tensor", seg(RPb), seg(LGb), bc8(R8a),
               ALU.subtract)
        v_sm = P.mark("vector", "V")
        P.wait("scalar", "V", v_sm)
        P.emit("scalar", "activation", RPb, RPb, AF.Exp)
        v_exp = P.mark("scalar", "S")
        P.wait("vector", "S", v_exp)
        segred(R8a, RPb, ALU.add)
        P.emit("vector", "reciprocal", R8a, R8a)
        P.emit("vector", "tensor_tensor", seg(RPb), seg(RPb), bc8(R8a),
               ALU.mult)
        P.emit("vector", "tensor_copy", RCb, RPb)
        for k in range(8):
            P.emit("vector", "tensor_reduce", MKS3[:, :, k:k + 1], seg(RCb),
                   axis=AX.X, op=ALU.max)
            P.emit("vector", "tensor_tensor", seg(RSb), seg(RCb),
                   MKS3[:, :, k:k + 1].broadcast_to([128, 8, 32]), ALU.is_equal)
            P.emit("vector", "scalar_tensor_tensor", RCb, RSb, -1e30, RCb,
                   ALU.mult, ALU.add)
        for k in range(8):
            P.emit("vector", "tensor_tensor", seg(RSb), seg(RPb),
                   MKS3[:, :, k:k + 1].broadcast_to([128, 8, 32]), ALU.is_ge)
            P.emit("vector", "tensor_reduce",
                   CNTS.rearrange("p (a b) -> p a b", b=8)[:, :, k:k + 1],
                   seg(RSb), axis=AX.X, op=ALU.add)
        P.emit("vector", "tensor_scalar", CNTS, CNTS, 8.0, None, ALU.is_ge)
        P.emit("vector", "tensor_tensor", CNTS, CNTS, MKS, ALU.mult)
        P.emit("vector", "tensor_reduce", col8(R8b),
               CNTS.rearrange("p (a b) -> p a b", b=8), axis=AX.X, op=ALU.max)
        P.emit("vector", "tensor_tensor", seg(RCb), seg(RPb), bc8(R8b),
               ALU.is_ge)
        P.emit("vector", "tensor_tensor", seg(RSb), seg(RPb), seg(RCb),
               ALU.mult)
        segred(R8b, RSb, ALU.add)
        P.emit("vector", "reciprocal", R8b, R8b)
        P.wait("vector", "S", v_alb)
        P.emit("vector", "tensor_scalar", R8b, R8b, albB, None, ALU.mult)
        P.emit("vector", "tensor_tensor", seg(RSb), seg(RSb), bc8(R8b),
               ALU.mult)
        P.emit("vector", "tensor_scalar", RSb, RSb, TAU, None, ALU.is_ge)
        segred(R8b, RSb, ALU.max)
        P.emit("vector", "tensor_scalar", R8b, R8b, 0.0, None, ALU.is_equal)
        segred(R8c, RPb, ALU.max)
        P.emit("vector", "tensor_tensor", seg(RCb), seg(RPb), bc8(R8c),
               ALU.is_equal)
        P.emit("vector", "tensor_tensor", seg(RCb), seg(RCb), bc8(R8b),
               ALU.mult)
        P.emit("vector", "tensor_tensor", RSb, RSb, RCb, ALU.max)
        P.emit("vector", "tensor_tensor", RSb, RPb, RSb, ALU.mult)
        segred(R8c, RSb, ALU.add)
        P.emit("vector", "tensor_scalar", R8c, R8c, 1e-12, None, ALU.max)
        P.emit("vector", "reciprocal", R8c, R8c)
        P.emit("vector", "tensor_tensor", seg(comb[:, 0:256]), seg(RSb),
               bc8(R8c), ALU.mult)
        v_comb_all = P.mark("vector", "V")
        if "comb" in dbg_out:
            P.wait("sync", "V", v_comb_all)
            P.wait("sync", "Ly", P.ctr["Ly"])
            d = P.dma("sync", dbg_out["comb"], comb, sem="Ly")
            for eng in ("vector", "scalar", "tensor", "gpsimd"):
                P.wait(eng, "Ly", d)

        gu_sp = [B(O_F1 + 4096 * s, 4096) for s in range(2)]
        heT = B(O_F1 + 8192, 6144)
        sgb = [B(O_F1 + 14336, 512), B(O_F1 + 14848, 512)]
        dn_s = [B(O_VB + 3072 * s, 3072) for s in range(4)]

        gu_used = {}
        dn_used = {}
        d_gu = {}
        d_dn = {}

        def prefetch(e, pr):
            idx = e * 6 + pr
            P.wait("gpsimd", "T", gu_used.get(idx - 2, 0))
            if idx == 0:
                P.wait("gpsimd", "T", v_wo_used[2])
                P.wait("gpsimd", "S", v_h2T[NT - 1])
            d_gu[idx] = P.dma("gpsimd",
                              gu_sp[idx % 2].rearrange("p (b a m) -> p b a m", b=2, m=128),
                              gup[e, pr].rearrange("b (a p) m -> p b a m", p=128),
                              sem="Lgu%d" % (idx % 2))

        def prefetch_dn(e, nch):
            idx = e * 4 + nch
            P.wait("gpsimd", "T", dn_used.get(idx - 4, 0))
            if idx == 0:
                P.wait("gpsimd", "T", v_wo_used[0])
                P.wait("gpsimd", "T", v_wo_used[1])
            d_dn[idx] = P.dma("gpsimd",
                              dn_s[idx % 4].rearrange("p (a m) -> p a m", m=512),
                              dnp[e, nch].rearrange("(a p) m -> p a m", p=128),
                              sem="Ldn%d" % (idx % 4))

        for pr in range(2):
            prefetch(0, pr)
        for nch in range(4):
            prefetch_dn(0, nch)

        v_last_acc = None
        pe_cycle = [(pAl, pAh), (pBl, pBh), (pCl, pCh), (pDl, pDh)]
        v_he = {}
        sg_busy = {}
        for e in range(NEXP):
            pe_i = 0
            for pr in range(6):
                idx = e * 6 + pr
                sl = gu_sp[idx % 2]
                P.wait("tensor", "Lgu%d" % (idx % 2), d_gu[idx])
                if e == 0 and pr == 0:
                    P.wait("tensor", "S", v_h2T[NT - 1])
                for tch in range(2):
                    pg, pu = pe_cycle[pe_i % 4]
                    pe_i += 1
                    pg.twait()
                    pu.twait()
                    P.wait("tensor", "V", v_he.get((e - 1, pr, tch), 0))
                    for kt2 in range(KT):
                        P.emit("tensor", "matmul", pg.ap,
                               sl[:, 128 * kt2:128 * (kt2 + 1)],
                               h2T(kt2, 512 * tch, 512),
                               start=(kt2 == 0), stop=(kt2 == KT - 1))
                    for kt2 in range(KT):
                        P.emit("tensor", "matmul", pu.ap,
                               sl[:, 2048 + 128 * kt2:2048 + 128 * (kt2 + 1)],
                               h2T(kt2, 512 * tch, 512),
                               start=(kt2 == 0), stop=(kt2 == KT - 1))
                    vmm = P.mark("tensor", "T")
                    if tch == 1:
                        gu_used[idx] = vmm
                        if pr + 2 < 6:
                            prefetch(e, pr + 2)
                        elif e + 1 < NEXP:
                            prefetch(e + 1, pr + 2 - 6)
                    P.wait("scalar", "T", vmm)
                    sgi = pe_i % 2
                    sg = sgb[sgi]
                    P.wait("scalar", "V", sg_busy.get(sgi, 0))
                    P.emit("scalar", "activation", sg, pg.ap, AF.Silu)
                    vsg = P.mark("scalar", "S")
                    P.wait("vector", "S", vsg)
                    P.emit("vector", "tensor_tensor",
                           heT[:, 1024 * pr + 512 * tch:1024 * pr + 512 * (tch + 1)],
                           sg, pu.ap, ALU.mult)
                    vhe = P.mark("vector", "V")
                    sg_busy[sgi] = vhe
                    v_he[(e, pr, tch)] = vhe
                    pg.release("V", vhe)
                    pu.release("V", vhe)
            if e == NEXP - 1:
                v_gbB = bcast_batch(
                    [bc0[:, 512 * j:512 * (j + 1)] for j in range(4)],
                    20, rgrow, gates=(("V", v_h2_done[NT - 1]),),
                    row_math=gamath_w_late)
                d_y = {}
            dn_cycle = [pAl, pAh, pBl, pBh]
            pe_i = 0
            for tt in range(NT):
                for nch in range(4):
                    idx = e * 4 + nch
                    pres = dn_cycle[pe_i % 4]
                    pe_i += 1
                    pres.twait()
                    P.wait("tensor", "Ldn%d" % (idx % 4), d_dn[idx])
                    if tt == 0:
                        for pr in range(6):
                            P.wait("tensor", "V", v_he[(e, pr, 0)])
                            P.wait("tensor", "V", v_he[(e, pr, 1)])
                    for kt2 in range(6):
                        P.emit("tensor", "matmul", pres.ap,
                               heT[:, 1024 * kt2 + 128 * tt:1024 * kt2 + 128 * (tt + 1)],
                               dn_s[idx % 4][:, 512 * kt2:512 * (kt2 + 1)],
                               start=(kt2 == 0), stop=(kt2 == 5))
                    vmm = P.mark("tensor", "T")
                    if tt == NT - 1:
                        dn_used[idx] = vmm
                        if e + 1 < NEXP:
                            prefetch_dn(e + 1, nch)
                    P.wait("vector", "T", vmm)
                    if e == 0:
                        P.wait("vector", "V", v_accr[tt])
                        P.wait("vector", "V", v_comb_all)
                    P.emit("vector", "scalar_tensor_tensor",
                           accv(tt, 512 * nch, 512), pres.ap,
                           comb[:, 32 * tt + e:32 * tt + e + 1],
                           accv(tt, 512 * nch, 512), ALU.mult, ALU.add)
                    va = P.mark("vector", "V")
                    pres.release("V", va)
                    if e == NEXP - 1 and nch == 3:
                        P.wait("vector", "S", v_gbB)
                        P.emit("vector", "tensor_tensor",
                               accv(tt, 0, 2048), accv(tt, 0, 2048),
                               bc0, ALU.mult)
                        vo = P.mark("vector", "V")
                        P.wait("sync", "V", vo)
                        d_y[tt] = P.dma("sync",
                                        y[128 * tt:128 * (tt + 1), :],
                                        accv(tt, 0, 2048), sem="Ly")
            v_last_acc = va

        P.wait("sync", "Ly", d_y[NT - 1])

        @block.tensor
        def _(e):
            for f in P.ops["tensor"]:
                f(e)

        @block.vector
        def _(e):
            for f in P.ops["vector"]:
                f(e)

        @block.scalar
        def _(e):
            for f in P.ops["scalar"]:
                f(e)

        @block.gpsimd
        def _(e):
            for f in P.ops["gpsimd"]:
                f(e)

        @block.sync
        def _(e):
            for f in P.ops["sync"]:
                f(e)

    return nc


_cache = {}


def _prep(inputs):
    f32 = np.float32
    x = np.ascontiguousarray(np.asarray(inputs["L0_attn_x"], f32).reshape(T, D))
    wq = np.asarray(inputs["L0_attn_Wq"], f32)
    wk = np.asarray(inputs["L0_attn_Wk"], f32)
    wv = np.asarray(inputs["L0_attn_Wv"], f32)
    wg = np.asarray(inputs["L0_attn_Wg"], f32)
    wo = np.asarray(inputs["L0_attn_Wo"], f32)
    gnw = np.asarray(inputs["L0_attn_gnorm_w"], f32).reshape(1, 512)
    gwt = np.asarray(inputs["L0_ffn_gate_weight"], f32)
    gu = np.asarray(inputs["L0_ffn_gate_up_proj"], f32)
    dn = np.asarray(inputs["L0_ffn_down_proj"], f32)
    alpha = np.asarray(inputs["L0_ffn_layer_alpha"], f32).reshape(1, 1)
    cv = np.asarray(inputs["L0_c"], f32).reshape(-1)
    wmod = np.asarray(inputs["L0_wmod"], f32)

    wq_m = np.ascontiguousarray(wq.T.reshape(D, 8, 128).transpose(1, 0, 2))
    wk_m = np.ascontiguousarray(wk.T.reshape(D, 8, 128).transpose(1, 0, 2))
    wv_n = np.ascontiguousarray(wv.T.reshape(D, 4, 512).transpose(1, 0, 2))
    wg_n = np.ascontiguousarray(wg.T.reshape(D, 4, 512).transpose(1, 0, 2))
    wo_g = np.ascontiguousarray(wo.T.reshape(4, 512, D))
    gu_t = gu.transpose(0, 2, 1).reshape(NEXP, D, 12, 128).transpose(0, 2, 1, 3)
    gup = np.ascontiguousarray(
        gu_t.reshape(NEXP, 2, 6, D, 128).transpose(0, 2, 1, 3, 4))
    dnp = np.ascontiguousarray(
        dn.transpose(0, 2, 1).reshape(NEXP, INTER, 4, 512).transpose(0, 2, 1, 3))
    gatet = np.ascontiguousarray(gwt.T)
    c_cols = np.ascontiguousarray(cv.reshape(16, 128).T)

    tglob = np.arange(T, dtype=f32)
    invf = (1.0 / (10000.0 ** (np.arange(0, DK, 2, dtype=f32) / DK))).astype(f32)
    freqs = (tglob[:, None] * invf[None, :]).astype(f32)
    cosb = np.cos(freqs).astype(f32).T
    sinb = np.sin(freqs).astype(f32).T
    ip = np.arange(T, dtype=np.float64) % 128
    tabs_all = np.zeros((NCORE, 2, H, 2, 128, TLOC), f32)
    for h in range(H):
        qs = (np.float64(GAMMA[h]) ** (ip + 1) * DK ** -0.5)
        ks = (np.float64(GAMMA[h]) ** (-ip - 1))
        for c in range(NCORE):
            sl = slice(TLOC * c, TLOC * (c + 1))
            tabs_all[c, 0, h, 0] = cosb[:, sl] * qs[None, sl]
            tabs_all[c, 0, h, 1] = sinb[:, sl] * qs[None, sl]
            tabs_all[c, 1, h, 0] = cosb[:, sl] * ks[None, sl]
            tabs_all[c, 1, h, 1] = sinb[:, sl] * ks[None, sl]

    identity = np.eye(128, dtype=BFNP)
    ii = np.arange(128)
    causal = (ii[None, :] >= ii[:, None]).astype(BFNP)

    sinw_all = np.zeros((NCORE, 128, 32), f32)
    for c in range(NCORE):
        for h in range(H):
            for j in range(NCORE):
                if j < c:
                    sinw_all[c, :, 8 * h + j] = np.float64(GAMMA[h]) ** (1024.0 * (c - 1 - j))

    per_core = []
    for c in range(NCORE):
        m = dict(
            x_t=np.ascontiguousarray(x[TLOC * c:TLOC * (c + 1)]),
            wq_m=wq_m, wk_m=wk_m, wv_n=wv_n, wg_n=wg_n, wo_g=wo_g,
            gup=gup, dnp=dnp, gatet=gatet,
            wmod_s=np.ascontiguousarray(
                wmod[:, 1536 * c:1536 * (c + 1)].reshape(KT, 128, 1536)),
            c_cols=c_cols, gnorm=gnw, alpha2=alpha,
            tabs=np.ascontiguousarray(tabs_all[c]),
            ident=identity, causalT=causal,
            sinw=np.ascontiguousarray(sinw_all[c]),
        )
        per_core.append(m)
    return per_core


def kernel(**inputs):
    per_core = _prep(inputs)
    if "nc" not in _cache:
        _cache["nc"] = build()
    nc = _cache["nc"]
    res = run_bass_kernel_spmd(nc, per_core, core_ids=list(range(NCORE)))
    out = np.concatenate([np.asarray(r["y"]) for r in res.results], axis=0)
    return out.reshape(1, T, D).astype(np.float32)


